# version 01
# baseline (speedup 1.0000x reference)
import numpy as np
import ml_dtypes
import concourse.bass as bass
import concourse.mybir as mybir
from concourse.bass import IndirectOffsetOnAxis
from concourse.bass_utils import run_bass_kernel_spmd

F32 = mybir.dt.float32
BF16 = mybir.dt.bfloat16
I32 = mybir.dt.int32
AX = mybir.AxisListType.X
AF = mybir.ActivationFunctionType
AL = mybir.AluOpType

NCORE = 8
T = 2048
C = 1024
NSD = 256
HF = 2048
CAP = 2048
C2 = 32
EPS = 1e-5
NBIS = 30

DBG_STAGE = None


def build(dbg=None):
    nc = bass.Bass("TRN2", target_bir_lowering=False)

    def din(name, shape, dt=F32):
        return nc.dram_tensor(name, shape, dt, kind="ExternalInput").ap()

    x_tok = din("x_tok", [T, C])
    x_feat = din("x_feat", [C, T])
    wd_in = din("wd", [2, C, NSD])
    wi_in = din("wi", [2, C, NSD])
    wo_in = din("wo", [2, NSD, C])
    ws_in = din("ws", [2, NSD, C])
    bd_in = din("bdd", [2, NSD])
    bi_in = din("bii", [2, NSD])
    alog_in = din("alog", [2, NSD])
    bo_in = din("bo", [2, C])
    bs_in = din("bss", [2, C])
    ext_in = din("ext", [8, 128, 144])
    wgt_in = din("wgt", [C, 8])
    w1t_in = din("w1t", [C, HF])
    w3t_in = din("w3t", [C, HF])
    w2t_in = din("w2t", [HF, C])
    kv_in = din("kvi", [128, 2])
    eab_in = din("eabi", [128, 1], I32)
    rot_in = din("rot", [1, 8], I32)
    ohp_in = din("ohpi", [128, 8])
    ohk_in = din("ohki", [128, 8])
    ohk16_in = din("ohk16", [1, 128])
    row16_in = din("row16", [1, 16], I32)

    out = nc.dram_tensor("out", [T, C], F32, kind="ExternalOutput").ap()

    def dint(name, shape, dt):
        return nc.dram_tensor(name, shape, dt, kind="Internal")

    ag_s_in = [dint(f"ag_s_in{l}", [4, 128], F32) for l in range(2)]
    ag_s_out = [dint(f"ag_s_out{l}", [16, 128], F32) for l in range(2)]
    ag_t_in = dint("ag_t_in", [128, 128], F32)
    ag_t_out = dint("ag_t_out", [1024, 128], F32)
    ag_x_in = dint("ag_x_in", [T, C], BF16)
    ag_x_out = dint("ag_x_out", [NCORE * T, C], BF16)
    ag_y_in = dint("ag_y_in", [CAP, C], BF16)
    y_all = dint("y_all", [NCORE * CAP + 1, C], BF16)
    l_list = dint("l_list", [1, CAP + 1152], I32)
    acc_d = dint("acc_d", [T + 1, C], BF16)
    orow_d = dint("orow_d", [1, 128], F32)
    vc_d = dint("vc_d", [128, 1024], I32)
    di_d = dint("di_d", [128, C2], I32)

    sbt = nc.alloc_sbuf_tensor
    sat = nc.alloc_sbuf_tensor_at

    def addr(h):
        return nc.lookup_mloc(h).addr

    bT = sbt("bT", [128, 8, T], BF16)
    wdv = sbt("wdv", [128, 2, 8, NSD], BF16)
    wiv = sbt("wiv", [128, 2, 8, NSD], BF16)
    wov = sbt("wov", [128, 2, 2, C], BF16)
    wsv = sbt("wsv", [128, 2, 2, C], BF16)
    sA = sbt("sA", [128, 2, T], F32)
    sB = sbt("sB", [128, 2, T], F32)
    sC = sbt("sC", [128, 2, T], F32)
    sD = sbt("sD", [128, 2, T], F32)
    A_bt = addr(bT)
    A_w = addr(wdv)
    A_s = addr(sA)

    w2v = sbt("w2v", [128, 16, C], BF16)
    zeros2 = sbt("zeros2", [128, 2048], F32)
    lnt = sbt("lnt", [128, 2, C], F32)
    bbvb = sbt("bbvb", [1, 2, C], BF16)
    ct2 = sbt("ct2", [128, C], F32)
    x1s = sbt("x1s", [128, 2, C], BF16)
    rowT = sbt("rowT", [128, 16], I32)
    colT = sbt("colT", [128, 16], F32)
    idT = sbt("idT", [128, 128], F32)
    extv = sbt("extv", [128, 8, 144], BF16)
    wgv = sbt("wgv", [128, 8, 8], BF16)
    bd16 = sbt("bd16", [128, 128], F32)
    iocf = sbt("iocf", [128, 1024], F32)
    mskF = sbt("mskF", [128, 1024], F32)
    msI = sbt("msI", [128, 1024], I32)
    bdv = sbt("bdv", [128, 2, 2], F32)
    biv = sbt("biv", [128, 2, 2], F32)
    spav = sbt("spav", [128, 2, 2], F32)
    kv0 = sbt("kv0", [128, 2], F32)
    eab = sbt("eab", [128, 1], I32)
    ohp = sbt("ohp", [128, 8], F32)
    ohk = sbt("ohk", [128, 8], F32)
    exsb = sbt("exsb", [16, 128], F32)
    exch = sbt("exch", [128, 16], F32)
    sm4 = sbt("sm4", [128, 4], F32)
    hins = sbt("hins", [128, 12], F32)
    stb = sbt("stb", [128, 2, 128], BF16)
    bis = sbt("bis", [128, 8], F32)
    bisI = sbt("bisI", [128, 1], I32)
    onesf = sbt("onesf", [1, 128], F32)
    ones1b = sbt("ones1b", [1, 128], BF16)
    cnts = sbt("cnts", [128, 1], F32)
    orow = sbt("orow", [1, 128], F32)
    orow8 = sbt("orow8", [8, 16], F32)
    obase = sbt("obase", [128, 1], F32)
    oint = sbt("oint", [1, 16], I32)
    FL = sbt("FL", [128, 16], I32)
    wraw = sbt("wraw", [128, 16], F32)
    wnrm = sbt("wnrm", [128, 16], F32)
    srcI = sbt("srcI", [128, C2], I32)
    dstI = sbt("dstI", [128, C2], I32)
    sentA = sbt("sentA", [128, C2], I32)
    sentB = sbt("sentB", [128, C2], I32)
    cnt2 = sbt("cnt2", [128, 1], F32)
    st1 = sbt("st1", [128, 16], F32)
    st2 = sbt("st2", [128, 16], F32)
    stt = sbt("stt", [128, 8], F32)
    scr8 = sbt("scr8", [128, 8], F32)
    epsv = sbt("epsv", [128, 1], F32)
    onesc = sbt("onesc", [128, 1], F32)
    iop = sbt("iop", [128, 1], I32)
    iopf = sbt("iopf", [128, 1], F32)
    ctTb = sbt("ctTb", [128, 8], BF16)

    KB = 1024
    lnsq = sat("lnsq", [128, C], F32, offset=A_s + 32 * KB)
    Ssc = sat("Ssc", [128, 1024], F32, offset=A_bt + 0)
    vid = sat("vid", [128, 1024], F32, offset=A_bt + 4 * KB)
    junk = sat("junk", [128, 1024], F32, offset=A_bt + 8 * KB)
    Vb = sat("Vb", [128, 1024], F32, offset=A_bt + 12 * KB)
    Vb2 = sat("Vb2", [128, 1024], F32, offset=A_bt + 16 * KB)
    Db = sat("Db", [128, 1024], I32, offset=A_bt + 20 * KB)
    Db2 = sat("Db2", [128, 1024], I32, offset=A_bt + 24 * KB)
    DmS = sat("DmS", [128, 1024], I32, offset=A_bt + 28 * KB)
    yst = sat("yst", [128, 2, C], BF16, offset=A_bt + 0)
    scr512 = sat("scr512", [128, 512], F32, offset=A_bt + 4 * KB)
    scrG = sat("scrG", [128, 512], F32, offset=A_w + 0)
    cc16 = sat("cc16", [16, 1024], I32, offset=A_w + 4 * KB)
    dstE = sat("dstE", [16, 8, C2], I32, offset=A_w + 4 * KB)
    Gbig = sat("Gbig", [128, C2, C], BF16, offset=A_s + 0)
    Vc = Db
    xgt = sat("xgt", [128, 4, 8, 128], BF16, offset=A_w + 0)
    xg4 = sat("xg4", [128, 4, C], BF16, offset=A_w + 8 * KB)
    hpr = sat("hpr", [128, 16, 512], BF16, offset=A_w + 16 * KB)
    zrowb = sat("zrowb", [128, C], BF16, offset=A_s + 56 * KB)
    ohk16 = sat("ohk16s", [1, 128], F32, offset=addr(stb))
    row16 = sat("row16s", [1, 16], I32, offset=addr(exch))
    hsb = sat("hsb", [128, 2, T], BF16, offset=A_s + 0)
    xpt = sat("xpt", [128, 2, C], F32, offset=A_s + 8 * KB)
    w1v = sat("w1v", [128, 8, HF], BF16, offset=A_s + 0)
    w3v = sat("w3v", [128, 8, HF], BF16, offset=A_s + 32 * KB)
    Gb = sat("Gb", [128, 2, C], BF16, offset=A_s + 0)
    acct = sat("acct", [128, 2, C], BF16, offset=A_s + 8 * KB)
    outt = sat("outt", [128, 2, C], F32, offset=A_s + 16 * KB)

    pjA = nc.alloc_psum_tensor("pjA", [128, 512], F32)
    pjB = nc.alloc_psum_tensor("pjB", [128, 512], F32)
    pyA = nc.alloc_psum_tensor("pyA", [128, 1024], F32)
    pyB = nc.alloc_psum_tensor("pyB", [128, 1024], F32)
    pX = nc.alloc_psum_tensor("pX", [128, 128], F32)
    pSm = nc.alloc_psum_tensor("pSm", [128, 2], F32)

    g = nc.gpsimd
    sy = nc.sync
    v = nc.vector
    sc = nc.scalar
    te = nc.tensor
    BAR = nc.all_engine_barrier

    qg = nc.alloc_semaphore("qg")
    qs = nc.alloc_semaphore("qs")
    qc = nc.alloc_semaphore("qc")
    qh = nc.alloc_semaphore("qh")
    qo = nc.alloc_semaphore("qo")
    cg = [0]; cs = [0]; cc = [0]; ch = [0]; co = [0]

    def gdma(dst, src, **kw):
        r = g.dma_start(out=dst, in_=src, **kw).then_inc(qg, 16)
        cg[0] += 16
        return r

    def gwait():
        g.wait_ge(qg, cg[0])

    def sdma(dst, src, **kw):
        r = sy.dma_start(out=dst, in_=src, **kw).then_inc(qs, 16)
        cs[0] += 16
        return r

    def swait():
        sy.wait_ge(qs, cs[0])

    def coll(groups, ins, outs):
        g.collective_compute("AllGather", AL.bypass, replica_groups=groups,
                             ins=ins, outs=outs).then_inc(qc, 1)
        cc[0] += 1

    def mm(out_ps, lhsT, rhs, start, stop, width):
        nh = (width + 511) // 512
        for h in range(nh):
            w0 = h * 512
            w1 = min(width, w0 + 512)
            te.matmul(out_ps[:, w0:w1], lhsT, rhs[:, w0:w1], start=start, stop=stop)

    GRP8 = [list(range(8))]
    GRP4 = [[0, 1, 2, 3], [4, 5, 6, 7]]

    gdma(wdv[:, :, :, :], wd_in.rearrange("l (j p) n -> p l j n", p=128))
    gdma(wiv[:, :, :, :], wi_in.rearrange("l (j p) n -> p l j n", p=128))
    gdma(wov[:, :, :, :], wo_in.rearrange("l (a p) c -> p l a c", p=128))
    gdma(wsv[:, :, :, :], ws_in.rearrange("l (a p) c -> p l a c", p=128))
    gdma(extv[:, :, :], ext_in.rearrange("j p f -> p j f"))
    gdma(wgv[:, :, :], wgt_in.rearrange("(j p) e -> p j e", p=128))
    gdma(bT[:, :, :], x_feat.rearrange("(j p) t -> p j t", p=128))
    gdma(w2v[:, :, :], w2t_in.rearrange("(j p) c -> p j c", p=128))
    sdma(bdv[:, :, :], bd_in.rearrange("l (a p) -> p l a", p=128), allow_slow_non_contiguous=True)
    sdma(biv[:, :, :], bi_in.rearrange("l (a p) -> p l a", p=128), allow_slow_non_contiguous=True)
    sdma(spav[:, :, :], alog_in.rearrange("l (a p) -> p l a", p=128), allow_slow_non_contiguous=True)
    sdma(kv0[:, :], kv_in[:, :])
    sdma(eab[:, :], eab_in[:, :])
    sdma(ohp[:, :], ohp_in[:, :])
    sdma(ohk[:, :], ohk_in[:, :])
    sdma(lnt[0:1, 0, :], bo_in[0:1, :])
    sdma(lnt[0:1, 1, :], bs_in[0:1, :])
    sdma(ct2[0:1, 0:C], bo_in[1:2, :])
    sdma(sA[0:1, 0, 0:C], bs_in[1:2, :])
    g.iota(iop[:, :], pattern=[[0, 1]], base=0, channel_multiplier=1)
    g.iota(iocf[:, :], pattern=[[1, 1024]], base=0, channel_multiplier=0,
           allow_small_or_imprecise_dtypes=True)
    gwait()
    swait()
    BAR()
    v.tensor_add(bbvb[0:1, 0, :], lnt[0:1, 0, :], lnt[0:1, 1, :])
    v.tensor_add(bbvb[0:1, 1, :], ct2[0:1, 0:C], sA[0:1, 0, 0:C])
    v.memset(zeros2[:, :], 0.0)
    v.memset(zrowb[:, :], 0.0)
    v.memset(onesf[:, :], 1.0)
    v.memset(ones1b[:, :], 1.0)
    v.memset(sentA[:, :], NCORE * CAP)
    v.memset(sentB[:, :], T)
    v.memset(epsv[:, :], float(EPS))
    v.memset(onesc[:, :], 1.0)
    v.tensor_copy(iopf[:, :], iop[:, :])
    sc.activation(out=stt[:, 4:8], in_=spav[:, :, :].rearrange("p l a -> p (l a)"),
                  func=AF.Abs)
    BAR()
    sc.activation(out=stt[:, 4:8], in_=stt[:, 4:8], func=AF.Exp, scale=-1.0)
    BAR()
    sc.activation(out=stt[:, 4:8], in_=stt[:, 4:8], func=AF.Ln, bias=onesc[:, 0:1])
    BAR()
    v.tensor_scalar_max(st2[:, 4:8], spav[:, :, :].rearrange("p l a -> p (l a)"), 0.0)
    BAR()
    v.tensor_add(st2[:, 4:8], st2[:, 4:8], stt[:, 4:8])
    BAR()
    v.tensor_scalar_mul(spav[:, :, :].rearrange("p l a -> p (l a)"), st2[:, 4:8], -1.0)
    v.tensor_scalar(idT[:, :], iocf[:, 0:128], iopf[:, 0:1], None, op0=AL.is_equal)
    v.tensor_single_scalar(bisI[:, :], iop[:, :], 15.0, op=AL.bitwise_and)
    v.tensor_single_scalar(msI[:, 0:1], iop[:, :], 4.0, op=AL.logical_shift_right)
    g.iota(bd16[:, :], pattern=[[1, 8], [0, 16]], base=0, channel_multiplier=0,
           allow_small_or_imprecise_dtypes=True)
    BAR()
    v.tensor_copy(stt[:, 1:2], msI[:, 0:1])
    BAR()
    v.tensor_scalar(bd16[:, :], bd16[:, :], stt[:, 1:2], None, op0=AL.is_equal)
    for m in range(16):
        sdma(acc_d.ap()[0:T, :].rearrange("(m p) c -> p m c", p=128)[:, m:m + 1, :],
             zrowb[:, :].rearrange("p (o c) -> p o c", o=1))
    sdma(acc_d.ap()[T:T + 1, :], zrowb[0:1, :])
    sdma(y_all.ap()[NCORE * CAP:NCORE * CAP + 1, :], zrowb[0:1, :])
    swait()
    BAR()

    def ssm_layer(l, rhsbuf):
        for tsl in range(4):
            t0 = tsl * 512
            for a in range(2):
                for j in range(8):
                    te.matmul(pjA[:, :], wdv[:, l, j, a * 128:(a + 1) * 128],
                              rhsbuf[:, j, t0:t0 + 512], start=(j == 0), stop=(j == 7))
                for j in range(8):
                    te.matmul(pjB[:, :], wiv[:, l, j, a * 128:(a + 1) * 128],
                              rhsbuf[:, j, t0:t0 + 512], start=(j == 0), stop=(j == 7))
                BAR()
                v.tensor_scalar(sA[:, a, t0:t0 + 512], pjA[:, :],
                                bdv[:, l, a:a + 1], None, op0=AL.add)
                v.tensor_scalar(sB[:, a, t0:t0 + 512], pjB[:, :],
                                biv[:, l, a:a + 1], None, op0=AL.add)
                BAR()
        if build.dbg_layer == (l, "raw"):
            for a in range(2):
                for h in range(2):
                    sdma(build.dbg_out.rearrange("(z p) c -> p z c", p=128)[:, 2 * a + h:2 * a + h + 1, :],
                         sA[:, a, h * 1024:(h + 1) * 1024].rearrange("p (o c) -> p o c", o=1))
                    sdma(build.dbg_out.rearrange("(z p) c -> p z c", p=128)[:, 8 + 2 * a + h:8 + 2 * a + h + 1, :],
                         sB[:, a, h * 1024:(h + 1) * 1024].rearrange("p (o c) -> p o c", o=1))
            swait()
            BAR()
            build.dbg_early[0] = True
        if build.dbg_early[0]:
            return
        for a in range(2):
            sc.activation(out=sC[:, a, :], in_=sA[:, a, :], func=AF.Abs)
            BAR()
            sc.activation(out=sC[:, a, :], in_=sC[:, a, :], func=AF.Exp, scale=-1.0)
            BAR()
            sc.activation(out=sC[:, a, :], in_=sC[:, a, :], func=AF.Ln,
                          bias=onesc[:, 0:1])
            BAR()
            v.tensor_scalar_max(sA[:, a, :], sA[:, a, :], 0.0)
            BAR()
            v.tensor_add(sA[:, a, :], sA[:, a, :], sC[:, a, :])
            BAR()
        for a in range(2):
            v.tensor_tensor_scan(out=sC[:, a, :], data0=sA[:, a, :],
                                 data1=zeros2[:, :], initial=0.0,
                                 op0=AL.add, op1=AL.add)
            v.tensor_mul(sB[:, a, :], sA[:, a, :], sB[:, a, :])
            BAR()
            sc.activation(out=sA[:, a, :], in_=sA[:, a, :], func=AF.Exp,
                          scale=spav[:, l, a:a + 1])
            sc.activation(out=sm4[:, a:a + 1], in_=sC[:, a, T - 1:T], func=AF.Exp,
                          scale=spav[:, l, a:a + 1])
            sc.activation(out=sC[:, a, :], in_=sC[:, a, :], func=AF.Exp,
                          scale=spav[:, l, a:a + 1])
            BAR()
            v.tensor_tensor_scan(out=sD[:, a, :], data0=sA[:, a, :],
                                 data1=sB[:, a, :], initial=0.0,
                                 op0=AL.mult, op1=AL.add)
        BAR()
        v.tensor_copy(sm4[:, 2:3], sD[:, 0, T - 1:T])
        v.tensor_copy(sm4[:, 3:4], sD[:, 1, T - 1:T])
        BAR()
        te.transpose(pX[0:4, :], sm4[:, 0:4], idT[:, :])
        BAR()
        v.tensor_copy(exsb[0:4, :], pX[0:4, :])
        BAR()
        sdma(ag_s_in[l].ap()[:, :], exsb[0:4, :])
        swait()
        BAR()
        coll(GRP4, [ag_s_in[l].ap().opt()], [ag_s_out[l].ap().opt()])
        g.wait_ge(qc, cc[0])
        gdma(exsb[:, :], ag_s_out[l].ap()[:, :])
        gwait()
        BAR()
        te.transpose(pX[:, 0:16], exsb[0:16, :], idT[0:16, 0:16])
        BAR()
        v.tensor_copy(exch[:, 0:16], pX[:, 0:16])
        v.memset(hins[:, 0:2], 0.0)
        BAR()
        for m in range(1, 5):
            for cch in range(2):
                pcol = 4 * (m - 1) + cch
                hcol = 4 * (m - 1) + 2 + cch
                v.tensor_mul(stt[:, 0:1], exch[:, pcol:pcol + 1],
                             hins[:, 2 * (m - 1) + cch:2 * (m - 1) + cch + 1])
                BAR()
                v.tensor_add(hins[:, 2 * m + cch:2 * m + cch + 1], stt[:, 0:1],
                             exch[:, hcol:hcol + 1])
                BAR()
        v.tensor_mul(stt[:, 0:8], hins[:, 0:8], ohp[:, 0:8])
        BAR()
        v.reduce_sum(hins[:, 10:12], stt[:, 0:8].rearrange("p (m c) -> p c m", c=2),
                     axis=AX)
        BAR()
        for a in range(2):
            v.scalar_tensor_tensor(out=sD[:, a, :], in0=sC[:, a, :],
                                   scalar=hins[:, 10 + a:11 + a], in1=sD[:, a, :],
                                   op0=AL.mult, op1=AL.add)
        BAR()
        v.tensor_copy(hsb[:, 0, :], sD[:, 0, :])
        v.tensor_copy(hsb[:, 1, :], sD[:, 1, :])
        for a in range(2):
            v.tensor_scalar(stb[:, a, :], zeros2[:, 0:128], hins[:, 8 + a:9 + a],
                            None, op0=AL.add)
        BAR()

    build.dbg_layer = (0, "raw") if dbg == "raw1" else None
    build.dbg_out = out
    build.dbg_early = [False]
    ssm_layer(0, bT)

    if build.dbg_early[0]:
        return nc

    if dbg in ("hs1", "dec1", "drv1", "cum1"):
        buf = {"hs1": sD, "dec1": sA, "drv1": sB, "cum1": sC}[dbg]
        for a in range(2):
            for h in range(2):
                sdma(out.rearrange("(z p) c -> p z c", p=128)[:, 2 * a + h:2 * a + h + 1, :],
                     buf[:, a, h * 1024:(h + 1) * 1024].rearrange("p (o c) -> p o c", o=1))
        for z in range(4, 16):
            sdma(out.rearrange("(z p) c -> p z c", p=128)[:, z:z + 1, :],
                 sD[:, 0, 0:1024].rearrange("p (o c) -> p o c", o=1))
        swait()
        BAR()
        return nc

    for j in range(8):
        for a in range(2):
            te.matmul(pX[:, 0:1], wsv[:, 0, a, j * 128:(j + 1) * 128],
                      stb[:, a, 0:1], start=(a == 0), stop=False)
        te.matmul(pX[:, 0:1], bbvb[0:1, 0, j * 128:(j + 1) * 128], ones1b[0:1, 0:1],
                  start=False, stop=True)
        BAR()
        v.tensor_copy(ctTb[:, j:j + 1], pX[:, 0:1])
        BAR()
    for (wv, bv) in ((wdv, bdv), (wiv, biv)):
        for a in range(2):
            for j in range(8):
                te.matmul(pX[:, 1:2], wv[:, 1, j, a * 128:(a + 1) * 128],
                          ctTb[:, j:j + 1], start=(j == 0), stop=(j == 7))
            BAR()
            v.tensor_add(bv[:, 1, a:a + 1], bv[:, 1, a:a + 1], pX[:, 1:2])
            BAR()
    for j in range(8):
        for th in range(2):
            py = pyA if (j * 2 + th) % 2 == 0 else pyB
            for a in range(2):
                mm(py, wov[:, 0, a, j * 128:(j + 1) * 128],
                   hsb[:, a, th * 1024:(th + 1) * 1024], (a == 0), (a == 1), 1024)
            BAR()
            v.tensor_copy(bT[:, j, th * 1024:(th + 1) * 1024], py[:, :])
        BAR()

    ssm_layer(1, bT)

    for a in range(2):
        mm(pyA, stb[:, a, :], wsv[:, 1, a, :], (a == 0), False, C)
    mm(pyA, ones1b[0:1, 0:128], bbvb[0:1, 1, :], False, True, C)
    BAR()
    v.tensor_copy(ct2[:, :], pyA[:, :])
    BAR()

    bTf = bT.rearrange("p a t -> p (a t)")
    for m in range(16):
        py = pyA if m % 2 == 0 else pyB
        b = m % 2
        for a in range(2):
            mm(py, hsb[:, a, m * 128:(m + 1) * 128], wov[:, 1, a, :],
               (a == 0), (a == 1), C)
        sdma(xpt[:, b:b + 1, :],
             x_tok.rearrange("(m p) c -> p m c", p=128)[:, m:m + 1, :])
        swait()
        BAR()
        v.tensor_add(lnt[:, b, :], py[:, :], ct2[:, :])
        BAR()
        v.scalar_tensor_tensor(out=lnt[:, b, :], in0=lnt[:, b, :], scalar=1.0,
                               in1=xpt[:, b, :], op0=AL.mult, op1=AL.add,
                               accum_out=st1[:, m:m + 1])
        BAR()
        v.scalar_tensor_tensor(out=lnsq[:, :], in0=lnt[:, b, :], scalar=1.0,
                               in1=lnt[:, b, :], op0=AL.mult, op1=AL.mult,
                               accum_out=st2[:, m:m + 1])
        BAR()
        v.tensor_scalar_mul(st1[:, m:m + 1], st1[:, m:m + 1], 1.0 / C)
        v.tensor_scalar_mul(st2[:, m:m + 1], st2[:, m:m + 1], 1.0 / C)
        BAR()
        v.scalar_tensor_tensor(out=stt[:, 1:2], in0=st1[:, m:m + 1], scalar=-1.0,
                               in1=st1[:, m:m + 1], op0=AL.mult, op1=AL.mult)
        BAR()
        v.tensor_add(stt[:, 1:2], stt[:, 1:2], st2[:, m:m + 1])
        BAR()
        sc.activation(out=stt[:, 2:3], in_=stt[:, 1:2], func=AF.Sqrt, bias=epsv[:, 0:1])
        BAR()
        v.reciprocal(stt[:, 3:4], stt[:, 2:3])
        BAR()
        v.tensor_scalar(x1s[:, b, :], lnt[:, b, :], st1[:, m:m + 1],
                        stt[:, 3:4], op0=AL.subtract, op1=AL.mult)
        BAR()
        sdma(ag_x_in.ap().rearrange("(m p) c -> p m c", p=128)[:, m:m + 1, :],
             x1s[:, b:b + 1, :])
        sy.dma_start_transpose(
            bTf[:, m * 1024:(m + 1) * 1024].rearrange("p (a q) -> p a q", a=8),
            x1s[:, b, :].rearrange("p (a q) -> p a q", a=8)
        ).then_inc(qs, 16)
        cs[0] += 16
    swait()
    BAR()
    gdma(w1v[:, :, :], w1t_in.rearrange("(j p) h -> p j h", p=128))
    gdma(w3v[:, :, :], w3t_in.rearrange("(j p) h -> p j h", p=128))

    if dbg == "x1":
        for m in range(16):
            b = m % 2
            gdma(x1s[:, b:b + 1, :],
                 ag_x_in.ap().rearrange("(m p) c -> p m c", p=128)[:, m:m + 1, :])
            gwait()
            BAR()
            v.tensor_copy(outt[:, 0, :], x1s[:, b, :])
            BAR()
            sdma(out.rearrange("(m p) c -> p m c", p=128)[:, m:m + 1, :],
                 outt[:, 0:1, :])
            swait()
        BAR()
        return nc

    x1Tv = bTf.rearrange("p (m a q) -> p m a q", m=16, a=8)
    for rt in range(16):
        for j in range(8):
            te.matmul(pX[:, :], extv[:, j, 16 - rt:144 - rt],
                      x1Tv[:, :, j, :].rearrange("p m (q r) -> p r m q", r=16)[:, rt, :, :],
                      start=(rt == 0 and j == 0), stop=(rt == 15 and j == 7))
    BAR()
    v.tensor_copy(junk[:, 0:128], pX[:, :])
    BAR()
    sdma(ag_t_in.ap()[:, :], junk[:, 0:128])
    swait()
    BAR()
    coll(GRP8, [ag_t_in.ap().opt()], [ag_t_out.ap().opt()])
    cc_agt = cc[0]
    g.wait_ge(qc, cc[0])
    coll(GRP8, [ag_x_in.ap().opt()], [ag_x_out.ap().opt()])
    sy.wait_ge(qc, cc_agt)
    sdma(Ssc[:, :].rearrange("er (kk c) -> er kk c", c=128),
         ag_t_out.ap().rearrange("(kk er) c -> er kk c", er=128))
    sdma(ohk16[:, :], ohk16_in[:, :])
    sdma(row16[:, :], row16_in[:, :])
    swait()
    BAR()
    if dbg == "bis":
        v.tensor_copy(lnt[:, 0, 233:361], Ssc[:, 0:128])
        v.reduce_max(lnt[:, 0, 361:362], Ssc[:, :], axis=AX)
        v.reduce_max(lnt[:, 0, 362:363], Ssc[:, 0:128], axis=AX)
        BAR()

    v.tensor_single_scalar(bisI[:, :], iop[:, :], 15.0, op=AL.bitwise_and)
    BAR()
    v.tensor_copy(stt[:, 0:1], bisI[:, :])
    BAR()
    v.tensor_scalar(vid[:, :], iocf[:, :], 16.0, stt[:, 0:1], op0=AL.mult, op1=AL.add)
    v.reduce_max(bis[:, 0:1], Ssc[:, :], axis=AX)
    v.tensor_reduce(bis[:, 1:2], Ssc[:, :], axis=AX, op=AL.min, negate=True)
    BAR()
    if dbg == "bis":
        v.tensor_copy(lnt[:, 0, 100:102], bis[:, 0:2])
        BAR()
    te.transpose(pX[0:2, :], bis[:, 0:2], idT[:, :])
    BAR()
    v.tensor_copy(scrG[0:2, 0:128], pX[0:2, 0:128])
    BAR()
    if dbg == "bis":
        v.tensor_copy(lnt[0:2, 0, 102:230], scrG[0:2, 0:128])
        BAR()
    v.reduce_max(scr8[0:2, 0:1], scrG[0:2, 0:128], axis=AX)
    BAR()
    te.transpose(pX[0:1, 0:2], scr8[0:2, 0:1], idT[0:2, 0:2])
    BAR()
    v.tensor_copy(stt[0:1, 0:2], pX[0:1, 0:2])
    BAR()
    if dbg == "bis":
        v.tensor_copy(lnt[0:2, 0, 230:231], scr8[0:2, 0:1])
        v.tensor_copy(lnt[0:1, 0, 231:233], stt[0:1, 0:2])
        BAR()
    te.matmul(pSm[:, 0:2], onesf[0:1, 0:128], stt[0:1, 0:2], start=True, stop=True)
    BAR()
    v.tensor_scalar_add(bis[:, 0:1], pSm[:, 0:1], 1.0)
    v.tensor_scalar(bis[:, 1:2], pSm[:, 1:2], -1.0, -1.0, op0=AL.mult, op1=AL.add)
    v.tensor_scalar_mul(bis[:, 4:5], pSm[:, 0:1], -1.0)
    BAR()
    if dbg == "bis":
        v.tensor_copy(lnt[:, 0, 0:8], bis[:, 0:8])
        BAR()
    for it in range(NBIS if dbg != "bis" else 1):
        v.tensor_add(bis[:, 2:3], bis[:, 0:1], bis[:, 1:2])
        BAR()
        v.tensor_scalar_mul(bis[:, 2:3], bis[:, 2:3], 0.5)
        BAR()
        v.tensor_scalar(junk[:, :], Ssc[:, :], bis[:, 2:3], None, op0=AL.is_ge)
        BAR()
        v.reduce_sum(cnts[:, :], junk[:, :], axis=AX)
        BAR()
        te.matmul(pSm[:, 0:1], bd16[:, :], cnts[:, :], start=True, stop=True)
        BAR()
        v.tensor_single_scalar(bisI[:, :], pSm[:, 0:1], 2048.0, op=AL.is_ge)
        BAR()
        v.select(bis[:, 1:2], bisI[:, :], bis[:, 2:3], bis[:, 1:2])
        v.select(bis[:, 0:1], bisI[:, :], bis[:, 0:1], bis[:, 2:3])
        BAR()
    if dbg == "bis":
        v.tensor_copy(lnt[:, 0, 8:9], cnts[:, :])
        v.tensor_copy(lnt[:, 0, 9:10], pSm[:, 0:1])
        v.tensor_copy(lnt[:, 0, 10:11], bisI[:, :])
        v.tensor_copy(lnt[:, 0, 11:14], bis[:, 0:3])
        v.tensor_copy(lnt[:, 0, 14:142], Ssc[:, 0:128])
        BAR()
        for m in range(16):
            sdma(out.rearrange("(m p) c -> p m c", p=128)[:, m:m + 1, :],
                 lnt[:, 0:1, :])
        swait()
        BAR()
        return nc
    v.tensor_scalar(msI[:, :], Ssc[:, :], bis[:, 1:2], None, op0=AL.is_ge)
    v.tensor_scalar(mskF[:, :], Ssc[:, :], bis[:, 1:2], None, op0=AL.is_ge)
    BAR()
    v.reduce_sum(cnts[:, :], mskF[:, :], axis=AX)
    sc.activation(out=junk[:, :], in_=Ssc[:, :], func=AF.Exp, bias=bis[:, 4:5])
    BAR()
    v.tensor_mul(junk[:, :], junk[:, :], mskF[:, :])
    BAR()
    v.reduce_sum(st1[:, 0:1], junk[:, :], axis=AX)
    BAR()
    te.transpose(pX[0:1, :], st1[:, 0:1], idT[:, :])
    BAR()
    v.tensor_copy(orow[0:1, :], pX[0:1, 0:128])
    BAR()
    v.tensor_mul(orow[0:1, :], orow[0:1, :], ohk16[0:1, :])
    BAR()
    v.reduce_sum(stt[0:1, 4:5], orow[0:1, :], axis=AX)
    BAR()
    te.matmul(pSm[:, 1:2], onesf[0:1, 0:128], stt[0:1, 4:5], start=True, stop=True)
    BAR()
    v.reciprocal(bis[:, 5:6], pSm[:, 1:2])
    v.scalar_tensor_tensor(out=Vb[:, :], in0=vid[:, :], scalar=1.0, in1=mskF[:, :],
                           op0=AL.add, op1=AL.mult)
    v.tensor_tensor_scan(out=junk[:, :], data0=mskF[:, :], data1=zeros2[:, 0:1024],
                         initial=0.0, op0=AL.add, op1=AL.add)
    BAR()
    v.tensor_scalar_add(Vb[:, :], Vb[:, :], -1.0)
    v.tensor_sub(junk[:, :], iocf[:, :], junk[:, :])
    BAR()
    v.tensor_scalar_add(junk[:, :], junk[:, :], 1.0)
    BAR()
    v.tensor_mul(junk[:, :], junk[:, :], mskF[:, :])
    BAR()
    v.tensor_copy(Db[:, :], junk[:, :])
    BAR()

    def butterfly(Vb_, Vb2_, Db_, Db2_):
        Vcur, Vnx, Dcur, Dnx = Vb_, Vb2_, Db_, Db2_
        for b in range(10):
            s = 1 << b
            v.tensor_single_scalar(msI[:, :], Dcur[:, :], float(s), op=AL.bitwise_and)
            v.tensor_single_scalar(DmS[:, :], Dcur[:, :], float(s), op=AL.subtract)
            BAR()
            v.select(Vnx[:, 0:1024 - s], msI[:, s:], Vcur[:, s:], Vcur[:, 0:1024 - s])
            v.tensor_copy(Vnx[:, 1024 - s:], Vcur[:, 1024 - s:])
            v.select(Dnx[:, 0:1024 - s], msI[:, s:], DmS[:, s:], Dcur[:, 0:1024 - s])
            v.tensor_copy(Dnx[:, 1024 - s:], Dcur[:, 1024 - s:])
            BAR()
            Vcur, Vnx, Dcur, Dnx = Vnx, Vcur, Dnx, Dcur
        return Vcur

    CT = butterfly(Vb, Vb2, Db, Db2)
    te.transpose(pX[0:1, :], cnts[:, :], idT[:, :])
    BAR()
    v.tensor_copy(orow[0:1, :], pX[0:1, 0:128])
    BAR()
    sdma(orow_d.ap()[:, :], orow[0:1, :])
    swait()
    BAR()
    gdma(orow8[:, :], orow_d.ap().rearrange("o (e r) -> (o e) r", e=8))
    gwait()
    BAR()
    v.tensor_tensor_scan(out=orow8[:, :], data0=orow8[:, :], data1=zeros2[0:8, 0:16],
                         initial=0.0, op0=AL.add, op1=AL.add)
    BAR()
    gdma(exsb[0:8, 0:16], orow_d.ap().rearrange("o (e r) -> (o e) r", e=8))
    gwait()
    BAR()
    v.tensor_sub(orow8[:, :], orow8[:, :], exsb[0:8, 0:16])
    BAR()
    gdma(orow_d.ap().rearrange("o (e r) -> (o e) r", e=8), orow8[:, :])
    gwait()
    BAR()
    sdma(orow[0:1, :], orow_d.ap()[:, :])
    swait()
    BAR()
    v.tensor_mul(exsb[0:1, 0:128], orow[0:1, :], ohk16[0:1, :])
    BAR()
    v.reduce_sum(wraw[0:1, 0:16], exsb[0:1, 0:128].rearrange("o (e r) -> o r e", e=8),
                 axis=AX)
    BAR()
    v.tensor_copy(oint[0:1, :], wraw[0:1, 0:16])
    te.transpose(pX[:, 0:1], orow[0:1, :], idT[0:1, 0:1])
    BAR()
    v.tensor_copy(obase[:, :], pX[:, 0:1])
    v.tensor_copy(Vc[:, :], CT[:, :])
    BAR()
    sdma(vc_d.ap()[:, :], Vc[:, :])
    swait()
    BAR()
    rv = nc.values_load(row16[0:1, 0:1], engines=[mybir.EngineType.SP])
    sy.dma_start(out=cc16[:, :], in_=vc_d.ap()[bass.ds(rv, 16), :]).then_inc(qh, 16)
    ch[0] += 16
    sy.wait_ge(qh, ch[0])
    sc.wait_ge(qh, ch[0])
    for r in range(16):
        eng = mybir.EngineType.SP if r < 8 else mybir.EngineType.Activation
        ov = nc.values_load(oint[0:1, r:r + 1], engines=[eng])
        issuer = sy if r < 8 else sc
        issuer.wait_ge(qh, ch[0])
        issuer.dma_start(out=l_list.ap()[0:1, bass.ds(ov, 1024)],
                         in_=cc16[r:r + 1, :]).then_inc(qh, 16)
        ch[0] += 16
    sy.wait_ge(qh, ch[0])
    g.wait_ge(qh, ch[0])
    gdma(FL[:, :], l_list.ap()[0:1, 0:2048].rearrange("o (p q) -> (o p) q", q=16))
    gwait()
    g.wait_ge(qc, cc[0])
    BAR()

    if dbg in ("sel", "sel2"):
        v.tensor_copy(lnt[:, 0, 0:16], FL[:, :])
        v.tensor_copy(lnt[:, 0, 16:17], cnts[:, :])
        v.tensor_copy(lnt[:, 0, 17:18], bis[:, 1:2])
        v.tensor_copy(lnt[:, 0, 18:19], obase[:, :])
        v.tensor_copy(lnt[:, 0, 19:20], bis[:, 5:6])
        v.memset(lnt[:, 0, 20:1024], 0.0)
        BAR()
        for m in range(16):
            sdma(out.rearrange("(m p) c -> p m c", p=128)[:, m:m + 1, :],
                 lnt[:, 0:1, :])
        swait()
        BAR()
        return nc

    v.tensor_copy(bisI[:, :], kv0[:, 0:1])
    BAR()
    v.tensor_single_scalar(bisI[:, :], bisI[:, :], 7.0, op=AL.logical_shift_right)
    v.tensor_single_scalar(rowT[:, :], FL[:, :], 11.0, op=AL.logical_shift_right)
    v.tensor_single_scalar(msI[:, 0:16], FL[:, :], 15.0, op=AL.bitwise_and)
    BAR()
    v.tensor_single_scalar(rowT[:, :], rowT[:, :], 7.0, op=AL.logical_shift_left)
    BAR()
    v.tensor_tensor(rowT[:, :], rowT[:, :], msI[:, 0:16], op=AL.bitwise_or)
    BAR()
    v.tensor_scalar(rowT[:, :], rowT[:, :], bisI[:, 0:1], None, op0=AL.bitwise_or)
    v.tensor_single_scalar(msI[:, 0:16], FL[:, :], 4.0, op=AL.logical_shift_right)
    BAR()
    v.tensor_single_scalar(msI[:, 0:16], msI[:, 0:16], 127.0, op=AL.bitwise_and)
    BAR()
    v.tensor_copy(colT[:, :], msI[:, 0:16])
    BAR()
    gdone = []
    for gi in range(16):
        g.indirect_dma_start(
            out=scrG[:, 128 * (gi % 2):128 * (gi % 2) + 128], out_offset=None,
            in_=ag_t_out.ap(),
            in_offset=IndirectOffsetOnAxis(rowT[:, gi:gi + 1], axis=0),
        ).then_inc(qg, 16)
        cg[0] += 16
        gdone.append(cg[0])
        if gi > 0:
            v.wait_ge(qg, gdone[gi - 1])
            b0 = 128 * ((gi - 1) % 2)
            v.tensor_scalar(junk[:, 0:128], iocf[:, 0:128],
                            colT[:, gi - 1:gi], None, op0=AL.is_equal)
            BAR()
            v.tensor_mul(junk[:, 0:128], junk[:, 0:128], scrG[:, b0:b0 + 128])
            BAR()
            v.reduce_sum(wraw[:, gi - 1:gi], junk[:, 0:128], axis=AX)
    v.wait_ge(qg, gdone[15])
    v.tensor_scalar(junk[:, 0:128], iocf[:, 0:128], colT[:, 15:16], None,
                    op0=AL.is_equal)
    BAR()
    v.tensor_mul(junk[:, 0:128], junk[:, 0:128], scrG[:, 128:256])
    BAR()
    v.reduce_sum(wraw[:, 15:16], junk[:, 0:128], axis=AX)
    BAR()
    sc.activation(out=wnrm[:, :], in_=wraw[:, :], func=AF.Exp, bias=bis[:, 4:5])
    BAR()
    v.tensor_scalar(wnrm[:, :], wnrm[:, :], bis[:, 5:6], None, op0=AL.mult)
    BAR()

    for G in range(4):
        for m4 in range(4):
            gi = 4 * G + m4
            g.indirect_dma_start(
                out=xg4[:, m4, :], out_offset=None,
                in_=ag_x_out.ap(),
                in_offset=IndirectOffsetOnAxis(FL[:, gi:gi + 1], axis=0),
            ).then_inc(qg, 16)
            cg[0] += 16
        gwait()
        BAR()
        for m4 in range(4):
            sy.dma_start_transpose(
                xgt[:, m4, :, :],
                xg4[:, m4, :].rearrange("p (a q) -> p a q", a=8)
            ).then_inc(qs, 16)
            cs[0] += 16
        swait()
        BAR()
        for hc in range(16):
            for a in range(8):
                te.matmul(pjA[:, :], w1v[:, a, hc * 128:(hc + 1) * 128],
                          xgt[:, :, a, :],
                          start=(a == 0), stop=(a == 7))
            for a in range(8):
                te.matmul(pjB[:, :], w3v[:, a, hc * 128:(hc + 1) * 128],
                          xgt[:, :, a, :],
                          start=(a == 0), stop=(a == 7))
            BAR()
            sc.activation(out=scr512[:, :], in_=pjA[:, :], func=AF.Silu)
            BAR()
            v.tensor_mul(hpr[:, hc, :], scr512[:, :], pjB[:, :])
            BAR()
        for d in range(4):
            py = pyA if d % 2 == 0 else pyB
            for hc in range(16):
                mm(py, hpr[:, hc, d * 128:(d + 1) * 128], w2v[:, hc, :],
                   (hc == 0), (hc == 15), C)
            BAR()
            gi = 4 * G + d
            v.tensor_scalar(yst[:, d % 2, :], py[:, :], wnrm[:, gi:gi + 1], None,
                            op0=AL.mult)
            BAR()
            sdma(ag_y_in.ap().rearrange("(p q) c -> p q c", q=16)[:, gi:gi + 1, :],
                 yst[:, d % 2:d % 2 + 1, :])
        swait()
        BAR()
    coll(GRP8, [ag_y_in.ap().opt()], [y_all.ap()[0:NCORE * CAP, :].opt()])

    v.tensor_scalar(mskF[:, :], iocf[:, :], cnts[:, 0:1], None, op0=AL.is_lt)
    BAR()
    v.scalar_tensor_tensor(out=CT[:, :], in0=CT[:, :], scalar=1.0, in1=mskF[:, :],
                           op0=AL.add, op1=AL.mult)
    BAR()
    v.tensor_scalar_add(CT[:, :], CT[:, :], -1.0)
    BAR()
    v.tensor_scalar(junk[:, :], iocf[:, :], obase[:, 0:1], None, op0=AL.add)
    BAR()
    v.scalar_tensor_tensor(out=junk[:, :], in0=junk[:, :], scalar=2048.0,
                           in1=CT[:, :], op0=AL.mult, op1=AL.add)
    BAR()
    v.tensor_scalar(junk[:, :], junk[:, :], kv0[:, 0:1], None, op0=AL.subtract)
    BAR()
    v.tensor_scalar(msI[:, :], CT[:, :], kv0[:, 0:1], None, op0=AL.is_ge)
    v.tensor_scalar(DmS[:, :], CT[:, :], kv0[:, 1:2], None, op0=AL.is_lt)
    BAR()
    v.tensor_tensor(msI[:, :], msI[:, :], DmS[:, :], op=AL.bitwise_and)
    BAR()
    v.tensor_copy(mskF[:, :], msI[:, :])
    BAR()
    v.scalar_tensor_tensor(out=Vb2[:, :], in0=junk[:, :], scalar=1.0, in1=mskF[:, :],
                           op0=AL.add, op1=AL.mult)
    v.reduce_sum(cnt2[:, :], mskF[:, :], axis=AX)
    v.tensor_tensor_scan(out=junk[:, :], data0=mskF[:, :], data1=zeros2[:, 0:1024],
                         initial=0.0, op0=AL.add, op1=AL.add)
    BAR()
    v.tensor_scalar_add(Vb2[:, :], Vb2[:, :], -1.0)
    v.tensor_sub(junk[:, :], iocf[:, :], junk[:, :])
    BAR()
    v.tensor_scalar_add(junk[:, :], junk[:, :], 1.0)
    BAR()
    v.tensor_mul(junk[:, :], junk[:, :], mskF[:, :])
    BAR()
    v.tensor_copy(Vb[:, :], Vb2[:, :])
    v.tensor_copy(Db2[:, :], junk[:, :])
    BAR()
    v.tensor_copy(Db[:, :], Db2[:, :])
    BAR()
    CB = butterfly(Vb, Vb2, Db, Db2)
    v.tensor_copy(Db2[:, 0:C2], CB[:, 0:C2])
    BAR()
    v.tensor_single_scalar(srcI[:, :], Db2[:, 0:C2], 11.0, op=AL.logical_shift_right)
    v.tensor_single_scalar(dstI[:, :], Db2[:, 0:C2], 2047.0, op=AL.bitwise_and)
    v.tensor_scalar(msI[:, 0:C2], iocf[:, 0:C2], cnt2[:, 0:1], None, op0=AL.is_lt)
    BAR()
    v.tensor_scalar(srcI[:, :], srcI[:, :], eab[:, 0:1], None, op0=AL.bitwise_or)
    BAR()
    if dbg == "bwd":
        v.tensor_copy(lnt[:, 0, 67:99], msI[:, 0:C2])
        v.tensor_copy(lnt[:, 0, 99:131], srcI[:, :])
        v.tensor_copy(lnt[:, 0, 131:163], dstI[:, :])
        BAR()
    v.select(DmS[:, 0:C2], msI[:, 0:C2], srcI[:, :], sentA[:, :])
    v.select(DmS[:, C2:2 * C2], msI[:, 0:C2], dstI[:, :], sentB[:, :])
    BAR()
    v.tensor_copy(srcI[:, :], DmS[:, 0:C2])
    v.tensor_copy(dstI[:, :], DmS[:, C2:2 * C2])
    g.wait_ge(qc, cc[0])
    BAR()
    if dbg == "bwd":
        v.tensor_copy(lnt[:, 0, 0:32], srcI[:, :])
        v.tensor_copy(lnt[:, 0, 32:64], dstI[:, :])
        v.tensor_copy(lnt[:, 0, 64:65], cnt2[:, :])
        v.tensor_copy(lnt[:, 0, 65:66], obase[:, :])
        v.tensor_copy(lnt[:, 0, 66:67], cnts[:, :])
        v.memset(lnt[:, 0, 163:1024], 0.0)
        BAR()
        for m in range(16):
            sdma(out.rearrange("(m p) c -> p m c", p=128)[:, m:m + 1, :],
                 lnt[:, 0:1, :])
        swait()
        BAR()
        return nc
    sdma(di_d.ap()[:, :], dstI[:, :])
    swait()
    BAR()
    sdma(dstE[:, :, :], di_d.ap().rearrange("(e r) gi -> r e gi", e=8))
    swait()
    BAR()
    for gi in range(C2):
        g.indirect_dma_start(
            out=Gbig[:, gi, :], out_offset=None,
            in_=y_all.ap(),
            in_offset=IndirectOffsetOnAxis(srcI[:, gi:gi + 1], axis=0),
        ).then_inc(qg, 16)
        cg[0] += 16
    gwait()
    BAR()
    for e in range(8):
        for gi in range(C2):
            g.indirect_dma_start(
                out=acc_d.ap(),
                out_offset=IndirectOffsetOnAxis(dstE[0:16, e, gi:gi + 1], axis=0),
                in_=Gbig[16 * e:16 * e + 16, gi, :], in_offset=None,
                compute_op=AL.add,
            ).then_inc(qh, 16)
            ch[0] += 16
        g.wait_ge(qh, ch[0])
    BAR()

    for m in range(16):
        b = m % 2
        sdma(acct[:, b:b + 1, :],
             acc_d.ap()[0:T, :].rearrange("(m p) c -> p m c", p=128)[:, m:m + 1, :])
        gdma(x1s[:, b:b + 1, :],
             ag_x_in.ap().rearrange("(m p) c -> p m c", p=128)[:, m:m + 1, :])
        swait()
        gwait()
        BAR()
        v.scalar_tensor_tensor(out=lnt[:, b, :], in0=x1s[:, b, :], scalar=1.0,
                               in1=acct[:, b, :], op0=AL.mult, op1=AL.add,
                               accum_out=st1[:, m:m + 1])
        BAR()
        v.scalar_tensor_tensor(out=lnsq[:, :], in0=lnt[:, b, :], scalar=1.0,
                               in1=lnt[:, b, :], op0=AL.mult, op1=AL.mult,
                               accum_out=st2[:, m:m + 1])
        BAR()
        v.tensor_scalar_mul(st1[:, m:m + 1], st1[:, m:m + 1], 1.0 / C)
        v.tensor_scalar_mul(st2[:, m:m + 1], st2[:, m:m + 1], 1.0 / C)
        BAR()
        v.scalar_tensor_tensor(out=stt[:, 1:2], in0=st1[:, m:m + 1], scalar=-1.0,
                               in1=st1[:, m:m + 1], op0=AL.mult, op1=AL.mult)
        BAR()
        v.tensor_add(stt[:, 1:2], stt[:, 1:2], st2[:, m:m + 1])
        BAR()
        sc.activation(out=stt[:, 2:3], in_=stt[:, 1:2], func=AF.Sqrt, bias=epsv[:, 0:1])
        BAR()
        v.reciprocal(stt[:, 3:4], stt[:, 2:3])
        BAR()
        v.tensor_scalar(outt[:, b, :], lnt[:, b, :], st1[:, m:m + 1], stt[:, 3:4],
                        op0=AL.subtract, op1=AL.mult)
        BAR()
        sy.dma_start(out=out.rearrange("(m p) c -> p m c", p=128)[:, m:m + 1, :],
                     in_=outt[:, b:b + 1, :]).then_inc(qo, 16)
        co[0] += 16
    sy.wait_ge(qo, co[0])
    BAR()
    return nc


def _host_prep(inputs):
    x = np.asarray(inputs["L0_attn_shot_embeddings"], np.float32).reshape(16384, C)
    Wd = np.ascontiguousarray(np.asarray(inputs["L0_attn_Wd"], np.float32))
    Wi = np.ascontiguousarray(np.asarray(inputs["L0_attn_Wi"], np.float32))
    Wo = np.ascontiguousarray(np.asarray(inputs["L0_attn_Wo"], np.float32))
    Ws = np.ascontiguousarray(np.asarray(inputs["L0_attn_Ws"], np.float32))
    bd = np.ascontiguousarray(np.asarray(inputs["L0_attn_bd"], np.float32))
    bi = np.ascontiguousarray(np.asarray(inputs["L0_attn_bi"], np.float32))
    alog = np.ascontiguousarray(np.asarray(inputs["L0_attn_A_log"], np.float32))
    bo = np.ascontiguousarray(np.asarray(inputs["L0_attn_bo"], np.float32))
    bs = np.ascontiguousarray(np.asarray(inputs["L0_attn_bs"], np.float32))
    Wg = np.asarray(inputs["L0_ffn_Wg"], np.float32)
    W1 = np.asarray(inputs["L0_ffn_W1"], np.float32)
    W2 = np.asarray(inputs["L0_ffn_W2"], np.float32)
    W3 = np.asarray(inputs["L0_ffn_W3"], np.float32)

    ext = np.zeros((8, 128, 144), np.float32)
    for e in range(8):
        for j in range(8):
            ext[j, :, 16 + 16 * e] = Wg[e, 128 * j:128 * (j + 1)]

    maps = []
    for k in range(NCORE):
        xk = np.ascontiguousarray(x[2048 * k:2048 * (k + 1)])
        kv = np.zeros((128, 2), np.float32)
        kv[:, 0] = 2048.0 * k
        kv[:, 1] = 2048.0 * (k + 1)
        eabv = np.zeros((128, 1), np.int32)
        for p in range(128):
            eabv[p, 0] = 2048 * (p // 16)
        ohk16v = np.zeros((1, 128), np.float32)
        ohk16v[0, 16 * k:16 * (k + 1)] = 1.0
        row16v = np.arange(16 * k, 16 * k + 16, dtype=np.int32).reshape(1, 16)
        rot = np.array([[(k + e) % 8 for e in range(8)]], np.int32)
        ohpv = np.zeros((128, 8), np.float32)
        pos = k % 4
        ohpv[:, 2 * pos] = 1.0
        ohpv[:, 2 * pos + 1] = 1.0
        ohkv = np.zeros((128, 8), np.float32)
        ohkv[:, k] = 1.0
        maps.append({
            "x_tok": xk,
            "x_feat": np.ascontiguousarray(xk.T),
            "wd": Wd, "wi": Wi, "wo": Wo, "ws": Ws,
            "bdd": bd, "bii": bi, "alog": alog, "bo": bo, "bss": bs,
            "ext": ext,
            "wgt": np.ascontiguousarray(Wg.T),
            "w1t": np.ascontiguousarray(W1[k].T),
            "w3t": np.ascontiguousarray(W3[k].T),
            "w2t": np.ascontiguousarray(W2[k].T),
            "kvi": kv, "eabi": eabv, "rot": rot, "ohpi": ohpv, "ohki": ohkv,
            "ohk16": ohk16v, "row16": row16v,
        })
    return maps


def kernel(**inputs):
    maps = _host_prep(inputs)
    nc = build(dbg=DBG_STAGE)
    res = run_bass_kernel_spmd(nc, maps, core_ids=list(range(NCORE)))
    outs = np.concatenate([np.asarray(r["out"]) for r in res.results], axis=0)
    return outs.reshape(2, 8192, 1024).astype(np.float32)


# version 02
# speedup vs baseline: 1.0388x; 1.0388x over previous
import numpy as np
import ml_dtypes
import concourse.bass as bass
import concourse.mybir as mybir
from concourse.bass import IndirectOffsetOnAxis
from concourse.bass_utils import run_bass_kernel_spmd

F32 = mybir.dt.float32
BF16 = mybir.dt.bfloat16
I32 = mybir.dt.int32
AX = mybir.AxisListType.X
AF = mybir.ActivationFunctionType
AL = mybir.AluOpType

NCORE = 8
T = 2048
C = 1024
NSD = 256
HF = 2048
CAP = 2048
C2 = 32
EPS = 1e-5
NBIS = 30

DBG_STAGE = None


def build(dbg=None):
    nc = bass.Bass("TRN2", target_bir_lowering=False)

    def din(name, shape, dt=F32):
        return nc.dram_tensor(name, shape, dt, kind="ExternalInput").ap()

    x_tok = din("x_tok", [T, C])
    x_feat = din("x_feat", [C, T])
    wd_in = din("wd", [2, C, NSD])
    wi_in = din("wi", [2, C, NSD])
    wo_in = din("wo", [2, NSD, C])
    ws_in = din("ws", [2, NSD, C])
    bd_in = din("bdd", [2, NSD])
    bi_in = din("bii", [2, NSD])
    alog_in = din("alog", [2, NSD])
    bo_in = din("bo", [2, C])
    bs_in = din("bss", [2, C])
    ext_in = din("ext", [8, 128, 144])
    wgt_in = din("wgt", [C, 8])
    w1t_in = din("w1t", [C, HF])
    w3t_in = din("w3t", [C, HF])
    w2t_in = din("w2t", [HF, C])
    kv_in = din("kvi", [128, 2])
    eab_in = din("eabi", [128, 1], I32)
    rot_in = din("rot", [1, 8], I32)
    ohp_in = din("ohpi", [128, 8])
    ohk_in = din("ohki", [128, 8])
    ohk16_in = din("ohk16", [1, 128])
    row16_in = din("row16", [1, 16], I32)

    out = nc.dram_tensor("out", [T, C], F32, kind="ExternalOutput").ap()

    def dint(name, shape, dt):
        return nc.dram_tensor(name, shape, dt, kind="Internal")

    ag_s_in = [dint(f"ag_s_in{l}", [4, 128], F32) for l in range(2)]
    ag_s_out = [dint(f"ag_s_out{l}", [16, 128], F32) for l in range(2)]
    ag_t_in = dint("ag_t_in", [128, 128], F32)
    ag_t_out = dint("ag_t_out", [1024, 128], F32)
    ag_x_in = dint("ag_x_in", [T, C], BF16)
    ag_x_out = dint("ag_x_out", [NCORE * T, C], BF16)
    ag_y_in = dint("ag_y_in", [CAP, C], BF16)
    y_all = dint("y_all", [NCORE * CAP + 1, C], BF16)
    l_list = dint("l_list", [1, CAP + 1152], I32)
    acc_d = dint("acc_d", [T + 1, C], BF16)
    orow_d = dint("orow_d", [1, 128], F32)
    vc_d = dint("vc_d", [128, 1024], I32)
    di_d = dint("di_d", [128, C2], I32)

    sbt = nc.alloc_sbuf_tensor
    sat = nc.alloc_sbuf_tensor_at

    def addr(h):
        return nc.lookup_mloc(h).addr

    bT = sbt("bT", [128, 8, T], BF16)
    wdv = sbt("wdv", [128, 2, 8, NSD], BF16)
    wiv = sbt("wiv", [128, 2, 8, NSD], BF16)
    wov = sbt("wov", [128, 2, 2, C], BF16)
    wsv = sbt("wsv", [128, 2, 2, C], BF16)
    sA = sbt("sA", [128, 2, T], F32)
    sB = sbt("sB", [128, 2, T], F32)
    sC = sbt("sC", [128, 2, T], F32)
    sD = sbt("sD", [128, 2, T], F32)
    A_bt = addr(bT)
    A_w = addr(wdv)
    A_s = addr(sA)

    w2v = sbt("w2v", [128, 16, C], BF16)
    zeros2 = sbt("zeros2", [128, 2048], F32)
    lnt = sbt("lnt", [128, 2, C], F32)
    bbvb = sbt("bbvb", [1, 2, C], BF16)
    ct2 = sbt("ct2", [128, C], F32)
    x1s = sbt("x1s", [128, 2, C], BF16)
    rowT = sbt("rowT", [128, 16], I32)
    colT = sbt("colT", [128, 16], F32)
    idT = sbt("idT", [128, 128], F32)
    extv = sbt("extv", [128, 8, 144], BF16)
    wgv = sbt("wgv", [128, 8, 8], BF16)
    bd16 = sbt("bd16", [128, 128], F32)
    iocf = sbt("iocf", [128, 1024], F32)
    mskF = sbt("mskF", [128, 1024], F32)
    msI = sbt("msI", [128, 1024], I32)
    bdv = sbt("bdv", [128, 2, 2], F32)
    biv = sbt("biv", [128, 2, 2], F32)
    spav = sbt("spav", [128, 2, 2], F32)
    kv0 = sbt("kv0", [128, 2], F32)
    eab = sbt("eab", [128, 1], I32)
    ohp = sbt("ohp", [128, 8], F32)
    ohk = sbt("ohk", [128, 8], F32)
    exsb = sbt("exsb", [16, 128], F32)
    exch = sbt("exch", [128, 16], F32)
    sm4 = sbt("sm4", [128, 4], F32)
    hins = sbt("hins", [128, 12], F32)
    stb = sbt("stb", [128, 2, 128], BF16)
    bis = sbt("bis", [128, 8], F32)
    bisI = sbt("bisI", [128, 1], I32)
    onesf = sbt("onesf", [1, 128], F32)
    ones1b = sbt("ones1b", [1, 128], BF16)
    cnts = sbt("cnts", [128, 1], F32)
    orow = sbt("orow", [1, 128], F32)
    orow8 = sbt("orow8", [8, 16], F32)
    obase = sbt("obase", [128, 1], F32)
    oint = sbt("oint", [1, 16], I32)
    FL = sbt("FL", [128, 16], I32)
    wraw = sbt("wraw", [128, 16], F32)
    wnrm = sbt("wnrm", [128, 16], F32)
    srcI = sbt("srcI", [128, C2], I32)
    dstI = sbt("dstI", [128, C2], I32)
    sentA = sbt("sentA", [128, C2], I32)
    sentB = sbt("sentB", [128, C2], I32)
    cnt2 = sbt("cnt2", [128, 1], F32)
    st1 = sbt("st1", [128, 16], F32)
    st2 = sbt("st2", [128, 16], F32)
    stt = sbt("stt", [128, 8], F32)
    scr8 = sbt("scr8", [128, 8], F32)
    epsv = sbt("epsv", [128, 1], F32)
    onesc = sbt("onesc", [128, 1], F32)
    iop = sbt("iop", [128, 1], I32)
    iopf = sbt("iopf", [128, 1], F32)
    ctTb = sbt("ctTb", [128, 8], BF16)

    KB = 1024
    lnsq = sat("lnsq", [128, C], F32, offset=A_s + 32 * KB)
    Ssc = sat("Ssc", [128, 1024], F32, offset=A_bt + 0)
    vid = sat("vid", [128, 1024], F32, offset=A_bt + 4 * KB)
    junk = sat("junk", [128, 1024], F32, offset=A_bt + 8 * KB)
    Vb = sat("Vb", [128, 1024], F32, offset=A_bt + 12 * KB)
    Vb2 = sat("Vb2", [128, 1024], F32, offset=A_bt + 16 * KB)
    Db = sat("Db", [128, 1024], I32, offset=A_bt + 20 * KB)
    Db2 = sat("Db2", [128, 1024], I32, offset=A_bt + 24 * KB)
    DmS = sat("DmS", [128, 1024], I32, offset=A_bt + 28 * KB)
    yst = sat("yst", [128, 2, C], BF16, offset=A_bt + 0)
    scr512 = sat("scr512", [128, 512], F32, offset=A_bt + 4 * KB)
    scrG = sat("scrG", [128, 512], F32, offset=A_w + 0)
    cc16 = sat("cc16", [16, 1024], I32, offset=A_w + 4 * KB)
    dstE = sat("dstE", [16, 8, C2], I32, offset=A_w + 4 * KB)
    Gbig = sat("Gbig", [128, C2, C], BF16, offset=A_s + 0)
    Vc = Db
    xgt = sat("xgt", [128, 4, 8, 128], BF16, offset=A_w + 0)
    xg4 = sat("xg4", [128, 4, C], BF16, offset=A_w + 8 * KB)
    hpr = sat("hpr", [128, 16, 512], BF16, offset=A_w + 16 * KB)
    zrowb = sat("zrowb", [128, C], BF16, offset=A_s + 56 * KB)
    ohk16 = sat("ohk16s", [1, 128], F32, offset=addr(stb))
    row16 = sat("row16s", [1, 16], I32, offset=addr(exch))
    hsb = sat("hsb", [128, 2, T], BF16, offset=A_s + 0)
    xpt = sat("xpt", [128, 2, C], F32, offset=A_s + 8 * KB)
    w1v = sat("w1v", [128, 8, HF], BF16, offset=A_s + 0)
    w3v = sat("w3v", [128, 8, HF], BF16, offset=A_s + 32 * KB)
    Gb = sat("Gb", [128, 2, C], BF16, offset=A_s + 0)
    acct = sat("acct", [128, 2, C], BF16, offset=A_s + 8 * KB)
    outt = sat("outt", [128, 2, C], F32, offset=A_s + 16 * KB)

    pjA = nc.alloc_psum_tensor("pjA", [128, 512], F32)
    pjB = nc.alloc_psum_tensor("pjB", [128, 512], F32)
    pyA = nc.alloc_psum_tensor("pyA", [128, 1024], F32)
    pyB = nc.alloc_psum_tensor("pyB", [128, 1024], F32)
    pX = nc.alloc_psum_tensor("pX", [128, 128], F32)
    pSm = nc.alloc_psum_tensor("pSm", [128, 2], F32)

    g = nc.gpsimd
    sy = nc.sync
    v = nc.vector
    sc = nc.scalar
    te = nc.tensor
    BAR = nc.all_engine_barrier

    qg = nc.alloc_semaphore("qg")
    qs = nc.alloc_semaphore("qs")
    qc = nc.alloc_semaphore("qc")
    qh = nc.alloc_semaphore("qh")
    qo = nc.alloc_semaphore("qo")
    cg = [0]; cs = [0]; cc = [0]; ch = [0]; co = [0]

    def gdma(dst, src, **kw):
        r = g.dma_start(out=dst, in_=src, **kw).then_inc(qg, 16)
        cg[0] += 16
        return r

    def gwait():
        g.wait_ge(qg, cg[0])

    def sdma(dst, src, **kw):
        r = sy.dma_start(out=dst, in_=src, **kw).then_inc(qs, 16)
        cs[0] += 16
        return r

    def swait():
        sy.wait_ge(qs, cs[0])

    def coll(groups, ins, outs):
        g.collective_compute("AllGather", AL.bypass, replica_groups=groups,
                             ins=ins, outs=outs).then_inc(qc, 1)
        cc[0] += 1

    def mm(out_ps, lhsT, rhs, start, stop, width):
        nh = (width + 511) // 512
        r = None
        for h in range(nh):
            w0 = h * 512
            w1 = min(width, w0 + 512)
            r = te.matmul(out_ps[:, w0:w1], lhsT, rhs[:, w0:w1], start=start, stop=stop)
        return r

    class Edge:
        __slots__ = ("sem", "n")
        def __init__(self, name):
            self.sem = nc.alloc_semaphore(name)
            self.n = 0
        def inc(self, r):
            r.then_inc(self.sem, 1)
            self.n += 1
            return self.n
        def wait(self, eng, n=None):
            eng.wait_ge(self.sem, self.n if n is None else n)

    eTV = Edge("eTV")
    eVT = Edge("eVT")
    eVS = Edge("eVS")
    eSV = Edge("eSV")
    eVX = Edge("eVX")
    eVD = Edge("eVD")
    eVG = Edge("eVG")

    GRP8 = [list(range(8))]
    GRP4 = [[0, 1, 2, 3], [4, 5, 6, 7]]

    gdma(wdv[:, :, :, :], wd_in.rearrange("l (j p) n -> p l j n", p=128))
    gdma(wiv[:, :, :, :], wi_in.rearrange("l (j p) n -> p l j n", p=128))
    gdma(wov[:, :, :, :], wo_in.rearrange("l (a p) c -> p l a c", p=128))
    gdma(wsv[:, :, :, :], ws_in.rearrange("l (a p) c -> p l a c", p=128))
    gdma(extv[:, :, :], ext_in.rearrange("j p f -> p j f"))
    gdma(wgv[:, :, :], wgt_in.rearrange("(j p) e -> p j e", p=128))
    gdma(bT[:, :, :], x_feat.rearrange("(j p) t -> p j t", p=128))
    gdma(w2v[:, :, :], w2t_in.rearrange("(j p) c -> p j c", p=128))
    sdma(bdv[:, :, :], bd_in.rearrange("l (a p) -> p l a", p=128), allow_slow_non_contiguous=True)
    sdma(biv[:, :, :], bi_in.rearrange("l (a p) -> p l a", p=128), allow_slow_non_contiguous=True)
    sdma(spav[:, :, :], alog_in.rearrange("l (a p) -> p l a", p=128), allow_slow_non_contiguous=True)
    sdma(kv0[:, :], kv_in[:, :])
    sdma(eab[:, :], eab_in[:, :])
    sdma(ohp[:, :], ohp_in[:, :])
    sdma(ohk[:, :], ohk_in[:, :])
    sdma(lnt[0:1, 0, :], bo_in[0:1, :])
    sdma(lnt[0:1, 1, :], bs_in[0:1, :])
    sdma(ct2[0:1, 0:C], bo_in[1:2, :])
    sdma(sA[0:1, 0, 0:C], bs_in[1:2, :])
    g.iota(iop[:, :], pattern=[[0, 1]], base=0, channel_multiplier=1)
    g.iota(iocf[:, :], pattern=[[1, 1024]], base=0, channel_multiplier=0,
           allow_small_or_imprecise_dtypes=True)
    gwait()
    swait()
    BAR()
    v.tensor_add(bbvb[0:1, 0, :], lnt[0:1, 0, :], lnt[0:1, 1, :])
    v.tensor_add(bbvb[0:1, 1, :], ct2[0:1, 0:C], sA[0:1, 0, 0:C])
    v.memset(zeros2[:, :], 0.0)
    v.memset(zrowb[:, :], 0.0)
    v.memset(onesf[:, :], 1.0)
    v.memset(ones1b[:, :], 1.0)
    v.memset(sentA[:, :], NCORE * CAP)
    v.memset(sentB[:, :], T)
    v.memset(epsv[:, :], float(EPS))
    v.memset(onesc[:, :], 1.0)
    v.tensor_copy(iopf[:, :], iop[:, :])
    sc.activation(out=stt[:, 4:8], in_=spav[:, :, :].rearrange("p l a -> p (l a)"),
                  func=AF.Abs)
    BAR()
    sc.activation(out=stt[:, 4:8], in_=stt[:, 4:8], func=AF.Exp, scale=-1.0)
    BAR()
    sc.activation(out=stt[:, 4:8], in_=stt[:, 4:8], func=AF.Ln, bias=onesc[:, 0:1])
    BAR()
    v.tensor_scalar_max(st2[:, 4:8], spav[:, :, :].rearrange("p l a -> p (l a)"), 0.0)
    BAR()
    v.tensor_add(st2[:, 4:8], st2[:, 4:8], stt[:, 4:8])
    BAR()
    v.tensor_scalar_mul(spav[:, :, :].rearrange("p l a -> p (l a)"), st2[:, 4:8], -1.0)
    v.tensor_scalar(idT[:, :], iocf[:, 0:128], iopf[:, 0:1], None, op0=AL.is_equal)
    v.tensor_single_scalar(bisI[:, :], iop[:, :], 15.0, op=AL.bitwise_and)
    v.tensor_single_scalar(msI[:, 0:1], iop[:, :], 4.0, op=AL.logical_shift_right)
    g.iota(bd16[:, :], pattern=[[1, 8], [0, 16]], base=0, channel_multiplier=0,
           allow_small_or_imprecise_dtypes=True)
    BAR()
    v.tensor_copy(stt[:, 1:2], msI[:, 0:1])
    BAR()
    v.tensor_scalar(bd16[:, :], bd16[:, :], stt[:, 1:2], None, op0=AL.is_equal)
    for m in range(16):
        sdma(acc_d.ap()[0:T, :].rearrange("(m p) c -> p m c", p=128)[:, m:m + 1, :],
             zrowb[:, :].rearrange("p (o c) -> p o c", o=1))
    sdma(acc_d.ap()[T:T + 1, :], zrowb[0:1, :])
    sdma(y_all.ap()[NCORE * CAP:NCORE * CAP + 1, :], zrowb[0:1, :])
    swait()
    BAR()

    def ssm_layer(l, rhsbuf):
        for tsl in range(4):
            t0 = tsl * 512
            for a in range(2):
                for j in range(8):
                    te.matmul(pjA[:, :], wdv[:, l, j, a * 128:(a + 1) * 128],
                              rhsbuf[:, j, t0:t0 + 512], start=(j == 0), stop=(j == 7))
                for j in range(8):
                    te.matmul(pjB[:, :], wiv[:, l, j, a * 128:(a + 1) * 128],
                              rhsbuf[:, j, t0:t0 + 512], start=(j == 0), stop=(j == 7))
                BAR()
                v.tensor_scalar(sA[:, a, t0:t0 + 512], pjA[:, :],
                                bdv[:, l, a:a + 1], None, op0=AL.add)
                v.tensor_scalar(sB[:, a, t0:t0 + 512], pjB[:, :],
                                biv[:, l, a:a + 1], None, op0=AL.add)
                BAR()
        if build.dbg_layer == (l, "raw"):
            for a in range(2):
                for h in range(2):
                    sdma(build.dbg_out.rearrange("(z p) c -> p z c", p=128)[:, 2 * a + h:2 * a + h + 1, :],
                         sA[:, a, h * 1024:(h + 1) * 1024].rearrange("p (o c) -> p o c", o=1))
                    sdma(build.dbg_out.rearrange("(z p) c -> p z c", p=128)[:, 8 + 2 * a + h:8 + 2 * a + h + 1, :],
                         sB[:, a, h * 1024:(h + 1) * 1024].rearrange("p (o c) -> p o c", o=1))
            swait()
            BAR()
            build.dbg_early[0] = True
        if build.dbg_early[0]:
            return
        for a in range(2):
            sc.activation(out=sC[:, a, :], in_=sA[:, a, :], func=AF.Abs)
            BAR()
            sc.activation(out=sC[:, a, :], in_=sC[:, a, :], func=AF.Exp, scale=-1.0)
            BAR()
            sc.activation(out=sC[:, a, :], in_=sC[:, a, :], func=AF.Ln,
                          bias=onesc[:, 0:1])
            BAR()
            v.tensor_scalar_max(sA[:, a, :], sA[:, a, :], 0.0)
            BAR()
            v.tensor_add(sA[:, a, :], sA[:, a, :], sC[:, a, :])
            BAR()
        for a in range(2):
            v.tensor_tensor_scan(out=sC[:, a, :], data0=sA[:, a, :],
                                 data1=zeros2[:, :], initial=0.0,
                                 op0=AL.add, op1=AL.add)
            v.tensor_mul(sB[:, a, :], sA[:, a, :], sB[:, a, :])
            BAR()
            sc.activation(out=sA[:, a, :], in_=sA[:, a, :], func=AF.Exp,
                          scale=spav[:, l, a:a + 1])
            sc.activation(out=sm4[:, a:a + 1], in_=sC[:, a, T - 1:T], func=AF.Exp,
                          scale=spav[:, l, a:a + 1])
            sc.activation(out=sC[:, a, :], in_=sC[:, a, :], func=AF.Exp,
                          scale=spav[:, l, a:a + 1])
            BAR()
            v.tensor_tensor_scan(out=sD[:, a, :], data0=sA[:, a, :],
                                 data1=sB[:, a, :], initial=0.0,
                                 op0=AL.mult, op1=AL.add)
        BAR()
        v.tensor_copy(sm4[:, 2:3], sD[:, 0, T - 1:T])
        v.tensor_copy(sm4[:, 3:4], sD[:, 1, T - 1:T])
        BAR()
        te.transpose(pX[0:4, :], sm4[:, 0:4], idT[:, :])
        BAR()
        v.tensor_copy(exsb[0:4, :], pX[0:4, :])
        BAR()
        sdma(ag_s_in[l].ap()[:, :], exsb[0:4, :])
        swait()
        BAR()
        coll(GRP4, [ag_s_in[l].ap().opt()], [ag_s_out[l].ap().opt()])
        g.wait_ge(qc, cc[0])
        gdma(exsb[:, :], ag_s_out[l].ap()[:, :])
        gwait()
        BAR()
        te.transpose(pX[:, 0:16], exsb[0:16, :], idT[0:16, 0:16])
        BAR()
        v.tensor_copy(exch[:, 0:16], pX[:, 0:16])
        v.memset(hins[:, 0:2], 0.0)
        BAR()
        for m in range(1, 5):
            for cch in range(2):
                pcol = 4 * (m - 1) + cch
                hcol = 4 * (m - 1) + 2 + cch
                v.tensor_mul(stt[:, 0:1], exch[:, pcol:pcol + 1],
                             hins[:, 2 * (m - 1) + cch:2 * (m - 1) + cch + 1])
                BAR()
                v.tensor_add(hins[:, 2 * m + cch:2 * m + cch + 1], stt[:, 0:1],
                             exch[:, hcol:hcol + 1])
                BAR()
        v.tensor_mul(stt[:, 0:8], hins[:, 0:8], ohp[:, 0:8])
        BAR()
        v.reduce_sum(hins[:, 10:12], stt[:, 0:8].rearrange("p (m c) -> p c m", c=2),
                     axis=AX)
        BAR()
        for a in range(2):
            v.scalar_tensor_tensor(out=sD[:, a, :], in0=sC[:, a, :],
                                   scalar=hins[:, 10 + a:11 + a], in1=sD[:, a, :],
                                   op0=AL.mult, op1=AL.add)
        BAR()
        v.tensor_copy(hsb[:, 0, :], sD[:, 0, :])
        v.tensor_copy(hsb[:, 1, :], sD[:, 1, :])
        for a in range(2):
            v.tensor_scalar(stb[:, a, :], zeros2[:, 0:128], hins[:, 8 + a:9 + a],
                            None, op0=AL.add)
        BAR()

    build.dbg_layer = (0, "raw") if dbg == "raw1" else None
    build.dbg_out = out
    build.dbg_early = [False]
    ssm_layer(0, bT)

    if build.dbg_early[0]:
        return nc

    if dbg in ("hs1", "dec1", "drv1", "cum1"):
        buf = {"hs1": sD, "dec1": sA, "drv1": sB, "cum1": sC}[dbg]
        for a in range(2):
            for h in range(2):
                sdma(out.rearrange("(z p) c -> p z c", p=128)[:, 2 * a + h:2 * a + h + 1, :],
                     buf[:, a, h * 1024:(h + 1) * 1024].rearrange("p (o c) -> p o c", o=1))
        for z in range(4, 16):
            sdma(out.rearrange("(z p) c -> p z c", p=128)[:, z:z + 1, :],
                 sD[:, 0, 0:1024].rearrange("p (o c) -> p o c", o=1))
        swait()
        BAR()
        return nc

    for j in range(8):
        for a in range(2):
            te.matmul(pX[:, 0:1], wsv[:, 0, a, j * 128:(j + 1) * 128],
                      stb[:, a, 0:1], start=(a == 0), stop=False)
        te.matmul(pX[:, 0:1], bbvb[0:1, 0, j * 128:(j + 1) * 128], ones1b[0:1, 0:1],
                  start=False, stop=True)
        BAR()
        v.tensor_copy(ctTb[:, j:j + 1], pX[:, 0:1])
        BAR()
    for (wv, bv) in ((wdv, bdv), (wiv, biv)):
        for a in range(2):
            for j in range(8):
                te.matmul(pX[:, 1:2], wv[:, 1, j, a * 128:(a + 1) * 128],
                          ctTb[:, j:j + 1], start=(j == 0), stop=(j == 7))
            BAR()
            v.tensor_add(bv[:, 1, a:a + 1], bv[:, 1, a:a + 1], pX[:, 1:2])
            BAR()
    for j in range(8):
        for th in range(2):
            py = pyA if (j * 2 + th) % 2 == 0 else pyB
            for a in range(2):
                mm(py, wov[:, 0, a, j * 128:(j + 1) * 128],
                   hsb[:, a, th * 1024:(th + 1) * 1024], (a == 0), (a == 1), 1024)
            BAR()
            v.tensor_copy(bT[:, j, th * 1024:(th + 1) * 1024], py[:, :])
        BAR()

    ssm_layer(1, bT)

    for a in range(2):
        mm(pyA, stb[:, a, :], wsv[:, 1, a, :], (a == 0), False, C)
    mm(pyA, ones1b[0:1, 0:128], bbvb[0:1, 1, :], False, True, C)
    BAR()
    v.tensor_copy(ct2[:, :], pyA[:, :])
    BAR()

    bTf = bT.rearrange("p a t -> p (a t)")
    ln1_base = (eTV.n, eVT.n, eVS.n, eSV.n, eVX.n, eVD.n)
    xpt_cnt = []
    tr_cnt = []
    for m in range(16):
        py = pyA if m % 2 == 0 else pyB
        b = m % 2
        if m >= 2:
            eVT.wait(te, ln1_base[1] + m - 1)
        r = None
        for a in range(2):
            r = mm(py, hsb[:, a, m * 128:(m + 1) * 128], wov[:, 1, a, :],
                   (a == 0), (a == 1), C)
        eTV.inc(r)
        if m >= 2:
            sy.wait_ge(eVX.sem, ln1_base[4] + m - 1)
        sdma(xpt[:, b:b + 1, :],
             x_tok.rearrange("(m p) c -> p m c", p=128)[:, m:m + 1, :])
        xpt_cnt.append(cs[0])
        eTV.wait(v)
        r = v.tensor_add(lnt[:, b, :], py[:, :], ct2[:, :])
        eVT.inc(r)
        v.wait_ge(qs, xpt_cnt[m])
        r = v.scalar_tensor_tensor(out=lnt[:, b, :], in0=lnt[:, b, :], scalar=1.0,
                                   in1=xpt[:, b, :], op0=AL.mult, op1=AL.add,
                                   accum_out=st1[:, m:m + 1])
        eVX.inc(r)
        v.scalar_tensor_tensor(out=lnsq[:, :], in0=lnt[:, b, :], scalar=1.0,
                               in1=lnt[:, b, :], op0=AL.mult, op1=AL.mult,
                               accum_out=st2[:, m:m + 1])
        if m >= 2:
            eSV.wait(v, ln1_base[3] + 2 * (m - 1))
        r = v.tensor_scalar_mul(st1[:, m:m + 1], st1[:, m:m + 1], 1.0 / C)
        eVS.inc(r)
        eVS.wait(sc)
        r = sc.activation(out=stt[:, 4 + b:5 + b], in_=st1[:, m:m + 1],
                          func=AF.Square)
        eSV.inc(r)
        eSV.wait(v)
        r = v.tensor_scalar(stt[:, b:b + 1], st2[:, m:m + 1], 1.0 / C,
                            stt[:, 4 + b:5 + b], op0=AL.mult, op1=AL.subtract)
        eVS.inc(r)
        eVS.wait(sc)
        r = sc.activation(out=stt[:, 2 + b:3 + b], in_=stt[:, b:b + 1],
                          func=AF.Sqrt, bias=epsv[:, 0:1])
        eSV.inc(r)
        eSV.wait(v)
        v.reciprocal(stt[:, 6 + b:7 + b], stt[:, 2 + b:3 + b])
        v.tensor_copy(lnsq[:, 0:384], zeros2[:, 0:384])
        if m >= 2:
            v.wait_ge(qs, tr_cnt[m - 2])
        r = v.tensor_scalar(x1s[:, b, :], lnt[:, b, :], st1[:, m:m + 1],
                            stt[:, 6 + b:7 + b], op0=AL.subtract, op1=AL.mult)
        eVD.inc(r)
        eVD.wait(sy)
        sdma(ag_x_in.ap().rearrange("(m p) c -> p m c", p=128)[:, m:m + 1, :],
             x1s[:, b:b + 1, :])
        sy.dma_start_transpose(
            bTf[:, m * 1024:(m + 1) * 1024].rearrange("p (a q) -> p a q", a=8),
            x1s[:, b, :].rearrange("p (a q) -> p a q", a=8)
        ).then_inc(qs, 16)
        cs[0] += 16
        tr_cnt.append(cs[0])
    swait()
    BAR()
    gdma(w1v[:, :, :], w1t_in.rearrange("(j p) h -> p j h", p=128))
    gdma(w3v[:, :, :], w3t_in.rearrange("(j p) h -> p j h", p=128))

    if dbg == "x1":
        for m in range(16):
            b = m % 2
            gdma(x1s[:, b:b + 1, :],
                 ag_x_in.ap().rearrange("(m p) c -> p m c", p=128)[:, m:m + 1, :])
            gwait()
            BAR()
            v.tensor_copy(outt[:, 0, :], x1s[:, b, :])
            BAR()
            sdma(out.rearrange("(m p) c -> p m c", p=128)[:, m:m + 1, :],
                 outt[:, 0:1, :])
            swait()
        BAR()
        return nc

    x1Tv = bTf.rearrange("p (m a q) -> p m a q", m=16, a=8)
    for rt in range(16):
        for j in range(8):
            te.matmul(pX[:, :], extv[:, j, 16 - rt:144 - rt],
                      x1Tv[:, :, j, :].rearrange("p m (q r) -> p r m q", r=16)[:, rt, :, :],
                      start=(rt == 0 and j == 0), stop=(rt == 15 and j == 7))
    BAR()
    v.tensor_copy(junk[:, 0:128], pX[:, :])
    BAR()
    sdma(ag_t_in.ap()[:, :], junk[:, 0:128])
    swait()
    BAR()
    coll(GRP8, [ag_t_in.ap().opt()], [ag_t_out.ap().opt()])
    cc_agt = cc[0]
    g.wait_ge(qc, cc[0])
    coll(GRP8, [ag_x_in.ap().opt()], [ag_x_out.ap().opt()])
    sy.wait_ge(qc, cc_agt)
    sdma(Ssc[:, :].rearrange("er (kk c) -> er kk c", c=128),
         ag_t_out.ap().rearrange("(kk er) c -> er kk c", er=128))
    sdma(ohk16[:, :], ohk16_in[:, :])
    sdma(row16[:, :], row16_in[:, :])
    swait()
    BAR()
    if dbg == "bis":
        v.tensor_copy(lnt[:, 0, 233:361], Ssc[:, 0:128])
        v.reduce_max(lnt[:, 0, 361:362], Ssc[:, :], axis=AX)
        v.reduce_max(lnt[:, 0, 362:363], Ssc[:, 0:128], axis=AX)
        BAR()

    v.tensor_single_scalar(bisI[:, :], iop[:, :], 15.0, op=AL.bitwise_and)
    BAR()
    v.tensor_copy(stt[:, 0:1], bisI[:, :])
    BAR()
    v.tensor_scalar(vid[:, :], iocf[:, :], 16.0, stt[:, 0:1], op0=AL.mult, op1=AL.add)
    v.reduce_max(bis[:, 0:1], Ssc[:, :], axis=AX)
    v.tensor_reduce(bis[:, 1:2], Ssc[:, :], axis=AX, op=AL.min, negate=True)
    BAR()
    if dbg == "bis":
        v.tensor_copy(lnt[:, 0, 100:102], bis[:, 0:2])
        BAR()
    te.transpose(pX[0:2, :], bis[:, 0:2], idT[:, :])
    BAR()
    v.tensor_copy(scrG[0:2, 0:128], pX[0:2, 0:128])
    BAR()
    if dbg == "bis":
        v.tensor_copy(lnt[0:2, 0, 102:230], scrG[0:2, 0:128])
        BAR()
    v.reduce_max(scr8[0:2, 0:1], scrG[0:2, 0:128], axis=AX)
    BAR()
    te.transpose(pX[0:1, 0:2], scr8[0:2, 0:1], idT[0:2, 0:2])
    BAR()
    v.tensor_copy(stt[0:1, 0:2], pX[0:1, 0:2])
    BAR()
    if dbg == "bis":
        v.tensor_copy(lnt[0:2, 0, 230:231], scr8[0:2, 0:1])
        v.tensor_copy(lnt[0:1, 0, 231:233], stt[0:1, 0:2])
        BAR()
    te.matmul(pSm[:, 0:2], onesf[0:1, 0:128], stt[0:1, 0:2], start=True, stop=True)
    BAR()
    v.tensor_scalar_add(bis[:, 0:1], pSm[:, 0:1], 1.0)
    v.tensor_scalar(bis[:, 1:2], pSm[:, 1:2], -1.0, -1.0, op0=AL.mult, op1=AL.add)
    v.tensor_scalar_mul(bis[:, 4:5], pSm[:, 0:1], -1.0)
    BAR()
    if dbg == "bis":
        v.tensor_copy(lnt[:, 0, 0:8], bis[:, 0:8])
        BAR()
    for it in range(NBIS if dbg != "bis" else 1):
        v.tensor_add(bis[:, 2:3], bis[:, 0:1], bis[:, 1:2])
        BAR()
        v.tensor_scalar_mul(bis[:, 2:3], bis[:, 2:3], 0.5)
        BAR()
        v.tensor_scalar(junk[:, :], Ssc[:, :], bis[:, 2:3], None, op0=AL.is_ge)
        BAR()
        v.reduce_sum(cnts[:, :], junk[:, :], axis=AX)
        BAR()
        te.matmul(pSm[:, 0:1], bd16[:, :], cnts[:, :], start=True, stop=True)
        BAR()
        v.tensor_single_scalar(bisI[:, :], pSm[:, 0:1], 2048.0, op=AL.is_ge)
        BAR()
        v.select(bis[:, 1:2], bisI[:, :], bis[:, 2:3], bis[:, 1:2])
        v.select(bis[:, 0:1], bisI[:, :], bis[:, 0:1], bis[:, 2:3])
        BAR()
    if dbg == "bis":
        v.tensor_copy(lnt[:, 0, 8:9], cnts[:, :])
        v.tensor_copy(lnt[:, 0, 9:10], pSm[:, 0:1])
        v.tensor_copy(lnt[:, 0, 10:11], bisI[:, :])
        v.tensor_copy(lnt[:, 0, 11:14], bis[:, 0:3])
        v.tensor_copy(lnt[:, 0, 14:142], Ssc[:, 0:128])
        BAR()
        for m in range(16):
            sdma(out.rearrange("(m p) c -> p m c", p=128)[:, m:m + 1, :],
                 lnt[:, 0:1, :])
        swait()
        BAR()
        return nc
    v.tensor_scalar(msI[:, :], Ssc[:, :], bis[:, 1:2], None, op0=AL.is_ge)
    v.tensor_scalar(mskF[:, :], Ssc[:, :], bis[:, 1:2], None, op0=AL.is_ge)
    BAR()
    v.reduce_sum(cnts[:, :], mskF[:, :], axis=AX)
    sc.activation(out=junk[:, :], in_=Ssc[:, :], func=AF.Exp, bias=bis[:, 4:5])
    BAR()
    v.tensor_mul(junk[:, :], junk[:, :], mskF[:, :])
    BAR()
    v.reduce_sum(st1[:, 0:1], junk[:, :], axis=AX)
    BAR()
    te.transpose(pX[0:1, :], st1[:, 0:1], idT[:, :])
    BAR()
    v.tensor_copy(orow[0:1, :], pX[0:1, 0:128])
    BAR()
    v.tensor_mul(orow[0:1, :], orow[0:1, :], ohk16[0:1, :])
    BAR()
    v.reduce_sum(stt[0:1, 4:5], orow[0:1, :], axis=AX)
    BAR()
    te.matmul(pSm[:, 1:2], onesf[0:1, 0:128], stt[0:1, 4:5], start=True, stop=True)
    BAR()
    v.reciprocal(bis[:, 5:6], pSm[:, 1:2])
    v.scalar_tensor_tensor(out=Vb[:, :], in0=vid[:, :], scalar=1.0, in1=mskF[:, :],
                           op0=AL.add, op1=AL.mult)
    v.tensor_tensor_scan(out=junk[:, :], data0=mskF[:, :], data1=zeros2[:, 0:1024],
                         initial=0.0, op0=AL.add, op1=AL.add)
    BAR()
    v.tensor_scalar_add(Vb[:, :], Vb[:, :], -1.0)
    v.tensor_sub(junk[:, :], iocf[:, :], junk[:, :])
    BAR()
    v.tensor_scalar_add(junk[:, :], junk[:, :], 1.0)
    BAR()
    v.tensor_mul(junk[:, :], junk[:, :], mskF[:, :])
    BAR()
    v.tensor_copy(Db[:, :], junk[:, :])
    BAR()

    def butterfly(Vb_, Vb2_, Db_, Db2_):
        Vcur, Vnx, Dcur, Dnx = Vb_, Vb2_, Db_, Db2_
        for b in range(10):
            s = 1 << b
            v.tensor_single_scalar(msI[:, :], Dcur[:, :], float(s), op=AL.bitwise_and)
            v.tensor_single_scalar(DmS[:, :], Dcur[:, :], float(s), op=AL.subtract)
            BAR()
            v.select(Vnx[:, 0:1024 - s], msI[:, s:], Vcur[:, s:], Vcur[:, 0:1024 - s])
            v.tensor_copy(Vnx[:, 1024 - s:], Vcur[:, 1024 - s:])
            v.select(Dnx[:, 0:1024 - s], msI[:, s:], DmS[:, s:], Dcur[:, 0:1024 - s])
            v.tensor_copy(Dnx[:, 1024 - s:], Dcur[:, 1024 - s:])
            BAR()
            Vcur, Vnx, Dcur, Dnx = Vnx, Vcur, Dnx, Dcur
        return Vcur

    CT = butterfly(Vb, Vb2, Db, Db2)
    te.transpose(pX[0:1, :], cnts[:, :], idT[:, :])
    BAR()
    v.tensor_copy(orow[0:1, :], pX[0:1, 0:128])
    BAR()
    sdma(orow_d.ap()[:, :], orow[0:1, :])
    swait()
    BAR()
    gdma(orow8[:, :], orow_d.ap().rearrange("o (e r) -> (o e) r", e=8))
    gwait()
    BAR()
    v.tensor_tensor_scan(out=orow8[:, :], data0=orow8[:, :], data1=zeros2[0:8, 0:16],
                         initial=0.0, op0=AL.add, op1=AL.add)
    BAR()
    gdma(exsb[0:8, 0:16], orow_d.ap().rearrange("o (e r) -> (o e) r", e=8))
    gwait()
    BAR()
    v.tensor_sub(orow8[:, :], orow8[:, :], exsb[0:8, 0:16])
    BAR()
    gdma(orow_d.ap().rearrange("o (e r) -> (o e) r", e=8), orow8[:, :])
    gwait()
    BAR()
    sdma(orow[0:1, :], orow_d.ap()[:, :])
    swait()
    BAR()
    v.tensor_mul(exsb[0:1, 0:128], orow[0:1, :], ohk16[0:1, :])
    BAR()
    v.reduce_sum(wraw[0:1, 0:16], exsb[0:1, 0:128].rearrange("o (e r) -> o r e", e=8),
                 axis=AX)
    BAR()
    v.tensor_copy(oint[0:1, :], wraw[0:1, 0:16])
    te.transpose(pX[:, 0:1], orow[0:1, :], idT[0:1, 0:1])
    BAR()
    v.tensor_copy(obase[:, :], pX[:, 0:1])
    v.tensor_copy(Vc[:, :], CT[:, :])
    BAR()
    sdma(vc_d.ap()[:, :], Vc[:, :])
    swait()
    BAR()
    rv = nc.values_load(row16[0:1, 0:1], engines=[mybir.EngineType.SP])
    sy.dma_start(out=cc16[:, :], in_=vc_d.ap()[bass.ds(rv, 16), :]).then_inc(qh, 16)
    ch[0] += 16
    sy.wait_ge(qh, ch[0])
    sc.wait_ge(qh, ch[0])
    for r in range(16):
        eng = mybir.EngineType.SP if r < 8 else mybir.EngineType.Activation
        ov = nc.values_load(oint[0:1, r:r + 1], engines=[eng])
        issuer = sy if r < 8 else sc
        issuer.wait_ge(qh, ch[0])
        issuer.dma_start(out=l_list.ap()[0:1, bass.ds(ov, 1024)],
                         in_=cc16[r:r + 1, :]).then_inc(qh, 16)
        ch[0] += 16
    sy.wait_ge(qh, ch[0])
    g.wait_ge(qh, ch[0])
    gdma(FL[:, :], l_list.ap()[0:1, 0:2048].rearrange("o (p q) -> (o p) q", q=16))
    gwait()
    g.wait_ge(qc, cc[0])
    BAR()

    if dbg in ("sel", "sel2"):
        v.tensor_copy(lnt[:, 0, 0:16], FL[:, :])
        v.tensor_copy(lnt[:, 0, 16:17], cnts[:, :])
        v.tensor_copy(lnt[:, 0, 17:18], bis[:, 1:2])
        v.tensor_copy(lnt[:, 0, 18:19], obase[:, :])
        v.tensor_copy(lnt[:, 0, 19:20], bis[:, 5:6])
        v.memset(lnt[:, 0, 20:1024], 0.0)
        BAR()
        for m in range(16):
            sdma(out.rearrange("(m p) c -> p m c", p=128)[:, m:m + 1, :],
                 lnt[:, 0:1, :])
        swait()
        BAR()
        return nc

    v.tensor_copy(bisI[:, :], kv0[:, 0:1])
    BAR()
    v.tensor_single_scalar(bisI[:, :], bisI[:, :], 7.0, op=AL.logical_shift_right)
    v.tensor_single_scalar(rowT[:, :], FL[:, :], 11.0, op=AL.logical_shift_right)
    v.tensor_single_scalar(msI[:, 0:16], FL[:, :], 15.0, op=AL.bitwise_and)
    BAR()
    v.tensor_single_scalar(rowT[:, :], rowT[:, :], 7.0, op=AL.logical_shift_left)
    BAR()
    v.tensor_tensor(rowT[:, :], rowT[:, :], msI[:, 0:16], op=AL.bitwise_or)
    BAR()
    v.tensor_scalar(rowT[:, :], rowT[:, :], bisI[:, 0:1], None, op0=AL.bitwise_or)
    v.tensor_single_scalar(msI[:, 0:16], FL[:, :], 4.0, op=AL.logical_shift_right)
    BAR()
    v.tensor_single_scalar(msI[:, 0:16], msI[:, 0:16], 127.0, op=AL.bitwise_and)
    BAR()
    v.tensor_copy(colT[:, :], msI[:, 0:16])
    BAR()
    gdone = []
    for gi in range(16):
        g.indirect_dma_start(
            out=scrG[:, 128 * (gi % 2):128 * (gi % 2) + 128], out_offset=None,
            in_=ag_t_out.ap(),
            in_offset=IndirectOffsetOnAxis(rowT[:, gi:gi + 1], axis=0),
        ).then_inc(qg, 16)
        cg[0] += 16
        gdone.append(cg[0])
        if gi > 0:
            v.wait_ge(qg, gdone[gi - 1])
            b0 = 128 * ((gi - 1) % 2)
            v.tensor_scalar(junk[:, 0:128], iocf[:, 0:128],
                            colT[:, gi - 1:gi], None, op0=AL.is_equal)
            BAR()
            v.tensor_mul(junk[:, 0:128], junk[:, 0:128], scrG[:, b0:b0 + 128])
            BAR()
            v.reduce_sum(wraw[:, gi - 1:gi], junk[:, 0:128], axis=AX)
    v.wait_ge(qg, gdone[15])
    v.tensor_scalar(junk[:, 0:128], iocf[:, 0:128], colT[:, 15:16], None,
                    op0=AL.is_equal)
    BAR()
    v.tensor_mul(junk[:, 0:128], junk[:, 0:128], scrG[:, 128:256])
    BAR()
    v.reduce_sum(wraw[:, 15:16], junk[:, 0:128], axis=AX)
    BAR()
    sc.activation(out=wnrm[:, :], in_=wraw[:, :], func=AF.Exp, bias=bis[:, 4:5])
    BAR()
    v.tensor_scalar(wnrm[:, :], wnrm[:, :], bis[:, 5:6], None, op0=AL.mult)
    BAR()

    for G in range(4):
        for m4 in range(4):
            gi = 4 * G + m4
            g.indirect_dma_start(
                out=xg4[:, m4, :], out_offset=None,
                in_=ag_x_out.ap(),
                in_offset=IndirectOffsetOnAxis(FL[:, gi:gi + 1], axis=0),
            ).then_inc(qg, 16)
            cg[0] += 16
        gwait()
        BAR()
        for m4 in range(4):
            sy.dma_start_transpose(
                xgt[:, m4, :, :],
                xg4[:, m4, :].rearrange("p (a q) -> p a q", a=8)
            ).then_inc(qs, 16)
            cs[0] += 16
        swait()
        BAR()
        for hc in range(16):
            for a in range(8):
                te.matmul(pjA[:, :], w1v[:, a, hc * 128:(hc + 1) * 128],
                          xgt[:, :, a, :],
                          start=(a == 0), stop=(a == 7))
            for a in range(8):
                te.matmul(pjB[:, :], w3v[:, a, hc * 128:(hc + 1) * 128],
                          xgt[:, :, a, :],
                          start=(a == 0), stop=(a == 7))
            BAR()
            sc.activation(out=scr512[:, :], in_=pjA[:, :], func=AF.Silu)
            BAR()
            v.tensor_mul(hpr[:, hc, :], scr512[:, :], pjB[:, :])
            BAR()
        for d in range(4):
            py = pyA if d % 2 == 0 else pyB
            for hc in range(16):
                mm(py, hpr[:, hc, d * 128:(d + 1) * 128], w2v[:, hc, :],
                   (hc == 0), (hc == 15), C)
            BAR()
            gi = 4 * G + d
            v.tensor_scalar(yst[:, d % 2, :], py[:, :], wnrm[:, gi:gi + 1], None,
                            op0=AL.mult)
            BAR()
            sdma(ag_y_in.ap().rearrange("(p q) c -> p q c", q=16)[:, gi:gi + 1, :],
                 yst[:, d % 2:d % 2 + 1, :])
        swait()
        BAR()
    coll(GRP8, [ag_y_in.ap().opt()], [y_all.ap()[0:NCORE * CAP, :].opt()])

    v.tensor_scalar(mskF[:, :], iocf[:, :], cnts[:, 0:1], None, op0=AL.is_lt)
    BAR()
    v.scalar_tensor_tensor(out=CT[:, :], in0=CT[:, :], scalar=1.0, in1=mskF[:, :],
                           op0=AL.add, op1=AL.mult)
    BAR()
    v.tensor_scalar_add(CT[:, :], CT[:, :], -1.0)
    BAR()
    v.tensor_scalar(junk[:, :], iocf[:, :], obase[:, 0:1], None, op0=AL.add)
    BAR()
    v.scalar_tensor_tensor(out=junk[:, :], in0=junk[:, :], scalar=2048.0,
                           in1=CT[:, :], op0=AL.mult, op1=AL.add)
    BAR()
    v.tensor_scalar(junk[:, :], junk[:, :], kv0[:, 0:1], None, op0=AL.subtract)
    BAR()
    v.tensor_scalar(msI[:, :], CT[:, :], kv0[:, 0:1], None, op0=AL.is_ge)
    v.tensor_scalar(DmS[:, :], CT[:, :], kv0[:, 1:2], None, op0=AL.is_lt)
    BAR()
    v.tensor_tensor(msI[:, :], msI[:, :], DmS[:, :], op=AL.bitwise_and)
    BAR()
    v.tensor_copy(mskF[:, :], msI[:, :])
    BAR()
    v.scalar_tensor_tensor(out=Vb2[:, :], in0=junk[:, :], scalar=1.0, in1=mskF[:, :],
                           op0=AL.add, op1=AL.mult)
    v.reduce_sum(cnt2[:, :], mskF[:, :], axis=AX)
    v.tensor_tensor_scan(out=junk[:, :], data0=mskF[:, :], data1=zeros2[:, 0:1024],
                         initial=0.0, op0=AL.add, op1=AL.add)
    BAR()
    v.tensor_scalar_add(Vb2[:, :], Vb2[:, :], -1.0)
    v.tensor_sub(junk[:, :], iocf[:, :], junk[:, :])
    BAR()
    v.tensor_scalar_add(junk[:, :], junk[:, :], 1.0)
    BAR()
    v.tensor_mul(junk[:, :], junk[:, :], mskF[:, :])
    BAR()
    v.tensor_copy(Vb[:, :], Vb2[:, :])
    v.tensor_copy(Db2[:, :], junk[:, :])
    BAR()
    v.tensor_copy(Db[:, :], Db2[:, :])
    BAR()
    CB = butterfly(Vb, Vb2, Db, Db2)
    v.tensor_copy(Db2[:, 0:C2], CB[:, 0:C2])
    BAR()
    v.tensor_single_scalar(srcI[:, :], Db2[:, 0:C2], 11.0, op=AL.logical_shift_right)
    v.tensor_single_scalar(dstI[:, :], Db2[:, 0:C2], 2047.0, op=AL.bitwise_and)
    v.tensor_scalar(msI[:, 0:C2], iocf[:, 0:C2], cnt2[:, 0:1], None, op0=AL.is_lt)
    BAR()
    v.tensor_scalar(srcI[:, :], srcI[:, :], eab[:, 0:1], None, op0=AL.bitwise_or)
    BAR()
    if dbg == "bwd":
        v.tensor_copy(lnt[:, 0, 67:99], msI[:, 0:C2])
        v.tensor_copy(lnt[:, 0, 99:131], srcI[:, :])
        v.tensor_copy(lnt[:, 0, 131:163], dstI[:, :])
        BAR()
    v.select(DmS[:, 0:C2], msI[:, 0:C2], srcI[:, :], sentA[:, :])
    v.select(DmS[:, C2:2 * C2], msI[:, 0:C2], dstI[:, :], sentB[:, :])
    BAR()
    v.tensor_copy(srcI[:, :], DmS[:, 0:C2])
    v.tensor_copy(dstI[:, :], DmS[:, C2:2 * C2])
    g.wait_ge(qc, cc[0])
    BAR()
    if dbg == "bwd":
        v.tensor_copy(lnt[:, 0, 0:32], srcI[:, :])
        v.tensor_copy(lnt[:, 0, 32:64], dstI[:, :])
        v.tensor_copy(lnt[:, 0, 64:65], cnt2[:, :])
        v.tensor_copy(lnt[:, 0, 65:66], obase[:, :])
        v.tensor_copy(lnt[:, 0, 66:67], cnts[:, :])
        v.memset(lnt[:, 0, 163:1024], 0.0)
        BAR()
        for m in range(16):
            sdma(out.rearrange("(m p) c -> p m c", p=128)[:, m:m + 1, :],
                 lnt[:, 0:1, :])
        swait()
        BAR()
        return nc
    sdma(di_d.ap()[:, :], dstI[:, :])
    swait()
    BAR()
    sdma(dstE[:, :, :], di_d.ap().rearrange("(e r) gi -> r e gi", e=8))
    swait()
    BAR()
    for gi in range(C2):
        g.indirect_dma_start(
            out=Gbig[:, gi, :], out_offset=None,
            in_=y_all.ap(),
            in_offset=IndirectOffsetOnAxis(srcI[:, gi:gi + 1], axis=0),
        ).then_inc(qg, 16)
        cg[0] += 16
    gwait()
    BAR()
    for e in range(8):
        for gi in range(C2):
            g.indirect_dma_start(
                out=acc_d.ap(),
                out_offset=IndirectOffsetOnAxis(dstE[0:16, e, gi:gi + 1], axis=0),
                in_=Gbig[16 * e:16 * e + 16, gi, :], in_offset=None,
                compute_op=AL.add,
            ).then_inc(qh, 16)
            ch[0] += 16
        g.wait_ge(qh, ch[0])
    BAR()

    ln2_base = (eVX.n, None, eSV.n, eVD.n)
    acct_cnt = []
    x1g_cnt = []
    for m in range(16):
        b = m % 2
        if m >= 2:
            sy.wait_ge(eVX.sem, ln2_base[0] + m - 1)
            g.wait_ge(eVX.sem, ln2_base[0] + m - 1)
        sdma(acct[:, b:b + 1, :],
             acc_d.ap()[0:T, :].rearrange("(m p) c -> p m c", p=128)[:, m:m + 1, :])
        acct_cnt.append(cs[0])
        gdma(x1s[:, b:b + 1, :],
             ag_x_in.ap().rearrange("(m p) c -> p m c", p=128)[:, m:m + 1, :])
        x1g_cnt.append(cg[0])
        v.wait_ge(qs, acct_cnt[m])
        v.wait_ge(qg, x1g_cnt[m])
        r = v.scalar_tensor_tensor(out=lnt[:, b, :], in0=x1s[:, b, :], scalar=1.0,
                                   in1=acct[:, b, :], op0=AL.mult, op1=AL.add,
                                   accum_out=st1[:, m:m + 1])
        eVX.inc(r)
        v.scalar_tensor_tensor(out=lnsq[:, :], in0=lnt[:, b, :], scalar=1.0,
                               in1=lnt[:, b, :], op0=AL.mult, op1=AL.mult,
                               accum_out=st2[:, m:m + 1])
        if m >= 2:
            eSV.wait(v, ln2_base[2] + 2 * (m - 1))
        r = v.tensor_scalar_mul(st1[:, m:m + 1], st1[:, m:m + 1], 1.0 / C)
        eVS.inc(r)
        eVS.wait(sc)
        r = sc.activation(out=stt[:, 4 + b:5 + b], in_=st1[:, m:m + 1],
                          func=AF.Square)
        eSV.inc(r)
        eSV.wait(v)
        r = v.tensor_scalar(stt[:, b:b + 1], st2[:, m:m + 1], 1.0 / C,
                            stt[:, 4 + b:5 + b], op0=AL.mult, op1=AL.subtract)
        eVS.inc(r)
        eVS.wait(sc)
        r = sc.activation(out=stt[:, 2 + b:3 + b], in_=stt[:, b:b + 1],
                          func=AF.Sqrt, bias=epsv[:, 0:1])
        eSV.inc(r)
        eSV.wait(v)
        v.reciprocal(stt[:, 6 + b:7 + b], stt[:, 2 + b:3 + b])
        v.tensor_copy(lnsq[:, 0:384], zeros2[:, 0:384])
        if m >= 2:
            v.wait_ge(qo, 16 * (m - 1))
        r = v.tensor_scalar(outt[:, b, :], lnt[:, b, :], st1[:, m:m + 1],
                            stt[:, 6 + b:7 + b], op0=AL.subtract, op1=AL.mult)
        eVD.inc(r)
        eVD.wait(sy)
        sy.dma_start(out=out.rearrange("(m p) c -> p m c", p=128)[:, m:m + 1, :],
                     in_=outt[:, b:b + 1, :]).then_inc(qo, 16)
        co[0] += 16
    sy.wait_ge(qo, co[0])
    BAR()
    return nc


def _host_prep(inputs):
    x = np.asarray(inputs["L0_attn_shot_embeddings"], np.float32).reshape(16384, C)
    Wd = np.ascontiguousarray(np.asarray(inputs["L0_attn_Wd"], np.float32))
    Wi = np.ascontiguousarray(np.asarray(inputs["L0_attn_Wi"], np.float32))
    Wo = np.ascontiguousarray(np.asarray(inputs["L0_attn_Wo"], np.float32))
    Ws = np.ascontiguousarray(np.asarray(inputs["L0_attn_Ws"], np.float32))
    bd = np.ascontiguousarray(np.asarray(inputs["L0_attn_bd"], np.float32))
    bi = np.ascontiguousarray(np.asarray(inputs["L0_attn_bi"], np.float32))
    alog = np.ascontiguousarray(np.asarray(inputs["L0_attn_A_log"], np.float32))
    bo = np.ascontiguousarray(np.asarray(inputs["L0_attn_bo"], np.float32))
    bs = np.ascontiguousarray(np.asarray(inputs["L0_attn_bs"], np.float32))
    Wg = np.asarray(inputs["L0_ffn_Wg"], np.float32)
    W1 = np.asarray(inputs["L0_ffn_W1"], np.float32)
    W2 = np.asarray(inputs["L0_ffn_W2"], np.float32)
    W3 = np.asarray(inputs["L0_ffn_W3"], np.float32)

    ext = np.zeros((8, 128, 144), np.float32)
    for e in range(8):
        for j in range(8):
            ext[j, :, 16 + 16 * e] = Wg[e, 128 * j:128 * (j + 1)]

    maps = []
    for k in range(NCORE):
        xk = np.ascontiguousarray(x[2048 * k:2048 * (k + 1)])
        kv = np.zeros((128, 2), np.float32)
        kv[:, 0] = 2048.0 * k
        kv[:, 1] = 2048.0 * (k + 1)
        eabv = np.zeros((128, 1), np.int32)
        for p in range(128):
            eabv[p, 0] = 2048 * (p // 16)
        ohk16v = np.zeros((1, 128), np.float32)
        ohk16v[0, 16 * k:16 * (k + 1)] = 1.0
        row16v = np.arange(16 * k, 16 * k + 16, dtype=np.int32).reshape(1, 16)
        rot = np.array([[(k + e) % 8 for e in range(8)]], np.int32)
        ohpv = np.zeros((128, 8), np.float32)
        pos = k % 4
        ohpv[:, 2 * pos] = 1.0
        ohpv[:, 2 * pos + 1] = 1.0
        ohkv = np.zeros((128, 8), np.float32)
        ohkv[:, k] = 1.0
        maps.append({
            "x_tok": xk,
            "x_feat": np.ascontiguousarray(xk.T),
            "wd": Wd, "wi": Wi, "wo": Wo, "ws": Ws,
            "bdd": bd, "bii": bi, "alog": alog, "bo": bo, "bss": bs,
            "ext": ext,
            "wgt": np.ascontiguousarray(Wg.T),
            "w1t": np.ascontiguousarray(W1[k].T),
            "w3t": np.ascontiguousarray(W3[k].T),
            "w2t": np.ascontiguousarray(W2[k].T),
            "kvi": kv, "eabi": eabv, "rot": rot, "ohpi": ohpv, "ohki": ohkv,
            "ohk16": ohk16v, "row16": row16v,
        })
    return maps


def kernel(**inputs):
    maps = _host_prep(inputs)
    nc = build(dbg=DBG_STAGE)
    res = run_bass_kernel_spmd(nc, maps, core_ids=list(range(NCORE)))
    outs = np.concatenate([np.asarray(r["out"]) for r in res.results], axis=0)
    return outs.reshape(2, 8192, 1024).astype(np.float32)


# version 03
# speedup vs baseline: 1.1598x; 1.1165x over previous
import numpy as np
import ml_dtypes
import concourse.bass as bass
import concourse.mybir as mybir
from concourse.bass import IndirectOffsetOnAxis
from concourse.bass_utils import run_bass_kernel_spmd

F32 = mybir.dt.float32
BF16 = mybir.dt.bfloat16
I32 = mybir.dt.int32
AX = mybir.AxisListType.X
AF = mybir.ActivationFunctionType
AL = mybir.AluOpType

NCORE = 8
T = 2048
C = 1024
NSD = 256
HF = 2048
CAP = 2048
C2 = 32
EPS = 1e-5
NBIS = 30

DBG_STAGE = None


def build(dbg=None):
    nc = bass.Bass("TRN2", target_bir_lowering=False)

    def din(name, shape, dt=F32):
        return nc.dram_tensor(name, shape, dt, kind="ExternalInput").ap()

    x_tok = din("x_tok", [T, C])
    x_feat = din("x_feat", [C, T])
    wd_in = din("wd", [2, C, NSD])
    wi_in = din("wi", [2, C, NSD])
    wo_in = din("wo", [2, NSD, C])
    ws_in = din("ws", [2, NSD, C])
    bd_in = din("bdd", [2, NSD])
    bi_in = din("bii", [2, NSD])
    alog_in = din("alog", [2, NSD])
    bo_in = din("bo", [2, C])
    bs_in = din("bss", [2, C])
    ext_in = din("ext", [8, 128, 144])
    wgt_in = din("wgt", [C, 8])
    w1t_in = din("w1t", [C, HF])
    w3t_in = din("w3t", [C, HF])
    w2t_in = din("w2t", [HF, C])
    kv_in = din("kvi", [128, 2])
    eab_in = din("eabi", [128, 1], I32)
    rot_in = din("rot", [1, 8], I32)
    ohp_in = din("ohpi", [128, 8])
    ohk_in = din("ohki", [128, 8])
    ohk16_in = din("ohk16", [1, 128])
    row16_in = din("row16", [1, 16], I32)

    out = nc.dram_tensor("out", [T, C], F32, kind="ExternalOutput").ap()

    def dint(name, shape, dt):
        return nc.dram_tensor(name, shape, dt, kind="Internal")

    ag_s_in = [dint(f"ag_s_in{l}", [4, 128], F32) for l in range(2)]
    ag_s_out = [dint(f"ag_s_out{l}", [16, 128], F32) for l in range(2)]
    ag_t_in = dint("ag_t_in", [128, 128], F32)
    ag_t_out = dint("ag_t_out", [1024, 128], F32)
    ag_x_in = dint("ag_x_in", [T, C], BF16)
    ag_x_out = dint("ag_x_out", [NCORE * T, C], BF16)
    ag_y_in = dint("ag_y_in", [CAP, C], BF16)
    y_all = dint("y_all", [NCORE * CAP + 1, C], BF16)
    l_list = dint("l_list", [1, CAP + 1152], I32)
    acc_d = dint("acc_d", [T + 1, C], BF16)
    orow_d = dint("orow_d", [1, 128], F32)
    vc_d = dint("vc_d", [128, 1024], I32)
    di_d = dint("di_d", [128, C2], I32)

    sbt = nc.alloc_sbuf_tensor
    sat = nc.alloc_sbuf_tensor_at

    def addr(h):
        return nc.lookup_mloc(h).addr

    bT = sbt("bT", [128, 8, T], BF16)
    wdv = sbt("wdv", [128, 2, 8, NSD], BF16)
    wiv = sbt("wiv", [128, 2, 8, NSD], BF16)
    wov = sbt("wov", [128, 2, 2, C], BF16)
    wsv = sbt("wsv", [128, 2, 2, C], BF16)
    sA = sbt("sA", [128, 2, T], F32)
    sB = sbt("sB", [128, 2, T], F32)
    sC = sbt("sC", [128, 2, T], F32)
    sD = sbt("sD", [128, 2, T], F32)
    A_bt = addr(bT)
    A_w = addr(wdv)
    A_s = addr(sA)

    w2v = sbt("w2v", [128, 16, C], BF16)
    zeros2 = sbt("zeros2", [128, 2048], F32)
    lnt = sbt("lnt", [128, 2, C], F32)
    bbvb = sbt("bbvb", [1, 2, C], BF16)
    ct2 = sbt("ct2", [128, C], F32)
    x1s = sbt("x1s", [128, 2, C], BF16)
    rowT = sbt("rowT", [128, 16], I32)
    colT = sbt("colT", [128, 16], F32)
    idT = sbt("idT", [128, 128], F32)
    extv = sbt("extv", [128, 8, 144], BF16)
    wgv = sbt("wgv", [128, 8, 8], BF16)
    bd16 = sbt("bd16", [128, 128], F32)
    iocf = sbt("iocf", [128, 1024], F32)
    mskF = sbt("mskF", [128, 1024], F32)
    msI = sbt("msI", [128, 1024], I32)
    bdv = sbt("bdv", [128, 2, 2], F32)
    biv = sbt("biv", [128, 2, 2], F32)
    spav = sbt("spav", [128, 2, 2], F32)
    kv0 = sbt("kv0", [128, 2], F32)
    eab = sbt("eab", [128, 1], I32)
    ohp = sbt("ohp", [128, 8], F32)
    ohk = sbt("ohk", [128, 8], F32)
    exsb = sbt("exsb", [16, 128], F32)
    exch = sbt("exch", [128, 16], F32)
    sm4 = sbt("sm4", [128, 4], F32)
    hins = sbt("hins", [128, 12], F32)
    stb = sbt("stb", [128, 2, 128], BF16)
    bis = sbt("bis", [128, 8], F32)
    bisI = sbt("bisI", [128, 1], I32)
    onesf = sbt("onesf", [1, 128], F32)
    ones1b = sbt("ones1b", [1, 128], BF16)
    cnts = sbt("cnts", [128, 1], F32)
    orow = sbt("orow", [1, 128], F32)
    orow8 = sbt("orow8", [8, 16], F32)
    obase = sbt("obase", [128, 1], F32)
    oint = sbt("oint", [1, 16], I32)
    FL = sbt("FL", [128, 16], I32)
    wraw = sbt("wraw", [128, 16], F32)
    wnrm = sbt("wnrm", [128, 16], F32)
    srcI = sbt("srcI", [128, C2], I32)
    dstI = sbt("dstI", [128, C2], I32)
    sentA = sbt("sentA", [128, C2], I32)
    sentB = sbt("sentB", [128, C2], I32)
    cnt2 = sbt("cnt2", [128, 1], F32)
    st1 = sbt("st1", [128, 16], F32)
    st2 = sbt("st2", [128, 16], F32)
    stt = sbt("stt", [128, 8], F32)
    scr8 = sbt("scr8", [128, 8], F32)
    epsv = sbt("epsv", [128, 1], F32)
    onesc = sbt("onesc", [128, 1], F32)
    iop = sbt("iop", [128, 1], I32)
    iopf = sbt("iopf", [128, 1], F32)
    ctTb = sbt("ctTb", [128, 8], BF16)

    KB = 1024
    lnsq = sat("lnsq", [128, C], F32, offset=A_s + 32 * KB)
    Ssc = sat("Ssc", [128, 1024], F32, offset=A_bt + 0)
    vid = sat("vid", [128, 1024], F32, offset=A_bt + 4 * KB)
    junk = sat("junk", [128, 1024], F32, offset=A_bt + 8 * KB)
    Vb = sat("Vb", [128, 1024], F32, offset=A_bt + 12 * KB)
    Vb2 = sat("Vb2", [128, 1024], F32, offset=A_bt + 16 * KB)
    Db = sat("Db", [128, 1024], I32, offset=A_bt + 20 * KB)
    Db2 = sat("Db2", [128, 1024], I32, offset=A_bt + 24 * KB)
    DmS = sat("DmS", [128, 1024], I32, offset=A_bt + 28 * KB)
    yst = sat("yst", [128, 2, C], BF16, offset=A_bt + 0)
    scr512 = sat("scr512", [128, 512], F32, offset=A_bt + 4 * KB)
    scrG = sat("scrG", [128, 512], F32, offset=A_w + 0)
    cc16 = sat("cc16", [16, 1024], I32, offset=A_w + 4 * KB)
    dstE = sat("dstE", [16, 8, C2], I32, offset=A_w + 4 * KB)
    Gbig = sat("Gbig", [128, C2, C], BF16, offset=A_s + 0)
    Vc = Db
    xgt = sat("xgt", [128, 4, 8, 128], BF16, offset=A_w + 0)
    xg4 = sat("xg4", [128, 4, C], BF16, offset=A_w + 8 * KB)
    xg4b = sat("xg4b", [128, 4, C], BF16, offset=A_bt + 24 * KB)
    hpr = sat("hpr", [128, 16, 512], BF16, offset=A_w + 16 * KB)
    zrowb = sat("zrowb", [128, C], BF16, offset=A_s + 56 * KB)
    ohk16 = sat("ohk16s", [1, 128], F32, offset=addr(stb))
    row16 = sat("row16s", [1, 16], I32, offset=addr(exch))
    hsb = sat("hsb", [128, 2, T], BF16, offset=A_s + 0)
    xpt = sat("xpt", [128, 2, C], F32, offset=A_s + 8 * KB)
    w1v = sat("w1v", [128, 8, HF], BF16, offset=A_s + 0)
    w3v = sat("w3v", [128, 8, HF], BF16, offset=A_s + 32 * KB)
    Gb = sat("Gb", [128, 2, C], BF16, offset=A_s + 0)
    acct = sat("acct", [128, 2, C], BF16, offset=A_s + 8 * KB)
    outt = sat("outt", [128, 2, C], F32, offset=A_s + 16 * KB)

    pjA = nc.alloc_psum_tensor("pjA", [128, 512], F32)
    pjB = nc.alloc_psum_tensor("pjB", [128, 512], F32)
    pyA = nc.alloc_psum_tensor("pyA", [128, 1024], F32)
    pyB = nc.alloc_psum_tensor("pyB", [128, 1024], F32)
    pX = nc.alloc_psum_tensor("pX", [128, 128], F32)
    pSm = nc.alloc_psum_tensor("pSm", [128, 2], F32)

    g = nc.gpsimd
    sy = nc.sync
    v = nc.vector
    sc = nc.scalar
    te = nc.tensor
    BAR = nc.all_engine_barrier

    qg = nc.alloc_semaphore("qg")
    qs = nc.alloc_semaphore("qs")
    qc = nc.alloc_semaphore("qc")
    qh = nc.alloc_semaphore("qh")
    qo = nc.alloc_semaphore("qo")
    cg = [0]; cs = [0]; cc = [0]; ch = [0]; co = [0]

    def gdma(dst, src, **kw):
        r = g.dma_start(out=dst, in_=src, **kw).then_inc(qg, 16)
        cg[0] += 16
        return r

    def gwait():
        g.wait_ge(qg, cg[0])

    def sdma(dst, src, **kw):
        r = sy.dma_start(out=dst, in_=src, **kw).then_inc(qs, 16)
        cs[0] += 16
        return r

    def swait():
        sy.wait_ge(qs, cs[0])

    def coll(groups, ins, outs):
        g.collective_compute("AllGather", AL.bypass, replica_groups=groups,
                             ins=ins, outs=outs).then_inc(qc, 1)
        cc[0] += 1

    def mm(out_ps, lhsT, rhs, start, stop, width):
        nh = (width + 511) // 512
        r = None
        for h in range(nh):
            w0 = h * 512
            w1 = min(width, w0 + 512)
            r = te.matmul(out_ps[:, w0:w1], lhsT, rhs[:, w0:w1], start=start, stop=stop)
        return r

    class Edge:
        __slots__ = ("sem", "n")
        def __init__(self, name):
            self.sem = nc.alloc_semaphore(name)
            self.n = 0
        def inc(self, r):
            r.then_inc(self.sem, 1)
            self.n += 1
            return self.n
        def wait(self, eng, n=None):
            eng.wait_ge(self.sem, self.n if n is None else n)

    eTV = Edge("eTV")
    eVT = Edge("eVT")
    eVS = Edge("eVS")
    eSV = Edge("eSV")
    eVX = Edge("eVX")
    eVD = Edge("eVD")
    eVG = Edge("eVG")
    eTS = Edge("eTS")

    GRP8 = [list(range(8))]
    GRP4 = [[0, 1, 2, 3], [4, 5, 6, 7]]

    gdma(wdv[:, :, :, :], wd_in.rearrange("l (j p) n -> p l j n", p=128))
    gdma(wiv[:, :, :, :], wi_in.rearrange("l (j p) n -> p l j n", p=128))
    gdma(wov[:, :, :, :], wo_in.rearrange("l (a p) c -> p l a c", p=128))
    gdma(wsv[:, :, :, :], ws_in.rearrange("l (a p) c -> p l a c", p=128))
    gdma(extv[:, :, :], ext_in.rearrange("j p f -> p j f"))
    gdma(wgv[:, :, :], wgt_in.rearrange("(j p) e -> p j e", p=128))
    gdma(bT[:, :, :], x_feat.rearrange("(j p) t -> p j t", p=128))
    gdma(w2v[:, :, :], w2t_in.rearrange("(j p) c -> p j c", p=128))
    sdma(bdv[:, :, :], bd_in.rearrange("l (a p) -> p l a", p=128), allow_slow_non_contiguous=True)
    sdma(biv[:, :, :], bi_in.rearrange("l (a p) -> p l a", p=128), allow_slow_non_contiguous=True)
    sdma(spav[:, :, :], alog_in.rearrange("l (a p) -> p l a", p=128), allow_slow_non_contiguous=True)
    sdma(kv0[:, :], kv_in[:, :])
    sdma(eab[:, :], eab_in[:, :])
    sdma(ohp[:, :], ohp_in[:, :])
    sdma(ohk[:, :], ohk_in[:, :])
    sdma(lnt[0:1, 0, :], bo_in[0:1, :])
    sdma(lnt[0:1, 1, :], bs_in[0:1, :])
    sdma(ct2[0:1, 0:C], bo_in[1:2, :])
    sdma(sA[0:1, 0, 0:C], bs_in[1:2, :])
    g.iota(iop[:, :], pattern=[[0, 1]], base=0, channel_multiplier=1)
    g.iota(iocf[:, :], pattern=[[1, 1024]], base=0, channel_multiplier=0,
           allow_small_or_imprecise_dtypes=True)
    gwait()
    swait()
    BAR()
    v.tensor_add(bbvb[0:1, 0, :], lnt[0:1, 0, :], lnt[0:1, 1, :])
    v.tensor_add(bbvb[0:1, 1, :], ct2[0:1, 0:C], sA[0:1, 0, 0:C])
    v.memset(zeros2[:, :], 0.0)
    v.memset(zrowb[:, :], 0.0)
    v.memset(onesf[:, :], 1.0)
    v.memset(ones1b[:, :], 1.0)
    v.memset(sentA[:, :], NCORE * CAP)
    v.memset(sentB[:, :], T)
    v.memset(epsv[:, :], float(EPS))
    v.memset(onesc[:, :], 1.0)
    v.tensor_copy(iopf[:, :], iop[:, :])
    sc.activation(out=stt[:, 4:8], in_=spav[:, :, :].rearrange("p l a -> p (l a)"),
                  func=AF.Abs)
    BAR()
    sc.activation(out=stt[:, 4:8], in_=stt[:, 4:8], func=AF.Exp, scale=-1.0)
    BAR()
    sc.activation(out=stt[:, 4:8], in_=stt[:, 4:8], func=AF.Ln, bias=onesc[:, 0:1])
    BAR()
    v.tensor_scalar_max(st2[:, 4:8], spav[:, :, :].rearrange("p l a -> p (l a)"), 0.0)
    BAR()
    v.tensor_add(st2[:, 4:8], st2[:, 4:8], stt[:, 4:8])
    BAR()
    v.tensor_scalar_mul(spav[:, :, :].rearrange("p l a -> p (l a)"), st2[:, 4:8], -1.0)
    v.tensor_scalar(idT[:, :], iocf[:, 0:128], iopf[:, 0:1], None, op0=AL.is_equal)
    v.tensor_single_scalar(bisI[:, :], iop[:, :], 15.0, op=AL.bitwise_and)
    v.tensor_single_scalar(msI[:, 0:1], iop[:, :], 4.0, op=AL.logical_shift_right)
    g.iota(bd16[:, :], pattern=[[1, 8], [0, 16]], base=0, channel_multiplier=0,
           allow_small_or_imprecise_dtypes=True)
    BAR()
    v.tensor_copy(stt[:, 1:2], msI[:, 0:1])
    BAR()
    v.tensor_scalar(bd16[:, :], bd16[:, :], stt[:, 1:2], None, op0=AL.is_equal)
    for m in range(16):
        sdma(acc_d.ap()[0:T, :].rearrange("(m p) c -> p m c", p=128)[:, m:m + 1, :],
             zrowb[:, :].rearrange("p (o c) -> p o c", o=1))
    sdma(acc_d.ap()[T:T + 1, :], zrowb[0:1, :])
    sdma(y_all.ap()[NCORE * CAP:NCORE * CAP + 1, :], zrowb[0:1, :])
    swait()
    BAR()

    def ssm_layer(l, rhsbuf):
        evac_cnt = []
        for it in range(8):
            tsl, a = it // 2, it % 2
            t0 = tsl * 512
            pA = pjA[:, :] if it % 2 == 0 else pyA[:, 0:512]
            pB = pjB[:, :] if it % 2 == 0 else pyA[:, 512:1024]
            if it >= 2:
                te.wait_ge(eVT.sem, evac_cnt[it - 2])
            r = None
            for j in range(8):
                r = te.matmul(pA, wdv[:, l, j, a * 128:(a + 1) * 128],
                              rhsbuf[:, j, t0:t0 + 512], start=(j == 0), stop=(j == 7))
            for j in range(8):
                r = te.matmul(pB, wiv[:, l, j, a * 128:(a + 1) * 128],
                              rhsbuf[:, j, t0:t0 + 512], start=(j == 0), stop=(j == 7))
            eTV.inc(r)
            v.wait_ge(eTV.sem, eTV.n)
            v.tensor_scalar(sA[:, a, t0:t0 + 512], pA,
                            bdv[:, l, a:a + 1], None, op0=AL.add)
            r = v.tensor_scalar(sB[:, a, t0:t0 + 512], pB,
                                biv[:, l, a:a + 1], None, op0=AL.add)
            eVT.inc(r)
            evac_cnt.append(eVT.n)
        if build.dbg_layer == (l, "raw"):
            for a in range(2):
                for h in range(2):
                    sdma(build.dbg_out.rearrange("(z p) c -> p z c", p=128)[:, 2 * a + h:2 * a + h + 1, :],
                         sA[:, a, h * 1024:(h + 1) * 1024].rearrange("p (o c) -> p o c", o=1))
                    sdma(build.dbg_out.rearrange("(z p) c -> p z c", p=128)[:, 8 + 2 * a + h:8 + 2 * a + h + 1, :],
                         sB[:, a, h * 1024:(h + 1) * 1024].rearrange("p (o c) -> p o c", o=1))
            swait()
            BAR()
            build.dbg_early[0] = True
        if build.dbg_early[0]:
            return
        for a in range(2):
            sc.wait_ge(eVT.sem, evac_cnt[6 + a])
            r = sc.activation(out=sC[:, a, :], in_=sA[:, a, :], func=AF.Abs)
            eSV.inc(r)
            sc.activation(out=sC[:, a, :], in_=sC[:, a, :], func=AF.Exp, scale=-1.0)
            r = sc.activation(out=sC[:, a, :], in_=sC[:, a, :], func=AF.Ln,
                              bias=onesc[:, 0:1])
            eSV.inc(r)
        sv_sp = eSV.n
        for a in range(2):
            eSV.wait(v, sv_sp - 3 + 2 * a)
            v.tensor_scalar_max(sA[:, a, :], sA[:, a, :], 0.0)
            eSV.wait(v, sv_sp - 2 + 2 * a)
            v.tensor_add(sA[:, a, :], sA[:, a, :], sC[:, a, :])
            v.tensor_tensor_scan(out=sC[:, a, :], data0=sA[:, a, :],
                                 data1=zeros2[:, :], initial=0.0,
                                 op0=AL.add, op1=AL.add)
            r = v.tensor_mul(sB[:, a, :], sA[:, a, :], sB[:, a, :])
            eVS.inc(r)
            eVS.wait(sc)
            sc.activation(out=sA[:, a, :], in_=sA[:, a, :], func=AF.Exp,
                          scale=spav[:, l, a:a + 1])
            sc.activation(out=sm4[:, a:a + 1], in_=sC[:, a, T - 1:T], func=AF.Exp,
                          scale=spav[:, l, a:a + 1])
            r = sc.activation(out=sC[:, a, :], in_=sC[:, a, :], func=AF.Exp,
                              scale=spav[:, l, a:a + 1])
            eSV.inc(r)
            eSV.wait(v)
            r = v.tensor_tensor_scan(out=sD[:, a, :], data0=sA[:, a, :],
                                     data1=sB[:, a, :], initial=0.0,
                                     op0=AL.mult, op1=AL.add)
        v.tensor_copy(sm4[:, 2:3], sD[:, 0, T - 1:T])
        r = v.tensor_copy(sm4[:, 3:4], sD[:, 1, T - 1:T])
        eVT.inc(r)
        te.wait_ge(eVT.sem, eVT.n)
        te.wait_ge(eSV.sem, eSV.n)
        r = te.transpose(pX[0:4, :], sm4[:, 0:4], idT[:, :])
        eTV.inc(r)
        eTV.wait(v)
        r = v.tensor_copy(exsb[0:4, :], pX[0:4, :])
        eVD.inc(r)
        eVD.wait(sy)
        sdma(ag_s_in[l].ap()[:, :], exsb[0:4, :])
        g.wait_ge(qs, cs[0])
        coll(GRP4, [ag_s_in[l].ap().opt()], [ag_s_out[l].ap().opt()])
        g.wait_ge(qc, cc[0])
        gdma(exsb[:, :], ag_s_out[l].ap()[:, :])
        te.wait_ge(qg, cg[0])
        r = te.transpose(pX[:, 0:16], exsb[0:16, :], idT[0:16, 0:16])
        eTV.inc(r)
        eTV.wait(v)
        v.tensor_copy(exch[:, 0:16], pX[:, 0:16])
        v.memset(hins[:, 0:2], 0.0)
        v.tensor_copy(lnt[:, 0, 0:384], zeros2[:, 0:384])
        for m in range(1, 5):
            for cch in range(2):
                pcol = 4 * (m - 1) + cch
                hcol = 4 * (m - 1) + 2 + cch
                v.tensor_mul(stt[:, cch:cch + 1], exch[:, pcol:pcol + 1],
                             hins[:, 2 * (m - 1) + cch:2 * (m - 1) + cch + 1])
            v.tensor_copy(lnt[:, 0, 0:384], zeros2[:, 0:384])
            for cch in range(2):
                hcol = 4 * (m - 1) + 2 + cch
                v.tensor_add(hins[:, 2 * m + cch:2 * m + cch + 1],
                             stt[:, cch:cch + 1], exch[:, hcol:hcol + 1])
            v.tensor_copy(lnt[:, 0, 0:384], zeros2[:, 0:384])
        v.tensor_mul(stt[:, 0:8], hins[:, 0:8], ohp[:, 0:8])
        v.tensor_copy(lnt[:, 0, 0:384], zeros2[:, 0:384])
        v.reduce_sum(hins[:, 10:12], stt[:, 0:8].rearrange("p (m c) -> p c m", c=2),
                     axis=AX)
        v.tensor_copy(lnt[:, 0, 0:384], zeros2[:, 0:384])
        for a in range(2):
            v.scalar_tensor_tensor(out=sD[:, a, :], in0=sC[:, a, :],
                                   scalar=hins[:, 10 + a:11 + a], in1=sD[:, a, :],
                                   op0=AL.mult, op1=AL.add)
        if build.dbg_layer == (l, "exch"):
            v.tensor_copy(lnt[:, 0, 0:16], exch[:, 0:16])
            v.tensor_copy(lnt[:, 0, 16:28], hins[:, 0:12])
            v.tensor_copy(lnt[:, 0, 28:32], sm4[:, 0:4])
            v.memset(lnt[:, 0, 32:1024], 0.0)
            BAR()
            for mm2 in range(16):
                sdma(out.rearrange("(m p) c -> p m c", p=128)[:, mm2:mm2 + 1, :],
                     lnt[:, 0:1, :])
            swait()
            BAR()
            build.dbg_early[0] = True
            return
        v.tensor_copy(hsb[:, 0, :], sD[:, 0, :])
        v.tensor_copy(hsb[:, 1, :], sD[:, 1, :])
        for a in range(2):
            r = v.tensor_scalar(stb[:, a, :], zeros2[:, 0:128], hins[:, 8 + a:9 + a],
                                None, op0=AL.add)
        eVT.inc(r)
        te.wait_ge(eVT.sem, eVT.n)

    build.dbg_layer = (0, "raw") if dbg == "raw1" else (
        (0, "exch") if dbg == "exch" else (
            (1, "exch") if dbg == "exch2" else None))
    build.dbg_out = out
    build.dbg_early = [False]
    ssm_layer(0, bT)

    if build.dbg_early[0]:
        return nc

    if dbg in ("hs1", "dec1", "drv1", "cum1"):
        buf = {"hs1": sD, "dec1": sA, "drv1": sB, "cum1": sC}[dbg]
        for a in range(2):
            for h in range(2):
                sdma(out.rearrange("(z p) c -> p z c", p=128)[:, 2 * a + h:2 * a + h + 1, :],
                     buf[:, a, h * 1024:(h + 1) * 1024].rearrange("p (o c) -> p o c", o=1))
        for z in range(4, 16):
            sdma(out.rearrange("(z p) c -> p z c", p=128)[:, z:z + 1, :],
                 sD[:, 0, 0:1024].rearrange("p (o c) -> p o c", o=1))
        swait()
        BAR()
        return nc

    for j in range(8):
        for a in range(2):
            te.matmul(pX[:, 0:1], wsv[:, 0, a, j * 128:(j + 1) * 128],
                      stb[:, a, 0:1], start=(a == 0), stop=False)
        te.matmul(pX[:, 0:1], bbvb[0:1, 0, j * 128:(j + 1) * 128], ones1b[0:1, 0:1],
                  start=False, stop=True)
        BAR()
        v.tensor_copy(ctTb[:, j:j + 1], pX[:, 0:1])
        BAR()
    for (wv, bv) in ((wdv, bdv), (wiv, biv)):
        for a in range(2):
            for j in range(8):
                te.matmul(pX[:, 1:2], wv[:, 1, j, a * 128:(a + 1) * 128],
                          ctTb[:, j:j + 1], start=(j == 0), stop=(j == 7))
            BAR()
            v.tensor_add(bv[:, 1, a:a + 1], bv[:, 1, a:a + 1], pX[:, 1:2])
            BAR()
    for j in range(8):
        for th in range(2):
            py = pyA if (j * 2 + th) % 2 == 0 else pyB
            for a in range(2):
                mm(py, wov[:, 0, a, j * 128:(j + 1) * 128],
                   hsb[:, a, th * 1024:(th + 1) * 1024], (a == 0), (a == 1), 1024)
            BAR()
            v.tensor_copy(bT[:, j, th * 1024:(th + 1) * 1024], py[:, :])
        BAR()

    ssm_layer(1, bT)
    if build.dbg_early[0]:
        return nc

    for a in range(2):
        mm(pyA, stb[:, a, :], wsv[:, 1, a, :], (a == 0), False, C)
    mm(pyA, ones1b[0:1, 0:128], bbvb[0:1, 1, :], False, True, C)
    BAR()
    v.tensor_copy(ct2[:, :], pyA[:, :])
    BAR()

    bTf = bT.rearrange("p a t -> p (a t)")
    ln1_base = (eTV.n, eVT.n, eVS.n, eSV.n, eVX.n, eVD.n)
    xpt_cnt = []
    tr_cnt = []
    for m in range(16):
        py = pyA if m % 2 == 0 else pyB
        b = m % 2
        if m >= 2:
            eVT.wait(te, ln1_base[1] + m - 1)
        r = None
        for a in range(2):
            r = mm(py, hsb[:, a, m * 128:(m + 1) * 128], wov[:, 1, a, :],
                   (a == 0), (a == 1), C)
        eTV.inc(r)
        if m >= 2:
            sy.wait_ge(eVX.sem, ln1_base[4] + m - 1)
        sdma(xpt[:, b:b + 1, :],
             x_tok.rearrange("(m p) c -> p m c", p=128)[:, m:m + 1, :])
        xpt_cnt.append(cs[0])
        eTV.wait(v)
        r = v.tensor_add(lnt[:, b, :], py[:, :], ct2[:, :])
        eVT.inc(r)
        v.wait_ge(qs, xpt_cnt[m])
        r = v.scalar_tensor_tensor(out=lnt[:, b, :], in0=lnt[:, b, :], scalar=1.0,
                                   in1=xpt[:, b, :], op0=AL.mult, op1=AL.add,
                                   accum_out=st1[:, m:m + 1])
        eVX.inc(r)
        v.scalar_tensor_tensor(out=lnsq[:, :], in0=lnt[:, b, :], scalar=1.0,
                               in1=lnt[:, b, :], op0=AL.mult, op1=AL.mult,
                               accum_out=st2[:, m:m + 1])
        if m >= 2:
            eSV.wait(v, ln1_base[3] + 2 * (m - 1))
        r = v.tensor_scalar_mul(st1[:, m:m + 1], st1[:, m:m + 1], 1.0 / C)
        eVS.inc(r)
        eVS.wait(sc)
        r = sc.activation(out=stt[:, 4 + b:5 + b], in_=st1[:, m:m + 1],
                          func=AF.Square)
        eSV.inc(r)
        eSV.wait(v)
        r = v.tensor_scalar(stt[:, b:b + 1], st2[:, m:m + 1], 1.0 / C,
                            stt[:, 4 + b:5 + b], op0=AL.mult, op1=AL.subtract)
        eVS.inc(r)
        eVS.wait(sc)
        r = sc.activation(out=stt[:, 2 + b:3 + b], in_=stt[:, b:b + 1],
                          func=AF.Sqrt, bias=epsv[:, 0:1])
        eSV.inc(r)
        eSV.wait(v)
        v.reciprocal(stt[:, 6 + b:7 + b], stt[:, 2 + b:3 + b])
        v.tensor_copy(lnsq[:, 0:384], zeros2[:, 0:384])
        if m >= 2:
            v.wait_ge(qs, tr_cnt[m - 2])
        r = v.tensor_scalar(x1s[:, b, :], lnt[:, b, :], st1[:, m:m + 1],
                            stt[:, 6 + b:7 + b], op0=AL.subtract, op1=AL.mult)
        eVD.inc(r)
        eVD.wait(sy)
        sdma(ag_x_in.ap().rearrange("(m p) c -> p m c", p=128)[:, m:m + 1, :],
             x1s[:, b:b + 1, :])
        sy.dma_start_transpose(
            bTf[:, m * 1024:(m + 1) * 1024].rearrange("p (a q) -> p a q", a=8),
            x1s[:, b, :].rearrange("p (a q) -> p a q", a=8)
        ).then_inc(qs, 16)
        cs[0] += 16
        tr_cnt.append(cs[0])
    swait()
    BAR()
    gdma(w1v[:, :, :], w1t_in.rearrange("(j p) h -> p j h", p=128))
    gdma(w3v[:, :, :], w3t_in.rearrange("(j p) h -> p j h", p=128))

    if dbg == "x1":
        for m in range(16):
            b = m % 2
            gdma(x1s[:, b:b + 1, :],
                 ag_x_in.ap().rearrange("(m p) c -> p m c", p=128)[:, m:m + 1, :])
            gwait()
            BAR()
            v.tensor_copy(outt[:, 0, :], x1s[:, b, :])
            BAR()
            sdma(out.rearrange("(m p) c -> p m c", p=128)[:, m:m + 1, :],
                 outt[:, 0:1, :])
            swait()
        BAR()
        return nc

    x1Tv = bTf.rearrange("p (m a q) -> p m a q", m=16, a=8)
    for rt in range(16):
        for j in range(8):
            te.matmul(pX[:, :], extv[:, j, 16 - rt:144 - rt],
                      x1Tv[:, :, j, :].rearrange("p m (q r) -> p r m q", r=16)[:, rt, :, :],
                      start=(rt == 0 and j == 0), stop=(rt == 15 and j == 7))
    BAR()
    v.tensor_copy(junk[:, 0:128], pX[:, :])
    BAR()
    sdma(ag_t_in.ap()[:, :], junk[:, 0:128])
    swait()
    BAR()
    coll(GRP8, [ag_t_in.ap().opt()], [ag_t_out.ap().opt()])
    cc_agt = cc[0]
    g.wait_ge(qc, cc[0])
    coll(GRP8, [ag_x_in.ap().opt()], [ag_x_out.ap().opt()])
    sy.wait_ge(qc, cc_agt)
    sdma(Ssc[:, :].rearrange("er (kk c) -> er kk c", c=128),
         ag_t_out.ap().rearrange("(kk er) c -> er kk c", er=128))
    sdma(ohk16[:, :], ohk16_in[:, :])
    sdma(row16[:, :], row16_in[:, :])
    swait()
    BAR()
    if dbg == "bis":
        v.tensor_copy(lnt[:, 0, 233:361], Ssc[:, 0:128])
        v.reduce_max(lnt[:, 0, 361:362], Ssc[:, :], axis=AX)
        v.reduce_max(lnt[:, 0, 362:363], Ssc[:, 0:128], axis=AX)
        BAR()

    v.tensor_single_scalar(bisI[:, :], iop[:, :], 15.0, op=AL.bitwise_and)
    BAR()
    v.tensor_copy(stt[:, 0:1], bisI[:, :])
    BAR()
    v.tensor_scalar(vid[:, :], iocf[:, :], 16.0, stt[:, 0:1], op0=AL.mult, op1=AL.add)
    v.reduce_max(bis[:, 0:1], Ssc[:, :], axis=AX)
    v.tensor_reduce(bis[:, 1:2], Ssc[:, :], axis=AX, op=AL.min, negate=True)
    BAR()
    if dbg == "bis":
        v.tensor_copy(lnt[:, 0, 100:102], bis[:, 0:2])
        BAR()
    te.transpose(pX[0:2, :], bis[:, 0:2], idT[:, :])
    BAR()
    v.tensor_copy(scrG[0:2, 0:128], pX[0:2, 0:128])
    BAR()
    if dbg == "bis":
        v.tensor_copy(lnt[0:2, 0, 102:230], scrG[0:2, 0:128])
        BAR()
    v.reduce_max(scr8[0:2, 0:1], scrG[0:2, 0:128], axis=AX)
    BAR()
    te.transpose(pX[0:1, 0:2], scr8[0:2, 0:1], idT[0:2, 0:2])
    BAR()
    v.tensor_copy(stt[0:1, 0:2], pX[0:1, 0:2])
    BAR()
    if dbg == "bis":
        v.tensor_copy(lnt[0:2, 0, 230:231], scr8[0:2, 0:1])
        v.tensor_copy(lnt[0:1, 0, 231:233], stt[0:1, 0:2])
        BAR()
    te.matmul(pSm[:, 0:2], onesf[0:1, 0:128], stt[0:1, 0:2], start=True, stop=True)
    BAR()
    v.tensor_scalar_add(bis[:, 0:1], pSm[:, 0:1], 1.0)
    v.tensor_scalar(bis[:, 1:2], pSm[:, 1:2], -1.0, -1.0, op0=AL.mult, op1=AL.add)
    v.tensor_scalar_mul(bis[:, 4:5], pSm[:, 0:1], -1.0)
    BAR()
    if dbg == "bis":
        v.tensor_copy(lnt[:, 0, 0:8], bis[:, 0:8])
        BAR()
    for it in range(NBIS if dbg != "bis" else 1):
        v.tensor_add(bis[:, 2:3], bis[:, 0:1], bis[:, 1:2])
        BAR()
        v.tensor_scalar_mul(bis[:, 2:3], bis[:, 2:3], 0.5)
        BAR()
        v.tensor_scalar(junk[:, :], Ssc[:, :], bis[:, 2:3], None, op0=AL.is_ge)
        BAR()
        v.reduce_sum(cnts[:, :], junk[:, :], axis=AX)
        BAR()
        te.matmul(pSm[:, 0:1], bd16[:, :], cnts[:, :], start=True, stop=True)
        BAR()
        v.tensor_single_scalar(bisI[:, :], pSm[:, 0:1], 2048.0, op=AL.is_ge)
        BAR()
        v.select(bis[:, 1:2], bisI[:, :], bis[:, 2:3], bis[:, 1:2])
        v.select(bis[:, 0:1], bisI[:, :], bis[:, 0:1], bis[:, 2:3])
        BAR()
    if dbg == "bis":
        v.tensor_copy(lnt[:, 0, 8:9], cnts[:, :])
        v.tensor_copy(lnt[:, 0, 9:10], pSm[:, 0:1])
        v.tensor_copy(lnt[:, 0, 10:11], bisI[:, :])
        v.tensor_copy(lnt[:, 0, 11:14], bis[:, 0:3])
        v.tensor_copy(lnt[:, 0, 14:142], Ssc[:, 0:128])
        BAR()
        for m in range(16):
            sdma(out.rearrange("(m p) c -> p m c", p=128)[:, m:m + 1, :],
                 lnt[:, 0:1, :])
        swait()
        BAR()
        return nc
    v.tensor_scalar(msI[:, :], Ssc[:, :], bis[:, 1:2], None, op0=AL.is_ge)
    v.tensor_scalar(mskF[:, :], Ssc[:, :], bis[:, 1:2], None, op0=AL.is_ge)
    BAR()
    v.reduce_sum(cnts[:, :], mskF[:, :], axis=AX)
    sc.activation(out=junk[:, :], in_=Ssc[:, :], func=AF.Exp, bias=bis[:, 4:5])
    BAR()
    v.tensor_mul(junk[:, :], junk[:, :], mskF[:, :])
    BAR()
    v.reduce_sum(st1[:, 0:1], junk[:, :], axis=AX)
    BAR()
    te.transpose(pX[0:1, :], st1[:, 0:1], idT[:, :])
    BAR()
    v.tensor_copy(orow[0:1, :], pX[0:1, 0:128])
    BAR()
    v.tensor_mul(orow[0:1, :], orow[0:1, :], ohk16[0:1, :])
    BAR()
    v.reduce_sum(stt[0:1, 4:5], orow[0:1, :], axis=AX)
    BAR()
    te.matmul(pSm[:, 1:2], onesf[0:1, 0:128], stt[0:1, 4:5], start=True, stop=True)
    BAR()
    v.reciprocal(bis[:, 5:6], pSm[:, 1:2])
    v.scalar_tensor_tensor(out=Vb[:, :], in0=vid[:, :], scalar=1.0, in1=mskF[:, :],
                           op0=AL.add, op1=AL.mult)
    v.tensor_tensor_scan(out=junk[:, :], data0=mskF[:, :], data1=zeros2[:, 0:1024],
                         initial=0.0, op0=AL.add, op1=AL.add)
    BAR()
    v.tensor_scalar_add(Vb[:, :], Vb[:, :], -1.0)
    v.tensor_sub(junk[:, :], iocf[:, :], junk[:, :])
    BAR()
    v.tensor_scalar_add(junk[:, :], junk[:, :], 1.0)
    BAR()
    v.tensor_mul(junk[:, :], junk[:, :], mskF[:, :])
    BAR()
    v.tensor_copy(Db[:, :], junk[:, :])
    BAR()

    def butterfly(Vb_, Vb2_, Db_, Db2_):
        Vcur, Vnx, Dcur, Dnx = Vb_, Vb2_, Db_, Db2_
        for b in range(10):
            s = 1 << b
            v.tensor_single_scalar(msI[:, :], Dcur[:, :], float(s), op=AL.bitwise_and)
            v.tensor_single_scalar(DmS[:, :], Dcur[:, :], float(s), op=AL.subtract)
            BAR()
            v.select(Vnx[:, 0:1024 - s], msI[:, s:], Vcur[:, s:], Vcur[:, 0:1024 - s])
            v.tensor_copy(Vnx[:, 1024 - s:], Vcur[:, 1024 - s:])
            v.select(Dnx[:, 0:1024 - s], msI[:, s:], DmS[:, s:], Dcur[:, 0:1024 - s])
            v.tensor_copy(Dnx[:, 1024 - s:], Dcur[:, 1024 - s:])
            BAR()
            Vcur, Vnx, Dcur, Dnx = Vnx, Vcur, Dnx, Dcur
        return Vcur

    CT = butterfly(Vb, Vb2, Db, Db2)
    te.transpose(pX[0:1, :], cnts[:, :], idT[:, :])
    BAR()
    v.tensor_copy(orow[0:1, :], pX[0:1, 0:128])
    BAR()
    sdma(orow_d.ap()[:, :], orow[0:1, :])
    swait()
    BAR()
    gdma(orow8[:, :], orow_d.ap().rearrange("o (e r) -> (o e) r", e=8))
    gwait()
    BAR()
    v.tensor_tensor_scan(out=orow8[:, :], data0=orow8[:, :], data1=zeros2[0:8, 0:16],
                         initial=0.0, op0=AL.add, op1=AL.add)
    BAR()
    gdma(exsb[0:8, 0:16], orow_d.ap().rearrange("o (e r) -> (o e) r", e=8))
    gwait()
    BAR()
    v.tensor_sub(orow8[:, :], orow8[:, :], exsb[0:8, 0:16])
    BAR()
    gdma(orow_d.ap().rearrange("o (e r) -> (o e) r", e=8), orow8[:, :])
    gwait()
    BAR()
    sdma(orow[0:1, :], orow_d.ap()[:, :])
    swait()
    BAR()
    v.tensor_mul(exsb[0:1, 0:128], orow[0:1, :], ohk16[0:1, :])
    BAR()
    v.reduce_sum(wraw[0:1, 0:16], exsb[0:1, 0:128].rearrange("o (e r) -> o r e", e=8),
                 axis=AX)
    BAR()
    v.tensor_copy(oint[0:1, :], wraw[0:1, 0:16])
    te.transpose(pX[:, 0:1], orow[0:1, :], idT[0:1, 0:1])
    BAR()
    v.tensor_copy(obase[:, :], pX[:, 0:1])
    v.tensor_copy(Vc[:, :], CT[:, :])
    BAR()
    sdma(vc_d.ap()[:, :], Vc[:, :])
    swait()
    BAR()
    rv = nc.values_load(row16[0:1, 0:1], engines=[mybir.EngineType.SP])
    sy.dma_start(out=cc16[:, :], in_=vc_d.ap()[bass.ds(rv, 16), :]).then_inc(qh, 16)
    ch[0] += 16
    sy.wait_ge(qh, ch[0])
    sc.wait_ge(qh, ch[0])
    for r in range(16):
        eng = mybir.EngineType.SP if r < 8 else mybir.EngineType.Activation
        ov = nc.values_load(oint[0:1, r:r + 1], engines=[eng])
        issuer = sy if r < 8 else sc
        issuer.wait_ge(qh, ch[0])
        issuer.dma_start(out=l_list.ap()[0:1, bass.ds(ov, 1024)],
                         in_=cc16[r:r + 1, :]).then_inc(qh, 16)
        ch[0] += 16
    sy.wait_ge(qh, ch[0])
    g.wait_ge(qh, ch[0])
    gdma(FL[:, :], l_list.ap()[0:1, 0:2048].rearrange("o (p q) -> (o p) q", q=16))
    gwait()
    g.wait_ge(qc, cc[0])
    BAR()

    if dbg in ("sel", "sel2"):
        v.tensor_copy(lnt[:, 0, 0:16], FL[:, :])
        v.tensor_copy(lnt[:, 0, 16:17], cnts[:, :])
        v.tensor_copy(lnt[:, 0, 17:18], bis[:, 1:2])
        v.tensor_copy(lnt[:, 0, 18:19], obase[:, :])
        v.tensor_copy(lnt[:, 0, 19:20], bis[:, 5:6])
        v.memset(lnt[:, 0, 20:1024], 0.0)
        BAR()
        for m in range(16):
            sdma(out.rearrange("(m p) c -> p m c", p=128)[:, m:m + 1, :],
                 lnt[:, 0:1, :])
        swait()
        BAR()
        return nc

    v.tensor_copy(bisI[:, :], kv0[:, 0:1])
    BAR()
    v.tensor_single_scalar(bisI[:, :], bisI[:, :], 7.0, op=AL.logical_shift_right)
    v.tensor_single_scalar(rowT[:, :], FL[:, :], 11.0, op=AL.logical_shift_right)
    v.tensor_single_scalar(msI[:, 0:16], FL[:, :], 15.0, op=AL.bitwise_and)
    BAR()
    v.tensor_single_scalar(rowT[:, :], rowT[:, :], 7.0, op=AL.logical_shift_left)
    BAR()
    v.tensor_tensor(rowT[:, :], rowT[:, :], msI[:, 0:16], op=AL.bitwise_or)
    BAR()
    v.tensor_scalar(rowT[:, :], rowT[:, :], bisI[:, 0:1], None, op0=AL.bitwise_or)
    v.tensor_single_scalar(msI[:, 0:16], FL[:, :], 4.0, op=AL.logical_shift_right)
    BAR()
    v.tensor_single_scalar(msI[:, 0:16], msI[:, 0:16], 127.0, op=AL.bitwise_and)
    BAR()
    v.tensor_copy(colT[:, :], msI[:, 0:16])
    BAR()
    gdone = []
    for gi in range(16):
        g.indirect_dma_start(
            out=scrG[:, 128 * (gi % 2):128 * (gi % 2) + 128], out_offset=None,
            in_=ag_t_out.ap(),
            in_offset=IndirectOffsetOnAxis(rowT[:, gi:gi + 1], axis=0),
        ).then_inc(qg, 16)
        cg[0] += 16
        gdone.append(cg[0])
        if gi > 0:
            v.wait_ge(qg, gdone[gi - 1])
            b0 = 128 * ((gi - 1) % 2)
            v.tensor_scalar(junk[:, 0:128], iocf[:, 0:128],
                            colT[:, gi - 1:gi], None, op0=AL.is_equal)
            BAR()
            v.tensor_mul(junk[:, 0:128], junk[:, 0:128], scrG[:, b0:b0 + 128])
            BAR()
            v.reduce_sum(wraw[:, gi - 1:gi], junk[:, 0:128], axis=AX)
    v.wait_ge(qg, gdone[15])
    v.tensor_scalar(junk[:, 0:128], iocf[:, 0:128], colT[:, 15:16], None,
                    op0=AL.is_equal)
    BAR()
    v.tensor_mul(junk[:, 0:128], junk[:, 0:128], scrG[:, 128:256])
    BAR()
    v.reduce_sum(wraw[:, 15:16], junk[:, 0:128], axis=AX)
    BAR()
    sc.activation(out=wnrm[:, :], in_=wraw[:, :], func=AF.Exp, bias=bis[:, 4:5])
    BAR()
    v.tensor_scalar(wnrm[:, :], wnrm[:, :], bis[:, 5:6], None, op0=AL.mult)
    BAR()

    xgB = [xg4, xg4b]
    ffn_gq = []
    ffn_tq = []
    ydma_cnt = []
    for m4 in range(4):
        g.indirect_dma_start(
            out=xgB[0][:, m4, :], out_offset=None, in_=ag_x_out.ap(),
            in_offset=IndirectOffsetOnAxis(FL[:, m4:m4 + 1], axis=0),
        ).then_inc(qg, 16)
        cg[0] += 16
    ffn_gq.append(cg[0])
    tv0 = eTV.n
    vt0 = eVT.n
    ts0 = eTS.n
    xgt_free = []
    for G in range(4):
        b = G % 2
        sy.wait_ge(qg, ffn_gq[G])
        if G >= 1:
            sy.wait_ge(eTV.sem, xgt_free[G - 1])
        for m4 in range(4):
            sy.dma_start_transpose(
                xgt[:, m4, :, :],
                xgB[b][:, m4, :].rearrange("p (a q) -> p a q", a=8)
            ).then_inc(qs, 16)
            cs[0] += 16
        ffn_tq.append(cs[0])
        if G < 3:
            if G >= 1:
                g.wait_ge(qs, ffn_tq[G - 1])
            for m4 in range(4):
                gi = 4 * (G + 1) + m4
                g.indirect_dma_start(
                    out=xgB[(G + 1) % 2][:, m4, :], out_offset=None,
                    in_=ag_x_out.ap(),
                    in_offset=IndirectOffsetOnAxis(FL[:, gi:gi + 1], axis=0),
                ).then_inc(qg, 16)
                cg[0] += 16
            ffn_gq.append(cg[0])
        te.wait_ge(qs, ffn_tq[G])
        last_w3 = None
        for hc in range(16):
            hcg = 16 * G + hc
            pA = pjA[:, :] if hc % 2 == 0 else pyA[:, 0:512]
            pB = pjB[:, :] if hc % 2 == 0 else pyA[:, 512:1024]
            if hcg >= 2:
                te.wait_ge(eVT.sem, vt0 + hcg - 1)
            r = None
            for a in range(8):
                r = te.matmul(pA, w1v[:, a, hc * 128:(hc + 1) * 128],
                              xgt[:, :, a, :], start=(a == 0), stop=(a == 7))
            eTS.inc(r)
            ts_cnt = eTS.n
            for a in range(8):
                r = te.matmul(pB, w3v[:, a, hc * 128:(hc + 1) * 128],
                              xgt[:, :, a, :], start=(a == 0), stop=(a == 7))
            eTV.inc(r)
            last_w3 = eTV.n
            sc.wait_ge(eTS.sem, ts_cnt)
            if hcg >= 1:
                sc.wait_ge(eVT.sem, vt0 + hcg)
            r = sc.activation(out=scr512[:, :], in_=pA, func=AF.Silu)
            eSV.inc(r)
            eSV.wait(v)
            v.wait_ge(eTV.sem, last_w3)
            r = v.tensor_mul(hpr[:, hc, :], scr512[:, :], pB)
            eVT.inc(r)
        xgt_free.append(last_w3)
        for d in range(4):
            py = pyB if d % 2 == 0 else pyA
            gi = 4 * G + d
            if d % 2 == 1:
                te.wait_ge(eVT.sem, vt0 + 16 * G + 16)
            if 4 * G + d >= 2:
                te.wait_ge(eVD.sem, eVD.n - 2 + 1)
            r = None
            for hc in range(16):
                r = mm(py, hpr[:, hc, d * 128:(d + 1) * 128], w2v[:, hc, :],
                       (hc == 0), (hc == 15), C)
            eTV.inc(r)
            v.wait_ge(eTV.sem, eTV.n)
            if len(ydma_cnt) >= 2:
                v.wait_ge(qs, ydma_cnt[-2])
            r = v.tensor_scalar(yst[:, d % 2, :], py[:, :], wnrm[:, gi:gi + 1],
                                None, op0=AL.mult)
            eVD.inc(r)
            eVD.wait(sy)
            sdma(ag_y_in.ap().rearrange("(p q) c -> p q c", q=16)[:, gi:gi + 1, :],
                 yst[:, d % 2:d % 2 + 1, :])
            ydma_cnt.append(cs[0])
    swait()
    BAR()
    coll(GRP8, [ag_y_in.ap().opt()], [y_all.ap()[0:NCORE * CAP, :].opt()])

    v.tensor_scalar(mskF[:, :], iocf[:, :], cnts[:, 0:1], None, op0=AL.is_lt)
    BAR()
    v.scalar_tensor_tensor(out=CT[:, :], in0=CT[:, :], scalar=1.0, in1=mskF[:, :],
                           op0=AL.add, op1=AL.mult)
    BAR()
    v.tensor_scalar_add(CT[:, :], CT[:, :], -1.0)
    BAR()
    v.tensor_scalar(junk[:, :], iocf[:, :], obase[:, 0:1], None, op0=AL.add)
    BAR()
    v.scalar_tensor_tensor(out=junk[:, :], in0=junk[:, :], scalar=2048.0,
                           in1=CT[:, :], op0=AL.mult, op1=AL.add)
    BAR()
    v.tensor_scalar(junk[:, :], junk[:, :], kv0[:, 0:1], None, op0=AL.subtract)
    BAR()
    v.tensor_scalar(msI[:, :], CT[:, :], kv0[:, 0:1], None, op0=AL.is_ge)
    v.tensor_scalar(DmS[:, :], CT[:, :], kv0[:, 1:2], None, op0=AL.is_lt)
    BAR()
    v.tensor_tensor(msI[:, :], msI[:, :], DmS[:, :], op=AL.bitwise_and)
    BAR()
    v.tensor_copy(mskF[:, :], msI[:, :])
    BAR()
    v.scalar_tensor_tensor(out=Vb2[:, :], in0=junk[:, :], scalar=1.0, in1=mskF[:, :],
                           op0=AL.add, op1=AL.mult)
    v.reduce_sum(cnt2[:, :], mskF[:, :], axis=AX)
    v.tensor_tensor_scan(out=junk[:, :], data0=mskF[:, :], data1=zeros2[:, 0:1024],
                         initial=0.0, op0=AL.add, op1=AL.add)
    BAR()
    v.tensor_scalar_add(Vb2[:, :], Vb2[:, :], -1.0)
    v.tensor_sub(junk[:, :], iocf[:, :], junk[:, :])
    BAR()
    v.tensor_scalar_add(junk[:, :], junk[:, :], 1.0)
    BAR()
    v.tensor_mul(junk[:, :], junk[:, :], mskF[:, :])
    BAR()
    v.tensor_copy(Vb[:, :], Vb2[:, :])
    v.tensor_copy(Db2[:, :], junk[:, :])
    BAR()
    v.tensor_copy(Db[:, :], Db2[:, :])
    BAR()
    CB = butterfly(Vb, Vb2, Db, Db2)
    v.tensor_copy(Db2[:, 0:C2], CB[:, 0:C2])
    BAR()
    v.tensor_single_scalar(srcI[:, :], Db2[:, 0:C2], 11.0, op=AL.logical_shift_right)
    v.tensor_single_scalar(dstI[:, :], Db2[:, 0:C2], 2047.0, op=AL.bitwise_and)
    v.tensor_scalar(msI[:, 0:C2], iocf[:, 0:C2], cnt2[:, 0:1], None, op0=AL.is_lt)
    BAR()
    v.tensor_scalar(srcI[:, :], srcI[:, :], eab[:, 0:1], None, op0=AL.bitwise_or)
    BAR()
    if dbg == "bwd":
        v.tensor_copy(lnt[:, 0, 67:99], msI[:, 0:C2])
        v.tensor_copy(lnt[:, 0, 99:131], srcI[:, :])
        v.tensor_copy(lnt[:, 0, 131:163], dstI[:, :])
        BAR()
    v.select(DmS[:, 0:C2], msI[:, 0:C2], srcI[:, :], sentA[:, :])
    v.select(DmS[:, C2:2 * C2], msI[:, 0:C2], dstI[:, :], sentB[:, :])
    BAR()
    v.tensor_copy(srcI[:, :], DmS[:, 0:C2])
    v.tensor_copy(dstI[:, :], DmS[:, C2:2 * C2])
    g.wait_ge(qc, cc[0])
    BAR()
    if dbg == "bwd":
        v.tensor_copy(lnt[:, 0, 0:32], srcI[:, :])
        v.tensor_copy(lnt[:, 0, 32:64], dstI[:, :])
        v.tensor_copy(lnt[:, 0, 64:65], cnt2[:, :])
        v.tensor_copy(lnt[:, 0, 65:66], obase[:, :])
        v.tensor_copy(lnt[:, 0, 66:67], cnts[:, :])
        v.memset(lnt[:, 0, 163:1024], 0.0)
        BAR()
        for m in range(16):
            sdma(out.rearrange("(m p) c -> p m c", p=128)[:, m:m + 1, :],
                 lnt[:, 0:1, :])
        swait()
        BAR()
        return nc
    sdma(di_d.ap()[:, :], dstI[:, :])
    swait()
    BAR()
    sdma(dstE[:, :, :], di_d.ap().rearrange("(e r) gi -> r e gi", e=8))
    swait()
    BAR()
    for gi in range(C2):
        g.indirect_dma_start(
            out=Gbig[:, gi, :], out_offset=None,
            in_=y_all.ap(),
            in_offset=IndirectOffsetOnAxis(srcI[:, gi:gi + 1], axis=0),
        ).then_inc(qg, 16)
        cg[0] += 16
    gwait()
    BAR()
    for e in range(8):
        for gi in range(C2):
            g.indirect_dma_start(
                out=acc_d.ap(),
                out_offset=IndirectOffsetOnAxis(dstE[0:16, e, gi:gi + 1], axis=0),
                in_=Gbig[16 * e:16 * e + 16, gi, :], in_offset=None,
                compute_op=AL.add,
            ).then_inc(qh, 16)
            ch[0] += 16
        g.wait_ge(qh, ch[0])
    BAR()

    ln2_base = (eVX.n, None, eSV.n, eVD.n)
    acct_cnt = []
    x1g_cnt = []
    for m in range(16):
        b = m % 2
        if m >= 2:
            sy.wait_ge(eVX.sem, ln2_base[0] + m - 1)
            g.wait_ge(eVX.sem, ln2_base[0] + m - 1)
        sdma(acct[:, b:b + 1, :],
             acc_d.ap()[0:T, :].rearrange("(m p) c -> p m c", p=128)[:, m:m + 1, :])
        acct_cnt.append(cs[0])
        gdma(x1s[:, b:b + 1, :],
             ag_x_in.ap().rearrange("(m p) c -> p m c", p=128)[:, m:m + 1, :])
        x1g_cnt.append(cg[0])
        v.wait_ge(qs, acct_cnt[m])
        v.wait_ge(qg, x1g_cnt[m])
        r = v.scalar_tensor_tensor(out=lnt[:, b, :], in0=x1s[:, b, :], scalar=1.0,
                                   in1=acct[:, b, :], op0=AL.mult, op1=AL.add,
                                   accum_out=st1[:, m:m + 1])
        eVX.inc(r)
        v.scalar_tensor_tensor(out=lnsq[:, :], in0=lnt[:, b, :], scalar=1.0,
                               in1=lnt[:, b, :], op0=AL.mult, op1=AL.mult,
                               accum_out=st2[:, m:m + 1])
        if m >= 2:
            eSV.wait(v, ln2_base[2] + 2 * (m - 1))
        r = v.tensor_scalar_mul(st1[:, m:m + 1], st1[:, m:m + 1], 1.0 / C)
        eVS.inc(r)
        eVS.wait(sc)
        r = sc.activation(out=stt[:, 4 + b:5 + b], in_=st1[:, m:m + 1],
                          func=AF.Square)
        eSV.inc(r)
        eSV.wait(v)
        r = v.tensor_scalar(stt[:, b:b + 1], st2[:, m:m + 1], 1.0 / C,
                            stt[:, 4 + b:5 + b], op0=AL.mult, op1=AL.subtract)
        eVS.inc(r)
        eVS.wait(sc)
        r = sc.activation(out=stt[:, 2 + b:3 + b], in_=stt[:, b:b + 1],
                          func=AF.Sqrt, bias=epsv[:, 0:1])
        eSV.inc(r)
        eSV.wait(v)
        v.reciprocal(stt[:, 6 + b:7 + b], stt[:, 2 + b:3 + b])
        v.tensor_copy(lnsq[:, 0:384], zeros2[:, 0:384])
        if m >= 2:
            v.wait_ge(qo, 16 * (m - 1))
        r = v.tensor_scalar(outt[:, b, :], lnt[:, b, :], st1[:, m:m + 1],
                            stt[:, 6 + b:7 + b], op0=AL.subtract, op1=AL.mult)
        eVD.inc(r)
        eVD.wait(sy)
        sy.dma_start(out=out.rearrange("(m p) c -> p m c", p=128)[:, m:m + 1, :],
                     in_=outt[:, b:b + 1, :]).then_inc(qo, 16)
        co[0] += 16
    sy.wait_ge(qo, co[0])
    BAR()
    return nc


def _host_prep(inputs):
    x = np.asarray(inputs["L0_attn_shot_embeddings"], np.float32).reshape(16384, C)
    Wd = np.ascontiguousarray(np.asarray(inputs["L0_attn_Wd"], np.float32))
    Wi = np.ascontiguousarray(np.asarray(inputs["L0_attn_Wi"], np.float32))
    Wo = np.ascontiguousarray(np.asarray(inputs["L0_attn_Wo"], np.float32))
    Ws = np.ascontiguousarray(np.asarray(inputs["L0_attn_Ws"], np.float32))
    bd = np.ascontiguousarray(np.asarray(inputs["L0_attn_bd"], np.float32))
    bi = np.ascontiguousarray(np.asarray(inputs["L0_attn_bi"], np.float32))
    alog = np.ascontiguousarray(np.asarray(inputs["L0_attn_A_log"], np.float32))
    bo = np.ascontiguousarray(np.asarray(inputs["L0_attn_bo"], np.float32))
    bs = np.ascontiguousarray(np.asarray(inputs["L0_attn_bs"], np.float32))
    Wg = np.asarray(inputs["L0_ffn_Wg"], np.float32)
    W1 = np.asarray(inputs["L0_ffn_W1"], np.float32)
    W2 = np.asarray(inputs["L0_ffn_W2"], np.float32)
    W3 = np.asarray(inputs["L0_ffn_W3"], np.float32)

    ext = np.zeros((8, 128, 144), np.float32)
    for e in range(8):
        for j in range(8):
            ext[j, :, 16 + 16 * e] = Wg[e, 128 * j:128 * (j + 1)]

    maps = []
    for k in range(NCORE):
        xk = np.ascontiguousarray(x[2048 * k:2048 * (k + 1)])
        kv = np.zeros((128, 2), np.float32)
        kv[:, 0] = 2048.0 * k
        kv[:, 1] = 2048.0 * (k + 1)
        eabv = np.zeros((128, 1), np.int32)
        for p in range(128):
            eabv[p, 0] = 2048 * (p // 16)
        ohk16v = np.zeros((1, 128), np.float32)
        ohk16v[0, 16 * k:16 * (k + 1)] = 1.0
        row16v = np.arange(16 * k, 16 * k + 16, dtype=np.int32).reshape(1, 16)
        rot = np.array([[(k + e) % 8 for e in range(8)]], np.int32)
        ohpv = np.zeros((128, 8), np.float32)
        pos = k % 4
        ohpv[:, 2 * pos] = 1.0
        ohpv[:, 2 * pos + 1] = 1.0
        ohkv = np.zeros((128, 8), np.float32)
        ohkv[:, k] = 1.0
        maps.append({
            "x_tok": xk,
            "x_feat": np.ascontiguousarray(xk.T),
            "wd": Wd, "wi": Wi, "wo": Wo, "ws": Ws,
            "bdd": bd, "bii": bi, "alog": alog, "bo": bo, "bss": bs,
            "ext": ext,
            "wgt": np.ascontiguousarray(Wg.T),
            "w1t": np.ascontiguousarray(W1[k].T),
            "w3t": np.ascontiguousarray(W3[k].T),
            "w2t": np.ascontiguousarray(W2[k].T),
            "kvi": kv, "eabi": eabv, "rot": rot, "ohpi": ohpv, "ohki": ohkv,
            "ohk16": ohk16v, "row16": row16v,
        })
    return maps


def kernel(**inputs):
    maps = _host_prep(inputs)
    nc = build(dbg=DBG_STAGE)
    res = run_bass_kernel_spmd(nc, maps, core_ids=list(range(NCORE)))
    outs = np.concatenate([np.asarray(r["out"]) for r in res.results], axis=0)
    return outs.reshape(2, 8192, 1024).astype(np.float32)


# version 04
# speedup vs baseline: 1.1994x; 1.0341x over previous
import numpy as np
import ml_dtypes
import concourse.bass as bass
import concourse.mybir as mybir
from concourse.bass import IndirectOffsetOnAxis
from concourse.bass_utils import run_bass_kernel_spmd

F32 = mybir.dt.float32
BF16 = mybir.dt.bfloat16
I32 = mybir.dt.int32
AX = mybir.AxisListType.X
AF = mybir.ActivationFunctionType
AL = mybir.AluOpType

NCORE = 8
T = 2048
C = 1024
NSD = 256
HF = 2048
CAP = 2048
C2 = 32
EPS = 1e-5
NBIS = 30

DBG_STAGE = None


def build(dbg=None):
    nc = bass.Bass("TRN2", target_bir_lowering=False)

    def din(name, shape, dt=F32):
        return nc.dram_tensor(name, shape, dt, kind="ExternalInput").ap()

    x_tok = din("x_tok", [T, C])
    x_feat = din("x_feat", [C, T])
    wd_in = din("wd", [2, C, NSD])
    wi_in = din("wi", [2, C, NSD])
    wo_in = din("wo", [2, NSD, C])
    ws_in = din("ws", [2, NSD, C])
    bd_in = din("bdd", [2, NSD])
    bi_in = din("bii", [2, NSD])
    alog_in = din("alog", [2, NSD])
    bo_in = din("bo", [2, C])
    bs_in = din("bss", [2, C])
    ext_in = din("ext", [8, 128, 144])
    wgt_in = din("wgt", [C, 8])
    w1t_in = din("w1t", [C, HF])
    w3t_in = din("w3t", [C, HF])
    w2t_in = din("w2t", [HF, C])
    kv_in = din("kvi", [128, 2])
    eab_in = din("eabi", [128, 1], I32)
    rot_in = din("rot", [1, 8], I32)
    ohp_in = din("ohpi", [128, 8])
    ohk_in = din("ohki", [128, 8])
    ohk16_in = din("ohk16", [1, 128])
    row16_in = din("row16", [1, 16], I32)

    out = nc.dram_tensor("out", [T, C], F32, kind="ExternalOutput").ap()

    def dint(name, shape, dt):
        return nc.dram_tensor(name, shape, dt, kind="Internal")

    ag_s_in = [dint(f"ag_s_in{l}", [4, 128], F32) for l in range(2)]
    ag_s_out = [dint(f"ag_s_out{l}", [16, 128], F32) for l in range(2)]
    ag_t_in = dint("ag_t_in", [128, 128], F32)
    ag_t_out = dint("ag_t_out", [1024, 128], F32)
    ag_x_in = dint("ag_x_in", [T, C], BF16)
    ag_x_out = dint("ag_x_out", [NCORE * T, C], BF16)
    ag_y_in = dint("ag_y_in", [CAP, C], BF16)
    y_all = dint("y_all", [NCORE * CAP + 1, C], BF16)
    l_list = dint("l_list", [1, CAP + 1152], I32)
    acc_d = dint("acc_d", [T + 1, C], BF16)
    orow_d = dint("orow_d", [1, 128], F32)
    vc_d = dint("vc_d", [128, 1024], I32)
    di_d = dint("di_d", [128, C2], I32)

    sbt = nc.alloc_sbuf_tensor
    sat = nc.alloc_sbuf_tensor_at

    def addr(h):
        return nc.lookup_mloc(h).addr

    bT = sbt("bT", [128, 8, T], BF16)
    wdv = sbt("wdv", [128, 2, 8, NSD], BF16)
    wiv = sbt("wiv", [128, 2, 8, NSD], BF16)
    wov = sbt("wov", [128, 2, 2, C], BF16)
    wsv = sbt("wsv", [128, 2, 2, C], BF16)
    sA = sbt("sA", [128, 2, T], F32)
    sB = sbt("sB", [128, 2, T], F32)
    sC = sbt("sC", [128, 2, T], F32)
    sD = sbt("sD", [128, 2, T], F32)
    A_bt = addr(bT)
    A_w = addr(wdv)
    A_s = addr(sA)

    w2v = sbt("w2v", [128, 16, C], BF16)
    zeros2 = sbt("zeros2", [128, 2048], F32)
    lnt = sbt("lnt", [128, 2, C], F32)
    bbvb = sbt("bbvb", [1, 2, C], BF16)
    ct2 = sbt("ct2", [128, C], F32)
    x1s = sbt("x1s", [128, 2, C], BF16)
    rowT = sbt("rowT", [128, 16], I32)
    colT = sbt("colT", [128, 16], F32)
    idT = sbt("idT", [128, 128], F32)
    extv = sbt("extv", [128, 8, 144], BF16)
    wgv = sbt("wgv", [128, 8, 8], BF16)
    bd16 = sbt("bd16", [128, 128], F32)
    iocf = sbt("iocf", [128, 1024], F32)
    mskF = sbt("mskF", [128, 1024], F32)
    msI = sbt("msI", [128, 1024], I32)
    bdv = sbt("bdv", [128, 2, 2], F32)
    biv = sbt("biv", [128, 2, 2], F32)
    spav = sbt("spav", [128, 2, 2], F32)
    kv0 = sbt("kv0", [128, 2], F32)
    eab = sbt("eab", [128, 1], I32)
    ohp = sbt("ohp", [128, 8], F32)
    ohk = sbt("ohk", [128, 8], F32)
    exsb = sbt("exsb", [16, 128], F32)
    exch = sbt("exch", [128, 16], F32)
    sm4 = sbt("sm4", [128, 4], F32)
    hins = sbt("hins", [128, 12], F32)
    stb = sbt("stb", [128, 2, 128], BF16)
    bis = sbt("bis", [128, 8], F32)
    bisI = sbt("bisI", [128, 1], I32)
    onesf = sbt("onesf", [1, 128], F32)
    ones1b = sbt("ones1b", [1, 128], BF16)
    cnts = sbt("cnts", [128, 1], F32)
    orow = sbt("orow", [1, 128], F32)
    orow8 = sbt("orow8", [8, 16], F32)
    obase = sbt("obase", [128, 1], F32)
    oint = sbt("oint", [1, 16], I32)
    FL = sbt("FL", [128, 16], I32)
    wraw = sbt("wraw", [128, 16], F32)
    wnrm = sbt("wnrm", [128, 16], F32)
    srcI = sbt("srcI", [128, C2], I32)
    dstI = sbt("dstI", [128, C2], I32)
    sentA = sbt("sentA", [128, C2], I32)
    sentB = sbt("sentB", [128, C2], I32)
    cnt2 = sbt("cnt2", [128, 1], F32)
    st1 = sbt("st1", [128, 16], F32)
    st2 = sbt("st2", [128, 16], F32)
    stt = sbt("stt", [128, 8], F32)
    scr8 = sbt("scr8", [128, 8], F32)
    epsv = sbt("epsv", [128, 1], F32)
    onesc = sbt("onesc", [128, 1], F32)
    iop = sbt("iop", [128, 1], I32)
    iopf = sbt("iopf", [128, 1], F32)
    ctTb = sbt("ctTb", [128, 8], BF16)

    KB = 1024
    lnsq = sat("lnsq", [128, C], F32, offset=A_s + 32 * KB)
    Ssc = sat("Ssc", [128, 1024], F32, offset=A_bt + 0)
    vid = sat("vid", [128, 1024], F32, offset=A_bt + 4 * KB)
    junk = sat("junk", [128, 1024], F32, offset=A_bt + 8 * KB)
    Vb = sat("Vb", [128, 1024], F32, offset=A_bt + 12 * KB)
    Vb2 = sat("Vb2", [128, 1024], F32, offset=A_bt + 16 * KB)
    Db = sat("Db", [128, 1024], I32, offset=A_bt + 20 * KB)
    Db2 = sat("Db2", [128, 1024], I32, offset=A_bt + 24 * KB)
    DmS = sat("DmS", [128, 1024], I32, offset=A_bt + 28 * KB)
    yst = sat("yst", [128, 2, C], BF16, offset=A_bt + 0)
    scr512 = sat("scr512", [128, 512], F32, offset=A_bt + 4 * KB)
    scrG = sat("scrG", [128, 512], F32, offset=A_w + 0)
    cc16 = sat("cc16", [16, 1024], I32, offset=A_w + 4 * KB)
    dstE = sat("dstE", [16, 8, C2], I32, offset=A_w + 4 * KB)
    Gbig = sat("Gbig", [128, C2, C], BF16, offset=A_s + 0)
    Vc = Db
    xgt = sat("xgt", [128, 4, 8, 128], BF16, offset=A_w + 0)
    xg4 = sat("xg4", [128, 4, C], BF16, offset=A_w + 8 * KB)
    xg4b = sat("xg4b", [128, 4, C], BF16, offset=A_bt + 24 * KB)
    hpr = sat("hpr", [128, 16, 512], BF16, offset=A_w + 16 * KB)
    zrowb = sat("zrowb", [128, C], BF16, offset=A_s + 56 * KB)
    ohk16 = sat("ohk16s", [1, 128], F32, offset=addr(stb))
    row16 = sat("row16s", [1, 16], I32, offset=addr(exch))
    hsb = sat("hsb", [128, 2, T], BF16, offset=A_s + 0)
    xpt = sat("xpt", [128, 2, C], F32, offset=A_s + 8 * KB)
    w1v = sat("w1v", [128, 8, HF], BF16, offset=A_s + 0)
    w3v = sat("w3v", [128, 8, HF], BF16, offset=A_s + 32 * KB)
    Gb = sat("Gb", [128, 2, C], BF16, offset=A_s + 0)
    acct = sat("acct", [128, 2, C], BF16, offset=A_s + 8 * KB)
    outt = sat("outt", [128, 2, C], F32, offset=A_s + 16 * KB)

    pjA = nc.alloc_psum_tensor("pjA", [128, 512], F32)
    pjB = nc.alloc_psum_tensor("pjB", [128, 512], F32)
    pyA = nc.alloc_psum_tensor("pyA", [128, 1024], F32)
    pyB = nc.alloc_psum_tensor("pyB", [128, 1024], F32)
    pX = nc.alloc_psum_tensor("pX", [128, 128], F32)
    pSm = nc.alloc_psum_tensor("pSm", [128, 2], F32)

    g = nc.gpsimd
    sy = nc.sync
    v = nc.vector
    sc = nc.scalar
    te = nc.tensor
    BAR = nc.all_engine_barrier

    qg = nc.alloc_semaphore("qg")
    qs = nc.alloc_semaphore("qs")
    qc = nc.alloc_semaphore("qc")
    qh = nc.alloc_semaphore("qh")
    qo = nc.alloc_semaphore("qo")
    cg = [0]; cs = [0]; cc = [0]; ch = [0]; co = [0]

    def gdma(dst, src, **kw):
        r = g.dma_start(out=dst, in_=src, **kw).then_inc(qg, 16)
        cg[0] += 16
        return r

    def gwait():
        g.wait_ge(qg, cg[0])

    def sdma(dst, src, **kw):
        r = sy.dma_start(out=dst, in_=src, **kw).then_inc(qs, 16)
        cs[0] += 16
        return r

    def swait():
        sy.wait_ge(qs, cs[0])

    def coll(groups, ins, outs):
        g.collective_compute("AllGather", AL.bypass, replica_groups=groups,
                             ins=ins, outs=outs).then_inc(qc, 1)
        cc[0] += 1

    def mm(out_ps, lhsT, rhs, start, stop, width):
        nh = (width + 511) // 512
        r = None
        for h in range(nh):
            w0 = h * 512
            w1 = min(width, w0 + 512)
            r = te.matmul(out_ps[:, w0:w1], lhsT, rhs[:, w0:w1], start=start, stop=stop)
        return r

    class Edge:
        __slots__ = ("sem", "n")
        def __init__(self, name):
            self.sem = nc.alloc_semaphore(name)
            self.n = 0
        def inc(self, r):
            r.then_inc(self.sem, 1)
            self.n += 1
            return self.n
        def wait(self, eng, n=None):
            eng.wait_ge(self.sem, self.n if n is None else n)

    eTV = Edge("eTV")
    eVT = Edge("eVT")
    eVS = Edge("eVS")
    eSV = Edge("eSV")
    eVX = Edge("eVX")
    eVD = Edge("eVD")
    eVG = Edge("eVG")
    eTS = Edge("eTS")

    GRP8 = [list(range(8))]
    GRP4 = [[0, 1, 2, 3], [4, 5, 6, 7]]

    gdma(wdv[:, :, :, :], wd_in.rearrange("l (j p) n -> p l j n", p=128))
    gdma(wiv[:, :, :, :], wi_in.rearrange("l (j p) n -> p l j n", p=128))
    c_wdwi = cg[0]
    bt_cnt = []
    for j in range(8):
        gdma(bT[:, j, :],
             x_feat.rearrange("(j p) t -> p j t", p=128)[:, j, :])
        bt_cnt.append(cg[0])
    gdma(wov[:, :, :, :], wo_in.rearrange("l (a p) c -> p l a c", p=128))
    c_wov = cg[0]
    gdma(wsv[:, :, :, :], ws_in.rearrange("l (a p) c -> p l a c", p=128))
    c_wsv = cg[0]
    gdma(extv[:, :, :], ext_in.rearrange("j p f -> p j f"))
    c_ext = cg[0]
    gdma(wgv[:, :, :], wgt_in.rearrange("(j p) e -> p j e", p=128))
    gdma(w2v[:, :, :], w2t_in.rearrange("(j p) c -> p j c", p=128))
    c_w2v = cg[0]
    sdma(bdv[:, :, :], bd_in.rearrange("l (a p) -> p l a", p=128), allow_slow_non_contiguous=True)
    sdma(biv[:, :, :], bi_in.rearrange("l (a p) -> p l a", p=128), allow_slow_non_contiguous=True)
    sdma(spav[:, :, :], alog_in.rearrange("l (a p) -> p l a", p=128), allow_slow_non_contiguous=True)
    sdma(kv0[:, :], kv_in[:, :])
    sdma(eab[:, :], eab_in[:, :])
    sdma(ohp[:, :], ohp_in[:, :])
    sdma(ohk[:, :], ohk_in[:, :])
    sdma(lnt[0:1, 0, :], bo_in[0:1, :])
    sdma(lnt[0:1, 1, :], bs_in[0:1, :])
    sdma(ct2[0:1, 0:C], bo_in[1:2, :])
    sdma(sA[0:1, 0, 0:C], bs_in[1:2, :])
    g.iota(iop[:, :], pattern=[[0, 1]], base=0, channel_multiplier=1)
    g.iota(iocf[:, :], pattern=[[1, 1024]], base=0, channel_multiplier=0,
           allow_small_or_imprecise_dtypes=True)
    swait()
    BAR()
    v.tensor_add(bbvb[0:1, 0, :], lnt[0:1, 0, :], lnt[0:1, 1, :])
    v.tensor_add(bbvb[0:1, 1, :], ct2[0:1, 0:C], sA[0:1, 0, 0:C])
    v.memset(zeros2[:, :], 0.0)
    v.memset(zrowb[:, :], 0.0)
    v.memset(onesf[:, :], 1.0)
    v.memset(ones1b[:, :], 1.0)
    v.memset(sentA[:, :], NCORE * CAP)
    v.memset(sentB[:, :], T)
    v.memset(epsv[:, :], float(EPS))
    v.memset(onesc[:, :], 1.0)
    v.tensor_copy(iopf[:, :], iop[:, :])
    sc.activation(out=stt[:, 4:8], in_=spav[:, :, :].rearrange("p l a -> p (l a)"),
                  func=AF.Abs)
    BAR()
    sc.activation(out=stt[:, 4:8], in_=stt[:, 4:8], func=AF.Exp, scale=-1.0)
    BAR()
    sc.activation(out=stt[:, 4:8], in_=stt[:, 4:8], func=AF.Ln, bias=onesc[:, 0:1])
    BAR()
    v.tensor_scalar_max(st2[:, 4:8], spav[:, :, :].rearrange("p l a -> p (l a)"), 0.0)
    BAR()
    v.tensor_add(st2[:, 4:8], st2[:, 4:8], stt[:, 4:8])
    BAR()
    v.tensor_scalar_mul(spav[:, :, :].rearrange("p l a -> p (l a)"), st2[:, 4:8], -1.0)
    v.tensor_scalar(idT[:, :], iocf[:, 0:128], iopf[:, 0:1], None, op0=AL.is_equal)
    v.tensor_single_scalar(bisI[:, :], iop[:, :], 15.0, op=AL.bitwise_and)
    v.tensor_single_scalar(msI[:, 0:1], iop[:, :], 4.0, op=AL.logical_shift_right)
    g.iota(bd16[:, :], pattern=[[1, 8], [0, 16]], base=0, channel_multiplier=0,
           allow_small_or_imprecise_dtypes=True)
    BAR()
    v.tensor_copy(stt[:, 1:2], msI[:, 0:1])
    BAR()
    v.tensor_scalar(bd16[:, :], bd16[:, :], stt[:, 1:2], None, op0=AL.is_equal)
    for m in range(16):
        sdma(acc_d.ap()[0:T, :].rearrange("(m p) c -> p m c", p=128)[:, m:m + 1, :],
             zrowb[:, :].rearrange("p (o c) -> p o c", o=1))
    sdma(acc_d.ap()[T:T + 1, :], zrowb[0:1, :])
    sdma(y_all.ap()[NCORE * CAP:NCORE * CAP + 1, :], zrowb[0:1, :])
    swait()
    BAR()

    def ssm_layer(l, rhsbuf):
        evac_cnt = []
        if l == 0:
            te.wait_ge(qg, c_wdwi)
        for it in range(8):
            tsl, a = it // 2, it % 2
            t0 = tsl * 512
            pA = pjA[:, :] if it % 2 == 0 else pyA[:, 0:512]
            pB = pjB[:, :] if it % 2 == 0 else pyA[:, 512:1024]
            if it >= 2:
                te.wait_ge(eVT.sem, evac_cnt[it - 2])
            r = None
            for j in range(8):
                if l == 0 and it == 0:
                    te.wait_ge(qg, bt_cnt[j])
                r = te.matmul(pA, wdv[:, l, j, a * 128:(a + 1) * 128],
                              rhsbuf[:, j, t0:t0 + 512], start=(j == 0), stop=(j == 7))
            for j in range(8):
                r = te.matmul(pB, wiv[:, l, j, a * 128:(a + 1) * 128],
                              rhsbuf[:, j, t0:t0 + 512], start=(j == 0), stop=(j == 7))
            eTV.inc(r)
            v.wait_ge(eTV.sem, eTV.n)
            v.tensor_scalar(sA[:, a, t0:t0 + 512], pA,
                            bdv[:, l, a:a + 1], None, op0=AL.add)
            r = v.tensor_scalar(sB[:, a, t0:t0 + 512], pB,
                                biv[:, l, a:a + 1], None, op0=AL.add)
            eVT.inc(r)
            evac_cnt.append(eVT.n)
        if build.dbg_layer == (l, "raw"):
            for a in range(2):
                for h in range(2):
                    sdma(build.dbg_out.rearrange("(z p) c -> p z c", p=128)[:, 2 * a + h:2 * a + h + 1, :],
                         sA[:, a, h * 1024:(h + 1) * 1024].rearrange("p (o c) -> p o c", o=1))
                    sdma(build.dbg_out.rearrange("(z p) c -> p z c", p=128)[:, 8 + 2 * a + h:8 + 2 * a + h + 1, :],
                         sB[:, a, h * 1024:(h + 1) * 1024].rearrange("p (o c) -> p o c", o=1))
            swait()
            BAR()
            build.dbg_early[0] = True
        if build.dbg_early[0]:
            return
        for a in range(2):
            sc.wait_ge(eVT.sem, evac_cnt[6 + a])
            r = sc.activation(out=sC[:, a, :], in_=sA[:, a, :], func=AF.Abs)
            eSV.inc(r)
            sc.activation(out=sC[:, a, :], in_=sC[:, a, :], func=AF.Exp, scale=-1.0)
            r = sc.activation(out=sC[:, a, :], in_=sC[:, a, :], func=AF.Ln,
                              bias=onesc[:, 0:1])
            eSV.inc(r)
        sv_sp = eSV.n
        for a in range(2):
            eSV.wait(v, sv_sp - 3 + 2 * a)
            v.tensor_scalar_max(sA[:, a, :], sA[:, a, :], 0.0)
            eSV.wait(v, sv_sp - 2 + 2 * a)
            v.tensor_add(sA[:, a, :], sA[:, a, :], sC[:, a, :])
            v.tensor_tensor_scan(out=sC[:, a, :], data0=sA[:, a, :],
                                 data1=zeros2[:, :], initial=0.0,
                                 op0=AL.add, op1=AL.add)
            r = v.tensor_mul(sB[:, a, :], sA[:, a, :], sB[:, a, :])
            eVS.inc(r)
            eVS.wait(sc)
            sc.activation(out=sA[:, a, :], in_=sA[:, a, :], func=AF.Exp,
                          scale=spav[:, l, a:a + 1])
            sc.activation(out=sm4[:, a:a + 1], in_=sC[:, a, T - 1:T], func=AF.Exp,
                          scale=spav[:, l, a:a + 1])
            r = sc.activation(out=sC[:, a, :], in_=sC[:, a, :], func=AF.Exp,
                              scale=spav[:, l, a:a + 1])
            eSV.inc(r)
            eSV.wait(v)
            r = v.tensor_tensor_scan(out=sD[:, a, :], data0=sA[:, a, :],
                                     data1=sB[:, a, :], initial=0.0,
                                     op0=AL.mult, op1=AL.add)
        v.tensor_copy(sm4[:, 2:3], sD[:, 0, T - 1:T])
        r = v.tensor_copy(sm4[:, 3:4], sD[:, 1, T - 1:T])
        eVT.inc(r)
        te.wait_ge(eVT.sem, eVT.n)
        te.wait_ge(eSV.sem, eSV.n)
        r = te.transpose(pX[0:4, :], sm4[:, 0:4], idT[:, :])
        eTV.inc(r)
        eTV.wait(v)
        r = v.tensor_copy(exsb[0:4, :], pX[0:4, :])
        eVD.inc(r)
        eVD.wait(sy)
        sdma(ag_s_in[l].ap()[:, :], exsb[0:4, :])
        g.wait_ge(qs, cs[0])
        coll(GRP4, [ag_s_in[l].ap().opt()], [ag_s_out[l].ap().opt()])
        g.wait_ge(qc, cc[0])
        gdma(exsb[:, :], ag_s_out[l].ap()[:, :])
        te.wait_ge(qg, cg[0])
        r = te.transpose(pX[:, 0:16], exsb[0:16, :], idT[0:16, 0:16])
        eTV.inc(r)
        eTV.wait(v)
        v.tensor_copy(exch[:, 0:16], pX[:, 0:16])
        v.memset(hins[:, 0:2], 0.0)
        v.tensor_copy(lnt[:, 0, 0:384], zeros2[:, 0:384])
        for m in range(1, 5):
            for cch in range(2):
                pcol = 4 * (m - 1) + cch
                hcol = 4 * (m - 1) + 2 + cch
                v.tensor_mul(stt[:, cch:cch + 1], exch[:, pcol:pcol + 1],
                             hins[:, 2 * (m - 1) + cch:2 * (m - 1) + cch + 1])
            v.tensor_copy(lnt[:, 0, 0:384], zeros2[:, 0:384])
            for cch in range(2):
                hcol = 4 * (m - 1) + 2 + cch
                v.tensor_add(hins[:, 2 * m + cch:2 * m + cch + 1],
                             stt[:, cch:cch + 1], exch[:, hcol:hcol + 1])
            v.tensor_copy(lnt[:, 0, 0:384], zeros2[:, 0:384])
        v.tensor_mul(stt[:, 0:8], hins[:, 0:8], ohp[:, 0:8])
        v.tensor_copy(lnt[:, 0, 0:384], zeros2[:, 0:384])
        v.reduce_sum(hins[:, 10:12], stt[:, 0:8].rearrange("p (m c) -> p c m", c=2),
                     axis=AX)
        v.tensor_copy(lnt[:, 0, 0:384], zeros2[:, 0:384])
        for a in range(2):
            v.scalar_tensor_tensor(out=sD[:, a, :], in0=sC[:, a, :],
                                   scalar=hins[:, 10 + a:11 + a], in1=sD[:, a, :],
                                   op0=AL.mult, op1=AL.add)
        if build.dbg_layer == (l, "exch"):
            v.tensor_copy(lnt[:, 0, 0:16], exch[:, 0:16])
            v.tensor_copy(lnt[:, 0, 16:28], hins[:, 0:12])
            v.tensor_copy(lnt[:, 0, 28:32], sm4[:, 0:4])
            v.memset(lnt[:, 0, 32:1024], 0.0)
            BAR()
            for mm2 in range(16):
                sdma(out.rearrange("(m p) c -> p m c", p=128)[:, mm2:mm2 + 1, :],
                     lnt[:, 0:1, :])
            swait()
            BAR()
            build.dbg_early[0] = True
            return
        v.tensor_copy(hsb[:, 0, :], sD[:, 0, :])
        v.tensor_copy(hsb[:, 1, :], sD[:, 1, :])
        for a in range(2):
            r = v.tensor_scalar(stb[:, a, :], zeros2[:, 0:128], hins[:, 8 + a:9 + a],
                                None, op0=AL.add)
        eVT.inc(r)
        te.wait_ge(eVT.sem, eVT.n)

    build.dbg_layer = (0, "raw") if dbg == "raw1" else (
        (0, "exch") if dbg == "exch" else (
            (1, "exch") if dbg == "exch2" else None))
    build.dbg_out = out
    build.dbg_early = [False]
    ssm_layer(0, bT)

    if build.dbg_early[0]:
        return nc

    if dbg in ("hs1", "dec1", "drv1", "cum1"):
        buf = {"hs1": sD, "dec1": sA, "drv1": sB, "cum1": sC}[dbg]
        for a in range(2):
            for h in range(2):
                sdma(out.rearrange("(z p) c -> p z c", p=128)[:, 2 * a + h:2 * a + h + 1, :],
                     buf[:, a, h * 1024:(h + 1) * 1024].rearrange("p (o c) -> p o c", o=1))
        for z in range(4, 16):
            sdma(out.rearrange("(z p) c -> p z c", p=128)[:, z:z + 1, :],
                 sD[:, 0, 0:1024].rearrange("p (o c) -> p o c", o=1))
        swait()
        BAR()
        return nc

    te.wait_ge(qg, c_wsv)
    for j in range(8):
        for a in range(2):
            te.matmul(pX[:, 0:1], wsv[:, 0, a, j * 128:(j + 1) * 128],
                      stb[:, a, 0:1], start=(a == 0), stop=False)
        te.matmul(pX[:, 0:1], bbvb[0:1, 0, j * 128:(j + 1) * 128], ones1b[0:1, 0:1],
                  start=False, stop=True)
        BAR()
        v.tensor_copy(ctTb[:, j:j + 1], pX[:, 0:1])
        BAR()
    for (wv, bv) in ((wdv, bdv), (wiv, biv)):
        for a in range(2):
            for j in range(8):
                te.matmul(pX[:, 1:2], wv[:, 1, j, a * 128:(a + 1) * 128],
                          ctTb[:, j:j + 1], start=(j == 0), stop=(j == 7))
            BAR()
            v.tensor_add(bv[:, 1, a:a + 1], bv[:, 1, a:a + 1], pX[:, 1:2])
            BAR()
    te.wait_ge(qg, c_wov)
    y1_cnt = []
    for j in range(8):
        for th in range(2):
            it = 2 * j + th
            py = pyA if it % 2 == 0 else pyB
            if it >= 2:
                te.wait_ge(eVT.sem, y1_cnt[it - 2])
            r = None
            for a in range(2):
                r = mm(py, wov[:, 0, a, j * 128:(j + 1) * 128],
                       hsb[:, a, th * 1024:(th + 1) * 1024], (a == 0), (a == 1), 1024)
            eTV.inc(r)
            v.wait_ge(eTV.sem, eTV.n)
            r = v.tensor_copy(bT[:, j, th * 1024:(th + 1) * 1024], py[:, :])
            eVT.inc(r)
            y1_cnt.append(eVT.n)
    BAR()

    ssm_layer(1, bT)
    if build.dbg_early[0]:
        return nc

    for a in range(2):
        mm(pyA, stb[:, a, :], wsv[:, 1, a, :], (a == 0), False, C)
    mm(pyA, ones1b[0:1, 0:128], bbvb[0:1, 1, :], False, True, C)
    BAR()
    v.tensor_copy(ct2[:, :], pyA[:, :])
    BAR()

    bTf = bT.rearrange("p a t -> p (a t)")
    ln1_base = (eTV.n, eVT.n, eVS.n, eSV.n, eVX.n, eVD.n)
    xpt_cnt = []
    tr_cnt = []
    for m in range(16):
        py = pyA if m % 2 == 0 else pyB
        b = m % 2
        if m >= 2:
            eVT.wait(te, ln1_base[1] + m - 1)
        r = None
        for a in range(2):
            r = mm(py, hsb[:, a, m * 128:(m + 1) * 128], wov[:, 1, a, :],
                   (a == 0), (a == 1), C)
        eTV.inc(r)
        if m >= 2:
            sy.wait_ge(eVX.sem, ln1_base[4] + m - 1)
        sdma(xpt[:, b:b + 1, :],
             x_tok.rearrange("(m p) c -> p m c", p=128)[:, m:m + 1, :])
        xpt_cnt.append(cs[0])
        eTV.wait(v)
        r = v.tensor_add(lnt[:, b, :], py[:, :], ct2[:, :])
        eVT.inc(r)
        v.wait_ge(qs, xpt_cnt[m])
        r = v.scalar_tensor_tensor(out=lnt[:, b, :], in0=lnt[:, b, :], scalar=1.0,
                                   in1=xpt[:, b, :], op0=AL.mult, op1=AL.add,
                                   accum_out=st1[:, m:m + 1])
        eVX.inc(r)
        v.scalar_tensor_tensor(out=lnsq[:, :], in0=lnt[:, b, :], scalar=1.0,
                               in1=lnt[:, b, :], op0=AL.mult, op1=AL.mult,
                               accum_out=st2[:, m:m + 1])
        if m >= 2:
            eSV.wait(v, ln1_base[3] + 2 * (m - 1))
        r = v.tensor_scalar_mul(st1[:, m:m + 1], st1[:, m:m + 1], 1.0 / C)
        eVS.inc(r)
        eVS.wait(sc)
        r = sc.activation(out=stt[:, 4 + b:5 + b], in_=st1[:, m:m + 1],
                          func=AF.Square)
        eSV.inc(r)
        eSV.wait(v)
        r = v.tensor_scalar(stt[:, b:b + 1], st2[:, m:m + 1], 1.0 / C,
                            stt[:, 4 + b:5 + b], op0=AL.mult, op1=AL.subtract)
        eVS.inc(r)
        eVS.wait(sc)
        r = sc.activation(out=stt[:, 2 + b:3 + b], in_=stt[:, b:b + 1],
                          func=AF.Sqrt, bias=epsv[:, 0:1])
        eSV.inc(r)
        eSV.wait(v)
        v.reciprocal(stt[:, 6 + b:7 + b], stt[:, 2 + b:3 + b])
        v.tensor_copy(lnsq[:, 0:384], zeros2[:, 0:384])
        if m >= 2:
            v.wait_ge(qs, tr_cnt[m - 2])
        r = v.tensor_scalar(x1s[:, b, :], lnt[:, b, :], st1[:, m:m + 1],
                            stt[:, 6 + b:7 + b], op0=AL.subtract, op1=AL.mult)
        eVD.inc(r)
        eVD.wait(sy)
        sdma(ag_x_in.ap().rearrange("(m p) c -> p m c", p=128)[:, m:m + 1, :],
             x1s[:, b:b + 1, :])
        sy.dma_start_transpose(
            bTf[:, m * 1024:(m + 1) * 1024].rearrange("p (a q) -> p a q", a=8),
            x1s[:, b, :].rearrange("p (a q) -> p a q", a=8)
        ).then_inc(qs, 16)
        cs[0] += 16
        tr_cnt.append(cs[0])
    swait()
    BAR()
    gdma(w1v[:, :, :], w1t_in.rearrange("(j p) h -> p j h", p=128))
    gdma(w3v[:, :, :], w3t_in.rearrange("(j p) h -> p j h", p=128))

    if dbg == "x1":
        for m in range(16):
            b = m % 2
            gdma(x1s[:, b:b + 1, :],
                 ag_x_in.ap().rearrange("(m p) c -> p m c", p=128)[:, m:m + 1, :])
            gwait()
            BAR()
            v.tensor_copy(outt[:, 0, :], x1s[:, b, :])
            BAR()
            sdma(out.rearrange("(m p) c -> p m c", p=128)[:, m:m + 1, :],
                 outt[:, 0:1, :])
            swait()
        BAR()
        return nc

    x1Tv = bTf.rearrange("p (m a q) -> p m a q", m=16, a=8)
    te.wait_ge(qg, c_ext)
    for rt in range(16):
        for j in range(8):
            te.matmul(pX[:, :], extv[:, j, 16 - rt:144 - rt],
                      x1Tv[:, :, j, :].rearrange("p m (q r) -> p r m q", r=16)[:, rt, :, :],
                      start=(rt == 0 and j == 0), stop=(rt == 15 and j == 7))
    BAR()
    v.tensor_copy(junk[:, 0:128], pX[:, :])
    BAR()
    sdma(ag_t_in.ap()[:, :], junk[:, 0:128])
    swait()
    BAR()
    coll(GRP8, [ag_t_in.ap().opt()], [ag_t_out.ap().opt()])
    cc_agt = cc[0]
    g.wait_ge(qc, cc[0])
    coll(GRP8, [ag_x_in.ap().opt()], [ag_x_out.ap().opt()])
    sy.wait_ge(qc, cc_agt)
    sdma(Ssc[:, :].rearrange("er (kk c) -> er kk c", c=128),
         ag_t_out.ap().rearrange("(kk er) c -> er kk c", er=128))
    sdma(ohk16[:, :], ohk16_in[:, :])
    sdma(row16[:, :], row16_in[:, :])
    swait()
    BAR()
    if dbg == "bis":
        v.tensor_copy(lnt[:, 0, 233:361], Ssc[:, 0:128])
        v.reduce_max(lnt[:, 0, 361:362], Ssc[:, :], axis=AX)
        v.reduce_max(lnt[:, 0, 362:363], Ssc[:, 0:128], axis=AX)
        BAR()

    v.tensor_single_scalar(bisI[:, :], iop[:, :], 15.0, op=AL.bitwise_and)
    BAR()
    v.tensor_copy(stt[:, 0:1], bisI[:, :])
    BAR()
    v.tensor_scalar(vid[:, :], iocf[:, :], 16.0, stt[:, 0:1], op0=AL.mult, op1=AL.add)
    v.reduce_max(bis[:, 0:1], Ssc[:, :], axis=AX)
    v.tensor_reduce(bis[:, 1:2], Ssc[:, :], axis=AX, op=AL.min, negate=True)
    BAR()
    if dbg == "bis":
        v.tensor_copy(lnt[:, 0, 100:102], bis[:, 0:2])
        BAR()
    te.transpose(pX[0:2, :], bis[:, 0:2], idT[:, :])
    BAR()
    v.tensor_copy(scrG[0:2, 0:128], pX[0:2, 0:128])
    BAR()
    if dbg == "bis":
        v.tensor_copy(lnt[0:2, 0, 102:230], scrG[0:2, 0:128])
        BAR()
    v.reduce_max(scr8[0:2, 0:1], scrG[0:2, 0:128], axis=AX)
    BAR()
    te.transpose(pX[0:1, 0:2], scr8[0:2, 0:1], idT[0:2, 0:2])
    BAR()
    v.tensor_copy(stt[0:1, 0:2], pX[0:1, 0:2])
    BAR()
    if dbg == "bis":
        v.tensor_copy(lnt[0:2, 0, 230:231], scr8[0:2, 0:1])
        v.tensor_copy(lnt[0:1, 0, 231:233], stt[0:1, 0:2])
        BAR()
    te.matmul(pSm[:, 0:2], onesf[0:1, 0:128], stt[0:1, 0:2], start=True, stop=True)
    BAR()
    v.tensor_scalar_add(bis[:, 0:1], pSm[:, 0:1], 1.0)
    v.tensor_scalar(bis[:, 1:2], pSm[:, 1:2], -1.0, -1.0, op0=AL.mult, op1=AL.add)
    v.tensor_scalar_mul(bis[:, 4:5], pSm[:, 0:1], -1.0)
    BAR()
    if dbg == "bis":
        v.tensor_copy(lnt[:, 0, 0:8], bis[:, 0:8])
        BAR()
    for it in range(NBIS if dbg != "bis" else 1):
        v.tensor_add(bis[:, 2:3], bis[:, 0:1], bis[:, 1:2])
        BAR()
        v.tensor_scalar_mul(bis[:, 2:3], bis[:, 2:3], 0.5)
        BAR()
        v.tensor_scalar(junk[:, :], Ssc[:, :], bis[:, 2:3], None, op0=AL.is_ge)
        BAR()
        v.reduce_sum(cnts[:, :], junk[:, :], axis=AX)
        BAR()
        te.matmul(pSm[:, 0:1], bd16[:, :], cnts[:, :], start=True, stop=True)
        BAR()
        v.tensor_single_scalar(bisI[:, :], pSm[:, 0:1], 2048.0, op=AL.is_ge)
        BAR()
        v.select(bis[:, 1:2], bisI[:, :], bis[:, 2:3], bis[:, 1:2])
        v.select(bis[:, 0:1], bisI[:, :], bis[:, 0:1], bis[:, 2:3])
        BAR()
    if dbg == "bis":
        v.tensor_copy(lnt[:, 0, 8:9], cnts[:, :])
        v.tensor_copy(lnt[:, 0, 9:10], pSm[:, 0:1])
        v.tensor_copy(lnt[:, 0, 10:11], bisI[:, :])
        v.tensor_copy(lnt[:, 0, 11:14], bis[:, 0:3])
        v.tensor_copy(lnt[:, 0, 14:142], Ssc[:, 0:128])
        BAR()
        for m in range(16):
            sdma(out.rearrange("(m p) c -> p m c", p=128)[:, m:m + 1, :],
                 lnt[:, 0:1, :])
        swait()
        BAR()
        return nc
    v.tensor_scalar(msI[:, :], Ssc[:, :], bis[:, 1:2], None, op0=AL.is_ge)
    v.tensor_scalar(mskF[:, :], Ssc[:, :], bis[:, 1:2], None, op0=AL.is_ge)
    BAR()
    v.reduce_sum(cnts[:, :], mskF[:, :], axis=AX)
    sc.activation(out=junk[:, :], in_=Ssc[:, :], func=AF.Exp, bias=bis[:, 4:5])
    BAR()
    v.tensor_mul(junk[:, :], junk[:, :], mskF[:, :])
    BAR()
    v.reduce_sum(st1[:, 0:1], junk[:, :], axis=AX)
    BAR()
    te.transpose(pX[0:1, :], st1[:, 0:1], idT[:, :])
    BAR()
    v.tensor_copy(orow[0:1, :], pX[0:1, 0:128])
    BAR()
    v.tensor_mul(orow[0:1, :], orow[0:1, :], ohk16[0:1, :])
    BAR()
    v.reduce_sum(stt[0:1, 4:5], orow[0:1, :], axis=AX)
    BAR()
    te.matmul(pSm[:, 1:2], onesf[0:1, 0:128], stt[0:1, 4:5], start=True, stop=True)
    BAR()
    v.reciprocal(bis[:, 5:6], pSm[:, 1:2])
    v.scalar_tensor_tensor(out=Vb[:, :], in0=vid[:, :], scalar=1.0, in1=mskF[:, :],
                           op0=AL.add, op1=AL.mult)
    v.tensor_tensor_scan(out=junk[:, :], data0=mskF[:, :], data1=zeros2[:, 0:1024],
                         initial=0.0, op0=AL.add, op1=AL.add)
    BAR()
    v.tensor_scalar_add(Vb[:, :], Vb[:, :], -1.0)
    v.tensor_sub(junk[:, :], iocf[:, :], junk[:, :])
    BAR()
    v.tensor_scalar_add(junk[:, :], junk[:, :], 1.0)
    BAR()
    v.tensor_mul(junk[:, :], junk[:, :], mskF[:, :])
    BAR()
    v.tensor_copy(Db[:, :], junk[:, :])
    BAR()

    def butterfly(Vb_, Vb2_, Db_, Db2_):
        Vcur, Vnx, Dcur, Dnx = Vb_, Vb2_, Db_, Db2_
        for b in range(10):
            s = 1 << b
            v.tensor_single_scalar(msI[:, :], Dcur[:, :], float(s), op=AL.bitwise_and)
            v.tensor_single_scalar(DmS[:, :], Dcur[:, :], float(s), op=AL.subtract)
            BAR()
            v.select(Vnx[:, 0:1024 - s], msI[:, s:], Vcur[:, s:], Vcur[:, 0:1024 - s])
            v.tensor_copy(Vnx[:, 1024 - s:], Vcur[:, 1024 - s:])
            v.select(Dnx[:, 0:1024 - s], msI[:, s:], DmS[:, s:], Dcur[:, 0:1024 - s])
            v.tensor_copy(Dnx[:, 1024 - s:], Dcur[:, 1024 - s:])
            BAR()
            Vcur, Vnx, Dcur, Dnx = Vnx, Vcur, Dnx, Dcur
        return Vcur

    CT = butterfly(Vb, Vb2, Db, Db2)
    te.transpose(pX[0:1, :], cnts[:, :], idT[:, :])
    BAR()
    v.tensor_copy(orow[0:1, :], pX[0:1, 0:128])
    BAR()
    sdma(orow_d.ap()[:, :], orow[0:1, :])
    swait()
    BAR()
    gdma(orow8[:, :], orow_d.ap().rearrange("o (e r) -> (o e) r", e=8))
    gwait()
    BAR()
    v.tensor_tensor_scan(out=orow8[:, :], data0=orow8[:, :], data1=zeros2[0:8, 0:16],
                         initial=0.0, op0=AL.add, op1=AL.add)
    BAR()
    gdma(exsb[0:8, 0:16], orow_d.ap().rearrange("o (e r) -> (o e) r", e=8))
    gwait()
    BAR()
    v.tensor_sub(orow8[:, :], orow8[:, :], exsb[0:8, 0:16])
    BAR()
    gdma(orow_d.ap().rearrange("o (e r) -> (o e) r", e=8), orow8[:, :])
    gwait()
    BAR()
    sdma(orow[0:1, :], orow_d.ap()[:, :])
    swait()
    BAR()
    v.tensor_mul(exsb[0:1, 0:128], orow[0:1, :], ohk16[0:1, :])
    BAR()
    v.reduce_sum(wraw[0:1, 0:16], exsb[0:1, 0:128].rearrange("o (e r) -> o r e", e=8),
                 axis=AX)
    BAR()
    v.tensor_copy(oint[0:1, :], wraw[0:1, 0:16])
    te.transpose(pX[:, 0:1], orow[0:1, :], idT[0:1, 0:1])
    BAR()
    v.tensor_copy(obase[:, :], pX[:, 0:1])
    v.tensor_copy(Vc[:, :], CT[:, :])
    BAR()
    sdma(vc_d.ap()[:, :], Vc[:, :])
    swait()
    BAR()
    rv = nc.values_load(row16[0:1, 0:1], engines=[mybir.EngineType.SP])
    sy.dma_start(out=cc16[:, :], in_=vc_d.ap()[bass.ds(rv, 16), :]).then_inc(qh, 16)
    ch[0] += 16
    sy.wait_ge(qh, ch[0])
    sc.wait_ge(qh, ch[0])
    for r in range(16):
        eng = mybir.EngineType.SP if r < 8 else mybir.EngineType.Activation
        ov = nc.values_load(oint[0:1, r:r + 1], engines=[eng])
        issuer = sy if r < 8 else sc
        issuer.wait_ge(qh, ch[0])
        issuer.dma_start(out=l_list.ap()[0:1, bass.ds(ov, 1024)],
                         in_=cc16[r:r + 1, :]).then_inc(qh, 16)
        ch[0] += 16
    sy.wait_ge(qh, ch[0])
    g.wait_ge(qh, ch[0])
    gdma(FL[:, :], l_list.ap()[0:1, 0:2048].rearrange("o (p q) -> (o p) q", q=16))
    gwait()
    g.wait_ge(qc, cc[0])
    BAR()

    if dbg in ("sel", "sel2"):
        v.tensor_copy(lnt[:, 0, 0:16], FL[:, :])
        v.tensor_copy(lnt[:, 0, 16:17], cnts[:, :])
        v.tensor_copy(lnt[:, 0, 17:18], bis[:, 1:2])
        v.tensor_copy(lnt[:, 0, 18:19], obase[:, :])
        v.tensor_copy(lnt[:, 0, 19:20], bis[:, 5:6])
        v.memset(lnt[:, 0, 20:1024], 0.0)
        BAR()
        for m in range(16):
            sdma(out.rearrange("(m p) c -> p m c", p=128)[:, m:m + 1, :],
                 lnt[:, 0:1, :])
        swait()
        BAR()
        return nc

    v.tensor_copy(bisI[:, :], kv0[:, 0:1])
    BAR()
    v.tensor_single_scalar(bisI[:, :], bisI[:, :], 7.0, op=AL.logical_shift_right)
    v.tensor_single_scalar(rowT[:, :], FL[:, :], 11.0, op=AL.logical_shift_right)
    v.tensor_single_scalar(msI[:, 0:16], FL[:, :], 15.0, op=AL.bitwise_and)
    BAR()
    v.tensor_single_scalar(rowT[:, :], rowT[:, :], 7.0, op=AL.logical_shift_left)
    BAR()
    v.tensor_tensor(rowT[:, :], rowT[:, :], msI[:, 0:16], op=AL.bitwise_or)
    BAR()
    v.tensor_scalar(rowT[:, :], rowT[:, :], bisI[:, 0:1], None, op0=AL.bitwise_or)
    v.tensor_single_scalar(msI[:, 0:16], FL[:, :], 4.0, op=AL.logical_shift_right)
    BAR()
    v.tensor_single_scalar(msI[:, 0:16], msI[:, 0:16], 127.0, op=AL.bitwise_and)
    BAR()
    v.tensor_copy(colT[:, :], msI[:, 0:16])
    BAR()
    gdone = []
    for gi in range(16):
        g.indirect_dma_start(
            out=scrG[:, 128 * (gi % 2):128 * (gi % 2) + 128], out_offset=None,
            in_=ag_t_out.ap(),
            in_offset=IndirectOffsetOnAxis(rowT[:, gi:gi + 1], axis=0),
        ).then_inc(qg, 16)
        cg[0] += 16
        gdone.append(cg[0])
        if gi > 0:
            v.wait_ge(qg, gdone[gi - 1])
            b0 = 128 * ((gi - 1) % 2)
            v.tensor_scalar(junk[:, 0:128], iocf[:, 0:128],
                            colT[:, gi - 1:gi], None, op0=AL.is_equal)
            BAR()
            v.tensor_mul(junk[:, 0:128], junk[:, 0:128], scrG[:, b0:b0 + 128])
            BAR()
            v.reduce_sum(wraw[:, gi - 1:gi], junk[:, 0:128], axis=AX)
    v.wait_ge(qg, gdone[15])
    v.tensor_scalar(junk[:, 0:128], iocf[:, 0:128], colT[:, 15:16], None,
                    op0=AL.is_equal)
    BAR()
    v.tensor_mul(junk[:, 0:128], junk[:, 0:128], scrG[:, 128:256])
    BAR()
    v.reduce_sum(wraw[:, 15:16], junk[:, 0:128], axis=AX)
    BAR()
    sc.activation(out=wnrm[:, :], in_=wraw[:, :], func=AF.Exp, bias=bis[:, 4:5])
    BAR()
    v.tensor_scalar(wnrm[:, :], wnrm[:, :], bis[:, 5:6], None, op0=AL.mult)
    BAR()

    xgB = [xg4, xg4b]
    ffn_gq = []
    ffn_tq = []
    ydma_cnt = []
    for m4 in range(4):
        g.indirect_dma_start(
            out=xgB[0][:, m4, :], out_offset=None, in_=ag_x_out.ap(),
            in_offset=IndirectOffsetOnAxis(FL[:, m4:m4 + 1], axis=0),
        ).then_inc(qg, 16)
        cg[0] += 16
    ffn_gq.append(cg[0])
    tv0 = eTV.n
    vt0 = eVT.n
    ts0 = eTS.n
    xgt_free = []
    for G in range(4):
        b = G % 2
        sy.wait_ge(qg, ffn_gq[G])
        if G >= 1:
            sy.wait_ge(eTV.sem, xgt_free[G - 1])
        for m4 in range(4):
            sy.dma_start_transpose(
                xgt[:, m4, :, :],
                xgB[b][:, m4, :].rearrange("p (a q) -> p a q", a=8)
            ).then_inc(qs, 16)
            cs[0] += 16
        ffn_tq.append(cs[0])
        if G < 3:
            if G >= 1:
                g.wait_ge(qs, ffn_tq[G - 1])
            for m4 in range(4):
                gi = 4 * (G + 1) + m4
                g.indirect_dma_start(
                    out=xgB[(G + 1) % 2][:, m4, :], out_offset=None,
                    in_=ag_x_out.ap(),
                    in_offset=IndirectOffsetOnAxis(FL[:, gi:gi + 1], axis=0),
                ).then_inc(qg, 16)
                cg[0] += 16
            ffn_gq.append(cg[0])
        te.wait_ge(qs, ffn_tq[G])
        last_w3 = None
        for hc in range(16):
            hcg = 16 * G + hc
            pA = pjA[:, :] if hc % 2 == 0 else pyA[:, 0:512]
            pB = pjB[:, :] if hc % 2 == 0 else pyA[:, 512:1024]
            if hcg >= 2:
                te.wait_ge(eVT.sem, vt0 + hcg - 1)
            r = None
            for a in range(8):
                r = te.matmul(pA, w1v[:, a, hc * 128:(hc + 1) * 128],
                              xgt[:, :, a, :], start=(a == 0), stop=(a == 7))
            eTS.inc(r)
            ts_cnt = eTS.n
            for a in range(8):
                r = te.matmul(pB, w3v[:, a, hc * 128:(hc + 1) * 128],
                              xgt[:, :, a, :], start=(a == 0), stop=(a == 7))
            eTV.inc(r)
            last_w3 = eTV.n
            sc.wait_ge(eTS.sem, ts_cnt)
            if hcg >= 1:
                sc.wait_ge(eVT.sem, vt0 + hcg)
            r = sc.activation(out=scr512[:, :], in_=pA, func=AF.Silu)
            eSV.inc(r)
            eSV.wait(v)
            v.wait_ge(eTV.sem, last_w3)
            r = v.tensor_mul(hpr[:, hc, :], scr512[:, :], pB)
            eVT.inc(r)
        xgt_free.append(last_w3)
        for d in range(4):
            py = pyB if d % 2 == 0 else pyA
            gi = 4 * G + d
            if G == 0 and d == 0:
                te.wait_ge(qg, c_w2v)
            if d % 2 == 1:
                te.wait_ge(eVT.sem, vt0 + 16 * G + 16)
            if 4 * G + d >= 2:
                te.wait_ge(eVD.sem, eVD.n - 2 + 1)
            r = None
            for hc in range(16):
                r = mm(py, hpr[:, hc, d * 128:(d + 1) * 128], w2v[:, hc, :],
                       (hc == 0), (hc == 15), C)
            eTV.inc(r)
            v.wait_ge(eTV.sem, eTV.n)
            if len(ydma_cnt) >= 2:
                v.wait_ge(qs, ydma_cnt[-2])
            r = v.tensor_scalar(yst[:, d % 2, :], py[:, :], wnrm[:, gi:gi + 1],
                                None, op0=AL.mult)
            eVD.inc(r)
            eVD.wait(sy)
            sdma(ag_y_in.ap()[128 * gi:128 * gi + 128, :], yst[:, d % 2, :])
            ydma_cnt.append(cs[0])
        g.wait_ge(qs, cs[0])
        coll(GRP8,
             [ag_y_in.ap()[512 * G:512 * (G + 1), :].opt()],
             [y_all.ap()[4096 * G:4096 * (G + 1), :].opt()])
    swait()
    BAR()

    v.tensor_scalar(mskF[:, :], iocf[:, :], cnts[:, 0:1], None, op0=AL.is_lt)
    BAR()
    v.scalar_tensor_tensor(out=CT[:, :], in0=CT[:, :], scalar=1.0, in1=mskF[:, :],
                           op0=AL.add, op1=AL.mult)
    BAR()
    v.tensor_scalar_add(CT[:, :], CT[:, :], -1.0)
    BAR()
    v.tensor_scalar(junk[:, :], iocf[:, :], obase[:, 0:1], None, op0=AL.add)
    BAR()
    v.scalar_tensor_tensor(out=junk[:, :], in0=junk[:, :], scalar=2048.0,
                           in1=CT[:, :], op0=AL.mult, op1=AL.add)
    BAR()
    v.tensor_scalar(junk[:, :], junk[:, :], kv0[:, 0:1], None, op0=AL.subtract)
    BAR()
    v.tensor_scalar(msI[:, :], CT[:, :], kv0[:, 0:1], None, op0=AL.is_ge)
    v.tensor_scalar(DmS[:, :], CT[:, :], kv0[:, 1:2], None, op0=AL.is_lt)
    BAR()
    v.tensor_tensor(msI[:, :], msI[:, :], DmS[:, :], op=AL.bitwise_and)
    BAR()
    v.tensor_copy(mskF[:, :], msI[:, :])
    BAR()
    v.scalar_tensor_tensor(out=Vb2[:, :], in0=junk[:, :], scalar=1.0, in1=mskF[:, :],
                           op0=AL.add, op1=AL.mult)
    v.reduce_sum(cnt2[:, :], mskF[:, :], axis=AX)
    v.tensor_tensor_scan(out=junk[:, :], data0=mskF[:, :], data1=zeros2[:, 0:1024],
                         initial=0.0, op0=AL.add, op1=AL.add)
    BAR()
    v.tensor_scalar_add(Vb2[:, :], Vb2[:, :], -1.0)
    v.tensor_sub(junk[:, :], iocf[:, :], junk[:, :])
    BAR()
    v.tensor_scalar_add(junk[:, :], junk[:, :], 1.0)
    BAR()
    v.tensor_mul(junk[:, :], junk[:, :], mskF[:, :])
    BAR()
    v.tensor_copy(Vb[:, :], Vb2[:, :])
    v.tensor_copy(Db2[:, :], junk[:, :])
    BAR()
    v.tensor_copy(Db[:, :], Db2[:, :])
    BAR()
    CB = butterfly(Vb, Vb2, Db, Db2)
    v.tensor_copy(Db2[:, 0:C2], CB[:, 0:C2])
    BAR()
    v.tensor_single_scalar(srcI[:, :], Db2[:, 0:C2], 11.0, op=AL.logical_shift_right)
    v.tensor_single_scalar(dstI[:, :], Db2[:, 0:C2], 2047.0, op=AL.bitwise_and)
    v.tensor_scalar(msI[:, 0:C2], iocf[:, 0:C2], cnt2[:, 0:1], None, op0=AL.is_lt)
    v.tensor_copy(lnt[:, 0, 0:384], zeros2[:, 0:384])
    v.tensor_single_scalar(DmS[:, 0:C2], srcI[:, :], 15.0, op=AL.bitwise_and)
    v.tensor_single_scalar(DmS[:, C2:2 * C2], srcI[:, :], 4.0,
                           op=AL.logical_shift_right)
    v.tensor_copy(lnt[:, 0, 0:384], zeros2[:, 0:384])
    v.tensor_single_scalar(DmS[:, 2 * C2:3 * C2], DmS[:, 0:C2], 2.0,
                           op=AL.logical_shift_right)
    v.tensor_single_scalar(DmS[:, 0:C2], DmS[:, 0:C2], 3.0, op=AL.bitwise_and)
    v.tensor_copy(lnt[:, 0, 0:384], zeros2[:, 0:384])
    v.tensor_single_scalar(DmS[:, 2 * C2:3 * C2], DmS[:, 2 * C2:3 * C2], 12.0,
                           op=AL.logical_shift_left)
    v.tensor_single_scalar(DmS[:, 0:C2], DmS[:, 0:C2], 7.0,
                           op=AL.logical_shift_left)
    v.tensor_copy(lnt[:, 0, 0:384], zeros2[:, 0:384])
    v.tensor_tensor(srcI[:, :], DmS[:, C2:2 * C2], DmS[:, 0:C2],
                    op=AL.bitwise_or)
    v.tensor_copy(lnt[:, 0, 0:384], zeros2[:, 0:384])
    v.tensor_tensor(srcI[:, :], srcI[:, :], DmS[:, 2 * C2:3 * C2],
                    op=AL.bitwise_or)
    v.tensor_copy(lnt[:, 0, 0:384], zeros2[:, 0:384])
    v.tensor_scalar(srcI[:, :], srcI[:, :], eab[:, 0:1], None, op0=AL.bitwise_or)
    v.tensor_copy(lnt[:, 0, 0:384], zeros2[:, 0:384])
    BAR()
    if dbg == "bwd":
        v.tensor_copy(lnt[:, 0, 67:99], msI[:, 0:C2])
        v.tensor_copy(lnt[:, 0, 99:131], srcI[:, :])
        v.tensor_copy(lnt[:, 0, 131:163], dstI[:, :])
        BAR()
    v.select(DmS[:, 0:C2], msI[:, 0:C2], srcI[:, :], sentA[:, :])
    v.select(DmS[:, C2:2 * C2], msI[:, 0:C2], dstI[:, :], sentB[:, :])
    BAR()
    v.tensor_copy(srcI[:, :], DmS[:, 0:C2])
    v.tensor_copy(dstI[:, :], DmS[:, C2:2 * C2])
    g.wait_ge(qc, cc[0])
    BAR()
    if dbg == "bwd":
        v.tensor_copy(lnt[:, 0, 0:32], srcI[:, :])
        v.tensor_copy(lnt[:, 0, 32:64], dstI[:, :])
        v.tensor_copy(lnt[:, 0, 64:65], cnt2[:, :])
        v.tensor_copy(lnt[:, 0, 65:66], obase[:, :])
        v.tensor_copy(lnt[:, 0, 66:67], cnts[:, :])
        v.memset(lnt[:, 0, 163:1024], 0.0)
        BAR()
        for m in range(16):
            sdma(out.rearrange("(m p) c -> p m c", p=128)[:, m:m + 1, :],
                 lnt[:, 0:1, :])
        swait()
        BAR()
        return nc
    sdma(di_d.ap()[:, :], dstI[:, :])
    swait()
    BAR()
    sdma(dstE[:, :, :], di_d.ap().rearrange("(e r) gi -> r e gi", e=8))
    swait()
    BAR()
    for gi in range(C2):
        g.indirect_dma_start(
            out=Gbig[:, gi, :], out_offset=None,
            in_=y_all.ap(),
            in_offset=IndirectOffsetOnAxis(srcI[:, gi:gi + 1], axis=0),
        ).then_inc(qg, 16)
        cg[0] += 16
    gwait()
    BAR()
    for e in range(8):
        for gi in range(C2):
            g.indirect_dma_start(
                out=acc_d.ap(),
                out_offset=IndirectOffsetOnAxis(dstE[0:16, e, gi:gi + 1], axis=0),
                in_=Gbig[16 * e:16 * e + 16, gi, :], in_offset=None,
                compute_op=AL.add,
            ).then_inc(qh, 16)
            ch[0] += 16
        g.wait_ge(qh, ch[0])
    BAR()

    ln2_base = (eVX.n, None, eSV.n, eVD.n)
    acct_cnt = []
    x1g_cnt = []
    for m in range(16):
        b = m % 2
        if m >= 2:
            sy.wait_ge(eVX.sem, ln2_base[0] + m - 1)
            g.wait_ge(eVX.sem, ln2_base[0] + m - 1)
        sdma(acct[:, b:b + 1, :],
             acc_d.ap()[0:T, :].rearrange("(m p) c -> p m c", p=128)[:, m:m + 1, :])
        acct_cnt.append(cs[0])
        gdma(x1s[:, b:b + 1, :],
             ag_x_in.ap().rearrange("(m p) c -> p m c", p=128)[:, m:m + 1, :])
        x1g_cnt.append(cg[0])
        v.wait_ge(qs, acct_cnt[m])
        v.wait_ge(qg, x1g_cnt[m])
        r = v.scalar_tensor_tensor(out=lnt[:, b, :], in0=x1s[:, b, :], scalar=1.0,
                                   in1=acct[:, b, :], op0=AL.mult, op1=AL.add,
                                   accum_out=st1[:, m:m + 1])
        eVX.inc(r)
        v.scalar_tensor_tensor(out=lnsq[:, :], in0=lnt[:, b, :], scalar=1.0,
                               in1=lnt[:, b, :], op0=AL.mult, op1=AL.mult,
                               accum_out=st2[:, m:m + 1])
        if m >= 2:
            eSV.wait(v, ln2_base[2] + 2 * (m - 1))
        r = v.tensor_scalar_mul(st1[:, m:m + 1], st1[:, m:m + 1], 1.0 / C)
        eVS.inc(r)
        eVS.wait(sc)
        r = sc.activation(out=stt[:, 4 + b:5 + b], in_=st1[:, m:m + 1],
                          func=AF.Square)
        eSV.inc(r)
        eSV.wait(v)
        r = v.tensor_scalar(stt[:, b:b + 1], st2[:, m:m + 1], 1.0 / C,
                            stt[:, 4 + b:5 + b], op0=AL.mult, op1=AL.subtract)
        eVS.inc(r)
        eVS.wait(sc)
        r = sc.activation(out=stt[:, 2 + b:3 + b], in_=stt[:, b:b + 1],
                          func=AF.Sqrt, bias=epsv[:, 0:1])
        eSV.inc(r)
        eSV.wait(v)
        v.reciprocal(stt[:, 6 + b:7 + b], stt[:, 2 + b:3 + b])
        v.tensor_copy(lnsq[:, 0:384], zeros2[:, 0:384])
        if m >= 2:
            v.wait_ge(qo, 16 * (m - 1))
        r = v.tensor_scalar(outt[:, b, :], lnt[:, b, :], st1[:, m:m + 1],
                            stt[:, 6 + b:7 + b], op0=AL.subtract, op1=AL.mult)
        eVD.inc(r)
        eVD.wait(sy)
        sy.dma_start(out=out.rearrange("(m p) c -> p m c", p=128)[:, m:m + 1, :],
                     in_=outt[:, b:b + 1, :]).then_inc(qo, 16)
        co[0] += 16
    sy.wait_ge(qo, co[0])
    BAR()
    return nc


def _host_prep(inputs):
    x = np.asarray(inputs["L0_attn_shot_embeddings"], np.float32).reshape(16384, C)
    Wd = np.ascontiguousarray(np.asarray(inputs["L0_attn_Wd"], np.float32))
    Wi = np.ascontiguousarray(np.asarray(inputs["L0_attn_Wi"], np.float32))
    Wo = np.ascontiguousarray(np.asarray(inputs["L0_attn_Wo"], np.float32))
    Ws = np.ascontiguousarray(np.asarray(inputs["L0_attn_Ws"], np.float32))
    bd = np.ascontiguousarray(np.asarray(inputs["L0_attn_bd"], np.float32))
    bi = np.ascontiguousarray(np.asarray(inputs["L0_attn_bi"], np.float32))
    alog = np.ascontiguousarray(np.asarray(inputs["L0_attn_A_log"], np.float32))
    bo = np.ascontiguousarray(np.asarray(inputs["L0_attn_bo"], np.float32))
    bs = np.ascontiguousarray(np.asarray(inputs["L0_attn_bs"], np.float32))
    Wg = np.asarray(inputs["L0_ffn_Wg"], np.float32)
    W1 = np.asarray(inputs["L0_ffn_W1"], np.float32)
    W2 = np.asarray(inputs["L0_ffn_W2"], np.float32)
    W3 = np.asarray(inputs["L0_ffn_W3"], np.float32)

    ext = np.zeros((8, 128, 144), np.float32)
    for e in range(8):
        for j in range(8):
            ext[j, :, 16 + 16 * e] = Wg[e, 128 * j:128 * (j + 1)]

    maps = []
    for k in range(NCORE):
        xk = np.ascontiguousarray(x[2048 * k:2048 * (k + 1)])
        kv = np.zeros((128, 2), np.float32)
        kv[:, 0] = 2048.0 * k
        kv[:, 1] = 2048.0 * (k + 1)
        eabv = np.zeros((128, 1), np.int32)
        for p in range(128):
            eabv[p, 0] = 512 * (p // 16)
        ohk16v = np.zeros((1, 128), np.float32)
        ohk16v[0, 16 * k:16 * (k + 1)] = 1.0
        row16v = np.arange(16 * k, 16 * k + 16, dtype=np.int32).reshape(1, 16)
        rot = np.array([[(k + e) % 8 for e in range(8)]], np.int32)
        ohpv = np.zeros((128, 8), np.float32)
        pos = k % 4
        ohpv[:, 2 * pos] = 1.0
        ohpv[:, 2 * pos + 1] = 1.0
        ohkv = np.zeros((128, 8), np.float32)
        ohkv[:, k] = 1.0
        maps.append({
            "x_tok": xk,
            "x_feat": np.ascontiguousarray(xk.T),
            "wd": Wd, "wi": Wi, "wo": Wo, "ws": Ws,
            "bdd": bd, "bii": bi, "alog": alog, "bo": bo, "bss": bs,
            "ext": ext,
            "wgt": np.ascontiguousarray(Wg.T),
            "w1t": np.ascontiguousarray(W1[k].T),
            "w3t": np.ascontiguousarray(W3[k].T),
            "w2t": np.ascontiguousarray(W2[k].T),
            "kvi": kv, "eabi": eabv, "rot": rot, "ohpi": ohpv, "ohki": ohkv,
            "ohk16": ohk16v, "row16": row16v,
        })
    return maps


def kernel(**inputs):
    maps = _host_prep(inputs)
    nc = build(dbg=DBG_STAGE)
    res = run_bass_kernel_spmd(nc, maps, core_ids=list(range(NCORE)))
    outs = np.concatenate([np.asarray(r["out"]) for r in res.results], axis=0)
    return outs.reshape(2, 8192, 1024).astype(np.float32)


# version 05
# speedup vs baseline: 1.2380x; 1.0322x over previous
import numpy as np
import ml_dtypes
import concourse.bass as bass
import concourse.mybir as mybir
from concourse.bass import IndirectOffsetOnAxis
from concourse.bass_utils import run_bass_kernel_spmd

F32 = mybir.dt.float32
BF16 = mybir.dt.bfloat16
I32 = mybir.dt.int32
AX = mybir.AxisListType.X
AF = mybir.ActivationFunctionType
AL = mybir.AluOpType

NCORE = 8
T = 2048
C = 1024
NSD = 256
HF = 2048
CAP = 2048
C2 = 32
EPS = 1e-5
NBIS = 30

DBG_STAGE = None


def build(dbg=None):
    nc = bass.Bass("TRN2", target_bir_lowering=False)

    def din(name, shape, dt=F32):
        return nc.dram_tensor(name, shape, dt, kind="ExternalInput").ap()

    x_tok = din("x_tok", [T, C])
    x_feat = din("x_feat", [C, T])
    wd_in = din("wd", [2, C, NSD])
    wi_in = din("wi", [2, C, NSD])
    wo_in = din("wo", [2, NSD, C])
    ws_in = din("ws", [2, NSD, C])
    bd_in = din("bdd", [2, NSD])
    bi_in = din("bii", [2, NSD])
    alog_in = din("alog", [2, NSD])
    bo_in = din("bo", [2, C])
    bs_in = din("bss", [2, C])
    ext_in = din("ext", [8, 128, 144])
    wgt_in = din("wgt", [C, 8])
    w1t_in = din("w1t", [C, HF])
    w3t_in = din("w3t", [C, HF])
    w2t_in = din("w2t", [HF, C])
    kv_in = din("kvi", [128, 2])
    eab_in = din("eabi", [128, 1], I32)
    rot_in = din("rot", [1, 8], I32)
    ohp_in = din("ohpi", [128, 8])
    ohk_in = din("ohki", [128, 8])
    ohk16_in = din("ohk16", [1, 128])
    row16_in = din("row16", [1, 16], I32)

    out = nc.dram_tensor("out", [T, C], F32, kind="ExternalOutput").ap()

    def dint(name, shape, dt):
        return nc.dram_tensor(name, shape, dt, kind="Internal")

    ag_s_in = [dint(f"ag_s_in{l}", [4, 128], F32) for l in range(2)]
    ag_s_out = [dint(f"ag_s_out{l}", [16, 128], F32) for l in range(2)]
    ag_t_in = dint("ag_t_in", [128, 128], F32)
    ag_t_out = dint("ag_t_out", [1024, 128], F32)
    ag_x_in = dint("ag_x_in", [T, C], BF16)
    ag_x_out = dint("ag_x_out", [NCORE * T, C], BF16)
    ag_y_in = dint("ag_y_in", [CAP, C], BF16)
    y_all = dint("y_all", [NCORE * CAP + 1, C], BF16)
    l_list = dint("l_list", [1, CAP + 1152], I32)
    acc_d = dint("acc_d", [T + 1, C], BF16)
    orow_d = dint("orow_d", [1, 128], F32)
    vc_d = dint("vc_d", [128, 1024], I32)
    sc_d = dint("sc_d", [128, 1024], F32)
    s_list = dint("s_list", [1, CAP + 1152], F32)
    di_d = dint("di_d", [128, C2], I32)

    sbt = nc.alloc_sbuf_tensor
    sat = nc.alloc_sbuf_tensor_at

    def addr(h):
        return nc.lookup_mloc(h).addr

    bT = sbt("bT", [128, 8, T], BF16)
    wdv = sbt("wdv", [128, 2, 8, NSD], BF16)
    wiv = sbt("wiv", [128, 2, 8, NSD], BF16)
    wov = sbt("wov", [128, 2, 2, C], BF16)
    wsv = sbt("wsv", [128, 2, 2, C], BF16)
    sA = sbt("sA", [128, 2, T], F32)
    sB = sbt("sB", [128, 2, T], F32)
    sC = sbt("sC", [128, 2, T], F32)
    sD = sbt("sD", [128, 2, T], F32)
    A_bt = addr(bT)
    A_w = addr(wdv)
    A_s = addr(sA)

    w2v = sbt("w2v", [128, 16, C], BF16)
    zeros2 = sbt("zeros2", [128, 2048], F32)
    lnt = sbt("lnt", [128, 2, C], F32)
    bbvb = sbt("bbvb", [1, 2, C], BF16)
    ct2 = sbt("ct2", [128, C], F32)
    x1s = sbt("x1s", [128, 2, C], BF16)
    rowT = sbt("rowT", [128, 16], I32)
    colT = sbt("colT", [128, 16], F32)
    idT = sbt("idT", [128, 128], F32)
    extv = sbt("extv", [128, 8, 144], BF16)
    wgv = sbt("wgv", [128, 8, 8], BF16)
    bd16 = sbt("bd16", [128, 128], F32)
    iocf = sbt("iocf", [128, 1024], F32)
    mskF = sbt("mskF", [128, 1024], F32)
    msI = sbt("msI", [128, 1024], I32)
    bdv = sbt("bdv", [128, 2, 2], F32)
    biv = sbt("biv", [128, 2, 2], F32)
    spav = sbt("spav", [128, 2, 2], F32)
    kv0 = sbt("kv0", [128, 2], F32)
    eab = sbt("eab", [128, 1], I32)
    ohp = sbt("ohp", [128, 8], F32)
    ohk = sbt("ohk", [128, 8], F32)
    exsb = sbt("exsb", [16, 128], F32)
    exch = sbt("exch", [128, 16], F32)
    sm4 = sbt("sm4", [128, 4], F32)
    hins = sbt("hins", [128, 12], F32)
    stb = sbt("stb", [128, 2, 128], BF16)
    bis = sbt("bis", [128, 8], F32)
    bisI = sbt("bisI", [128, 1], I32)
    onesf = sbt("onesf", [1, 128], F32)
    ones1b = sbt("ones1b", [1, 128], BF16)
    cnts = sbt("cnts", [128, 1], F32)
    orow = sbt("orow", [1, 128], F32)
    orow8 = sbt("orow8", [8, 16], F32)
    obase = sbt("obase", [128, 1], F32)
    oint = sbt("oint", [1, 16], I32)
    FL = sbt("FL", [128, 16], I32)
    wraw = sbt("wraw", [128, 16], F32)
    wnrm = sbt("wnrm", [128, 16], F32)
    srcI = sbt("srcI", [128, C2], I32)
    dstI = sbt("dstI", [128, C2], I32)
    sentA = sbt("sentA", [128, C2], I32)
    sentB = sbt("sentB", [128, C2], I32)
    cnt2 = sbt("cnt2", [128, 1], F32)
    st1 = sbt("st1", [128, 16], F32)
    st2 = sbt("st2", [128, 16], F32)
    stt = sbt("stt", [128, 8], F32)
    scr8 = sbt("scr8", [128, 8], F32)
    epsv = sbt("epsv", [128, 1], F32)
    onesc = sbt("onesc", [128, 1], F32)
    iop = sbt("iop", [128, 1], I32)
    iopf = sbt("iopf", [128, 1], F32)
    ctTb = sbt("ctTb", [128, 8], BF16)

    KB = 1024
    lnsq = sat("lnsq", [128, C], F32, offset=A_s + 32 * KB)
    Ssc = sat("Ssc", [128, 1024], F32, offset=A_bt + 0)
    vid = sat("vid", [128, 1024], F32, offset=A_bt + 4 * KB)
    junk = sat("junk", [128, 1024], F32, offset=A_bt + 8 * KB)
    Vb = sat("Vb", [128, 1024], F32, offset=A_bt + 12 * KB)
    Vb2 = sat("Vb2", [128, 1024], F32, offset=A_bt + 16 * KB)
    Db = sat("Db", [128, 1024], I32, offset=A_bt + 20 * KB)
    Db2 = sat("Db2", [128, 1024], I32, offset=A_bt + 24 * KB)
    DmS = sat("DmS", [128, 1024], I32, offset=A_bt + 28 * KB)
    yst = sat("yst", [128, 2, C], BF16, offset=A_bt + 0)
    scr512 = sat("scr512", [128, 512], F32, offset=A_bt + 4 * KB)
    scrG = sat("scrG", [128, 512], F32, offset=A_w + 0)
    cc16 = sat("cc16", [16, 1024], I32, offset=A_w + 4 * KB)
    Sb = sat("Sb", [128, 1024], F32, offset=A_w + 8 * KB)
    Sb2 = sat("Sb2", [128, 1024], F32, offset=A_w + 12 * KB)
    cs16 = sat("cs16", [16, 1024], F32, offset=A_w + 16 * KB)
    dstE = sat("dstE", [16, 8, C2], I32, offset=A_w + 4 * KB)
    Gbig = sat("Gbig", [128, C2, C], BF16, offset=A_s + 0)
    Vc = Db
    xgt = sat("xgt", [128, 4, 8, 128], BF16, offset=A_w + 0)
    xg4 = sat("xg4", [128, 4, C], BF16, offset=A_w + 8 * KB)
    xg4b = sat("xg4b", [128, 4, C], BF16, offset=A_bt + 24 * KB)
    hpr = sat("hpr", [128, 16, 512], BF16, offset=A_w + 16 * KB)
    zrowb = sat("zrowb", [128, C], BF16, offset=A_s + 56 * KB)
    ohk16 = sat("ohk16s", [1, 128], F32, offset=addr(stb))
    row16 = sat("row16s", [1, 16], I32, offset=addr(exch))
    hsb = sat("hsb", [128, 2, T], BF16, offset=A_s + 0)
    xpt = sat("xpt", [128, 2, C], F32, offset=A_s + 8 * KB)
    w1v = sat("w1v", [128, 8, HF], BF16, offset=A_s + 0)
    w3v = sat("w3v", [128, 8, HF], BF16, offset=A_s + 32 * KB)
    Gb = sat("Gb", [128, 2, C], BF16, offset=A_s + 0)
    acct = sat("acct", [128, 2, C], BF16, offset=A_s + 8 * KB)
    outt = sat("outt", [128, 2, C], F32, offset=A_s + 16 * KB)

    pjA = nc.alloc_psum_tensor("pjA", [128, 512], F32)
    pjB = nc.alloc_psum_tensor("pjB", [128, 512], F32)
    pyA = nc.alloc_psum_tensor("pyA", [128, 1024], F32)
    pyB = nc.alloc_psum_tensor("pyB", [128, 1024], F32)
    pX = nc.alloc_psum_tensor("pX", [128, 128], F32)
    pSm = nc.alloc_psum_tensor("pSm", [128, 2], F32)

    g = nc.gpsimd
    sy = nc.sync
    v = nc.vector
    sc = nc.scalar
    te = nc.tensor
    BAR = nc.all_engine_barrier

    qg = nc.alloc_semaphore("qg")
    qs = nc.alloc_semaphore("qs")
    qc = nc.alloc_semaphore("qc")
    qh = nc.alloc_semaphore("qh")
    qh2 = nc.alloc_semaphore("qh2")
    qo = nc.alloc_semaphore("qo")
    cg = [0]; cs = [0]; cc = [0]; ch = [0]; co = [0]; ch2 = [0]

    def gdma(dst, src, **kw):
        r = g.dma_start(out=dst, in_=src, **kw).then_inc(qg, 16)
        cg[0] += 16
        return r

    def gwait():
        g.wait_ge(qg, cg[0])

    def sdma(dst, src, **kw):
        r = sy.dma_start(out=dst, in_=src, **kw).then_inc(qs, 16)
        cs[0] += 16
        return r

    def swait():
        sy.wait_ge(qs, cs[0])

    def coll(groups, ins, outs):
        g.collective_compute("AllGather", AL.bypass, replica_groups=groups,
                             ins=ins, outs=outs).then_inc(qc, 1)
        cc[0] += 1

    def mm(out_ps, lhsT, rhs, start, stop, width):
        nh = (width + 511) // 512
        r = None
        for h in range(nh):
            w0 = h * 512
            w1 = min(width, w0 + 512)
            r = te.matmul(out_ps[:, w0:w1], lhsT, rhs[:, w0:w1], start=start, stop=stop)
        return r

    class Edge:
        __slots__ = ("sem", "n")
        def __init__(self, name):
            self.sem = nc.alloc_semaphore(name)
            self.n = 0
        def inc(self, r):
            r.then_inc(self.sem, 1)
            self.n += 1
            return self.n
        def wait(self, eng, n=None):
            eng.wait_ge(self.sem, self.n if n is None else n)

    eTV = Edge("eTV")
    eVT = Edge("eVT")
    eVS = Edge("eVS")
    eSV = Edge("eSV")
    eVX = Edge("eVX")
    eVD = Edge("eVD")
    eVG = Edge("eVG")
    eTS = Edge("eTS")

    GRP8 = [list(range(8))]
    GRP4 = [[0, 1, 2, 3], [4, 5, 6, 7]]

    gdma(wdv[:, :, :, :], wd_in.rearrange("l (j p) n -> p l j n", p=128))
    gdma(wiv[:, :, :, :], wi_in.rearrange("l (j p) n -> p l j n", p=128))
    c_wdwi = cg[0]
    bt_cnt = []
    for j in range(8):
        gdma(bT[:, j, :],
             x_feat.rearrange("(j p) t -> p j t", p=128)[:, j, :])
        bt_cnt.append(cg[0])
    gdma(wov[:, :, :, :], wo_in.rearrange("l (a p) c -> p l a c", p=128))
    c_wov = cg[0]
    gdma(wsv[:, :, :, :], ws_in.rearrange("l (a p) c -> p l a c", p=128))
    c_wsv = cg[0]
    gdma(extv[:, :, :], ext_in.rearrange("j p f -> p j f"))
    c_ext = cg[0]
    gdma(wgv[:, :, :], wgt_in.rearrange("(j p) e -> p j e", p=128))
    gdma(w2v[:, :, :], w2t_in.rearrange("(j p) c -> p j c", p=128))
    c_w2v = cg[0]
    sdma(bdv[:, :, :], bd_in.rearrange("l (a p) -> p l a", p=128), allow_slow_non_contiguous=True)
    sdma(biv[:, :, :], bi_in.rearrange("l (a p) -> p l a", p=128), allow_slow_non_contiguous=True)
    sdma(spav[:, :, :], alog_in.rearrange("l (a p) -> p l a", p=128), allow_slow_non_contiguous=True)
    sdma(kv0[:, :], kv_in[:, :])
    sdma(eab[:, :], eab_in[:, :])
    sdma(ohp[:, :], ohp_in[:, :])
    sdma(ohk[:, :], ohk_in[:, :])
    sdma(lnt[0:1, 0, :], bo_in[0:1, :])
    sdma(lnt[0:1, 1, :], bs_in[0:1, :])
    sdma(ct2[0:1, 0:C], bo_in[1:2, :])
    sdma(sA[0:1, 0, 0:C], bs_in[1:2, :])
    g.iota(iop[:, :], pattern=[[0, 1]], base=0, channel_multiplier=1)
    g.iota(iocf[:, :], pattern=[[1, 1024]], base=0, channel_multiplier=0,
           allow_small_or_imprecise_dtypes=True)
    swait()
    BAR()
    v.tensor_add(bbvb[0:1, 0, :], lnt[0:1, 0, :], lnt[0:1, 1, :])
    v.tensor_add(bbvb[0:1, 1, :], ct2[0:1, 0:C], sA[0:1, 0, 0:C])
    v.memset(zeros2[:, :], 0.0)
    v.memset(zrowb[:, :], 0.0)
    v.memset(onesf[:, :], 1.0)
    v.memset(ones1b[:, :], 1.0)
    v.memset(sentA[:, :], NCORE * CAP)
    v.memset(sentB[:, :], T)
    v.memset(epsv[:, :], float(EPS))
    v.memset(onesc[:, :], 1.0)
    v.tensor_copy(iopf[:, :], iop[:, :])
    sc.activation(out=stt[:, 4:8], in_=spav[:, :, :].rearrange("p l a -> p (l a)"),
                  func=AF.Abs)
    BAR()
    sc.activation(out=stt[:, 4:8], in_=stt[:, 4:8], func=AF.Exp, scale=-1.0)
    BAR()
    sc.activation(out=stt[:, 4:8], in_=stt[:, 4:8], func=AF.Ln, bias=onesc[:, 0:1])
    BAR()
    v.tensor_scalar_max(st2[:, 4:8], spav[:, :, :].rearrange("p l a -> p (l a)"), 0.0)
    BAR()
    v.tensor_add(st2[:, 4:8], st2[:, 4:8], stt[:, 4:8])
    BAR()
    v.tensor_scalar_mul(spav[:, :, :].rearrange("p l a -> p (l a)"), st2[:, 4:8], -1.0)
    v.tensor_scalar(idT[:, :], iocf[:, 0:128], iopf[:, 0:1], None, op0=AL.is_equal)
    v.tensor_single_scalar(bisI[:, :], iop[:, :], 15.0, op=AL.bitwise_and)
    v.tensor_single_scalar(msI[:, 0:1], iop[:, :], 4.0, op=AL.logical_shift_right)
    g.iota(bd16[:, :], pattern=[[1, 8], [0, 16]], base=0, channel_multiplier=0,
           allow_small_or_imprecise_dtypes=True)
    BAR()
    v.tensor_copy(stt[:, 1:2], msI[:, 0:1])
    BAR()
    v.tensor_scalar(bd16[:, :], bd16[:, :], stt[:, 1:2], None, op0=AL.is_equal)
    for m in range(16):
        sdma(acc_d.ap()[0:T, :].rearrange("(m p) c -> p m c", p=128)[:, m:m + 1, :],
             zrowb[:, :].rearrange("p (o c) -> p o c", o=1))
    sdma(acc_d.ap()[T:T + 1, :], zrowb[0:1, :])
    sdma(y_all.ap()[NCORE * CAP:NCORE * CAP + 1, :], zrowb[0:1, :])
    swait()
    BAR()

    def ssm_layer(l, rhsbuf):
        evac_cnt = []
        if l == 0:
            te.wait_ge(qg, c_wdwi)
        for it in range(8):
            tsl, a = it % 4, it // 4
            t0 = tsl * 512
            pA = pjA[:, :] if it % 2 == 0 else pyA[:, 0:512]
            pB = pjB[:, :] if it % 2 == 0 else pyA[:, 512:1024]
            if it >= 2:
                te.wait_ge(eVT.sem, evac_cnt[it - 2])
            r = None
            for j in range(8):
                if l == 0 and it == 0:
                    te.wait_ge(qg, bt_cnt[j])
                r = te.matmul(pA, wdv[:, l, j, a * 128:(a + 1) * 128],
                              rhsbuf[:, j, t0:t0 + 512], start=(j == 0), stop=(j == 7))
            for j in range(8):
                r = te.matmul(pB, wiv[:, l, j, a * 128:(a + 1) * 128],
                              rhsbuf[:, j, t0:t0 + 512], start=(j == 0), stop=(j == 7))
            eTV.inc(r)
            v.wait_ge(eTV.sem, eTV.n)
            v.tensor_scalar(sA[:, a, t0:t0 + 512], pA,
                            bdv[:, l, a:a + 1], None, op0=AL.add)
            r = v.tensor_scalar(sB[:, a, t0:t0 + 512], pB,
                                biv[:, l, a:a + 1], None, op0=AL.add)
            eVT.inc(r)
            evac_cnt.append(eVT.n)
        if build.dbg_layer == (l, "raw"):
            for a in range(2):
                for h in range(2):
                    sdma(build.dbg_out.rearrange("(z p) c -> p z c", p=128)[:, 2 * a + h:2 * a + h + 1, :],
                         sA[:, a, h * 1024:(h + 1) * 1024].rearrange("p (o c) -> p o c", o=1))
                    sdma(build.dbg_out.rearrange("(z p) c -> p z c", p=128)[:, 8 + 2 * a + h:8 + 2 * a + h + 1, :],
                         sB[:, a, h * 1024:(h + 1) * 1024].rearrange("p (o c) -> p o c", o=1))
            swait()
            BAR()
            build.dbg_early[0] = True
        if build.dbg_early[0]:
            return
        for a in range(2):
            sc.wait_ge(eVT.sem, evac_cnt[3 + 4 * a])
            r = sc.activation(out=sC[:, a, :], in_=sA[:, a, :], func=AF.Abs)
            eSV.inc(r)
            sc.activation(out=sC[:, a, :], in_=sC[:, a, :], func=AF.Exp, scale=-1.0)
            r = sc.activation(out=sC[:, a, :], in_=sC[:, a, :], func=AF.Ln,
                              bias=onesc[:, 0:1])
            eSV.inc(r)
        sv_sp = eSV.n
        for a in range(2):
            eSV.wait(v, sv_sp - 3 + 2 * a)
            v.tensor_scalar_max(sA[:, a, :], sA[:, a, :], 0.0)
            eSV.wait(v, sv_sp - 2 + 2 * a)
            v.tensor_add(sA[:, a, :], sA[:, a, :], sC[:, a, :])
            v.tensor_tensor_scan(out=sC[:, a, :], data0=sA[:, a, :],
                                 data1=zeros2[:, :], initial=0.0,
                                 op0=AL.add, op1=AL.add)
            r = v.tensor_mul(sB[:, a, :], sA[:, a, :], sB[:, a, :])
            eVS.inc(r)
            eVS.wait(sc)
            sc.activation(out=sA[:, a, :], in_=sA[:, a, :], func=AF.Exp,
                          scale=spav[:, l, a:a + 1])
            sc.activation(out=sm4[:, a:a + 1], in_=sC[:, a, T - 1:T], func=AF.Exp,
                          scale=spav[:, l, a:a + 1])
            r = sc.activation(out=sC[:, a, :], in_=sC[:, a, :], func=AF.Exp,
                              scale=spav[:, l, a:a + 1])
            eSV.inc(r)
            eSV.wait(v)
            r = v.tensor_tensor_scan(out=sD[:, a, :], data0=sA[:, a, :],
                                     data1=sB[:, a, :], initial=0.0,
                                     op0=AL.mult, op1=AL.add)
        v.tensor_copy(sm4[:, 2:3], sD[:, 0, T - 1:T])
        r = v.tensor_copy(sm4[:, 3:4], sD[:, 1, T - 1:T])
        eVT.inc(r)
        te.wait_ge(eVT.sem, eVT.n)
        te.wait_ge(eSV.sem, eSV.n)
        r = te.transpose(pX[0:4, :], sm4[:, 0:4], idT[:, :])
        eTV.inc(r)
        eTV.wait(v)
        r = v.tensor_copy(exsb[0:4, :], pX[0:4, :])
        eVD.inc(r)
        eVD.wait(sy)
        sdma(ag_s_in[l].ap()[:, :], exsb[0:4, :])
        g.wait_ge(qs, cs[0])
        coll(GRP4, [ag_s_in[l].ap().opt()], [ag_s_out[l].ap().opt()])
        g.wait_ge(qc, cc[0])
        gdma(exsb[:, :], ag_s_out[l].ap()[:, :])
        te.wait_ge(qg, cg[0])
        r = te.transpose(pX[:, 0:16], exsb[0:16, :], idT[0:16, 0:16])
        eTV.inc(r)
        eTV.wait(v)
        v.tensor_copy(exch[:, 0:16], pX[:, 0:16])
        v.memset(hins[:, 0:2], 0.0)
        v.tensor_copy(lnt[:, 0, 0:384], zeros2[:, 0:384])
        for m in range(1, 5):
            for cch in range(2):
                pcol = 4 * (m - 1) + cch
                hcol = 4 * (m - 1) + 2 + cch
                v.tensor_mul(stt[:, cch:cch + 1], exch[:, pcol:pcol + 1],
                             hins[:, 2 * (m - 1) + cch:2 * (m - 1) + cch + 1])
            v.tensor_copy(lnt[:, 0, 0:384], zeros2[:, 0:384])
            for cch in range(2):
                hcol = 4 * (m - 1) + 2 + cch
                v.tensor_add(hins[:, 2 * m + cch:2 * m + cch + 1],
                             stt[:, cch:cch + 1], exch[:, hcol:hcol + 1])
            v.tensor_copy(lnt[:, 0, 0:384], zeros2[:, 0:384])
        v.tensor_mul(stt[:, 0:8], hins[:, 0:8], ohp[:, 0:8])
        v.tensor_copy(lnt[:, 0, 0:384], zeros2[:, 0:384])
        v.reduce_sum(hins[:, 10:12], stt[:, 0:8].rearrange("p (m c) -> p c m", c=2),
                     axis=AX)
        v.tensor_copy(lnt[:, 0, 0:384], zeros2[:, 0:384])
        for a in range(2):
            v.scalar_tensor_tensor(out=sD[:, a, :], in0=sC[:, a, :],
                                   scalar=hins[:, 10 + a:11 + a], in1=sD[:, a, :],
                                   op0=AL.mult, op1=AL.add)
        if build.dbg_layer == (l, "exch"):
            v.tensor_copy(lnt[:, 0, 0:16], exch[:, 0:16])
            v.tensor_copy(lnt[:, 0, 16:28], hins[:, 0:12])
            v.tensor_copy(lnt[:, 0, 28:32], sm4[:, 0:4])
            v.memset(lnt[:, 0, 32:1024], 0.0)
            BAR()
            for mm2 in range(16):
                sdma(out.rearrange("(m p) c -> p m c", p=128)[:, mm2:mm2 + 1, :],
                     lnt[:, 0:1, :])
            swait()
            BAR()
            build.dbg_early[0] = True
            return
        v.tensor_copy(hsb[:, 0, :], sD[:, 0, :])
        v.tensor_copy(hsb[:, 1, :], sD[:, 1, :])
        for a in range(2):
            r = v.tensor_scalar(stb[:, a, :], zeros2[:, 0:128], hins[:, 8 + a:9 + a],
                                None, op0=AL.add)
        eVT.inc(r)
        te.wait_ge(eVT.sem, eVT.n)

    build.dbg_layer = (0, "raw") if dbg == "raw1" else (
        (0, "exch") if dbg == "exch" else (
            (1, "exch") if dbg == "exch2" else None))
    build.dbg_out = out
    build.dbg_early = [False]
    ssm_layer(0, bT)

    if build.dbg_early[0]:
        return nc

    if dbg in ("hs1", "dec1", "drv1", "cum1"):
        buf = {"hs1": sD, "dec1": sA, "drv1": sB, "cum1": sC}[dbg]
        for a in range(2):
            for h in range(2):
                sdma(out.rearrange("(z p) c -> p z c", p=128)[:, 2 * a + h:2 * a + h + 1, :],
                     buf[:, a, h * 1024:(h + 1) * 1024].rearrange("p (o c) -> p o c", o=1))
        for z in range(4, 16):
            sdma(out.rearrange("(z p) c -> p z c", p=128)[:, z:z + 1, :],
                 sD[:, 0, 0:1024].rearrange("p (o c) -> p o c", o=1))
        swait()
        BAR()
        return nc

    te.wait_ge(qg, c_wsv)
    for j in range(8):
        for a in range(2):
            te.matmul(pX[:, 0:1], wsv[:, 0, a, j * 128:(j + 1) * 128],
                      stb[:, a, 0:1], start=(a == 0), stop=False)
        te.matmul(pX[:, 0:1], bbvb[0:1, 0, j * 128:(j + 1) * 128], ones1b[0:1, 0:1],
                  start=False, stop=True)
        BAR()
        v.tensor_copy(ctTb[:, j:j + 1], pX[:, 0:1])
        BAR()
    for (wv, bv) in ((wdv, bdv), (wiv, biv)):
        for a in range(2):
            for j in range(8):
                te.matmul(pX[:, 1:2], wv[:, 1, j, a * 128:(a + 1) * 128],
                          ctTb[:, j:j + 1], start=(j == 0), stop=(j == 7))
            BAR()
            v.tensor_add(bv[:, 1, a:a + 1], bv[:, 1, a:a + 1], pX[:, 1:2])
            BAR()
    te.wait_ge(qg, c_wov)
    y1_cnt = []
    for j in range(8):
        for th in range(2):
            it = 2 * j + th
            py = pyA if it % 2 == 0 else pyB
            if it >= 2:
                te.wait_ge(eVT.sem, y1_cnt[it - 2])
            r = None
            for a in range(2):
                r = mm(py, wov[:, 0, a, j * 128:(j + 1) * 128],
                       hsb[:, a, th * 1024:(th + 1) * 1024], (a == 0), (a == 1), 1024)
            eTV.inc(r)
            v.wait_ge(eTV.sem, eTV.n)
            r = v.tensor_copy(bT[:, j, th * 1024:(th + 1) * 1024], py[:, :])
            eVT.inc(r)
            y1_cnt.append(eVT.n)
    BAR()

    ssm_layer(1, bT)
    if build.dbg_early[0]:
        return nc

    for a in range(2):
        mm(pyA, stb[:, a, :], wsv[:, 1, a, :], (a == 0), False, C)
    mm(pyA, ones1b[0:1, 0:128], bbvb[0:1, 1, :], False, True, C)
    BAR()
    v.tensor_copy(ct2[:, :], pyA[:, :])
    BAR()

    bTf = bT.rearrange("p a t -> p (a t)")
    ln1_base = (eTV.n, eVT.n, eVS.n, eSV.n, eVX.n, eVD.n)
    xpt_cnt = []
    tr_cnt = []
    for m in range(16):
        py = pyA if m % 2 == 0 else pyB
        b = m % 2
        if m >= 2:
            eVT.wait(te, ln1_base[1] + m - 1)
        r = None
        for a in range(2):
            r = mm(py, hsb[:, a, m * 128:(m + 1) * 128], wov[:, 1, a, :],
                   (a == 0), (a == 1), C)
        eTV.inc(r)
        if m >= 2:
            sy.wait_ge(eVX.sem, ln1_base[4] + m - 1)
        sdma(xpt[:, b:b + 1, :],
             x_tok.rearrange("(m p) c -> p m c", p=128)[:, m:m + 1, :])
        xpt_cnt.append(cs[0])
        eTV.wait(v)
        r = v.tensor_add(lnt[:, b, :], py[:, :], ct2[:, :])
        eVT.inc(r)
        v.wait_ge(qs, xpt_cnt[m])
        r = v.scalar_tensor_tensor(out=lnt[:, b, :], in0=lnt[:, b, :], scalar=1.0,
                                   in1=xpt[:, b, :], op0=AL.mult, op1=AL.add,
                                   accum_out=st1[:, m:m + 1])
        eVX.inc(r)
        v.scalar_tensor_tensor(out=lnsq[:, :], in0=lnt[:, b, :], scalar=1.0,
                               in1=lnt[:, b, :], op0=AL.mult, op1=AL.mult,
                               accum_out=st2[:, m:m + 1])
        if m >= 2:
            eSV.wait(v, ln1_base[3] + 2 * (m - 1))
        r = v.tensor_scalar_mul(st1[:, m:m + 1], st1[:, m:m + 1], 1.0 / C)
        eVS.inc(r)
        eVS.wait(sc)
        r = sc.activation(out=stt[:, 4 + b:5 + b], in_=st1[:, m:m + 1],
                          func=AF.Square)
        eSV.inc(r)
        eSV.wait(v)
        r = v.tensor_scalar(stt[:, b:b + 1], st2[:, m:m + 1], 1.0 / C,
                            stt[:, 4 + b:5 + b], op0=AL.mult, op1=AL.subtract)
        eVS.inc(r)
        eVS.wait(sc)
        r = sc.activation(out=stt[:, 2 + b:3 + b], in_=stt[:, b:b + 1],
                          func=AF.Sqrt, bias=epsv[:, 0:1])
        eSV.inc(r)
        eSV.wait(v)
        v.reciprocal(stt[:, 6 + b:7 + b], stt[:, 2 + b:3 + b])
        v.tensor_copy(lnsq[:, 0:384], zeros2[:, 0:384])
        if m >= 2:
            v.wait_ge(qs, tr_cnt[m - 2])
        r = v.tensor_scalar(x1s[:, b, :], lnt[:, b, :], st1[:, m:m + 1],
                            stt[:, 6 + b:7 + b], op0=AL.subtract, op1=AL.mult)
        eVD.inc(r)
        eVD.wait(sy)
        sdma(ag_x_in.ap().rearrange("(m p) c -> p m c", p=128)[:, m:m + 1, :],
             x1s[:, b:b + 1, :])
        sy.dma_start_transpose(
            bTf[:, m * 1024:(m + 1) * 1024].rearrange("p (a q) -> p a q", a=8),
            x1s[:, b, :].rearrange("p (a q) -> p a q", a=8)
        ).then_inc(qs, 16)
        cs[0] += 16
        tr_cnt.append(cs[0])
    swait()
    BAR()
    gdma(w1v[:, :, :], w1t_in.rearrange("(j p) h -> p j h", p=128))
    gdma(w3v[:, :, :], w3t_in.rearrange("(j p) h -> p j h", p=128))

    if dbg == "x1":
        for m in range(16):
            b = m % 2
            gdma(x1s[:, b:b + 1, :],
                 ag_x_in.ap().rearrange("(m p) c -> p m c", p=128)[:, m:m + 1, :])
            gwait()
            BAR()
            v.tensor_copy(outt[:, 0, :], x1s[:, b, :])
            BAR()
            sdma(out.rearrange("(m p) c -> p m c", p=128)[:, m:m + 1, :],
                 outt[:, 0:1, :])
            swait()
        BAR()
        return nc

    x1Tv = bTf.rearrange("p (m a q) -> p m a q", m=16, a=8)
    te.wait_ge(qg, c_ext)
    for rt in range(16):
        for j in range(8):
            te.matmul(pX[:, :], extv[:, j, 16 - rt:144 - rt],
                      x1Tv[:, :, j, :].rearrange("p m (q r) -> p r m q", r=16)[:, rt, :, :],
                      start=(rt == 0 and j == 0), stop=(rt == 15 and j == 7))
    BAR()
    v.tensor_copy(junk[:, 0:128], pX[:, :])
    BAR()
    sdma(ag_t_in.ap()[:, :], junk[:, 0:128])
    swait()
    BAR()
    coll(GRP8, [ag_t_in.ap().opt()], [ag_t_out.ap().opt()])
    cc_agt = cc[0]
    g.wait_ge(qc, cc[0])
    coll(GRP8, [ag_x_in.ap().opt()], [ag_x_out.ap().opt()])
    sy.wait_ge(qc, cc_agt)
    sdma(Ssc[:, :].rearrange("er (kk c) -> er kk c", c=128),
         ag_t_out.ap().rearrange("(kk er) c -> er kk c", er=128))
    sdma(ohk16[:, :], ohk16_in[:, :])
    sdma(row16[:, :], row16_in[:, :])
    swait()
    BAR()
    if dbg == "bis":
        v.tensor_copy(lnt[:, 0, 233:361], Ssc[:, 0:128])
        v.reduce_max(lnt[:, 0, 361:362], Ssc[:, :], axis=AX)
        v.reduce_max(lnt[:, 0, 362:363], Ssc[:, 0:128], axis=AX)
        BAR()

    v.tensor_single_scalar(bisI[:, :], iop[:, :], 15.0, op=AL.bitwise_and)
    BAR()
    v.tensor_copy(stt[:, 0:1], bisI[:, :])
    BAR()
    v.tensor_scalar(vid[:, :], iocf[:, :], 16.0, stt[:, 0:1], op0=AL.mult, op1=AL.add)
    v.reduce_max(bis[:, 0:1], Ssc[:, :], axis=AX)
    v.tensor_reduce(bis[:, 1:2], Ssc[:, :], axis=AX, op=AL.min, negate=True)
    BAR()
    if dbg == "bis":
        v.tensor_copy(lnt[:, 0, 100:102], bis[:, 0:2])
        BAR()
    te.transpose(pX[0:2, :], bis[:, 0:2], idT[:, :])
    BAR()
    v.tensor_copy(scrG[0:2, 0:128], pX[0:2, 0:128])
    BAR()
    if dbg == "bis":
        v.tensor_copy(lnt[0:2, 0, 102:230], scrG[0:2, 0:128])
        BAR()
    v.reduce_max(scr8[0:2, 0:1], scrG[0:2, 0:128], axis=AX)
    BAR()
    te.transpose(pX[0:1, 0:2], scr8[0:2, 0:1], idT[0:2, 0:2])
    BAR()
    v.tensor_copy(stt[0:1, 0:2], pX[0:1, 0:2])
    BAR()
    if dbg == "bis":
        v.tensor_copy(lnt[0:2, 0, 230:231], scr8[0:2, 0:1])
        v.tensor_copy(lnt[0:1, 0, 231:233], stt[0:1, 0:2])
        BAR()
    te.matmul(pSm[:, 0:2], onesf[0:1, 0:128], stt[0:1, 0:2], start=True, stop=True)
    BAR()
    v.tensor_scalar_add(bis[:, 0:1], pSm[:, 0:1], 1.0)
    v.tensor_scalar(bis[:, 1:2], pSm[:, 1:2], -1.0, -1.0, op0=AL.mult, op1=AL.add)
    v.tensor_scalar_mul(bis[:, 4:5], pSm[:, 0:1], -1.0)
    BAR()
    if dbg == "bis":
        v.tensor_copy(lnt[:, 0, 0:8], bis[:, 0:8])
        BAR()
    for it in range(NBIS if dbg != "bis" else 1):
        v.tensor_add(bis[:, 2:3], bis[:, 0:1], bis[:, 1:2])
        BAR()
        v.tensor_scalar_mul(bis[:, 2:3], bis[:, 2:3], 0.5)
        BAR()
        v.tensor_scalar(junk[:, :], Ssc[:, :], bis[:, 2:3], None, op0=AL.is_ge)
        BAR()
        v.reduce_sum(cnts[:, :], junk[:, :], axis=AX)
        BAR()
        te.matmul(pSm[:, 0:1], bd16[:, :], cnts[:, :], start=True, stop=True)
        BAR()
        v.tensor_single_scalar(bisI[:, :], pSm[:, 0:1], 2048.0, op=AL.is_ge)
        BAR()
        v.select(bis[:, 1:2], bisI[:, :], bis[:, 2:3], bis[:, 1:2])
        v.select(bis[:, 0:1], bisI[:, :], bis[:, 0:1], bis[:, 2:3])
        BAR()
    if dbg == "bis":
        v.tensor_copy(lnt[:, 0, 8:9], cnts[:, :])
        v.tensor_copy(lnt[:, 0, 9:10], pSm[:, 0:1])
        v.tensor_copy(lnt[:, 0, 10:11], bisI[:, :])
        v.tensor_copy(lnt[:, 0, 11:14], bis[:, 0:3])
        v.tensor_copy(lnt[:, 0, 14:142], Ssc[:, 0:128])
        BAR()
        for m in range(16):
            sdma(out.rearrange("(m p) c -> p m c", p=128)[:, m:m + 1, :],
                 lnt[:, 0:1, :])
        swait()
        BAR()
        return nc
    v.tensor_scalar(msI[:, :], Ssc[:, :], bis[:, 1:2], None, op0=AL.is_ge)
    v.tensor_scalar(mskF[:, :], Ssc[:, :], bis[:, 1:2], None, op0=AL.is_ge)
    BAR()
    v.reduce_sum(cnts[:, :], mskF[:, :], axis=AX)
    sc.activation(out=junk[:, :], in_=Ssc[:, :], func=AF.Exp, bias=bis[:, 4:5])
    BAR()
    v.tensor_mul(junk[:, :], junk[:, :], mskF[:, :])
    BAR()
    v.reduce_sum(st1[:, 0:1], junk[:, :], axis=AX)
    BAR()
    te.transpose(pX[0:1, :], st1[:, 0:1], idT[:, :])
    BAR()
    v.tensor_copy(orow[0:1, :], pX[0:1, 0:128])
    BAR()
    v.tensor_mul(orow[0:1, :], orow[0:1, :], ohk16[0:1, :])
    BAR()
    v.reduce_sum(stt[0:1, 4:5], orow[0:1, :], axis=AX)
    BAR()
    te.matmul(pSm[:, 1:2], onesf[0:1, 0:128], stt[0:1, 4:5], start=True, stop=True)
    BAR()
    v.reciprocal(bis[:, 5:6], pSm[:, 1:2])
    v.scalar_tensor_tensor(out=Vb[:, :], in0=vid[:, :], scalar=1.0, in1=mskF[:, :],
                           op0=AL.add, op1=AL.mult)
    v.tensor_tensor_scan(out=junk[:, :], data0=mskF[:, :], data1=zeros2[:, 0:1024],
                         initial=0.0, op0=AL.add, op1=AL.add)
    BAR()
    v.tensor_scalar_add(Vb[:, :], Vb[:, :], -1.0)
    v.tensor_sub(junk[:, :], iocf[:, :], junk[:, :])
    BAR()
    v.tensor_scalar_add(junk[:, :], junk[:, :], 1.0)
    BAR()
    v.tensor_mul(junk[:, :], junk[:, :], mskF[:, :])
    BAR()
    v.tensor_copy(Db[:, :], junk[:, :])
    BAR()

    def butterfly(Vb_, Vb2_, Db_, Db2_, Sb_=None, Sb2_=None):
        Vcur, Vnx, Dcur, Dnx = Vb_, Vb2_, Db_, Db2_
        Scur, Snx = Sb_, Sb2_
        for b in range(10):
            s = 1 << b
            v.tensor_single_scalar(msI[:, :], Dcur[:, :], float(s), op=AL.bitwise_and)
            v.tensor_single_scalar(DmS[:, :], Dcur[:, :], float(s), op=AL.subtract)
            BAR()
            v.select(Vnx[:, 0:1024 - s], msI[:, s:], Vcur[:, s:], Vcur[:, 0:1024 - s])
            v.tensor_copy(Vnx[:, 1024 - s:], Vcur[:, 1024 - s:])
            if Scur is not None:
                v.select(Snx[:, 0:1024 - s], msI[:, s:], Scur[:, s:],
                         Scur[:, 0:1024 - s])
                v.tensor_copy(Snx[:, 1024 - s:], Scur[:, 1024 - s:])
                Scur, Snx = Snx, Scur
            v.select(Dnx[:, 0:1024 - s], msI[:, s:], DmS[:, s:], Dcur[:, 0:1024 - s])
            v.tensor_copy(Dnx[:, 1024 - s:], Dcur[:, 1024 - s:])
            BAR()
            Vcur, Vnx, Dcur, Dnx = Vnx, Vcur, Dnx, Dcur
        return Vcur, Scur

    v.tensor_copy(Sb[:, :], Ssc[:, :])
    BAR()
    CT, SCb = butterfly(Vb, Vb2, Db, Db2, Sb, Sb2)
    te.transpose(pX[0:1, :], cnts[:, :], idT[:, :])
    BAR()
    v.tensor_copy(orow[0:1, :], pX[0:1, 0:128])
    BAR()
    sdma(orow_d.ap()[:, :], orow[0:1, :])
    swait()
    BAR()
    gdma(orow8[:, :], orow_d.ap().rearrange("o (e r) -> (o e) r", e=8))
    gwait()
    BAR()
    v.tensor_tensor_scan(out=orow8[:, :], data0=orow8[:, :], data1=zeros2[0:8, 0:16],
                         initial=0.0, op0=AL.add, op1=AL.add)
    BAR()
    gdma(exsb[0:8, 0:16], orow_d.ap().rearrange("o (e r) -> (o e) r", e=8))
    gwait()
    BAR()
    v.tensor_sub(orow8[:, :], orow8[:, :], exsb[0:8, 0:16])
    BAR()
    gdma(orow_d.ap().rearrange("o (e r) -> (o e) r", e=8), orow8[:, :])
    gwait()
    BAR()
    sdma(orow[0:1, :], orow_d.ap()[:, :])
    swait()
    BAR()
    v.tensor_mul(exsb[0:1, 0:128], orow[0:1, :], ohk16[0:1, :])
    BAR()
    v.reduce_sum(wraw[0:1, 0:16], exsb[0:1, 0:128].rearrange("o (e r) -> o r e", e=8),
                 axis=AX)
    BAR()
    v.tensor_copy(oint[0:1, :], wraw[0:1, 0:16])
    te.transpose(pX[:, 0:1], orow[0:1, :], idT[0:1, 0:1])
    BAR()
    v.tensor_copy(obase[:, :], pX[:, 0:1])
    v.tensor_copy(Vc[:, :], CT[:, :])
    BAR()
    sdma(vc_d.ap()[:, :], Vc[:, :])
    sdma(sc_d.ap()[:, :], SCb[:, :])
    swait()
    BAR()
    rv = nc.values_load(row16[0:1, 0:1], engines=[mybir.EngineType.SP])
    sy.dma_start(out=cc16[:, :], in_=vc_d.ap()[bass.ds(rv, 16), :]).then_inc(qh, 16)
    ch[0] += 16
    sy.dma_start(out=cs16[:, :], in_=sc_d.ap()[bass.ds(rv, 16), :]).then_inc(qh2, 16)
    ch2[0] += 16
    sy.wait_ge(qh, ch[0])
    sy.wait_ge(qh2, ch2[0])
    sc.wait_ge(qh, ch[0])
    sc.wait_ge(qh2, ch2[0])
    for r in range(16):
        eng = mybir.EngineType.SP if r < 8 else mybir.EngineType.Activation
        eng2 = mybir.EngineType.Activation if r < 8 else mybir.EngineType.SP
        ov = nc.values_load(oint[0:1, r:r + 1], engines=[eng])
        ov2 = nc.values_load(oint[0:1, r:r + 1], engines=[eng2])
        issuer = sy if r < 8 else sc
        issuer2 = sc if r < 8 else sy
        issuer.wait_ge(qh, ch[0])
        issuer.dma_start(out=l_list.ap()[0:1, bass.ds(ov, 1024)],
                         in_=cc16[r:r + 1, :]).then_inc(qh, 16)
        ch[0] += 16
        issuer2.wait_ge(qh2, ch2[0])
        issuer2.dma_start(out=s_list.ap()[0:1, bass.ds(ov2, 1024)],
                          in_=cs16[r:r + 1, :]).then_inc(qh2, 16)
        ch2[0] += 16
    sy.wait_ge(qh, ch[0])
    g.wait_ge(qh, ch[0])
    g.wait_ge(qh2, ch2[0])
    gdma(FL[:, :], l_list.ap()[0:1, 0:2048].rearrange("o (p q) -> (o p) q", q=16))
    gdma(wraw[:, :], s_list.ap()[0:1, 0:2048].rearrange("o (p q) -> (o p) q", q=16))
    gwait()
    g.wait_ge(qc, cc[0])
    BAR()

    if dbg in ("sel", "sel2"):
        v.tensor_copy(lnt[:, 0, 0:16], FL[:, :])
        v.tensor_copy(lnt[:, 0, 16:17], cnts[:, :])
        v.tensor_copy(lnt[:, 0, 17:18], bis[:, 1:2])
        v.tensor_copy(lnt[:, 0, 18:19], obase[:, :])
        v.tensor_copy(lnt[:, 0, 19:20], bis[:, 5:6])
        v.memset(lnt[:, 0, 20:1024], 0.0)
        BAR()
        for m in range(16):
            sdma(out.rearrange("(m p) c -> p m c", p=128)[:, m:m + 1, :],
                 lnt[:, 0:1, :])
        swait()
        BAR()
        return nc

    sc.activation(out=wnrm[:, :], in_=wraw[:, :], func=AF.Exp, bias=bis[:, 4:5])
    BAR()
    v.tensor_scalar(wnrm[:, :], wnrm[:, :], bis[:, 5:6], None, op0=AL.mult)
    BAR()

    xgB = [xg4, xg4b]
    ffn_gq = []
    ffn_tq = []
    ydma_cnt = []
    for m4 in range(4):
        g.indirect_dma_start(
            out=xgB[0][:, m4, :], out_offset=None, in_=ag_x_out.ap(),
            in_offset=IndirectOffsetOnAxis(FL[:, m4:m4 + 1], axis=0),
        ).then_inc(qg, 16)
        cg[0] += 16
    ffn_gq.append(cg[0])
    tv0 = eTV.n
    vt0 = eVT.n
    ts0 = eTS.n
    xgt_free = []
    for G in range(4):
        b = G % 2
        sy.wait_ge(qg, ffn_gq[G])
        if G >= 1:
            sy.wait_ge(eTV.sem, xgt_free[G - 1])
        for m4 in range(4):
            sy.dma_start_transpose(
                xgt[:, m4, :, :],
                xgB[b][:, m4, :].rearrange("p (a q) -> p a q", a=8)
            ).then_inc(qs, 16)
            cs[0] += 16
        ffn_tq.append(cs[0])
        if G < 3:
            if G >= 1:
                g.wait_ge(qs, ffn_tq[G - 1])
            for m4 in range(4):
                gi = 4 * (G + 1) + m4
                g.indirect_dma_start(
                    out=xgB[(G + 1) % 2][:, m4, :], out_offset=None,
                    in_=ag_x_out.ap(),
                    in_offset=IndirectOffsetOnAxis(FL[:, gi:gi + 1], axis=0),
                ).then_inc(qg, 16)
                cg[0] += 16
            ffn_gq.append(cg[0])
        te.wait_ge(qs, ffn_tq[G])
        last_w3 = None
        for hc in range(16):
            hcg = 16 * G + hc
            pA = pjA[:, :] if hc % 2 == 0 else pyA[:, 0:512]
            pB = pjB[:, :] if hc % 2 == 0 else pyA[:, 512:1024]
            if hcg >= 2:
                te.wait_ge(eVT.sem, vt0 + hcg - 1)
            r = None
            for a in range(8):
                r = te.matmul(pA, w1v[:, a, hc * 128:(hc + 1) * 128],
                              xgt[:, :, a, :], start=(a == 0), stop=(a == 7))
            eTS.inc(r)
            ts_cnt = eTS.n
            for a in range(8):
                r = te.matmul(pB, w3v[:, a, hc * 128:(hc + 1) * 128],
                              xgt[:, :, a, :], start=(a == 0), stop=(a == 7))
            eTV.inc(r)
            last_w3 = eTV.n
            sc.wait_ge(eTS.sem, ts_cnt)
            if hcg >= 1:
                sc.wait_ge(eVT.sem, vt0 + hcg)
            r = sc.activation(out=scr512[:, :], in_=pA, func=AF.Silu)
            eSV.inc(r)
            eSV.wait(v)
            v.wait_ge(eTV.sem, last_w3)
            r = v.tensor_mul(hpr[:, hc, :], scr512[:, :], pB)
            eVT.inc(r)
        xgt_free.append(last_w3)
        for d in range(4):
            py = pyB if d % 2 == 0 else pyA
            gi = 4 * G + d
            if G == 0 and d == 0:
                te.wait_ge(qg, c_w2v)
            if d % 2 == 1:
                te.wait_ge(eVT.sem, vt0 + 16 * G + 16)
            if 4 * G + d >= 2:
                te.wait_ge(eVD.sem, eVD.n - 2 + 1)
            r = None
            for hc in range(16):
                r = mm(py, hpr[:, hc, d * 128:(d + 1) * 128], w2v[:, hc, :],
                       (hc == 0), (hc == 15), C)
            eTV.inc(r)
            v.wait_ge(eTV.sem, eTV.n)
            if len(ydma_cnt) >= 2:
                v.wait_ge(qs, ydma_cnt[-2])
            r = v.tensor_scalar(yst[:, d % 2, :], py[:, :], wnrm[:, gi:gi + 1],
                                None, op0=AL.mult)
            eVD.inc(r)
            eVD.wait(sy)
            sdma(ag_y_in.ap()[128 * gi:128 * gi + 128, :], yst[:, d % 2, :])
            ydma_cnt.append(cs[0])
        g.wait_ge(qs, cs[0])
        coll(GRP8,
             [ag_y_in.ap()[512 * G:512 * (G + 1), :].opt()],
             [y_all.ap()[4096 * G:4096 * (G + 1), :].opt()])
    swait()
    BAR()

    v.tensor_scalar(mskF[:, :], iocf[:, :], cnts[:, 0:1], None, op0=AL.is_lt)
    BAR()
    v.scalar_tensor_tensor(out=CT[:, :], in0=CT[:, :], scalar=1.0, in1=mskF[:, :],
                           op0=AL.add, op1=AL.mult)
    BAR()
    v.tensor_scalar_add(CT[:, :], CT[:, :], -1.0)
    BAR()
    v.tensor_scalar(junk[:, :], iocf[:, :], obase[:, 0:1], None, op0=AL.add)
    BAR()
    v.scalar_tensor_tensor(out=junk[:, :], in0=junk[:, :], scalar=2048.0,
                           in1=CT[:, :], op0=AL.mult, op1=AL.add)
    BAR()
    v.tensor_scalar(junk[:, :], junk[:, :], kv0[:, 0:1], None, op0=AL.subtract)
    BAR()
    v.tensor_scalar(msI[:, :], CT[:, :], kv0[:, 0:1], None, op0=AL.is_ge)
    v.tensor_scalar(DmS[:, :], CT[:, :], kv0[:, 1:2], None, op0=AL.is_lt)
    BAR()
    v.tensor_tensor(msI[:, :], msI[:, :], DmS[:, :], op=AL.bitwise_and)
    BAR()
    v.tensor_copy(mskF[:, :], msI[:, :])
    BAR()
    v.scalar_tensor_tensor(out=Vb2[:, :], in0=junk[:, :], scalar=1.0, in1=mskF[:, :],
                           op0=AL.add, op1=AL.mult)
    v.reduce_sum(cnt2[:, :], mskF[:, :], axis=AX)
    v.tensor_tensor_scan(out=junk[:, :], data0=mskF[:, :], data1=zeros2[:, 0:1024],
                         initial=0.0, op0=AL.add, op1=AL.add)
    BAR()
    v.tensor_scalar_add(Vb2[:, :], Vb2[:, :], -1.0)
    v.tensor_sub(junk[:, :], iocf[:, :], junk[:, :])
    BAR()
    v.tensor_scalar_add(junk[:, :], junk[:, :], 1.0)
    BAR()
    v.tensor_mul(junk[:, :], junk[:, :], mskF[:, :])
    BAR()
    v.tensor_copy(Vb[:, :], Vb2[:, :])
    v.tensor_copy(Db2[:, :], junk[:, :])
    BAR()
    v.tensor_copy(Db[:, :], Db2[:, :])
    BAR()
    CB, _ = butterfly(Vb, Vb2, Db, Db2)
    v.tensor_copy(Db2[:, 0:C2], CB[:, 0:C2])
    BAR()
    v.tensor_single_scalar(srcI[:, :], Db2[:, 0:C2], 11.0, op=AL.logical_shift_right)
    v.tensor_single_scalar(dstI[:, :], Db2[:, 0:C2], 2047.0, op=AL.bitwise_and)
    v.tensor_scalar(msI[:, 0:C2], iocf[:, 0:C2], cnt2[:, 0:1], None, op0=AL.is_lt)
    v.tensor_copy(lnt[:, 0, 0:384], zeros2[:, 0:384])
    v.tensor_single_scalar(DmS[:, 0:C2], srcI[:, :], 15.0, op=AL.bitwise_and)
    v.tensor_single_scalar(DmS[:, C2:2 * C2], srcI[:, :], 4.0,
                           op=AL.logical_shift_right)
    v.tensor_copy(lnt[:, 0, 0:384], zeros2[:, 0:384])
    v.tensor_single_scalar(DmS[:, 2 * C2:3 * C2], DmS[:, 0:C2], 2.0,
                           op=AL.logical_shift_right)
    v.tensor_single_scalar(DmS[:, 0:C2], DmS[:, 0:C2], 3.0, op=AL.bitwise_and)
    v.tensor_copy(lnt[:, 0, 0:384], zeros2[:, 0:384])
    v.tensor_single_scalar(DmS[:, 2 * C2:3 * C2], DmS[:, 2 * C2:3 * C2], 12.0,
                           op=AL.logical_shift_left)
    v.tensor_single_scalar(DmS[:, 0:C2], DmS[:, 0:C2], 7.0,
                           op=AL.logical_shift_left)
    v.tensor_copy(lnt[:, 0, 0:384], zeros2[:, 0:384])
    v.tensor_tensor(srcI[:, :], DmS[:, C2:2 * C2], DmS[:, 0:C2],
                    op=AL.bitwise_or)
    v.tensor_copy(lnt[:, 0, 0:384], zeros2[:, 0:384])
    v.tensor_tensor(srcI[:, :], srcI[:, :], DmS[:, 2 * C2:3 * C2],
                    op=AL.bitwise_or)
    v.tensor_copy(lnt[:, 0, 0:384], zeros2[:, 0:384])
    v.tensor_scalar(srcI[:, :], srcI[:, :], eab[:, 0:1], None, op0=AL.bitwise_or)
    v.tensor_copy(lnt[:, 0, 0:384], zeros2[:, 0:384])
    BAR()
    if dbg == "bwd":
        v.tensor_copy(lnt[:, 0, 67:99], msI[:, 0:C2])
        v.tensor_copy(lnt[:, 0, 99:131], srcI[:, :])
        v.tensor_copy(lnt[:, 0, 131:163], dstI[:, :])
        BAR()
    v.select(DmS[:, 0:C2], msI[:, 0:C2], srcI[:, :], sentA[:, :])
    v.select(DmS[:, C2:2 * C2], msI[:, 0:C2], dstI[:, :], sentB[:, :])
    BAR()
    v.tensor_copy(srcI[:, :], DmS[:, 0:C2])
    v.tensor_copy(dstI[:, :], DmS[:, C2:2 * C2])
    g.wait_ge(qc, cc[0])
    BAR()
    if dbg == "bwd":
        v.tensor_copy(lnt[:, 0, 0:32], srcI[:, :])
        v.tensor_copy(lnt[:, 0, 32:64], dstI[:, :])
        v.tensor_copy(lnt[:, 0, 64:65], cnt2[:, :])
        v.tensor_copy(lnt[:, 0, 65:66], obase[:, :])
        v.tensor_copy(lnt[:, 0, 66:67], cnts[:, :])
        v.memset(lnt[:, 0, 163:1024], 0.0)
        BAR()
        for m in range(16):
            sdma(out.rearrange("(m p) c -> p m c", p=128)[:, m:m + 1, :],
                 lnt[:, 0:1, :])
        swait()
        BAR()
        return nc
    sdma(di_d.ap()[:, :], dstI[:, :])
    swait()
    BAR()
    sdma(dstE[:, :, :], di_d.ap().rearrange("(e r) gi -> r e gi", e=8))
    swait()
    BAR()
    for gi in range(C2):
        g.indirect_dma_start(
            out=Gbig[:, gi, :], out_offset=None,
            in_=y_all.ap(),
            in_offset=IndirectOffsetOnAxis(srcI[:, gi:gi + 1], axis=0),
        ).then_inc(qg, 16)
        cg[0] += 16
    gwait()
    BAR()
    for e in range(8):
        for gi in range(C2):
            g.indirect_dma_start(
                out=acc_d.ap(),
                out_offset=IndirectOffsetOnAxis(dstE[0:16, e, gi:gi + 1], axis=0),
                in_=Gbig[16 * e:16 * e + 16, gi, :], in_offset=None,
                compute_op=AL.add,
            ).then_inc(qh, 16)
            ch[0] += 16
        g.wait_ge(qh, ch[0])
    BAR()

    ln2_base = (eVX.n, None, eSV.n, eVD.n)
    acct_cnt = []
    x1g_cnt = []
    for m in range(16):
        b = m % 2
        if m >= 2:
            sy.wait_ge(eVX.sem, ln2_base[0] + m - 1)
            g.wait_ge(eVX.sem, ln2_base[0] + m - 1)
        sdma(acct[:, b:b + 1, :],
             acc_d.ap()[0:T, :].rearrange("(m p) c -> p m c", p=128)[:, m:m + 1, :])
        acct_cnt.append(cs[0])
        gdma(x1s[:, b:b + 1, :],
             ag_x_in.ap().rearrange("(m p) c -> p m c", p=128)[:, m:m + 1, :])
        x1g_cnt.append(cg[0])
        v.wait_ge(qs, acct_cnt[m])
        v.wait_ge(qg, x1g_cnt[m])
        r = v.scalar_tensor_tensor(out=lnt[:, b, :], in0=x1s[:, b, :], scalar=1.0,
                                   in1=acct[:, b, :], op0=AL.mult, op1=AL.add,
                                   accum_out=st1[:, m:m + 1])
        eVX.inc(r)
        v.scalar_tensor_tensor(out=lnsq[:, :], in0=lnt[:, b, :], scalar=1.0,
                               in1=lnt[:, b, :], op0=AL.mult, op1=AL.mult,
                               accum_out=st2[:, m:m + 1])
        if m >= 2:
            eSV.wait(v, ln2_base[2] + 2 * (m - 1))
        r = v.tensor_scalar_mul(st1[:, m:m + 1], st1[:, m:m + 1], 1.0 / C)
        eVS.inc(r)
        eVS.wait(sc)
        r = sc.activation(out=stt[:, 4 + b:5 + b], in_=st1[:, m:m + 1],
                          func=AF.Square)
        eSV.inc(r)
        eSV.wait(v)
        r = v.tensor_scalar(stt[:, b:b + 1], st2[:, m:m + 1], 1.0 / C,
                            stt[:, 4 + b:5 + b], op0=AL.mult, op1=AL.subtract)
        eVS.inc(r)
        eVS.wait(sc)
        r = sc.activation(out=stt[:, 2 + b:3 + b], in_=stt[:, b:b + 1],
                          func=AF.Sqrt, bias=epsv[:, 0:1])
        eSV.inc(r)
        eSV.wait(v)
        v.reciprocal(stt[:, 6 + b:7 + b], stt[:, 2 + b:3 + b])
        v.tensor_copy(lnsq[:, 0:384], zeros2[:, 0:384])
        if m >= 2:
            v.wait_ge(qo, 16 * (m - 1))
        r = v.tensor_scalar(outt[:, b, :], lnt[:, b, :], st1[:, m:m + 1],
                            stt[:, 6 + b:7 + b], op0=AL.subtract, op1=AL.mult)
        eVD.inc(r)
        eVD.wait(sy)
        sy.dma_start(out=out.rearrange("(m p) c -> p m c", p=128)[:, m:m + 1, :],
                     in_=outt[:, b:b + 1, :]).then_inc(qo, 16)
        co[0] += 16
    sy.wait_ge(qo, co[0])
    BAR()
    return nc


def _host_prep(inputs):
    x = np.asarray(inputs["L0_attn_shot_embeddings"], np.float32).reshape(16384, C)
    Wd = np.ascontiguousarray(np.asarray(inputs["L0_attn_Wd"], np.float32))
    Wi = np.ascontiguousarray(np.asarray(inputs["L0_attn_Wi"], np.float32))
    Wo = np.ascontiguousarray(np.asarray(inputs["L0_attn_Wo"], np.float32))
    Ws = np.ascontiguousarray(np.asarray(inputs["L0_attn_Ws"], np.float32))
    bd = np.ascontiguousarray(np.asarray(inputs["L0_attn_bd"], np.float32))
    bi = np.ascontiguousarray(np.asarray(inputs["L0_attn_bi"], np.float32))
    alog = np.ascontiguousarray(np.asarray(inputs["L0_attn_A_log"], np.float32))
    bo = np.ascontiguousarray(np.asarray(inputs["L0_attn_bo"], np.float32))
    bs = np.ascontiguousarray(np.asarray(inputs["L0_attn_bs"], np.float32))
    Wg = np.asarray(inputs["L0_ffn_Wg"], np.float32)
    W1 = np.asarray(inputs["L0_ffn_W1"], np.float32)
    W2 = np.asarray(inputs["L0_ffn_W2"], np.float32)
    W3 = np.asarray(inputs["L0_ffn_W3"], np.float32)

    ext = np.zeros((8, 128, 144), np.float32)
    for e in range(8):
        for j in range(8):
            ext[j, :, 16 + 16 * e] = Wg[e, 128 * j:128 * (j + 1)]

    maps = []
    for k in range(NCORE):
        xk = np.ascontiguousarray(x[2048 * k:2048 * (k + 1)])
        kv = np.zeros((128, 2), np.float32)
        kv[:, 0] = 2048.0 * k
        kv[:, 1] = 2048.0 * (k + 1)
        eabv = np.zeros((128, 1), np.int32)
        for p in range(128):
            eabv[p, 0] = 512 * (p // 16)
        ohk16v = np.zeros((1, 128), np.float32)
        ohk16v[0, 16 * k:16 * (k + 1)] = 1.0
        row16v = np.arange(16 * k, 16 * k + 16, dtype=np.int32).reshape(1, 16)
        rot = np.array([[(k + e) % 8 for e in range(8)]], np.int32)
        ohpv = np.zeros((128, 8), np.float32)
        pos = k % 4
        ohpv[:, 2 * pos] = 1.0
        ohpv[:, 2 * pos + 1] = 1.0
        ohkv = np.zeros((128, 8), np.float32)
        ohkv[:, k] = 1.0
        maps.append({
            "x_tok": xk,
            "x_feat": np.ascontiguousarray(xk.T),
            "wd": Wd, "wi": Wi, "wo": Wo, "ws": Ws,
            "bdd": bd, "bii": bi, "alog": alog, "bo": bo, "bss": bs,
            "ext": ext,
            "wgt": np.ascontiguousarray(Wg.T),
            "w1t": np.ascontiguousarray(W1[k].T),
            "w3t": np.ascontiguousarray(W3[k].T),
            "w2t": np.ascontiguousarray(W2[k].T),
            "kvi": kv, "eabi": eabv, "rot": rot, "ohpi": ohpv, "ohki": ohkv,
            "ohk16": ohk16v, "row16": row16v,
        })
    return maps


def kernel(**inputs):
    maps = _host_prep(inputs)
    nc = build(dbg=DBG_STAGE)
    res = run_bass_kernel_spmd(nc, maps, core_ids=list(range(NCORE)))
    outs = np.concatenate([np.asarray(r["out"]) for r in res.results], axis=0)
    return outs.reshape(2, 8192, 1024).astype(np.float32)


# version 06
# speedup vs baseline: 1.2656x; 1.0223x over previous
import numpy as np
import ml_dtypes
import concourse.bass as bass
import concourse.mybir as mybir
from concourse.bass import IndirectOffsetOnAxis
from concourse.bass_utils import run_bass_kernel_spmd

F32 = mybir.dt.float32
BF16 = mybir.dt.bfloat16
I32 = mybir.dt.int32
AX = mybir.AxisListType.X
AF = mybir.ActivationFunctionType
AL = mybir.AluOpType

NCORE = 8
T = 2048
C = 1024
NSD = 256
HF = 2048
CAP = 2048
C2 = 32
EPS = 1e-5
NBIS = 30

DBG_STAGE = None


def build(dbg=None):
    nc = bass.Bass("TRN2", target_bir_lowering=False)

    def din(name, shape, dt=F32):
        return nc.dram_tensor(name, shape, dt, kind="ExternalInput").ap()

    x_tok = din("x_tok", [T, C])
    x_feat = din("x_feat", [C, T])
    wd_in = din("wd", [2, C, NSD])
    wi_in = din("wi", [2, C, NSD])
    wo_in = din("wo", [2, NSD, C])
    ws_in = din("ws", [2, NSD, C])
    bd_in = din("bdd", [2, NSD])
    bi_in = din("bii", [2, NSD])
    alog_in = din("alog", [2, NSD])
    bo_in = din("bo", [2, C])
    bs_in = din("bss", [2, C])
    ext_in = din("ext", [8, 128, 144])
    wgt_in = din("wgt", [C, 8])
    w1t_in = din("w1t", [C, HF])
    w3t_in = din("w3t", [C, HF])
    w2t_in = din("w2t", [HF, C])
    kv_in = din("kvi", [128, 2])
    eab_in = din("eabi", [128, 1], I32)
    rot_in = din("rot", [1, 8], I32)
    ohp_in = din("ohpi", [128, 8])
    ohk_in = din("ohki", [128, 8])
    ohk16_in = din("ohk16", [1, 128])
    row16_in = din("row16", [1, 16], I32)

    out = nc.dram_tensor("out", [T, C], F32, kind="ExternalOutput").ap()

    def dint(name, shape, dt):
        return nc.dram_tensor(name, shape, dt, kind="Internal")

    ag_s_in = [dint(f"ag_s_in{l}", [4, 128], F32) for l in range(2)]
    ag_s_out = [dint(f"ag_s_out{l}", [16, 128], F32) for l in range(2)]
    ag_t_in = dint("ag_t_in", [128, 128], F32)
    ag_t_out = dint("ag_t_out", [1024, 128], F32)
    ag_x_in = dint("ag_x_in", [T, C], BF16)
    ag_x_out = dint("ag_x_out", [NCORE * T, C], BF16)
    ag_y_in = dint("ag_y_in", [CAP, C], BF16)
    y_all = dint("y_all", [NCORE * CAP + 1, C], BF16)
    l_list = dint("l_list", [1, CAP + 1152], I32)
    acc_d = dint("acc_d", [T + 1, C], BF16)
    orow_d = dint("orow_d", [1, 128], F32)
    vc_d = dint("vc_d", [128, 1024], I32)
    sc_d = dint("sc_d", [128, 1024], F32)
    s_list = dint("s_list", [1, CAP + 1152], F32)
    di_d = dint("di_d", [128, C2], I32)

    sbt = nc.alloc_sbuf_tensor
    sat = nc.alloc_sbuf_tensor_at

    def addr(h):
        return nc.lookup_mloc(h).addr

    bT = sbt("bT", [128, 8, T], BF16)
    wdv = sbt("wdv", [128, 2, 8, NSD], BF16)
    wiv = sbt("wiv", [128, 2, 8, NSD], BF16)
    wov = sbt("wov", [128, 2, 2, C], BF16)
    wsv = sbt("wsv", [128, 2, 2, C], BF16)
    sA = sbt("sA", [128, 2, T], F32)
    sB = sbt("sB", [128, 2, T], F32)
    sC = sbt("sC", [128, 2, T], F32)
    sD = sbt("sD", [128, 2, T], F32)
    A_bt = addr(bT)
    A_w = addr(wdv)
    A_s = addr(sA)

    w2v = sbt("w2v", [128, 16, C], BF16)
    zeros2 = sbt("zeros2", [128, 2048], F32)
    lnt = sbt("lnt", [128, 2, C], F32)
    bbvb = sbt("bbvb", [1, 2, C], BF16)
    ct2 = sbt("ct2", [128, C], F32)
    x1s = sbt("x1s", [128, 2, C], BF16)
    rowT = sbt("rowT", [128, 16], I32)
    colT = sbt("colT", [128, 16], F32)
    idT = sbt("idT", [128, 128], F32)
    extv = sbt("extv", [128, 8, 144], BF16)
    wgv = sbt("wgv", [128, 8, 8], BF16)
    bd16 = sbt("bd16", [128, 128], F32)
    iocf = sbt("iocf", [128, 1024], F32)
    mskF = sbt("mskF", [128, 1024], F32)
    msI = sbt("msI", [128, 1024], I32)
    bdv = sbt("bdv", [128, 2, 2], F32)
    biv = sbt("biv", [128, 2, 2], F32)
    spav = sbt("spav", [128, 2, 2], F32)
    kv0 = sbt("kv0", [128, 2], F32)
    eab = sbt("eab", [128, 1], I32)
    ohp = sbt("ohp", [128, 8], F32)
    ohk = sbt("ohk", [128, 8], F32)
    exsb = sbt("exsb", [16, 128], F32)
    exch = sbt("exch", [128, 16], F32)
    sm4 = sbt("sm4", [128, 4], F32)
    hins = sbt("hins", [128, 12], F32)
    stb = sbt("stb", [128, 2, 128], BF16)
    bis = sbt("bis", [128, 8], F32)
    bisI = sbt("bisI", [128, 1], I32)
    onesf = sbt("onesf", [1, 128], F32)
    ones1b = sbt("ones1b", [1, 128], BF16)
    cnts = sbt("cnts", [128, 1], F32)
    orow = sbt("orow", [1, 128], F32)
    orow8 = sbt("orow8", [8, 16], F32)
    obase = sbt("obase", [128, 1], F32)
    oint = sbt("oint", [1, 16], I32)
    FL = sbt("FL", [128, 16], I32)
    wraw = sbt("wraw", [128, 16], F32)
    wnrm = sbt("wnrm", [128, 16], F32)
    srcI = sbt("srcI", [128, C2], I32)
    dstI = sbt("dstI", [128, C2], I32)
    sentA = sbt("sentA", [128, C2], I32)
    sentB = sbt("sentB", [128, C2], I32)
    cnt2 = sbt("cnt2", [128, 1], F32)
    st1 = sbt("st1", [128, 16], F32)
    st2 = sbt("st2", [128, 16], F32)
    stt = sbt("stt", [128, 8], F32)
    scr8 = sbt("scr8", [128, 8], F32)
    epsv = sbt("epsv", [128, 1], F32)
    onesc = sbt("onesc", [128, 1], F32)
    iop = sbt("iop", [128, 1], I32)
    iopf = sbt("iopf", [128, 1], F32)
    ctTb = sbt("ctTb", [128, 8], BF16)

    KB = 1024
    lnsq = sat("lnsq", [128, C], F32, offset=A_s + 32 * KB)
    Ssc = sat("Ssc", [128, 1024], F32, offset=A_bt + 0)
    vid = sat("vid", [128, 1024], F32, offset=A_bt + 4 * KB)
    junk = sat("junk", [128, 1024], F32, offset=A_bt + 8 * KB)
    Vb = sat("Vb", [128, 1024], F32, offset=A_bt + 12 * KB)
    Vb2 = sat("Vb2", [128, 1024], F32, offset=A_bt + 16 * KB)
    Db = sat("Db", [128, 1024], I32, offset=A_bt + 20 * KB)
    Db2 = sat("Db2", [128, 1024], I32, offset=A_bt + 24 * KB)
    DmS = sat("DmS", [128, 1024], I32, offset=A_bt + 28 * KB)
    yst = sat("yst", [128, 2, C], BF16, offset=A_bt + 0)
    scr512 = sat("scr512", [128, 512], F32, offset=A_bt + 4 * KB)
    scrG = sat("scrG", [128, 512], F32, offset=A_w + 0)
    cc16 = sat("cc16", [16, 1024], I32, offset=A_w + 4 * KB)
    Sb = sat("Sb", [128, 1024], F32, offset=A_w + 8 * KB)
    Sb2 = sat("Sb2", [128, 1024], F32, offset=A_w + 12 * KB)
    cs16 = sat("cs16", [16, 1024], F32, offset=A_w + 16 * KB)
    dstE = sat("dstE", [16, 8, C2], I32, offset=A_w + 4 * KB)
    Gbig = sat("Gbig", [128, C2, C], BF16, offset=A_s + 0)
    Vc = Db
    xgt = sat("xgt", [128, 4, 8, 128], BF16, offset=A_w + 0)
    xg4 = sat("xg4", [128, 4, C], BF16, offset=A_w + 8 * KB)
    xg4b = sat("xg4b", [128, 4, C], BF16, offset=A_bt + 24 * KB)
    hpr = sat("hpr", [128, 16, 512], BF16, offset=A_w + 16 * KB)
    zrowb = sat("zrowb", [128, C], BF16, offset=A_s + 56 * KB)
    ohk16 = sat("ohk16s", [1, 128], F32, offset=addr(stb))
    row16 = sat("row16s", [1, 16], I32, offset=addr(exch))
    hsb = sat("hsb", [128, 2, T], BF16, offset=A_s + 0)
    xpt = sat("xpt", [128, 2, C], F32, offset=A_s + 8 * KB)
    w1v = sat("w1v", [128, 8, HF], BF16, offset=A_s + 0)
    w3v = sat("w3v", [128, 8, HF], BF16, offset=A_s + 32 * KB)
    Gb = sat("Gb", [128, 2, C], BF16, offset=A_s + 0)
    acct = sat("acct", [128, 2, C], BF16, offset=A_s + 8 * KB)
    outt = sat("outt", [128, 2, C], F32, offset=A_s + 16 * KB)

    pjA = nc.alloc_psum_tensor("pjA", [128, 512], F32)
    pjB = nc.alloc_psum_tensor("pjB", [128, 512], F32)
    pyA = nc.alloc_psum_tensor("pyA", [128, 1024], F32)
    pyB = nc.alloc_psum_tensor("pyB", [128, 1024], F32)
    pX = nc.alloc_psum_tensor("pX", [128, 128], F32)
    pSm = nc.alloc_psum_tensor("pSm", [128, 2], F32)

    g = nc.gpsimd
    sy = nc.sync
    v = nc.vector
    sc = nc.scalar
    te = nc.tensor
    BAR = nc.all_engine_barrier

    qg = nc.alloc_semaphore("qg")
    qs = nc.alloc_semaphore("qs")
    qc = nc.alloc_semaphore("qc")
    qh = nc.alloc_semaphore("qh")
    qh2 = nc.alloc_semaphore("qh2")
    qo = nc.alloc_semaphore("qo")
    cg = [0]; cs = [0]; cc = [0]; ch = [0]; co = [0]; ch2 = [0]

    def gdma(dst, src, **kw):
        r = g.dma_start(out=dst, in_=src, **kw).then_inc(qg, 16)
        cg[0] += 16
        return r

    def gwait():
        g.wait_ge(qg, cg[0])

    def sdma(dst, src, **kw):
        r = sy.dma_start(out=dst, in_=src, **kw).then_inc(qs, 16)
        cs[0] += 16
        return r

    def swait():
        sy.wait_ge(qs, cs[0])

    def coll(groups, ins, outs):
        g.collective_compute("AllGather", AL.bypass, replica_groups=groups,
                             ins=ins, outs=outs).then_inc(qc, 1)
        cc[0] += 1

    def mm(out_ps, lhsT, rhs, start, stop, width):
        nh = (width + 511) // 512
        r = None
        for h in range(nh):
            w0 = h * 512
            w1 = min(width, w0 + 512)
            r = te.matmul(out_ps[:, w0:w1], lhsT, rhs[:, w0:w1], start=start, stop=stop)
        return r

    class Edge:
        __slots__ = ("sem", "n")
        def __init__(self, name):
            self.sem = nc.alloc_semaphore(name)
            self.n = 0
        def inc(self, r):
            r.then_inc(self.sem, 1)
            self.n += 1
            return self.n
        def wait(self, eng, n=None):
            eng.wait_ge(self.sem, self.n if n is None else n)

    eTV = Edge("eTV")
    eVT = Edge("eVT")
    eVS = Edge("eVS")
    eSV = Edge("eSV")
    eVX = Edge("eVX")
    eVD = Edge("eVD")
    eVG = Edge("eVG")
    eTS = Edge("eTS")

    GRP8 = [list(range(8))]
    GRP4 = [[0, 1, 2, 3], [4, 5, 6, 7]]

    gdma(wdv[:, 0:1, :, :], wd_in.rearrange("l (j p) n -> p l j n", p=128)[:, 0:1, :, :])
    gdma(wiv[:, 0:1, :, :], wi_in.rearrange("l (j p) n -> p l j n", p=128)[:, 0:1, :, :])
    c_wdwi = cg[0]
    bt_cnt = []
    for j in range(8):
        gdma(bT[:, j, :],
             x_feat.rearrange("(j p) t -> p j t", p=128)[:, j, :])
        bt_cnt.append(cg[0])
    gdma(wdv[:, 1:2, :, :], wd_in.rearrange("l (j p) n -> p l j n", p=128)[:, 1:2, :, :])
    gdma(wiv[:, 1:2, :, :], wi_in.rearrange("l (j p) n -> p l j n", p=128)[:, 1:2, :, :])
    c_wdwi2 = cg[0]
    gdma(wov[:, :, :, :], wo_in.rearrange("l (a p) c -> p l a c", p=128))
    c_wov = cg[0]
    gdma(wsv[:, :, :, :], ws_in.rearrange("l (a p) c -> p l a c", p=128))
    c_wsv = cg[0]
    gdma(extv[:, :, :], ext_in.rearrange("j p f -> p j f"))
    c_ext = cg[0]
    gdma(wgv[:, :, :], wgt_in.rearrange("(j p) e -> p j e", p=128))
    gdma(w2v[:, :, :], w2t_in.rearrange("(j p) c -> p j c", p=128))
    c_w2v = cg[0]
    sdma(bdv[:, :, :], bd_in.rearrange("l (a p) -> p l a", p=128), allow_slow_non_contiguous=True)
    sdma(biv[:, :, :], bi_in.rearrange("l (a p) -> p l a", p=128), allow_slow_non_contiguous=True)
    sdma(spav[:, :, :], alog_in.rearrange("l (a p) -> p l a", p=128), allow_slow_non_contiguous=True)
    sdma(kv0[:, :], kv_in[:, :])
    sdma(eab[:, :], eab_in[:, :])
    sdma(ohp[:, :], ohp_in[:, :])
    sdma(ohk[:, :], ohk_in[:, :])
    sdma(lnt[0:1, 0, :], bo_in[0:1, :])
    sdma(lnt[0:1, 1, :], bs_in[0:1, :])
    sdma(ct2[0:1, 0:C], bo_in[1:2, :])
    sdma(sA[0:1, 0, 0:C], bs_in[1:2, :])
    g.iota(iop[:, :], pattern=[[0, 1]], base=0, channel_multiplier=1)
    g.iota(iocf[:, :], pattern=[[1, 1024]], base=0, channel_multiplier=0,
           allow_small_or_imprecise_dtypes=True)
    swait()
    BAR()
    v.tensor_add(bbvb[0:1, 0, :], lnt[0:1, 0, :], lnt[0:1, 1, :])
    v.tensor_add(bbvb[0:1, 1, :], ct2[0:1, 0:C], sA[0:1, 0, 0:C])
    v.memset(zeros2[:, :], 0.0)
    v.memset(zrowb[:, :], 0.0)
    v.memset(onesf[:, :], 1.0)
    v.memset(ones1b[:, :], 1.0)
    v.memset(sentA[:, :], NCORE * CAP)
    v.memset(sentB[:, :], T)
    v.memset(epsv[:, :], float(EPS))
    v.memset(onesc[:, :], 1.0)
    v.tensor_copy(iopf[:, :], iop[:, :])
    sc.activation(out=stt[:, 4:8], in_=spav[:, :, :].rearrange("p l a -> p (l a)"),
                  func=AF.Abs)
    BAR()
    sc.activation(out=stt[:, 4:8], in_=stt[:, 4:8], func=AF.Exp, scale=-1.0)
    BAR()
    sc.activation(out=stt[:, 4:8], in_=stt[:, 4:8], func=AF.Ln, bias=onesc[:, 0:1])
    BAR()
    v.tensor_scalar_max(st2[:, 4:8], spav[:, :, :].rearrange("p l a -> p (l a)"), 0.0)
    BAR()
    v.tensor_add(st2[:, 4:8], st2[:, 4:8], stt[:, 4:8])
    BAR()
    v.tensor_scalar_mul(spav[:, :, :].rearrange("p l a -> p (l a)"), st2[:, 4:8], -1.0)
    v.tensor_scalar(idT[:, :], iocf[:, 0:128], iopf[:, 0:1], None, op0=AL.is_equal)
    v.tensor_single_scalar(bisI[:, :], iop[:, :], 15.0, op=AL.bitwise_and)
    v.tensor_single_scalar(msI[:, 0:1], iop[:, :], 4.0, op=AL.logical_shift_right)
    g.iota(bd16[:, :], pattern=[[1, 8], [0, 16]], base=0, channel_multiplier=0,
           allow_small_or_imprecise_dtypes=True)
    BAR()
    v.tensor_copy(stt[:, 1:2], msI[:, 0:1])
    BAR()
    v.tensor_scalar(bd16[:, :], bd16[:, :], stt[:, 1:2], None, op0=AL.is_equal)
    for m in range(16):
        sdma(acc_d.ap()[0:T, :].rearrange("(m p) c -> p m c", p=128)[:, m:m + 1, :],
             zrowb[:, :].rearrange("p (o c) -> p o c", o=1))
    sdma(acc_d.ap()[T:T + 1, :], zrowb[0:1, :])
    sdma(y_all.ap()[NCORE * CAP:NCORE * CAP + 1, :], zrowb[0:1, :])
    swait()
    BAR()

    def ssm_layer(l, rhsbuf):
        evac_cnt = []
        te.wait_ge(qg, c_wdwi if l == 0 else c_wdwi2)
        for it in range(8):
            tsl, a = it % 4, it // 4
            t0 = tsl * 512
            pA = pjA[:, :] if it % 2 == 0 else pyA[:, 0:512]
            pB = pjB[:, :] if it % 2 == 0 else pyA[:, 512:1024]
            if it >= 2:
                te.wait_ge(eVT.sem, evac_cnt[it - 2])
            r = None
            for j in range(8):
                if l == 0 and it == 0:
                    te.wait_ge(qg, bt_cnt[j])
                r = te.matmul(pA, wdv[:, l, j, a * 128:(a + 1) * 128],
                              rhsbuf[:, j, t0:t0 + 512], start=(j == 0), stop=(j == 7))
            for j in range(8):
                r = te.matmul(pB, wiv[:, l, j, a * 128:(a + 1) * 128],
                              rhsbuf[:, j, t0:t0 + 512], start=(j == 0), stop=(j == 7))
            eTV.inc(r)
            v.wait_ge(eTV.sem, eTV.n)
            v.tensor_scalar(sA[:, a, t0:t0 + 512], pA,
                            bdv[:, l, a:a + 1], None, op0=AL.add)
            r = v.tensor_scalar(sB[:, a, t0:t0 + 512], pB,
                                biv[:, l, a:a + 1], None, op0=AL.add)
            eVT.inc(r)
            evac_cnt.append(eVT.n)
        if build.dbg_layer == (l, "raw"):
            for a in range(2):
                for h in range(2):
                    sdma(build.dbg_out.rearrange("(z p) c -> p z c", p=128)[:, 2 * a + h:2 * a + h + 1, :],
                         sA[:, a, h * 1024:(h + 1) * 1024].rearrange("p (o c) -> p o c", o=1))
                    sdma(build.dbg_out.rearrange("(z p) c -> p z c", p=128)[:, 8 + 2 * a + h:8 + 2 * a + h + 1, :],
                         sB[:, a, h * 1024:(h + 1) * 1024].rearrange("p (o c) -> p o c", o=1))
            swait()
            BAR()
            build.dbg_early[0] = True
        if build.dbg_early[0]:
            return
        for a in range(2):
            sc.wait_ge(eVT.sem, evac_cnt[3 + 4 * a])
            r = sc.activation(out=sC[:, a, :], in_=sA[:, a, :], func=AF.Abs)
            eSV.inc(r)
            sc.activation(out=sC[:, a, :], in_=sC[:, a, :], func=AF.Exp, scale=-1.0)
            r = sc.activation(out=sC[:, a, :], in_=sC[:, a, :], func=AF.Ln,
                              bias=onesc[:, 0:1])
            eSV.inc(r)
        sv_sp = eSV.n
        for a in range(2):
            eSV.wait(v, sv_sp - 3 + 2 * a)
            v.tensor_scalar_max(sA[:, a, :], sA[:, a, :], 0.0)
            eSV.wait(v, sv_sp - 2 + 2 * a)
            v.tensor_add(sA[:, a, :], sA[:, a, :], sC[:, a, :])
            v.tensor_tensor_scan(out=sC[:, a, :], data0=sA[:, a, :],
                                 data1=zeros2[:, :], initial=0.0,
                                 op0=AL.add, op1=AL.add)
            r = v.tensor_mul(sB[:, a, :], sA[:, a, :], sB[:, a, :])
            eVS.inc(r)
            eVS.wait(sc)
            sc.activation(out=sA[:, a, :], in_=sA[:, a, :], func=AF.Exp,
                          scale=spav[:, l, a:a + 1])
            sc.activation(out=sm4[:, a:a + 1], in_=sC[:, a, T - 1:T], func=AF.Exp,
                          scale=spav[:, l, a:a + 1])
            r = sc.activation(out=sC[:, a, :], in_=sC[:, a, :], func=AF.Exp,
                              scale=spav[:, l, a:a + 1])
            eSV.inc(r)
            eSV.wait(v)
            r = v.tensor_tensor_scan(out=sD[:, a, :], data0=sA[:, a, :],
                                     data1=sB[:, a, :], initial=0.0,
                                     op0=AL.mult, op1=AL.add)
        v.tensor_copy(sm4[:, 2:3], sD[:, 0, T - 1:T])
        r = v.tensor_copy(sm4[:, 3:4], sD[:, 1, T - 1:T])
        eVT.inc(r)
        te.wait_ge(eVT.sem, eVT.n)
        te.wait_ge(eSV.sem, eSV.n)
        r = te.transpose(pX[0:4, :], sm4[:, 0:4], idT[:, :])
        eTV.inc(r)
        eTV.wait(v)
        r = v.tensor_copy(exsb[0:4, :], pX[0:4, :])
        eVD.inc(r)
        eVD.wait(sy)
        sdma(ag_s_in[l].ap()[:, :], exsb[0:4, :])
        g.wait_ge(qs, cs[0])
        coll(GRP4, [ag_s_in[l].ap().opt()], [ag_s_out[l].ap().opt()])
        g.wait_ge(qc, cc[0])
        gdma(exsb[:, :], ag_s_out[l].ap()[:, :])
        te.wait_ge(qg, cg[0])
        r = te.transpose(pX[:, 0:16], exsb[0:16, :], idT[0:16, 0:16])
        eTV.inc(r)
        eTV.wait(v)
        v.tensor_copy(exch[:, 0:16], pX[:, 0:16])
        v.memset(hins[:, 0:2], 0.0)
        v.tensor_copy(lnt[:, 0, 0:384], zeros2[:, 0:384])
        for m in range(1, 5):
            for cch in range(2):
                pcol = 4 * (m - 1) + cch
                hcol = 4 * (m - 1) + 2 + cch
                v.tensor_mul(stt[:, cch:cch + 1], exch[:, pcol:pcol + 1],
                             hins[:, 2 * (m - 1) + cch:2 * (m - 1) + cch + 1])
            v.tensor_copy(lnt[:, 0, 0:384], zeros2[:, 0:384])
            for cch in range(2):
                hcol = 4 * (m - 1) + 2 + cch
                v.tensor_add(hins[:, 2 * m + cch:2 * m + cch + 1],
                             stt[:, cch:cch + 1], exch[:, hcol:hcol + 1])
            v.tensor_copy(lnt[:, 0, 0:384], zeros2[:, 0:384])
        v.tensor_mul(stt[:, 0:8], hins[:, 0:8], ohp[:, 0:8])
        v.tensor_copy(lnt[:, 0, 0:384], zeros2[:, 0:384])
        v.reduce_sum(hins[:, 10:12], stt[:, 0:8].rearrange("p (m c) -> p c m", c=2),
                     axis=AX)
        v.tensor_copy(lnt[:, 0, 0:384], zeros2[:, 0:384])
        for a in range(2):
            v.scalar_tensor_tensor(out=sD[:, a, :], in0=sC[:, a, :],
                                   scalar=hins[:, 10 + a:11 + a], in1=sD[:, a, :],
                                   op0=AL.mult, op1=AL.add)
        if build.dbg_layer == (l, "exch"):
            v.tensor_copy(lnt[:, 0, 0:16], exch[:, 0:16])
            v.tensor_copy(lnt[:, 0, 16:28], hins[:, 0:12])
            v.tensor_copy(lnt[:, 0, 28:32], sm4[:, 0:4])
            v.memset(lnt[:, 0, 32:1024], 0.0)
            BAR()
            for mm2 in range(16):
                sdma(out.rearrange("(m p) c -> p m c", p=128)[:, mm2:mm2 + 1, :],
                     lnt[:, 0:1, :])
            swait()
            BAR()
            build.dbg_early[0] = True
            return
        v.tensor_copy(hsb[:, 0, :], sD[:, 0, :])
        v.tensor_copy(hsb[:, 1, :], sD[:, 1, :])
        for a in range(2):
            r = v.tensor_scalar(stb[:, a, :], zeros2[:, 0:128], hins[:, 8 + a:9 + a],
                                None, op0=AL.add)
        eVT.inc(r)
        te.wait_ge(eVT.sem, eVT.n)

    build.dbg_layer = (0, "raw") if dbg == "raw1" else (
        (0, "exch") if dbg == "exch" else (
            (1, "exch") if dbg == "exch2" else None))
    build.dbg_out = out
    build.dbg_early = [False]
    ssm_layer(0, bT)

    if build.dbg_early[0]:
        return nc

    if dbg in ("hs1", "dec1", "drv1", "cum1"):
        buf = {"hs1": sD, "dec1": sA, "drv1": sB, "cum1": sC}[dbg]
        for a in range(2):
            for h in range(2):
                sdma(out.rearrange("(z p) c -> p z c", p=128)[:, 2 * a + h:2 * a + h + 1, :],
                     buf[:, a, h * 1024:(h + 1) * 1024].rearrange("p (o c) -> p o c", o=1))
        for z in range(4, 16):
            sdma(out.rearrange("(z p) c -> p z c", p=128)[:, z:z + 1, :],
                 sD[:, 0, 0:1024].rearrange("p (o c) -> p o c", o=1))
        swait()
        BAR()
        return nc

    te.wait_ge(qg, c_wsv)
    for j in range(8):
        for a in range(2):
            te.matmul(pX[:, 0:1], wsv[:, 0, a, j * 128:(j + 1) * 128],
                      stb[:, a, 0:1], start=(a == 0), stop=False)
        te.matmul(pX[:, 0:1], bbvb[0:1, 0, j * 128:(j + 1) * 128], ones1b[0:1, 0:1],
                  start=False, stop=True)
        BAR()
        v.tensor_copy(ctTb[:, j:j + 1], pX[:, 0:1])
        BAR()
    for (wv, bv) in ((wdv, bdv), (wiv, biv)):
        for a in range(2):
            for j in range(8):
                te.matmul(pX[:, 1:2], wv[:, 1, j, a * 128:(a + 1) * 128],
                          ctTb[:, j:j + 1], start=(j == 0), stop=(j == 7))
            BAR()
            v.tensor_add(bv[:, 1, a:a + 1], bv[:, 1, a:a + 1], pX[:, 1:2])
            BAR()
    te.wait_ge(qg, c_wov)
    y1_cnt = []
    for j in range(8):
        for th in range(2):
            it = 2 * j + th
            py = pyA if it % 2 == 0 else pyB
            if it >= 2:
                te.wait_ge(eVT.sem, y1_cnt[it - 2])
            r = None
            for a in range(2):
                r = mm(py, wov[:, 0, a, j * 128:(j + 1) * 128],
                       hsb[:, a, th * 1024:(th + 1) * 1024], (a == 0), (a == 1), 1024)
            eTV.inc(r)
            v.wait_ge(eTV.sem, eTV.n)
            r = v.tensor_copy(bT[:, j, th * 1024:(th + 1) * 1024], py[:, :])
            eVT.inc(r)
            y1_cnt.append(eVT.n)
    BAR()

    ssm_layer(1, bT)
    if build.dbg_early[0]:
        return nc

    for a in range(2):
        mm(pyA, stb[:, a, :], wsv[:, 1, a, :], (a == 0), False, C)
    mm(pyA, ones1b[0:1, 0:128], bbvb[0:1, 1, :], False, True, C)
    BAR()
    v.tensor_copy(ct2[:, :], pyA[:, :])
    BAR()

    bTf = bT.rearrange("p a t -> p (a t)")
    ln1_base = (eTV.n, eVT.n, eVS.n, eSV.n, eVX.n, eVD.n)
    xpt_cnt = []
    tr_cnt = []
    for m in range(16):
        py = pyA if m % 2 == 0 else pyB
        b = m % 2
        if m >= 2:
            eVT.wait(te, ln1_base[1] + m - 1)
        r = None
        for a in range(2):
            r = mm(py, hsb[:, a, m * 128:(m + 1) * 128], wov[:, 1, a, :],
                   (a == 0), (a == 1), C)
        eTV.inc(r)
        if m >= 2:
            sy.wait_ge(eVX.sem, ln1_base[4] + m - 1)
        sdma(xpt[:, b:b + 1, :],
             x_tok.rearrange("(m p) c -> p m c", p=128)[:, m:m + 1, :])
        xpt_cnt.append(cs[0])
        eTV.wait(v)
        r = v.tensor_add(lnt[:, b, :], py[:, :], ct2[:, :])
        eVT.inc(r)
        v.wait_ge(qs, xpt_cnt[m])
        r = v.scalar_tensor_tensor(out=lnt[:, b, :], in0=lnt[:, b, :], scalar=1.0,
                                   in1=xpt[:, b, :], op0=AL.mult, op1=AL.add,
                                   accum_out=st1[:, m:m + 1])
        eVX.inc(r)
        v.scalar_tensor_tensor(out=lnsq[:, :], in0=lnt[:, b, :], scalar=1.0,
                               in1=lnt[:, b, :], op0=AL.mult, op1=AL.mult,
                               accum_out=st2[:, m:m + 1])
        if m >= 2:
            eSV.wait(v, ln1_base[3] + 2 * (m - 1))
        r = v.tensor_scalar_mul(st1[:, m:m + 1], st1[:, m:m + 1], 1.0 / C)
        eVS.inc(r)
        eVS.wait(sc)
        r = sc.activation(out=stt[:, 4 + b:5 + b], in_=st1[:, m:m + 1],
                          func=AF.Square)
        eSV.inc(r)
        eSV.wait(v)
        r = v.tensor_scalar(stt[:, b:b + 1], st2[:, m:m + 1], 1.0 / C,
                            stt[:, 4 + b:5 + b], op0=AL.mult, op1=AL.subtract)
        eVS.inc(r)
        eVS.wait(sc)
        r = sc.activation(out=stt[:, 2 + b:3 + b], in_=stt[:, b:b + 1],
                          func=AF.Sqrt, bias=epsv[:, 0:1])
        eSV.inc(r)
        eSV.wait(v)
        v.reciprocal(stt[:, 6 + b:7 + b], stt[:, 2 + b:3 + b])
        v.tensor_copy(lnsq[:, 0:384], zeros2[:, 0:384])
        if m >= 2:
            v.wait_ge(qs, tr_cnt[m - 2])
        r = v.tensor_scalar(x1s[:, b, :], lnt[:, b, :], st1[:, m:m + 1],
                            stt[:, 6 + b:7 + b], op0=AL.subtract, op1=AL.mult)
        eVD.inc(r)
        eVD.wait(sy)
        sdma(ag_x_in.ap().rearrange("(m p) c -> p m c", p=128)[:, m:m + 1, :],
             x1s[:, b:b + 1, :])
        sy.dma_start_transpose(
            bTf[:, m * 1024:(m + 1) * 1024].rearrange("p (a q) -> p a q", a=8),
            x1s[:, b, :].rearrange("p (a q) -> p a q", a=8)
        ).then_inc(qs, 16)
        cs[0] += 16
        tr_cnt.append(cs[0])
    swait()
    BAR()
    gdma(w1v[:, :, :], w1t_in.rearrange("(j p) h -> p j h", p=128))
    gdma(w3v[:, :, :], w3t_in.rearrange("(j p) h -> p j h", p=128))

    if dbg == "x1":
        for m in range(16):
            b = m % 2
            gdma(x1s[:, b:b + 1, :],
                 ag_x_in.ap().rearrange("(m p) c -> p m c", p=128)[:, m:m + 1, :])
            gwait()
            BAR()
            v.tensor_copy(outt[:, 0, :], x1s[:, b, :])
            BAR()
            sdma(out.rearrange("(m p) c -> p m c", p=128)[:, m:m + 1, :],
                 outt[:, 0:1, :])
            swait()
        BAR()
        return nc

    x1Tv = bTf.rearrange("p (m a q) -> p m a q", m=16, a=8)
    te.wait_ge(qg, c_ext)
    for rt in range(16):
        for j in range(8):
            te.matmul(pX[:, :], extv[:, j, 16 - rt:144 - rt],
                      x1Tv[:, :, j, :].rearrange("p m (q r) -> p r m q", r=16)[:, rt, :, :],
                      start=(rt == 0 and j == 0), stop=(rt == 15 and j == 7))
    BAR()
    v.tensor_copy(junk[:, 0:128], pX[:, :])
    BAR()
    sdma(ag_t_in.ap()[:, :], junk[:, 0:128])
    swait()
    BAR()
    coll(GRP8, [ag_t_in.ap().opt()], [ag_t_out.ap().opt()])
    cc_agt = cc[0]
    g.wait_ge(qc, cc[0])
    coll(GRP8, [ag_x_in.ap().opt()], [ag_x_out.ap().opt()])
    sy.wait_ge(qc, cc_agt)
    sdma(Ssc[:, :].rearrange("er (kk c) -> er kk c", c=128),
         ag_t_out.ap().rearrange("(kk er) c -> er kk c", er=128))
    sdma(ohk16[:, :], ohk16_in[:, :])
    sdma(row16[:, :], row16_in[:, :])
    swait()
    BAR()
    if dbg == "bis":
        v.tensor_copy(lnt[:, 0, 233:361], Ssc[:, 0:128])
        v.reduce_max(lnt[:, 0, 361:362], Ssc[:, :], axis=AX)
        v.reduce_max(lnt[:, 0, 362:363], Ssc[:, 0:128], axis=AX)
        BAR()

    v.tensor_single_scalar(bisI[:, :], iop[:, :], 15.0, op=AL.bitwise_and)
    BAR()
    v.tensor_copy(stt[:, 0:1], bisI[:, :])
    BAR()
    v.tensor_scalar(vid[:, :], iocf[:, :], 16.0, stt[:, 0:1], op0=AL.mult, op1=AL.add)
    v.reduce_max(bis[:, 0:1], Ssc[:, :], axis=AX)
    v.tensor_reduce(bis[:, 1:2], Ssc[:, :], axis=AX, op=AL.min, negate=True)
    BAR()
    if dbg == "bis":
        v.tensor_copy(lnt[:, 0, 100:102], bis[:, 0:2])
        BAR()
    te.transpose(pX[0:2, :], bis[:, 0:2], idT[:, :])
    BAR()
    v.tensor_copy(scrG[0:2, 0:128], pX[0:2, 0:128])
    BAR()
    if dbg == "bis":
        v.tensor_copy(lnt[0:2, 0, 102:230], scrG[0:2, 0:128])
        BAR()
    v.reduce_max(scr8[0:2, 0:1], scrG[0:2, 0:128], axis=AX)
    BAR()
    te.transpose(pX[0:1, 0:2], scr8[0:2, 0:1], idT[0:2, 0:2])
    BAR()
    v.tensor_copy(stt[0:1, 0:2], pX[0:1, 0:2])
    BAR()
    if dbg == "bis":
        v.tensor_copy(lnt[0:2, 0, 230:231], scr8[0:2, 0:1])
        v.tensor_copy(lnt[0:1, 0, 231:233], stt[0:1, 0:2])
        BAR()
    te.matmul(pSm[:, 0:2], onesf[0:1, 0:128], stt[0:1, 0:2], start=True, stop=True)
    BAR()
    v.tensor_scalar_add(bis[:, 0:1], pSm[:, 0:1], 1.0)
    v.tensor_scalar(bis[:, 1:2], pSm[:, 1:2], -1.0, -1.0, op0=AL.mult, op1=AL.add)
    v.tensor_scalar_mul(bis[:, 4:5], pSm[:, 0:1], -1.0)
    v.tensor_copy(lnt[:, 0, 0:384], zeros2[:, 0:384])
    v.tensor_sub(bis[:, 3:4], bis[:, 0:1], bis[:, 1:2])
    v.tensor_copy(lnt[:, 0, 0:384], zeros2[:, 0:384])
    v.tensor_scalar_mul(bis[:, 3:4], bis[:, 3:4], 0.5)
    v.tensor_copy(lnt[:, 0, 0:384], zeros2[:, 0:384])
    if dbg == "bis":
        v.tensor_copy(lnt[:, 0, 0:8], bis[:, 0:8])
        BAR()
    for it in range(NBIS if dbg != "bis" else 1):
        v.tensor_scalar(junk[:, :], Ssc[:, :], bis[:, 1:2], bis[:, 3:4],
                        op0=AL.subtract, op1=AL.is_ge)
        r = v.reduce_sum(cnts[:, :], junk[:, :], axis=AX)
        eVT.inc(r)
        te.wait_ge(eVT.sem, eVT.n)
        r = te.matmul(pSm[:, 0:1], bd16[:, :], cnts[:, :], start=True, stop=True)
        eTV.inc(r)
        v.wait_ge(eTV.sem, eTV.n)
        r = v.tensor_scalar(bis[:, 6:7], pSm[:, 0:1], 2048.0, bis[:, 3:4],
                            op0=AL.is_ge, op1=AL.mult)
        eVT.inc(r)
        v.tensor_copy(lnt[:, 0, 0:384], zeros2[:, 0:384])
        v.tensor_add(bis[:, 1:2], bis[:, 1:2], bis[:, 6:7])
        v.tensor_scalar_mul(bis[:, 3:4], bis[:, 3:4], 0.5)
        v.tensor_copy(lnt[:, 0, 0:384], zeros2[:, 0:384])
    if dbg == "bis":
        v.tensor_copy(lnt[:, 0, 8:9], cnts[:, :])
        v.tensor_copy(lnt[:, 0, 9:10], pSm[:, 0:1])
        v.tensor_copy(lnt[:, 0, 10:11], bisI[:, :])
        v.tensor_copy(lnt[:, 0, 11:14], bis[:, 0:3])
        v.tensor_copy(lnt[:, 0, 14:142], Ssc[:, 0:128])
        BAR()
        for m in range(16):
            sdma(out.rearrange("(m p) c -> p m c", p=128)[:, m:m + 1, :],
                 lnt[:, 0:1, :])
        swait()
        BAR()
        return nc
    v.tensor_scalar(msI[:, :], Ssc[:, :], bis[:, 1:2], None, op0=AL.is_ge)
    v.tensor_scalar(mskF[:, :], Ssc[:, :], bis[:, 1:2], None, op0=AL.is_ge)
    BAR()
    v.reduce_sum(cnts[:, :], mskF[:, :], axis=AX)
    sc.activation(out=junk[:, :], in_=Ssc[:, :], func=AF.Exp, bias=bis[:, 4:5])
    BAR()
    v.tensor_mul(junk[:, :], junk[:, :], mskF[:, :])
    BAR()
    v.reduce_sum(st1[:, 0:1], junk[:, :], axis=AX)
    BAR()
    te.transpose(pX[0:1, :], st1[:, 0:1], idT[:, :])
    BAR()
    v.tensor_copy(orow[0:1, :], pX[0:1, 0:128])
    BAR()
    v.tensor_mul(orow[0:1, :], orow[0:1, :], ohk16[0:1, :])
    BAR()
    v.reduce_sum(stt[0:1, 4:5], orow[0:1, :], axis=AX)
    BAR()
    te.matmul(pSm[:, 1:2], onesf[0:1, 0:128], stt[0:1, 4:5], start=True, stop=True)
    BAR()
    v.reciprocal(bis[:, 5:6], pSm[:, 1:2])
    v.scalar_tensor_tensor(out=Vb[:, :], in0=vid[:, :], scalar=1.0, in1=mskF[:, :],
                           op0=AL.add, op1=AL.mult)
    v.tensor_tensor_scan(out=junk[:, :], data0=mskF[:, :], data1=zeros2[:, 0:1024],
                         initial=0.0, op0=AL.add, op1=AL.add)
    BAR()
    v.tensor_scalar_add(Vb[:, :], Vb[:, :], -1.0)
    v.tensor_sub(junk[:, :], iocf[:, :], junk[:, :])
    BAR()
    v.tensor_scalar_add(junk[:, :], junk[:, :], 1.0)
    BAR()
    v.tensor_mul(junk[:, :], junk[:, :], mskF[:, :])
    BAR()
    v.tensor_copy(Db[:, :], junk[:, :])
    BAR()

    def butterfly(Vb_, Vb2_, Db_, Db2_, Sb_=None, Sb2_=None):
        Vcur, Vnx, Dcur, Dnx = Vb_, Vb2_, Db_, Db2_
        Scur, Snx = Sb_, Sb2_
        for b in range(10):
            s = 1 << b
            v.tensor_single_scalar(msI[:, :], Dcur[:, :], float(s), op=AL.bitwise_and)
            v.tensor_single_scalar(DmS[:, :], Dcur[:, :], float(s), op=AL.subtract)
            v.select(Vnx[:, 0:1024 - s], msI[:, s:], Vcur[:, s:], Vcur[:, 0:1024 - s])
            v.tensor_copy(Vnx[:, 1024 - s:], Vcur[:, 1024 - s:])
            if Scur is not None:
                v.select(Snx[:, 0:1024 - s], msI[:, s:], Scur[:, s:],
                         Scur[:, 0:1024 - s])
                v.tensor_copy(Snx[:, 1024 - s:], Scur[:, 1024 - s:])
                Scur, Snx = Snx, Scur
            v.select(Dnx[:, 0:1024 - s], msI[:, s:], DmS[:, s:], Dcur[:, 0:1024 - s])
            v.tensor_copy(Dnx[:, 1024 - s:], Dcur[:, 1024 - s:])
            Vcur, Vnx, Dcur, Dnx = Vnx, Vcur, Dnx, Dcur
        return Vcur, Scur

    v.tensor_copy(Sb[:, :], Ssc[:, :])
    BAR()
    CT, SCb = butterfly(Vb, Vb2, Db, Db2, Sb, Sb2)
    te.transpose(pX[0:1, :], cnts[:, :], idT[:, :])
    BAR()
    v.tensor_copy(orow[0:1, :], pX[0:1, 0:128])
    BAR()
    sdma(orow_d.ap()[:, :], orow[0:1, :])
    swait()
    BAR()
    gdma(orow8[:, :], orow_d.ap().rearrange("o (e r) -> (o e) r", e=8))
    gwait()
    BAR()
    v.tensor_tensor_scan(out=orow8[:, :], data0=orow8[:, :], data1=zeros2[0:8, 0:16],
                         initial=0.0, op0=AL.add, op1=AL.add)
    BAR()
    gdma(exsb[0:8, 0:16], orow_d.ap().rearrange("o (e r) -> (o e) r", e=8))
    gwait()
    BAR()
    v.tensor_sub(orow8[:, :], orow8[:, :], exsb[0:8, 0:16])
    BAR()
    gdma(orow_d.ap().rearrange("o (e r) -> (o e) r", e=8), orow8[:, :])
    gwait()
    BAR()
    sdma(orow[0:1, :], orow_d.ap()[:, :])
    swait()
    BAR()
    v.tensor_mul(exsb[0:1, 0:128], orow[0:1, :], ohk16[0:1, :])
    BAR()
    v.reduce_sum(wraw[0:1, 0:16], exsb[0:1, 0:128].rearrange("o (e r) -> o r e", e=8),
                 axis=AX)
    BAR()
    v.tensor_copy(oint[0:1, :], wraw[0:1, 0:16])
    te.transpose(pX[:, 0:1], orow[0:1, :], idT[0:1, 0:1])
    BAR()
    v.tensor_copy(obase[:, :], pX[:, 0:1])
    v.tensor_copy(Vc[:, :], CT[:, :])
    BAR()
    sdma(vc_d.ap()[:, :], Vc[:, :])
    sdma(sc_d.ap()[:, :], SCb[:, :])
    swait()
    BAR()
    rv = nc.values_load(row16[0:1, 0:1], engines=[mybir.EngineType.SP])
    sy.dma_start(out=cc16[:, :], in_=vc_d.ap()[bass.ds(rv, 16), :]).then_inc(qh, 16)
    ch[0] += 16
    sy.dma_start(out=cs16[:, :], in_=sc_d.ap()[bass.ds(rv, 16), :]).then_inc(qh2, 16)
    ch2[0] += 16
    sy.wait_ge(qh, ch[0])
    sy.wait_ge(qh2, ch2[0])
    sc.wait_ge(qh, ch[0])
    sc.wait_ge(qh2, ch2[0])
    for r in range(16):
        eng = mybir.EngineType.SP if r < 8 else mybir.EngineType.Activation
        eng2 = mybir.EngineType.Activation if r < 8 else mybir.EngineType.SP
        ov = nc.values_load(oint[0:1, r:r + 1], engines=[eng])
        ov2 = nc.values_load(oint[0:1, r:r + 1], engines=[eng2])
        issuer = sy if r < 8 else sc
        issuer2 = sc if r < 8 else sy
        issuer.wait_ge(qh, ch[0])
        issuer.dma_start(out=l_list.ap()[0:1, bass.ds(ov, 1024)],
                         in_=cc16[r:r + 1, :]).then_inc(qh, 16)
        ch[0] += 16
        issuer2.wait_ge(qh2, ch2[0])
        issuer2.dma_start(out=s_list.ap()[0:1, bass.ds(ov2, 1024)],
                          in_=cs16[r:r + 1, :]).then_inc(qh2, 16)
        ch2[0] += 16
    sy.wait_ge(qh, ch[0])
    g.wait_ge(qh, ch[0])
    g.wait_ge(qh2, ch2[0])
    gdma(FL[:, :], l_list.ap()[0:1, 0:2048].rearrange("o (p q) -> (o p) q", q=16))
    gdma(wraw[:, :], s_list.ap()[0:1, 0:2048].rearrange("o (p q) -> (o p) q", q=16))
    gwait()
    g.wait_ge(qc, cc[0])
    BAR()

    if dbg in ("sel", "sel2"):
        v.tensor_copy(lnt[:, 0, 0:16], FL[:, :])
        v.tensor_copy(lnt[:, 0, 16:17], cnts[:, :])
        v.tensor_copy(lnt[:, 0, 17:18], bis[:, 1:2])
        v.tensor_copy(lnt[:, 0, 18:19], obase[:, :])
        v.tensor_copy(lnt[:, 0, 19:20], bis[:, 5:6])
        v.memset(lnt[:, 0, 20:1024], 0.0)
        BAR()
        for m in range(16):
            sdma(out.rearrange("(m p) c -> p m c", p=128)[:, m:m + 1, :],
                 lnt[:, 0:1, :])
        swait()
        BAR()
        return nc

    sc.activation(out=wnrm[:, :], in_=wraw[:, :], func=AF.Exp, bias=bis[:, 4:5])
    BAR()
    v.tensor_scalar(wnrm[:, :], wnrm[:, :], bis[:, 5:6], None, op0=AL.mult)
    BAR()

    xgB = [xg4, xg4b]
    ffn_gq = []
    ffn_tq = []
    ydma_cnt = []
    for m4 in range(4):
        g.indirect_dma_start(
            out=xgB[0][:, m4, :], out_offset=None, in_=ag_x_out.ap(),
            in_offset=IndirectOffsetOnAxis(FL[:, m4:m4 + 1], axis=0),
        ).then_inc(qg, 16)
        cg[0] += 16
    ffn_gq.append(cg[0])
    tv0 = eTV.n
    vt0 = eVT.n
    ts0 = eTS.n
    xgt_free = []
    for G in range(4):
        b = G % 2
        sy.wait_ge(qg, ffn_gq[G])
        if G >= 1:
            sy.wait_ge(eTV.sem, xgt_free[G - 1])
        for m4 in range(4):
            sy.dma_start_transpose(
                xgt[:, m4, :, :],
                xgB[b][:, m4, :].rearrange("p (a q) -> p a q", a=8)
            ).then_inc(qs, 16)
            cs[0] += 16
        ffn_tq.append(cs[0])
        if G < 3:
            if G >= 1:
                g.wait_ge(qs, ffn_tq[G - 1])
            for m4 in range(4):
                gi = 4 * (G + 1) + m4
                g.indirect_dma_start(
                    out=xgB[(G + 1) % 2][:, m4, :], out_offset=None,
                    in_=ag_x_out.ap(),
                    in_offset=IndirectOffsetOnAxis(FL[:, gi:gi + 1], axis=0),
                ).then_inc(qg, 16)
                cg[0] += 16
            ffn_gq.append(cg[0])
        te.wait_ge(qs, ffn_tq[G])
        last_w3 = None
        for hc in range(16):
            hcg = 16 * G + hc
            pA = pjA[:, :] if hc % 2 == 0 else pyA[:, 0:512]
            pB = pjB[:, :] if hc % 2 == 0 else pyA[:, 512:1024]
            if hcg >= 2:
                te.wait_ge(eVT.sem, vt0 + hcg - 1)
            r = None
            for a in range(8):
                r = te.matmul(pA, w1v[:, a, hc * 128:(hc + 1) * 128],
                              xgt[:, :, a, :], start=(a == 0), stop=(a == 7))
            eTS.inc(r)
            ts_cnt = eTS.n
            for a in range(8):
                r = te.matmul(pB, w3v[:, a, hc * 128:(hc + 1) * 128],
                              xgt[:, :, a, :], start=(a == 0), stop=(a == 7))
            eTV.inc(r)
            last_w3 = eTV.n
            sc.wait_ge(eTS.sem, ts_cnt)
            if hcg >= 1:
                sc.wait_ge(eVT.sem, vt0 + hcg)
            r = sc.activation(out=scr512[:, :], in_=pA, func=AF.Silu)
            eSV.inc(r)
            eSV.wait(v)
            v.wait_ge(eTV.sem, last_w3)
            r = v.tensor_mul(hpr[:, hc, :], scr512[:, :], pB)
            eVT.inc(r)
        xgt_free.append(last_w3)
        for d in range(4):
            py = pyB if d % 2 == 0 else pyA
            gi = 4 * G + d
            if G == 0 and d == 0:
                te.wait_ge(qg, c_w2v)
            if d % 2 == 1:
                te.wait_ge(eVT.sem, vt0 + 16 * G + 16)
            if 4 * G + d >= 2:
                te.wait_ge(eVD.sem, eVD.n - 2 + 1)
            r = None
            for hc in range(16):
                r = mm(py, hpr[:, hc, d * 128:(d + 1) * 128], w2v[:, hc, :],
                       (hc == 0), (hc == 15), C)
            eTV.inc(r)
            v.wait_ge(eTV.sem, eTV.n)
            if len(ydma_cnt) >= 2:
                v.wait_ge(qs, ydma_cnt[-2])
            r = v.tensor_scalar(yst[:, d % 2, :], py[:, :], wnrm[:, gi:gi + 1],
                                None, op0=AL.mult)
            eVD.inc(r)
            eVD.wait(sy)
            sdma(ag_y_in.ap()[128 * gi:128 * gi + 128, :], yst[:, d % 2, :])
            ydma_cnt.append(cs[0])
        g.wait_ge(qs, cs[0])
        coll(GRP8,
             [ag_y_in.ap()[512 * G:512 * (G + 1), :].opt()],
             [y_all.ap()[4096 * G:4096 * (G + 1), :].opt()])
    swait()
    BAR()

    v.tensor_scalar(mskF[:, :], iocf[:, :], cnts[:, 0:1], None, op0=AL.is_lt)
    BAR()
    v.scalar_tensor_tensor(out=CT[:, :], in0=CT[:, :], scalar=1.0, in1=mskF[:, :],
                           op0=AL.add, op1=AL.mult)
    BAR()
    v.tensor_scalar_add(CT[:, :], CT[:, :], -1.0)
    BAR()
    v.tensor_scalar(junk[:, :], iocf[:, :], obase[:, 0:1], None, op0=AL.add)
    BAR()
    v.scalar_tensor_tensor(out=junk[:, :], in0=junk[:, :], scalar=2048.0,
                           in1=CT[:, :], op0=AL.mult, op1=AL.add)
    BAR()
    v.tensor_scalar(junk[:, :], junk[:, :], kv0[:, 0:1], None, op0=AL.subtract)
    BAR()
    v.tensor_scalar(msI[:, :], CT[:, :], kv0[:, 0:1], None, op0=AL.is_ge)
    v.tensor_scalar(DmS[:, :], CT[:, :], kv0[:, 1:2], None, op0=AL.is_lt)
    BAR()
    v.tensor_tensor(msI[:, :], msI[:, :], DmS[:, :], op=AL.bitwise_and)
    BAR()
    v.tensor_copy(mskF[:, :], msI[:, :])
    BAR()
    v.scalar_tensor_tensor(out=Vb2[:, :], in0=junk[:, :], scalar=1.0, in1=mskF[:, :],
                           op0=AL.add, op1=AL.mult)
    v.reduce_sum(cnt2[:, :], mskF[:, :], axis=AX)
    v.tensor_tensor_scan(out=junk[:, :], data0=mskF[:, :], data1=zeros2[:, 0:1024],
                         initial=0.0, op0=AL.add, op1=AL.add)
    BAR()
    v.tensor_scalar_add(Vb2[:, :], Vb2[:, :], -1.0)
    v.tensor_sub(junk[:, :], iocf[:, :], junk[:, :])
    BAR()
    v.tensor_scalar_add(junk[:, :], junk[:, :], 1.0)
    BAR()
    v.tensor_mul(junk[:, :], junk[:, :], mskF[:, :])
    BAR()
    v.tensor_copy(Vb[:, :], Vb2[:, :])
    v.tensor_copy(Db2[:, :], junk[:, :])
    BAR()
    v.tensor_copy(Db[:, :], Db2[:, :])
    BAR()
    CB, _ = butterfly(Vb, Vb2, Db, Db2)
    v.tensor_copy(Db2[:, 0:C2], CB[:, 0:C2])
    BAR()
    v.tensor_single_scalar(srcI[:, :], Db2[:, 0:C2], 11.0, op=AL.logical_shift_right)
    v.tensor_single_scalar(dstI[:, :], Db2[:, 0:C2], 2047.0, op=AL.bitwise_and)
    v.tensor_scalar(msI[:, 0:C2], iocf[:, 0:C2], cnt2[:, 0:1], None, op0=AL.is_lt)
    v.tensor_copy(lnt[:, 0, 0:384], zeros2[:, 0:384])
    v.tensor_single_scalar(DmS[:, 0:C2], srcI[:, :], 15.0, op=AL.bitwise_and)
    v.tensor_single_scalar(DmS[:, C2:2 * C2], srcI[:, :], 4.0,
                           op=AL.logical_shift_right)
    v.tensor_copy(lnt[:, 0, 0:384], zeros2[:, 0:384])
    v.tensor_single_scalar(DmS[:, 2 * C2:3 * C2], DmS[:, 0:C2], 2.0,
                           op=AL.logical_shift_right)
    v.tensor_single_scalar(DmS[:, 0:C2], DmS[:, 0:C2], 3.0, op=AL.bitwise_and)
    v.tensor_copy(lnt[:, 0, 0:384], zeros2[:, 0:384])
    v.tensor_single_scalar(DmS[:, 2 * C2:3 * C2], DmS[:, 2 * C2:3 * C2], 12.0,
                           op=AL.logical_shift_left)
    v.tensor_single_scalar(DmS[:, 0:C2], DmS[:, 0:C2], 7.0,
                           op=AL.logical_shift_left)
    v.tensor_copy(lnt[:, 0, 0:384], zeros2[:, 0:384])
    v.tensor_tensor(srcI[:, :], DmS[:, C2:2 * C2], DmS[:, 0:C2],
                    op=AL.bitwise_or)
    v.tensor_copy(lnt[:, 0, 0:384], zeros2[:, 0:384])
    v.tensor_tensor(srcI[:, :], srcI[:, :], DmS[:, 2 * C2:3 * C2],
                    op=AL.bitwise_or)
    v.tensor_copy(lnt[:, 0, 0:384], zeros2[:, 0:384])
    v.tensor_scalar(srcI[:, :], srcI[:, :], eab[:, 0:1], None, op0=AL.bitwise_or)
    v.tensor_copy(lnt[:, 0, 0:384], zeros2[:, 0:384])
    BAR()
    if dbg == "bwd":
        v.tensor_copy(lnt[:, 0, 67:99], msI[:, 0:C2])
        v.tensor_copy(lnt[:, 0, 99:131], srcI[:, :])
        v.tensor_copy(lnt[:, 0, 131:163], dstI[:, :])
        BAR()
    v.select(DmS[:, 0:C2], msI[:, 0:C2], srcI[:, :], sentA[:, :])
    v.select(DmS[:, C2:2 * C2], msI[:, 0:C2], dstI[:, :], sentB[:, :])
    BAR()
    v.tensor_copy(srcI[:, :], DmS[:, 0:C2])
    v.tensor_copy(dstI[:, :], DmS[:, C2:2 * C2])
    g.wait_ge(qc, cc[0])
    BAR()
    if dbg == "bwd":
        v.tensor_copy(lnt[:, 0, 0:32], srcI[:, :])
        v.tensor_copy(lnt[:, 0, 32:64], dstI[:, :])
        v.tensor_copy(lnt[:, 0, 64:65], cnt2[:, :])
        v.tensor_copy(lnt[:, 0, 65:66], obase[:, :])
        v.tensor_copy(lnt[:, 0, 66:67], cnts[:, :])
        v.memset(lnt[:, 0, 163:1024], 0.0)
        BAR()
        for m in range(16):
            sdma(out.rearrange("(m p) c -> p m c", p=128)[:, m:m + 1, :],
                 lnt[:, 0:1, :])
        swait()
        BAR()
        return nc
    sdma(di_d.ap()[:, :], dstI[:, :])
    swait()
    BAR()
    sdma(dstE[:, :, :], di_d.ap().rearrange("(e r) gi -> r e gi", e=8))
    swait()
    BAR()
    for gi in range(C2):
        g.indirect_dma_start(
            out=Gbig[:, gi, :], out_offset=None,
            in_=y_all.ap(),
            in_offset=IndirectOffsetOnAxis(srcI[:, gi:gi + 1], axis=0),
        ).then_inc(qg, 16)
        cg[0] += 16
    gwait()
    BAR()
    for e in range(8):
        for gi in range(C2):
            g.indirect_dma_start(
                out=acc_d.ap(),
                out_offset=IndirectOffsetOnAxis(dstE[0:16, e, gi:gi + 1], axis=0),
                in_=Gbig[16 * e:16 * e + 16, gi, :], in_offset=None,
                compute_op=AL.add,
            ).then_inc(qh, 16)
            ch[0] += 16
        g.wait_ge(qh, ch[0])
    BAR()

    ln2_base = (eVX.n, None, eSV.n, eVD.n)
    acct_cnt = []
    x1g_cnt = []
    for m in range(16):
        b = m % 2
        if m >= 2:
            sy.wait_ge(eVX.sem, ln2_base[0] + m - 1)
        sdma(acct[:, b:b + 1, :],
             acc_d.ap()[0:T, :].rearrange("(m p) c -> p m c", p=128)[:, m:m + 1, :])
        sdma(x1s[:, b:b + 1, :],
             ag_x_in.ap().rearrange("(m p) c -> p m c", p=128)[:, m:m + 1, :])
        acct_cnt.append(cs[0])
        v.wait_ge(qs, acct_cnt[m])
        r = v.scalar_tensor_tensor(out=lnt[:, b, :], in0=x1s[:, b, :], scalar=1.0,
                                   in1=acct[:, b, :], op0=AL.mult, op1=AL.add,
                                   accum_out=st1[:, m:m + 1])
        eVX.inc(r)
        v.scalar_tensor_tensor(out=lnsq[:, :], in0=lnt[:, b, :], scalar=1.0,
                               in1=lnt[:, b, :], op0=AL.mult, op1=AL.mult,
                               accum_out=st2[:, m:m + 1])
        if m >= 2:
            eSV.wait(v, ln2_base[2] + 2 * (m - 1))
        r = v.tensor_scalar_mul(st1[:, m:m + 1], st1[:, m:m + 1], 1.0 / C)
        eVS.inc(r)
        eVS.wait(sc)
        r = sc.activation(out=stt[:, 4 + b:5 + b], in_=st1[:, m:m + 1],
                          func=AF.Square)
        eSV.inc(r)
        eSV.wait(v)
        r = v.tensor_scalar(stt[:, b:b + 1], st2[:, m:m + 1], 1.0 / C,
                            stt[:, 4 + b:5 + b], op0=AL.mult, op1=AL.subtract)
        eVS.inc(r)
        eVS.wait(sc)
        r = sc.activation(out=stt[:, 2 + b:3 + b], in_=stt[:, b:b + 1],
                          func=AF.Sqrt, bias=epsv[:, 0:1])
        eSV.inc(r)
        eSV.wait(v)
        v.reciprocal(stt[:, 6 + b:7 + b], stt[:, 2 + b:3 + b])
        v.tensor_copy(lnsq[:, 0:384], zeros2[:, 0:384])
        if m >= 2:
            v.wait_ge(qo, 16 * (m - 1))
        r = v.tensor_scalar(outt[:, b, :], lnt[:, b, :], st1[:, m:m + 1],
                            stt[:, 6 + b:7 + b], op0=AL.subtract, op1=AL.mult)
        eVD.inc(r)
        eVD.wait(sy)
        sy.dma_start(out=out.rearrange("(m p) c -> p m c", p=128)[:, m:m + 1, :],
                     in_=outt[:, b:b + 1, :]).then_inc(qo, 16)
        co[0] += 16
    sy.wait_ge(qo, co[0])
    BAR()
    return nc


def _host_prep(inputs):
    x = np.asarray(inputs["L0_attn_shot_embeddings"], np.float32).reshape(16384, C)
    Wd = np.ascontiguousarray(np.asarray(inputs["L0_attn_Wd"], np.float32))
    Wi = np.ascontiguousarray(np.asarray(inputs["L0_attn_Wi"], np.float32))
    Wo = np.ascontiguousarray(np.asarray(inputs["L0_attn_Wo"], np.float32))
    Ws = np.ascontiguousarray(np.asarray(inputs["L0_attn_Ws"], np.float32))
    bd = np.ascontiguousarray(np.asarray(inputs["L0_attn_bd"], np.float32))
    bi = np.ascontiguousarray(np.asarray(inputs["L0_attn_bi"], np.float32))
    alog = np.ascontiguousarray(np.asarray(inputs["L0_attn_A_log"], np.float32))
    bo = np.ascontiguousarray(np.asarray(inputs["L0_attn_bo"], np.float32))
    bs = np.ascontiguousarray(np.asarray(inputs["L0_attn_bs"], np.float32))
    Wg = np.asarray(inputs["L0_ffn_Wg"], np.float32)
    W1 = np.asarray(inputs["L0_ffn_W1"], np.float32)
    W2 = np.asarray(inputs["L0_ffn_W2"], np.float32)
    W3 = np.asarray(inputs["L0_ffn_W3"], np.float32)

    ext = np.zeros((8, 128, 144), np.float32)
    for e in range(8):
        for j in range(8):
            ext[j, :, 16 + 16 * e] = Wg[e, 128 * j:128 * (j + 1)]

    maps = []
    for k in range(NCORE):
        xk = np.ascontiguousarray(x[2048 * k:2048 * (k + 1)])
        kv = np.zeros((128, 2), np.float32)
        kv[:, 0] = 2048.0 * k
        kv[:, 1] = 2048.0 * (k + 1)
        eabv = np.zeros((128, 1), np.int32)
        for p in range(128):
            eabv[p, 0] = 512 * (p // 16)
        ohk16v = np.zeros((1, 128), np.float32)
        ohk16v[0, 16 * k:16 * (k + 1)] = 1.0
        row16v = np.arange(16 * k, 16 * k + 16, dtype=np.int32).reshape(1, 16)
        rot = np.array([[(k + e) % 8 for e in range(8)]], np.int32)
        ohpv = np.zeros((128, 8), np.float32)
        pos = k % 4
        ohpv[:, 2 * pos] = 1.0
        ohpv[:, 2 * pos + 1] = 1.0
        ohkv = np.zeros((128, 8), np.float32)
        ohkv[:, k] = 1.0
        maps.append({
            "x_tok": xk,
            "x_feat": np.ascontiguousarray(xk.T),
            "wd": Wd, "wi": Wi, "wo": Wo, "ws": Ws,
            "bdd": bd, "bii": bi, "alog": alog, "bo": bo, "bss": bs,
            "ext": ext,
            "wgt": np.ascontiguousarray(Wg.T),
            "w1t": np.ascontiguousarray(W1[k].T),
            "w3t": np.ascontiguousarray(W3[k].T),
            "w2t": np.ascontiguousarray(W2[k].T),
            "kvi": kv, "eabi": eabv, "rot": rot, "ohpi": ohpv, "ohki": ohkv,
            "ohk16": ohk16v, "row16": row16v,
        })
    return maps


def kernel(**inputs):
    maps = _host_prep(inputs)
    nc = build(dbg=DBG_STAGE)
    res = run_bass_kernel_spmd(nc, maps, core_ids=list(range(NCORE)))
    outs = np.concatenate([np.asarray(r["out"]) for r in res.results], axis=0)
    return outs.reshape(2, 8192, 1024).astype(np.float32)


# version 08
# speedup vs baseline: 1.3157x; 1.0396x over previous
import numpy as np
import ml_dtypes
import concourse.bass as bass
import concourse.mybir as mybir
from concourse.bass import IndirectOffsetOnAxis
from concourse.bass_utils import run_bass_kernel_spmd

F32 = mybir.dt.float32
BF16 = mybir.dt.bfloat16
I32 = mybir.dt.int32
AX = mybir.AxisListType.X
AF = mybir.ActivationFunctionType
AL = mybir.AluOpType

NCORE = 8
T = 2048
C = 1024
NSD = 256
HF = 2048
CAP = 2048
C2 = 32
EPS = 1e-5
NBIS = 30

DBG_STAGE = None


def build(dbg=None):
    nc = bass.Bass("TRN2", target_bir_lowering=False)

    def din(name, shape, dt=F32):
        return nc.dram_tensor(name, shape, dt, kind="ExternalInput").ap()

    x_tok = din("x_tok", [T, C])
    x_feat = din("x_feat", [C, T])
    wd_in = din("wd", [2, C, NSD])
    wi_in = din("wi", [2, C, NSD])
    wo_in = din("wo", [2, NSD, C])
    ws_in = din("ws", [2, NSD, C])
    bd_in = din("bdd", [128, 2, 2])
    bi_in = din("bii", [128, 2, 2])
    alog_in = din("alog", [128, 2, 2])
    bo_in = din("bo", [2, C])
    bs_in = din("bss", [2, C])
    ext_in = din("ext", [8, 128, 144])
    wgt_in = din("wgt", [C, 8])
    w1t_in = din("w1t", [C, HF])
    w3t_in = din("w3t", [C, HF])
    w2t_in = din("w2t", [HF, C])
    kv_in = din("kvi", [128, 2])
    eab_in = din("eabi", [128, 1], I32)
    rot_in = din("rot", [1, 8], I32)
    ohp_in = din("ohpi", [128, 8])
    ohk_in = din("ohki", [128, 8])
    ohk16_in = din("ohk16", [1, 128])
    row16_in = din("row16", [1, 16], I32)

    out = nc.dram_tensor("out", [T, C], F32, kind="ExternalOutput").ap()

    def dint(name, shape, dt):
        return nc.dram_tensor(name, shape, dt, kind="Internal")

    ag_s_in = [dint(f"ag_s_in{l}", [4, 128], F32) for l in range(2)]
    ag_s_out = [dint(f"ag_s_out{l}", [16, 128], F32) for l in range(2)]
    ag_t_in = dint("ag_t_in", [128, 128], F32)
    ag_t_out = dint("ag_t_out", [1024, 128], F32)
    ag_x_in = dint("ag_x_in", [T, C], BF16)
    ag_x_out = dint("ag_x_out", [NCORE * T, C], BF16)
    ag_y_in = dint("ag_y_in", [CAP, C], BF16)
    y_all = dint("y_all", [NCORE * CAP + 1, C], BF16)
    l_list = dint("l_list", [1, CAP + 1152], I32)
    acc_d = dint("acc_d", [T + 1, C], BF16)
    orow_d = dint("orow_d", [1, 128], F32)
    vc_d = dint("vc_d", [128, 1024], I32)
    sc_d = dint("sc_d", [128, 1024], F32)
    s_list = dint("s_list", [1, CAP + 1152], F32)
    di_d = dint("di_d", [128, C2], I32)

    sbt = nc.alloc_sbuf_tensor
    sat = nc.alloc_sbuf_tensor_at

    def addr(h):
        return nc.lookup_mloc(h).addr

    bT = sbt("bT", [128, 8, T], BF16)
    wdv = sbt("wdv", [128, 2, 8, NSD], BF16)
    wiv = sbt("wiv", [128, 2, 8, NSD], BF16)
    wov = sbt("wov", [128, 2, 2, C], BF16)
    wsv = sbt("wsv", [128, 2, 2, C], BF16)
    sA = sbt("sA", [128, 2, T], F32)
    sB = sbt("sB", [128, 2, T], F32)
    sC = sbt("sC", [128, 2, T], F32)
    sD = sbt("sD", [128, 2, T], F32)
    A_bt = addr(bT)
    A_w = addr(wdv)
    A_s = addr(sA)

    w2v = sbt("w2v", [128, 16, C], BF16)
    zeros2 = sbt("zeros2", [128, 2048], F32)
    lnt = sbt("lnt", [128, 2, C], F32)
    bbvb = sbt("bbvb", [1, 2, C], BF16)
    ct2 = sbt("ct2", [128, C], F32)
    x1s = sbt("x1s", [128, 2, C], BF16)
    rowT = sbt("rowT", [128, 16], I32)
    colT = sbt("colT", [128, 16], F32)
    idT = sbt("idT", [128, 128], F32)
    extv = sbt("extv", [128, 8, 144], BF16)
    wgv = sbt("wgv", [128, 8, 8], BF16)
    bd16 = sbt("bd16", [128, 128], F32)
    iocf = sbt("iocf", [128, 1024], F32)
    mskF = sbt("mskF", [128, 1024], F32)
    msI = sbt("msI", [128, 1024], I32)
    bdv = sbt("bdv", [128, 2, 2], F32)
    biv = sbt("biv", [128, 2, 2], F32)
    spav = sbt("spav", [128, 2, 2], F32)
    kv0 = sbt("kv0", [128, 2], F32)
    eab = sbt("eab", [128, 1], I32)
    ohp = sbt("ohp", [128, 8], F32)
    ohk = sbt("ohk", [128, 8], F32)
    exsb = sbt("exsb", [16, 128], F32)
    exch = sbt("exch", [128, 16], F32)
    sm4 = sbt("sm4", [128, 4], F32)
    hins = sbt("hins", [128, 12], F32)
    stb = sbt("stb", [128, 2, 128], BF16)
    bis = sbt("bis", [128, 8], F32)
    bisI = sbt("bisI", [128, 1], I32)
    onesf = sbt("onesf", [1, 128], F32)
    ones1b = sbt("ones1b", [1, 128], BF16)
    cnts = sbt("cnts", [128, 1], F32)
    orow = sbt("orow", [1, 128], F32)
    orow8 = sbt("orow8", [8, 16], F32)
    obase = sbt("obase", [128, 1], F32)
    oint = sbt("oint", [1, 16], I32)
    FL = sbt("FL", [128, 16], I32)
    wraw = sbt("wraw", [128, 16], F32)
    wnrm = sbt("wnrm", [128, 16], F32)
    srcI = sbt("srcI", [128, C2], I32)
    dstI = sbt("dstI", [128, C2], I32)
    sentA = sbt("sentA", [128, C2], I32)
    sentB = sbt("sentB", [128, C2], I32)
    cnt2 = sbt("cnt2", [128, 1], F32)
    st1 = sbt("st1", [128, 16], F32)
    st2 = sbt("st2", [128, 16], F32)
    stt = sbt("stt", [128, 8], F32)
    scr8 = sbt("scr8", [128, 8], F32)
    epsv = sbt("epsv", [128, 1], F32)
    onesc = sbt("onesc", [128, 1], F32)
    iop = sbt("iop", [128, 1], I32)
    iopf = sbt("iopf", [128, 1], F32)
    ctTb = sbt("ctTb", [128, 8], BF16)

    KB = 1024
    lnsq = sat("lnsq", [128, C], F32, offset=A_s + 32 * KB)
    Ssc = sat("Ssc", [128, 1024], F32, offset=A_bt + 0)
    vid = sat("vid", [128, 1024], F32, offset=A_bt + 4 * KB)
    junk = sat("junk", [128, 1024], F32, offset=A_bt + 8 * KB)
    Vb = sat("Vb", [128, 1024], F32, offset=A_bt + 12 * KB)
    Vb2 = sat("Vb2", [128, 1024], F32, offset=A_bt + 16 * KB)
    Db = sat("Db", [128, 1024], I32, offset=A_bt + 20 * KB)
    Db2 = sat("Db2", [128, 1024], I32, offset=A_bt + 24 * KB)
    DmS = sat("DmS", [128, 1024], I32, offset=A_bt + 28 * KB)
    yst = sat("yst", [128, 2, C], BF16, offset=A_bt + 0)
    scr512 = sat("scr512", [128, 512], F32, offset=A_bt + 4 * KB)
    scrG = sat("scrG", [128, 512], F32, offset=A_w + 0)
    cc16 = sat("cc16", [16, 1024], I32, offset=A_w + 4 * KB)
    Sb = sat("Sb", [128, 1024], F32, offset=A_w + 8 * KB)
    Sb2 = sat("Sb2", [128, 1024], F32, offset=A_w + 12 * KB)
    cs16 = sat("cs16", [16, 1024], F32, offset=A_w + 16 * KB)
    dstE = sat("dstE", [16, 8, C2], I32, offset=A_w + 4 * KB)
    Gbig = sat("Gbig", [128, C2, C], BF16, offset=A_s + 0)
    Vc = Db
    xgt = sat("xgt", [128, 4, 8, 128], BF16, offset=A_w + 0)
    xg4 = sat("xg4", [128, 4, C], BF16, offset=A_w + 8 * KB)
    xg4b = sat("xg4b", [128, 4, C], BF16, offset=A_bt + 24 * KB)
    hpr = sat("hpr", [128, 16, 512], BF16, offset=A_w + 16 * KB)
    zrowb = sat("zrowb", [128, C], BF16, offset=addr(lnt) + 4 * KB)
    ohk16 = sat("ohk16s", [1, 128], F32, offset=addr(stb))
    row16 = sat("row16s", [1, 16], I32, offset=addr(exch))
    hsb = sat("hsb", [128, 2, T], BF16, offset=A_s + 0)
    xpt = sat("xpt", [128, 2, C], F32, offset=A_s + 8 * KB)
    w1v = sat("w1v", [128, 8, HF], BF16, offset=A_s + 0)
    w3v = sat("w3v", [128, 8, HF], BF16, offset=A_s + 32 * KB)
    Gb = sat("Gb", [128, 2, C], BF16, offset=A_s + 0)
    acct4 = sat("acct4", [128, 4, C], BF16, offset=A_s + 0)
    x1t4 = sat("x1t4", [128, 4, C], BF16, offset=A_s + 24 * KB)
    acct = sat("acct", [128, 2, C], BF16, offset=A_s + 8 * KB)
    outt = sat("outt", [128, 2, C], F32, offset=A_s + 16 * KB)

    pjA = nc.alloc_psum_tensor("pjA", [128, 512], F32)
    pjB = nc.alloc_psum_tensor("pjB", [128, 512], F32)
    pyA = nc.alloc_psum_tensor("pyA", [128, 1024], F32)
    pyB = nc.alloc_psum_tensor("pyB", [128, 1024], F32)
    pX = nc.alloc_psum_tensor("pX", [128, 128], F32)
    pSm = nc.alloc_psum_tensor("pSm", [128, 2], F32)

    g = nc.gpsimd
    sy = nc.sync
    v = nc.vector
    sc = nc.scalar
    te = nc.tensor
    BAR = nc.all_engine_barrier

    qg = nc.alloc_semaphore("qg")
    qs = nc.alloc_semaphore("qs")
    qc = nc.alloc_semaphore("qc")
    qh = nc.alloc_semaphore("qh")
    qh2 = nc.alloc_semaphore("qh2")
    qo = nc.alloc_semaphore("qo")
    cg = [0]; cs = [0]; cc = [0]; ch = [0]; co = [0]; ch2 = [0]

    def gdma(dst, src, **kw):
        r = g.dma_start(out=dst, in_=src, **kw).then_inc(qg, 16)
        cg[0] += 16
        return r

    def gwait():
        g.wait_ge(qg, cg[0])

    def sdma(dst, src, **kw):
        r = sy.dma_start(out=dst, in_=src, **kw).then_inc(qs, 16)
        cs[0] += 16
        return r

    def swait():
        sy.wait_ge(qs, cs[0])

    def coll(groups, ins, outs):
        g.collective_compute("AllGather", AL.bypass, replica_groups=groups,
                             ins=ins, outs=outs).then_inc(qc, 1)
        cc[0] += 1

    def mm(out_ps, lhsT, rhs, start, stop, width):
        nh = (width + 511) // 512
        r = None
        for h in range(nh):
            w0 = h * 512
            w1 = min(width, w0 + 512)
            r = te.matmul(out_ps[:, w0:w1], lhsT, rhs[:, w0:w1], start=start, stop=stop)
        return r

    class Edge:
        __slots__ = ("sem", "n")
        def __init__(self, name):
            self.sem = nc.alloc_semaphore(name)
            self.n = 0
        def inc(self, r):
            r.then_inc(self.sem, 1)
            self.n += 1
            return self.n
        def wait(self, eng, n=None):
            eng.wait_ge(self.sem, self.n if n is None else n)

    eTV = Edge("eTV")
    eVT = Edge("eVT")
    eVS = Edge("eVS")
    eSV = Edge("eSV")
    eVX = Edge("eVX")
    eVD = Edge("eVD")
    eVG = Edge("eVG")
    eTS = Edge("eTS")

    GRP8 = [list(range(8))]
    GRP4 = [[0, 1, 2, 3], [4, 5, 6, 7]]

    gdma(wdv[:, 0:1, :, :], wd_in.rearrange("l (j p) n -> p l j n", p=128)[:, 0:1, :, :])
    gdma(wiv[:, 0:1, :, :], wi_in.rearrange("l (j p) n -> p l j n", p=128)[:, 0:1, :, :])
    c_wdwi = cg[0]
    bt_cnt = []
    for j in range(8):
        gdma(bT[:, j, :],
             x_feat.rearrange("(j p) t -> p j t", p=128)[:, j, :])
        bt_cnt.append(cg[0])
    gdma(wdv[:, 1:2, :, :], wd_in.rearrange("l (j p) n -> p l j n", p=128)[:, 1:2, :, :])
    gdma(wiv[:, 1:2, :, :], wi_in.rearrange("l (j p) n -> p l j n", p=128)[:, 1:2, :, :])
    c_wdwi2 = cg[0]
    gdma(wov[:, :, :, :], wo_in.rearrange("l (a p) c -> p l a c", p=128))
    c_wov = cg[0]
    gdma(wsv[:, :, :, :], ws_in.rearrange("l (a p) c -> p l a c", p=128))
    c_wsv = cg[0]
    gdma(extv[:, :, :], ext_in.rearrange("j p f -> p j f"))
    c_ext = cg[0]
    gdma(wgv[:, :, :], wgt_in.rearrange("(j p) e -> p j e", p=128))
    gdma(w2v[:, :, :], w2t_in.rearrange("(j p) c -> p j c", p=128))
    c_w2v = cg[0]
    sdma(bdv[:, :, :], bd_in[:, :, :])
    sdma(biv[:, :, :], bi_in[:, :, :])
    sdma(spav[:, :, :], alog_in[:, :, :])
    sdma(kv0[:, :], kv_in[:, :])
    sdma(eab[:, :], eab_in[:, :])
    sdma(ohp[:, :], ohp_in[:, :])
    sdma(ohk[:, :], ohk_in[:, :])
    sdma(lnt[0:1, 0, :], bo_in[0:1, :])
    sdma(lnt[0:1, 1, :], bs_in[0:1, :])
    sdma(ct2[0:1, 0:C], bo_in[1:2, :])
    sdma(sA[0:1, 0, 0:C], bs_in[1:2, :])
    g.iota(iop[:, :], pattern=[[0, 1]], base=0, channel_multiplier=1)
    g.iota(iocf[:, :], pattern=[[1, 1024]], base=0, channel_multiplier=0,
           allow_small_or_imprecise_dtypes=True)
    swait()
    BAR()
    v.tensor_add(bbvb[0:1, 0, :], lnt[0:1, 0, :], lnt[0:1, 1, :])
    v.tensor_add(bbvb[0:1, 1, :], ct2[0:1, 0:C], sA[0:1, 0, 0:C])
    v.memset(zeros2[:, :], 0.0)
    v.memset(zrowb[:, :], 0.0)
    v.memset(onesf[:, :], 1.0)
    v.memset(ones1b[:, :], 1.0)
    v.memset(sentA[:, :], NCORE * CAP)
    v.memset(sentB[:, :], T)
    v.memset(epsv[:, :], float(EPS))
    v.memset(onesc[:, :], 1.0)
    v.tensor_copy(iopf[:, :], iop[:, :])
    sc.activation(out=stt[:, 4:8], in_=spav[:, :, :].rearrange("p l a -> p (l a)"),
                  func=AF.Abs)
    BAR()
    sc.activation(out=stt[:, 4:8], in_=stt[:, 4:8], func=AF.Exp, scale=-1.0)
    BAR()
    sc.activation(out=stt[:, 4:8], in_=stt[:, 4:8], func=AF.Ln, bias=onesc[:, 0:1])
    BAR()
    v.tensor_scalar_max(st2[:, 4:8], spav[:, :, :].rearrange("p l a -> p (l a)"), 0.0)
    BAR()
    v.tensor_add(st2[:, 4:8], st2[:, 4:8], stt[:, 4:8])
    BAR()
    v.tensor_scalar_mul(spav[:, :, :].rearrange("p l a -> p (l a)"), st2[:, 4:8], -1.0)
    v.tensor_scalar(idT[:, :], iocf[:, 0:128], iopf[:, 0:1], None, op0=AL.is_equal)
    v.tensor_single_scalar(bisI[:, :], iop[:, :], 15.0, op=AL.bitwise_and)
    v.tensor_single_scalar(msI[:, 0:1], iop[:, :], 4.0, op=AL.logical_shift_right)
    g.iota(bd16[:, :], pattern=[[1, 8], [0, 16]], base=0, channel_multiplier=0,
           allow_small_or_imprecise_dtypes=True)
    BAR()
    v.tensor_copy(stt[:, 1:2], msI[:, 0:1])
    BAR()
    v.tensor_scalar(bd16[:, :], bd16[:, :], stt[:, 1:2], None, op0=AL.is_equal)
    BAR()
    for m in range(16):
        sdma(acc_d.ap()[0:T, :].rearrange("(m p) c -> p m c", p=128)[:, m:m + 1, :],
             zrowb[:, :].rearrange("p (o c) -> p o c", o=1))
    sdma(acc_d.ap()[T:T + 1, :], zrowb[0:1, :])
    sdma(y_all.ap()[NCORE * CAP:NCORE * CAP + 1, :], zrowb[0:1, :])

    def ssm_layer(l, rhsbuf):
        evac_cnt = []
        te.wait_ge(qg, c_wdwi if l == 0 else c_wdwi2)
        for it in range(8):
            tsl, a = it % 4, it // 4
            t0 = tsl * 512
            pA = pjA[:, :] if it % 2 == 0 else pyA[:, 0:512]
            pB = pjB[:, :] if it % 2 == 0 else pyA[:, 512:1024]
            if it >= 2:
                te.wait_ge(eVT.sem, evac_cnt[it - 2])
            r = None
            for j in range(8):
                if l == 0 and it == 0:
                    te.wait_ge(qg, bt_cnt[j])
                r = te.matmul(pA, wdv[:, l, j, a * 128:(a + 1) * 128],
                              rhsbuf[:, j, t0:t0 + 512], start=(j == 0), stop=(j == 7))
            for j in range(8):
                r = te.matmul(pB, wiv[:, l, j, a * 128:(a + 1) * 128],
                              rhsbuf[:, j, t0:t0 + 512], start=(j == 0), stop=(j == 7))
            eTV.inc(r)
            v.wait_ge(eTV.sem, eTV.n)
            v.tensor_scalar(sA[:, a, t0:t0 + 512], pA,
                            bdv[:, l, a:a + 1], None, op0=AL.add)
            r = v.tensor_scalar(sB[:, a, t0:t0 + 512], pB,
                                biv[:, l, a:a + 1], None, op0=AL.add)
            eVT.inc(r)
            evac_cnt.append(eVT.n)
        if build.dbg_layer == (l, "raw"):
            for a in range(2):
                for h in range(2):
                    sdma(build.dbg_out.rearrange("(z p) c -> p z c", p=128)[:, 2 * a + h:2 * a + h + 1, :],
                         sA[:, a, h * 1024:(h + 1) * 1024].rearrange("p (o c) -> p o c", o=1))
                    sdma(build.dbg_out.rearrange("(z p) c -> p z c", p=128)[:, 8 + 2 * a + h:8 + 2 * a + h + 1, :],
                         sB[:, a, h * 1024:(h + 1) * 1024].rearrange("p (o c) -> p o c", o=1))
            swait()
            BAR()
            build.dbg_early[0] = True
        if build.dbg_early[0]:
            return
        for a in range(2):
            sc.wait_ge(eVT.sem, evac_cnt[3 + 4 * a])
            r = sc.activation(out=sC[:, a, :], in_=sA[:, a, :], func=AF.Abs)
            eSV.inc(r)
            sc.activation(out=sC[:, a, :], in_=sC[:, a, :], func=AF.Exp, scale=-1.0)
            r = sc.activation(out=sC[:, a, :], in_=sC[:, a, :], func=AF.Ln,
                              bias=onesc[:, 0:1])
            eSV.inc(r)
        sv_sp = eSV.n
        for a in range(2):
            eSV.wait(v, sv_sp - 3 + 2 * a)
            v.tensor_scalar_max(sA[:, a, :], sA[:, a, :], 0.0)
            eSV.wait(v, sv_sp - 2 + 2 * a)
            v.tensor_add(sA[:, a, :], sA[:, a, :], sC[:, a, :])
            v.tensor_tensor_scan(out=sC[:, a, :], data0=sA[:, a, :],
                                 data1=zeros2[:, :], initial=0.0,
                                 op0=AL.add, op1=AL.add)
            r = v.tensor_mul(sB[:, a, :], sA[:, a, :], sB[:, a, :])
            eVS.inc(r)
            eVS.wait(sc)
            sc.activation(out=sA[:, a, :], in_=sA[:, a, :], func=AF.Exp,
                          scale=spav[:, l, a:a + 1])
            sc.activation(out=sm4[:, a:a + 1], in_=sC[:, a, T - 1:T], func=AF.Exp,
                          scale=spav[:, l, a:a + 1])
            r = sc.activation(out=sC[:, a, :], in_=sC[:, a, :], func=AF.Exp,
                              scale=spav[:, l, a:a + 1])
            eSV.inc(r)
            eSV.wait(v)
            r = v.tensor_tensor_scan(out=sD[:, a, :], data0=sA[:, a, :],
                                     data1=sB[:, a, :], initial=0.0,
                                     op0=AL.mult, op1=AL.add)
        v.tensor_copy(sm4[:, 2:3], sD[:, 0, T - 1:T])
        r = v.tensor_copy(sm4[:, 3:4], sD[:, 1, T - 1:T])
        eVT.inc(r)
        te.wait_ge(eVT.sem, eVT.n)
        te.wait_ge(eSV.sem, eSV.n)
        r = te.transpose(pX[0:4, :], sm4[:, 0:4], idT[:, :])
        eTV.inc(r)
        eTV.wait(v)
        r = v.tensor_copy(exsb[0:4, :], pX[0:4, :])
        eVD.inc(r)
        eVD.wait(sy)
        sdma(ag_s_in[l].ap()[:, :], exsb[0:4, :])
        g.wait_ge(qs, cs[0])
        coll(GRP4, [ag_s_in[l].ap().opt()], [ag_s_out[l].ap().opt()])
        g.wait_ge(qc, cc[0])
        gdma(exsb[:, :], ag_s_out[l].ap()[:, :])
        te.wait_ge(qg, cg[0])
        r = te.transpose(pX[:, 0:16], exsb[0:16, :], idT[0:16, 0:16])
        eTV.inc(r)
        eTV.wait(v)
        v.tensor_copy(exch[:, 0:16], pX[:, 0:16])
        v.memset(hins[:, 0:2], 0.0)
        v.tensor_copy(lnt[:, 0, 0:384], zeros2[:, 0:384])
        for m in range(1, 5):
            for cch in range(2):
                pcol = 4 * (m - 1) + cch
                hcol = 4 * (m - 1) + 2 + cch
                v.tensor_mul(stt[:, cch:cch + 1], exch[:, pcol:pcol + 1],
                             hins[:, 2 * (m - 1) + cch:2 * (m - 1) + cch + 1])
            v.tensor_copy(lnt[:, 0, 0:384], zeros2[:, 0:384])
            for cch in range(2):
                hcol = 4 * (m - 1) + 2 + cch
                v.tensor_add(hins[:, 2 * m + cch:2 * m + cch + 1],
                             stt[:, cch:cch + 1], exch[:, hcol:hcol + 1])
            v.tensor_copy(lnt[:, 0, 0:384], zeros2[:, 0:384])
        v.tensor_mul(stt[:, 0:8], hins[:, 0:8], ohp[:, 0:8])
        v.tensor_copy(lnt[:, 0, 0:384], zeros2[:, 0:384])
        v.reduce_sum(hins[:, 10:12], stt[:, 0:8].rearrange("p (m c) -> p c m", c=2),
                     axis=AX)
        v.tensor_copy(lnt[:, 0, 0:384], zeros2[:, 0:384])
        for a in range(2):
            v.scalar_tensor_tensor(out=sD[:, a, :], in0=sC[:, a, :],
                                   scalar=hins[:, 10 + a:11 + a], in1=sD[:, a, :],
                                   op0=AL.mult, op1=AL.add)
        if build.dbg_layer == (l, "exch"):
            v.tensor_copy(lnt[:, 0, 0:16], exch[:, 0:16])
            v.tensor_copy(lnt[:, 0, 16:28], hins[:, 0:12])
            v.tensor_copy(lnt[:, 0, 28:32], sm4[:, 0:4])
            v.memset(lnt[:, 0, 32:1024], 0.0)
            BAR()
            for mm2 in range(16):
                sdma(out.rearrange("(m p) c -> p m c", p=128)[:, mm2:mm2 + 1, :],
                     lnt[:, 0:1, :])
            swait()
            BAR()
            build.dbg_early[0] = True
            return
        v.tensor_copy(hsb[:, 0, :], sD[:, 0, :])
        v.tensor_copy(hsb[:, 1, :], sD[:, 1, :])
        for a in range(2):
            r = v.tensor_scalar(stb[:, a, :], zeros2[:, 0:128], hins[:, 8 + a:9 + a],
                                None, op0=AL.add)
        eVT.inc(r)
        te.wait_ge(eVT.sem, eVT.n)

    build.dbg_layer = (0, "raw") if dbg == "raw1" else (
        (0, "exch") if dbg == "exch" else (
            (1, "exch") if dbg == "exch2" else None))
    build.dbg_out = out
    build.dbg_early = [False]
    ssm_layer(0, bT)

    if build.dbg_early[0]:
        return nc

    if dbg in ("hs1", "dec1", "drv1", "cum1"):
        buf = {"hs1": sD, "dec1": sA, "drv1": sB, "cum1": sC}[dbg]
        for a in range(2):
            for h in range(2):
                sdma(out.rearrange("(z p) c -> p z c", p=128)[:, 2 * a + h:2 * a + h + 1, :],
                     buf[:, a, h * 1024:(h + 1) * 1024].rearrange("p (o c) -> p o c", o=1))
        for z in range(4, 16):
            sdma(out.rearrange("(z p) c -> p z c", p=128)[:, z:z + 1, :],
                 sD[:, 0, 0:1024].rearrange("p (o c) -> p o c", o=1))
        swait()
        BAR()
        return nc

    te.wait_ge(qg, c_wsv)
    for j in range(8):
        for a in range(2):
            te.matmul(pX[:, 0:1], wsv[:, 0, a, j * 128:(j + 1) * 128],
                      stb[:, a, 0:1], start=(a == 0), stop=False)
        te.matmul(pX[:, 0:1], bbvb[0:1, 0, j * 128:(j + 1) * 128], ones1b[0:1, 0:1],
                  start=False, stop=True)
        BAR()
        v.tensor_copy(ctTb[:, j:j + 1], pX[:, 0:1])
        BAR()
    for (wv, bv) in ((wdv, bdv), (wiv, biv)):
        for a in range(2):
            for j in range(8):
                te.matmul(pX[:, 1:2], wv[:, 1, j, a * 128:(a + 1) * 128],
                          ctTb[:, j:j + 1], start=(j == 0), stop=(j == 7))
            BAR()
            v.tensor_add(bv[:, 1, a:a + 1], bv[:, 1, a:a + 1], pX[:, 1:2])
            BAR()
    te.wait_ge(qg, c_wov)
    y1_cnt = []
    for j in range(8):
        for th in range(2):
            it = 2 * j + th
            py = pyA if it % 2 == 0 else pyB
            if it >= 2:
                te.wait_ge(eVT.sem, y1_cnt[it - 2])
            r = None
            for a in range(2):
                r = mm(py, wov[:, 0, a, j * 128:(j + 1) * 128],
                       hsb[:, a, th * 1024:(th + 1) * 1024], (a == 0), (a == 1), 1024)
            eTV.inc(r)
            v.wait_ge(eTV.sem, eTV.n)
            r = v.tensor_copy(bT[:, j, th * 1024:(th + 1) * 1024], py[:, :])
            eVT.inc(r)
            y1_cnt.append(eVT.n)
    BAR()

    ssm_layer(1, bT)
    if build.dbg_early[0]:
        return nc

    for a in range(2):
        mm(pyA, stb[:, a, :], wsv[:, 1, a, :], (a == 0), False, C)
    mm(pyA, ones1b[0:1, 0:128], bbvb[0:1, 1, :], False, True, C)
    BAR()
    v.tensor_copy(ct2[:, :], pyA[:, :])
    BAR()

    bTf = bT.rearrange("p a t -> p (a t)")
    ln1_base = (eTV.n, eVT.n, eVS.n, eSV.n, eVX.n, eVD.n)
    xpt_cnt = []
    tr_cnt = []
    for m in range(16):
        py = pyA if m % 2 == 0 else pyB
        b = m % 2
        if m >= 2:
            eVT.wait(te, ln1_base[1] + m - 1)
        r = None
        for a in range(2):
            r = mm(py, hsb[:, a, m * 128:(m + 1) * 128], wov[:, 1, a, :],
                   (a == 0), (a == 1), C)
        eTV.inc(r)
        if m >= 2:
            sy.wait_ge(eVX.sem, ln1_base[4] + m - 1)
        sdma(xpt[:, b:b + 1, :],
             x_tok.rearrange("(m p) c -> p m c", p=128)[:, m:m + 1, :])
        xpt_cnt.append(cs[0])
        eTV.wait(v)
        r = v.tensor_add(lnt[:, b, :], py[:, :], ct2[:, :])
        eVT.inc(r)
        v.wait_ge(qs, xpt_cnt[m])
        r = v.scalar_tensor_tensor(out=lnt[:, b, :], in0=lnt[:, b, :], scalar=1.0,
                                   in1=xpt[:, b, :], op0=AL.mult, op1=AL.add,
                                   accum_out=st1[:, m:m + 1])
        eVX.inc(r)
        v.scalar_tensor_tensor(out=lnsq[:, :], in0=lnt[:, b, :], scalar=1.0,
                               in1=lnt[:, b, :], op0=AL.mult, op1=AL.mult,
                               accum_out=st2[:, m:m + 1])
        if m >= 2:
            eSV.wait(v, ln1_base[3] + 2 * (m - 1))
        r = v.tensor_scalar_mul(st1[:, m:m + 1], st1[:, m:m + 1], 1.0 / C)
        eVS.inc(r)
        eVS.wait(sc)
        r = sc.activation(out=stt[:, 4 + b:5 + b], in_=st1[:, m:m + 1],
                          func=AF.Square)
        eSV.inc(r)
        eSV.wait(v)
        r = v.tensor_scalar(stt[:, b:b + 1], st2[:, m:m + 1], 1.0 / C,
                            stt[:, 4 + b:5 + b], op0=AL.mult, op1=AL.subtract)
        eVS.inc(r)
        eVS.wait(sc)
        r = sc.activation(out=stt[:, 2 + b:3 + b], in_=stt[:, b:b + 1],
                          func=AF.Sqrt, bias=epsv[:, 0:1])
        eSV.inc(r)
        eSV.wait(v)
        v.reciprocal(stt[:, 6 + b:7 + b], stt[:, 2 + b:3 + b])
        v.tensor_copy(lnsq[:, 0:384], zeros2[:, 0:384])
        if m >= 2:
            v.wait_ge(qs, tr_cnt[m - 2])
        r = v.tensor_scalar(x1s[:, b, :], lnt[:, b, :], st1[:, m:m + 1],
                            stt[:, 6 + b:7 + b], op0=AL.subtract, op1=AL.mult)
        eVD.inc(r)
        eVD.wait(sy)
        sdma(ag_x_in.ap().rearrange("(m p) c -> p m c", p=128)[:, m:m + 1, :],
             x1s[:, b:b + 1, :])
        sy.dma_start_transpose(
            bTf[:, m * 1024:(m + 1) * 1024].rearrange("p (a q) -> p a q", a=8),
            x1s[:, b, :].rearrange("p (a q) -> p a q", a=8)
        ).then_inc(qs, 16)
        cs[0] += 16
        tr_cnt.append(cs[0])
    swait()
    BAR()
    gdma(w1v[:, :, :], w1t_in.rearrange("(j p) h -> p j h", p=128))
    gdma(w3v[:, :, :], w3t_in.rearrange("(j p) h -> p j h", p=128))

    if dbg == "x1":
        for m in range(16):
            b = m % 2
            gdma(x1s[:, b:b + 1, :],
                 ag_x_in.ap().rearrange("(m p) c -> p m c", p=128)[:, m:m + 1, :])
            gwait()
            BAR()
            v.tensor_copy(outt[:, 0, :], x1s[:, b, :])
            BAR()
            sdma(out.rearrange("(m p) c -> p m c", p=128)[:, m:m + 1, :],
                 outt[:, 0:1, :])
            swait()
        BAR()
        return nc

    x1Tv = bTf.rearrange("p (m a q) -> p m a q", m=16, a=8)
    te.wait_ge(qg, c_ext)
    for rt in range(16):
        for j in range(8):
            te.matmul(pX[:, :], extv[:, j, 16 - rt:144 - rt],
                      x1Tv[:, :, j, :].rearrange("p m (q r) -> p r m q", r=16)[:, rt, :, :],
                      start=(rt == 0 and j == 0), stop=(rt == 15 and j == 7))
    BAR()
    v.tensor_copy(junk[:, 0:128], pX[:, :])
    BAR()
    sdma(ag_t_in.ap()[:, :], junk[:, 0:128])
    swait()
    BAR()
    coll(GRP8, [ag_t_in.ap().opt()], [ag_t_out.ap().opt()])
    cc_agt = cc[0]
    g.wait_ge(qc, cc[0])
    coll(GRP8, [ag_x_in.ap().opt()], [ag_x_out.ap().opt()])
    sy.wait_ge(qc, cc_agt)
    sdma(Ssc[:, :].rearrange("er (kk c) -> er kk c", c=128),
         ag_t_out.ap().rearrange("(kk er) c -> er kk c", er=128))
    sdma(ohk16[:, :], ohk16_in[:, :])
    sdma(row16[:, :], row16_in[:, :])
    swait()
    BAR()
    if dbg == "bis":
        v.tensor_copy(lnt[:, 0, 233:361], Ssc[:, 0:128])
        v.reduce_max(lnt[:, 0, 361:362], Ssc[:, :], axis=AX)
        v.reduce_max(lnt[:, 0, 362:363], Ssc[:, 0:128], axis=AX)
        BAR()

    v.tensor_single_scalar(bisI[:, :], iop[:, :], 15.0, op=AL.bitwise_and)
    BAR()
    v.tensor_copy(stt[:, 0:1], bisI[:, :])
    BAR()
    v.tensor_scalar(vid[:, :], iocf[:, :], 16.0, stt[:, 0:1], op0=AL.mult, op1=AL.add)
    v.reduce_max(bis[:, 0:1], Ssc[:, :], axis=AX)
    v.tensor_reduce(bis[:, 1:2], Ssc[:, :], axis=AX, op=AL.min, negate=True)
    BAR()
    if dbg == "bis":
        v.tensor_copy(lnt[:, 0, 100:102], bis[:, 0:2])
        BAR()
    te.transpose(pX[0:2, :], bis[:, 0:2], idT[:, :])
    BAR()
    v.tensor_copy(scrG[0:2, 0:128], pX[0:2, 0:128])
    BAR()
    if dbg == "bis":
        v.tensor_copy(lnt[0:2, 0, 102:230], scrG[0:2, 0:128])
        BAR()
    v.reduce_max(scr8[0:2, 0:1], scrG[0:2, 0:128], axis=AX)
    BAR()
    te.transpose(pX[0:1, 0:2], scr8[0:2, 0:1], idT[0:2, 0:2])
    BAR()
    v.tensor_copy(stt[0:1, 0:2], pX[0:1, 0:2])
    BAR()
    if dbg == "bis":
        v.tensor_copy(lnt[0:2, 0, 230:231], scr8[0:2, 0:1])
        v.tensor_copy(lnt[0:1, 0, 231:233], stt[0:1, 0:2])
        BAR()
    te.matmul(pSm[:, 0:2], onesf[0:1, 0:128], stt[0:1, 0:2], start=True, stop=True)
    BAR()
    v.tensor_scalar_add(bis[:, 0:1], pSm[:, 0:1], 1.0)
    v.tensor_scalar(bis[:, 1:2], pSm[:, 1:2], -1.0, -1.0, op0=AL.mult, op1=AL.add)
    v.tensor_scalar_mul(bis[:, 4:5], pSm[:, 0:1], -1.0)
    v.tensor_copy(lnt[:, 0, 0:384], zeros2[:, 0:384])
    v.tensor_sub(bis[:, 3:4], bis[:, 0:1], bis[:, 1:2])
    v.tensor_copy(lnt[:, 0, 0:384], zeros2[:, 0:384])
    v.tensor_scalar_mul(bis[:, 3:4], bis[:, 3:4], 0.5)
    v.tensor_copy(lnt[:, 0, 0:384], zeros2[:, 0:384])
    if dbg == "bis":
        v.tensor_copy(lnt[:, 0, 0:8], bis[:, 0:8])
        BAR()
    for it in range(NBIS if dbg != "bis" else 1):
        v.tensor_scalar(junk[:, :], Ssc[:, :], bis[:, 1:2], bis[:, 3:4],
                        op0=AL.subtract, op1=AL.is_ge)
        r = v.reduce_sum(cnts[:, :], junk[:, :], axis=AX)
        eVT.inc(r)
        te.wait_ge(eVT.sem, eVT.n)
        r = te.matmul(pSm[:, 0:1], bd16[:, :], cnts[:, :], start=True, stop=True)
        eTV.inc(r)
        v.wait_ge(eTV.sem, eTV.n)
        r = v.tensor_scalar(bis[:, 6:7], pSm[:, 0:1], 2048.0, bis[:, 3:4],
                            op0=AL.is_ge, op1=AL.mult)
        eVT.inc(r)
        v.tensor_copy(lnt[:, 0, 0:384], zeros2[:, 0:384])
        v.tensor_add(bis[:, 1:2], bis[:, 1:2], bis[:, 6:7])
        v.tensor_scalar_mul(bis[:, 3:4], bis[:, 3:4], 0.5)
        v.tensor_copy(lnt[:, 0, 0:384], zeros2[:, 0:384])
    if dbg == "bis":
        v.tensor_copy(lnt[:, 0, 8:9], cnts[:, :])
        v.tensor_copy(lnt[:, 0, 9:10], pSm[:, 0:1])
        v.tensor_copy(lnt[:, 0, 10:11], bisI[:, :])
        v.tensor_copy(lnt[:, 0, 11:14], bis[:, 0:3])
        v.tensor_copy(lnt[:, 0, 14:142], Ssc[:, 0:128])
        BAR()
        for m in range(16):
            sdma(out.rearrange("(m p) c -> p m c", p=128)[:, m:m + 1, :],
                 lnt[:, 0:1, :])
        swait()
        BAR()
        return nc
    v.tensor_scalar(msI[:, :], Ssc[:, :], bis[:, 1:2], None, op0=AL.is_ge)
    v.tensor_scalar(mskF[:, :], Ssc[:, :], bis[:, 1:2], None, op0=AL.is_ge)
    BAR()
    v.reduce_sum(cnts[:, :], mskF[:, :], axis=AX)
    sc.activation(out=junk[:, :], in_=Ssc[:, :], func=AF.Exp, bias=bis[:, 4:5])
    BAR()
    v.tensor_mul(junk[:, :], junk[:, :], mskF[:, :])
    BAR()
    v.reduce_sum(st1[:, 0:1], junk[:, :], axis=AX)
    BAR()
    te.transpose(pX[0:1, :], st1[:, 0:1], idT[:, :])
    BAR()
    v.tensor_copy(orow[0:1, :], pX[0:1, 0:128])
    BAR()
    v.tensor_mul(orow[0:1, :], orow[0:1, :], ohk16[0:1, :])
    BAR()
    v.reduce_sum(stt[0:1, 4:5], orow[0:1, :], axis=AX)
    BAR()
    te.matmul(pSm[:, 1:2], onesf[0:1, 0:128], stt[0:1, 4:5], start=True, stop=True)
    BAR()
    v.reciprocal(bis[:, 5:6], pSm[:, 1:2])
    v.scalar_tensor_tensor(out=Vb[:, :], in0=vid[:, :], scalar=1.0, in1=mskF[:, :],
                           op0=AL.add, op1=AL.mult)
    v.tensor_tensor_scan(out=junk[:, :], data0=mskF[:, :], data1=zeros2[:, 0:1024],
                         initial=0.0, op0=AL.add, op1=AL.add)
    BAR()
    v.tensor_scalar_add(Vb[:, :], Vb[:, :], -1.0)
    v.tensor_sub(junk[:, :], iocf[:, :], junk[:, :])
    BAR()
    v.tensor_scalar_add(junk[:, :], junk[:, :], 1.0)
    BAR()
    v.tensor_mul(junk[:, :], junk[:, :], mskF[:, :])
    BAR()
    v.tensor_copy(Db[:, :], junk[:, :])
    BAR()

    def butterfly(Vb_, Vb2_, Db_, Db2_, Sb_=None, Sb2_=None):
        Vcur, Vnx, Dcur, Dnx = Vb_, Vb2_, Db_, Db2_
        Scur, Snx = Sb_, Sb2_
        for b in range(10):
            s = 1 << b
            v.tensor_single_scalar(msI[:, :], Dcur[:, :], float(s), op=AL.bitwise_and)
            v.tensor_single_scalar(DmS[:, :], Dcur[:, :], float(s), op=AL.subtract)
            v.select(Vnx[:, 0:1024 - s], msI[:, s:], Vcur[:, s:], Vcur[:, 0:1024 - s])
            v.tensor_copy(Vnx[:, 1024 - s:], Vcur[:, 1024 - s:])
            if Scur is not None:
                v.select(Snx[:, 0:1024 - s], msI[:, s:], Scur[:, s:],
                         Scur[:, 0:1024 - s])
                v.tensor_copy(Snx[:, 1024 - s:], Scur[:, 1024 - s:])
                Scur, Snx = Snx, Scur
            v.select(Dnx[:, 0:1024 - s], msI[:, s:], DmS[:, s:], Dcur[:, 0:1024 - s])
            v.tensor_copy(Dnx[:, 1024 - s:], Dcur[:, 1024 - s:])
            Vcur, Vnx, Dcur, Dnx = Vnx, Vcur, Dnx, Dcur
        return Vcur, Scur

    v.tensor_copy(Sb[:, :], Ssc[:, :])
    BAR()
    CT, SCb = butterfly(Vb, Vb2, Db, Db2, Sb, Sb2)
    te.transpose(pX[0:1, :], cnts[:, :], idT[:, :])
    BAR()
    v.tensor_copy(orow[0:1, :], pX[0:1, 0:128])
    BAR()
    sdma(orow_d.ap()[:, :], orow[0:1, :])
    swait()
    BAR()
    gdma(orow8[:, :], orow_d.ap().rearrange("o (e r) -> (o e) r", e=8))
    gwait()
    BAR()
    v.tensor_tensor_scan(out=orow8[:, :], data0=orow8[:, :], data1=zeros2[0:8, 0:16],
                         initial=0.0, op0=AL.add, op1=AL.add)
    BAR()
    gdma(exsb[0:8, 0:16], orow_d.ap().rearrange("o (e r) -> (o e) r", e=8))
    gwait()
    BAR()
    v.tensor_sub(orow8[:, :], orow8[:, :], exsb[0:8, 0:16])
    BAR()
    gdma(orow_d.ap().rearrange("o (e r) -> (o e) r", e=8), orow8[:, :])
    gwait()
    BAR()
    sdma(orow[0:1, :], orow_d.ap()[:, :])
    swait()
    BAR()
    v.tensor_mul(exsb[0:1, 0:128], orow[0:1, :], ohk16[0:1, :])
    BAR()
    v.reduce_sum(wraw[0:1, 0:16], exsb[0:1, 0:128].rearrange("o (e r) -> o r e", e=8),
                 axis=AX)
    BAR()
    v.tensor_copy(oint[0:1, :], wraw[0:1, 0:16])
    te.transpose(pX[:, 0:1], orow[0:1, :], idT[0:1, 0:1])
    BAR()
    v.tensor_copy(obase[:, :], pX[:, 0:1])
    v.tensor_copy(Vc[:, :], CT[:, :])
    BAR()
    sdma(vc_d.ap()[:, :], Vc[:, :])
    sdma(sc_d.ap()[:, :], SCb[:, :])
    swait()
    BAR()
    rv = nc.values_load(row16[0:1, 0:1], engines=[mybir.EngineType.SP])
    sy.dma_start(out=cc16[:, :], in_=vc_d.ap()[bass.ds(rv, 16), :]).then_inc(qh, 16)
    ch[0] += 16
    sy.dma_start(out=cs16[:, :], in_=sc_d.ap()[bass.ds(rv, 16), :]).then_inc(qh2, 16)
    ch2[0] += 16
    sy.wait_ge(qh, ch[0])
    sy.wait_ge(qh2, ch2[0])
    sc.wait_ge(qh, ch[0])
    sc.wait_ge(qh2, ch2[0])
    for r in range(16):
        eng = mybir.EngineType.SP if r < 8 else mybir.EngineType.Activation
        eng2 = mybir.EngineType.Activation if r < 8 else mybir.EngineType.SP
        ov = nc.values_load(oint[0:1, r:r + 1], engines=[eng])
        ov2 = nc.values_load(oint[0:1, r:r + 1], engines=[eng2])
        issuer = sy if r < 8 else sc
        issuer2 = sc if r < 8 else sy
        issuer.wait_ge(qh, ch[0])
        issuer.dma_start(out=l_list.ap()[0:1, bass.ds(ov, 1024)],
                         in_=cc16[r:r + 1, :]).then_inc(qh, 16)
        ch[0] += 16
        issuer2.wait_ge(qh2, ch2[0])
        issuer2.dma_start(out=s_list.ap()[0:1, bass.ds(ov2, 1024)],
                          in_=cs16[r:r + 1, :]).then_inc(qh2, 16)
        ch2[0] += 16
    sy.wait_ge(qh, ch[0])
    g.wait_ge(qh, ch[0])
    g.wait_ge(qh2, ch2[0])
    gdma(FL[:, :], l_list.ap()[0:1, 0:2048].rearrange("o (p q) -> (o p) q", q=16))
    gdma(wraw[:, :], s_list.ap()[0:1, 0:2048].rearrange("o (p q) -> (o p) q", q=16))
    gwait()
    g.wait_ge(qc, cc[0])
    BAR()

    if dbg in ("sel", "sel2"):
        v.tensor_copy(lnt[:, 0, 0:16], FL[:, :])
        v.tensor_copy(lnt[:, 0, 16:17], cnts[:, :])
        v.tensor_copy(lnt[:, 0, 17:18], bis[:, 1:2])
        v.tensor_copy(lnt[:, 0, 18:19], obase[:, :])
        v.tensor_copy(lnt[:, 0, 19:20], bis[:, 5:6])
        v.memset(lnt[:, 0, 20:1024], 0.0)
        BAR()
        for m in range(16):
            sdma(out.rearrange("(m p) c -> p m c", p=128)[:, m:m + 1, :],
                 lnt[:, 0:1, :])
        swait()
        BAR()
        return nc

    sc.activation(out=wnrm[:, :], in_=wraw[:, :], func=AF.Exp, bias=bis[:, 4:5])
    BAR()
    v.tensor_scalar(wnrm[:, :], wnrm[:, :], bis[:, 5:6], None, op0=AL.mult)
    BAR()

    xgB = [xg4, xg4b]
    ffn_gq = []
    ffn_tq = []
    ydma_cnt = []
    for m4 in range(4):
        g.indirect_dma_start(
            out=xgB[0][:, m4, :], out_offset=None, in_=ag_x_out.ap(),
            in_offset=IndirectOffsetOnAxis(FL[:, m4:m4 + 1], axis=0),
        ).then_inc(qg, 16)
        cg[0] += 16
    ffn_gq.append(cg[0])
    tv0 = eTV.n
    vt0 = eVT.n
    ts0 = eTS.n
    xgt_free = []
    for G in range(4):
        b = G % 2
        sy.wait_ge(qg, ffn_gq[G])
        if G >= 1:
            sy.wait_ge(eTV.sem, xgt_free[G - 1])
        for m4 in range(4):
            sy.dma_start_transpose(
                xgt[:, m4, :, :],
                xgB[b][:, m4, :].rearrange("p (a q) -> p a q", a=8)
            ).then_inc(qs, 16)
            cs[0] += 16
        ffn_tq.append(cs[0])
        if G < 3:
            if G >= 1:
                g.wait_ge(qs, ffn_tq[G - 1])
            for m4 in range(4):
                gi = 4 * (G + 1) + m4
                g.indirect_dma_start(
                    out=xgB[(G + 1) % 2][:, m4, :], out_offset=None,
                    in_=ag_x_out.ap(),
                    in_offset=IndirectOffsetOnAxis(FL[:, gi:gi + 1], axis=0),
                ).then_inc(qg, 16)
                cg[0] += 16
            ffn_gq.append(cg[0])
        te.wait_ge(qs, ffn_tq[G])
        if G >= 1:
            te.wait_ge(eVD.sem, eVD.n)
        last_w3 = None
        for hc in range(16):
            hcg = 16 * G + hc
            pA = (pjA[:, :], pyA[:, 0:512], pyB[:, 0:512])[hc % 3]
            pB = (pjB[:, :], pyA[:, 512:1024], pyB[:, 512:1024])[hc % 3]
            if hcg >= 3:
                te.wait_ge(eVT.sem, vt0 + hcg - 2)
            r = None
            for a in range(8):
                r = te.matmul(pA, w1v[:, a, hc * 128:(hc + 1) * 128],
                              xgt[:, :, a, :], start=(a == 0), stop=(a == 7))
            eTS.inc(r)
            ts_cnt = eTS.n
            for a in range(8):
                r = te.matmul(pB, w3v[:, a, hc * 128:(hc + 1) * 128],
                              xgt[:, :, a, :], start=(a == 0), stop=(a == 7))
            eTV.inc(r)
            last_w3 = eTV.n
            sc.wait_ge(eTS.sem, ts_cnt)
            if hcg >= 1:
                sc.wait_ge(eVT.sem, vt0 + hcg)
            r = sc.activation(out=scr512[:, :], in_=pA, func=AF.Silu)
            eSV.inc(r)
            eSV.wait(v)
            v.wait_ge(eTV.sem, last_w3)
            r = v.tensor_mul(hpr[:, hc, :], scr512[:, :], pB)
            eVT.inc(r)
        xgt_free.append(last_w3)
        for d in range(4):
            py = pyB if d % 2 == 0 else pyA
            gi = 4 * G + d
            if G == 0 and d == 0:
                te.wait_ge(qg, c_w2v)
            if d <= 1:
                te.wait_ge(eVT.sem, vt0 + 16 * G + 16)
            if 4 * G + d >= 2:
                te.wait_ge(eVD.sem, eVD.n - 2 + 1)
            r = None
            for hc in range(16):
                r = mm(py, hpr[:, hc, d * 128:(d + 1) * 128], w2v[:, hc, :],
                       (hc == 0), (hc == 15), C)
            eTV.inc(r)
            v.wait_ge(eTV.sem, eTV.n)
            if len(ydma_cnt) >= 2:
                v.wait_ge(qs, ydma_cnt[-2])
            r = v.tensor_scalar(yst[:, d % 2, :], py[:, :], wnrm[:, gi:gi + 1],
                                None, op0=AL.mult)
            eVD.inc(r)
            eVD.wait(sy)
            sdma(ag_y_in.ap()[128 * gi:128 * gi + 128, :], yst[:, d % 2, :])
            ydma_cnt.append(cs[0])
        g.wait_ge(qs, cs[0])
        coll(GRP8,
             [ag_y_in.ap()[512 * G:512 * (G + 1), :].opt()],
             [y_all.ap()[4096 * G:4096 * (G + 1), :].opt()])
    swait()
    BAR()

    v.tensor_scalar(mskF[:, :], iocf[:, :], cnts[:, 0:1], None, op0=AL.is_lt)
    BAR()
    v.scalar_tensor_tensor(out=CT[:, :], in0=CT[:, :], scalar=1.0, in1=mskF[:, :],
                           op0=AL.add, op1=AL.mult)
    BAR()
    v.tensor_scalar_add(CT[:, :], CT[:, :], -1.0)
    BAR()
    v.tensor_scalar(junk[:, :], iocf[:, :], obase[:, 0:1], None, op0=AL.add)
    BAR()
    v.scalar_tensor_tensor(out=junk[:, :], in0=junk[:, :], scalar=2048.0,
                           in1=CT[:, :], op0=AL.mult, op1=AL.add)
    BAR()
    v.tensor_scalar(junk[:, :], junk[:, :], kv0[:, 0:1], None, op0=AL.subtract)
    BAR()
    v.tensor_scalar(msI[:, :], CT[:, :], kv0[:, 0:1], None, op0=AL.is_ge)
    v.tensor_scalar(DmS[:, :], CT[:, :], kv0[:, 1:2], None, op0=AL.is_lt)
    BAR()
    v.tensor_tensor(msI[:, :], msI[:, :], DmS[:, :], op=AL.bitwise_and)
    BAR()
    v.tensor_copy(mskF[:, :], msI[:, :])
    BAR()
    v.scalar_tensor_tensor(out=Vb2[:, :], in0=junk[:, :], scalar=1.0, in1=mskF[:, :],
                           op0=AL.add, op1=AL.mult)
    v.reduce_sum(cnt2[:, :], mskF[:, :], axis=AX)
    v.tensor_tensor_scan(out=junk[:, :], data0=mskF[:, :], data1=zeros2[:, 0:1024],
                         initial=0.0, op0=AL.add, op1=AL.add)
    BAR()
    v.tensor_scalar_add(Vb2[:, :], Vb2[:, :], -1.0)
    v.tensor_sub(junk[:, :], iocf[:, :], junk[:, :])
    BAR()
    v.tensor_scalar_add(junk[:, :], junk[:, :], 1.0)
    BAR()
    v.tensor_mul(junk[:, :], junk[:, :], mskF[:, :])
    BAR()
    v.tensor_copy(Vb[:, :], Vb2[:, :])
    v.tensor_copy(Db2[:, :], junk[:, :])
    BAR()
    v.tensor_copy(Db[:, :], Db2[:, :])
    BAR()
    CB, _ = butterfly(Vb, Vb2, Db, Db2)
    v.tensor_copy(Db2[:, 0:C2], CB[:, 0:C2])
    BAR()
    v.tensor_single_scalar(srcI[:, :], Db2[:, 0:C2], 11.0, op=AL.logical_shift_right)
    v.tensor_single_scalar(dstI[:, :], Db2[:, 0:C2], 2047.0, op=AL.bitwise_and)
    v.tensor_scalar(msI[:, 0:C2], iocf[:, 0:C2], cnt2[:, 0:1], None, op0=AL.is_lt)
    v.tensor_copy(lnt[:, 0, 0:384], zeros2[:, 0:384])
    v.tensor_single_scalar(DmS[:, 0:C2], srcI[:, :], 15.0, op=AL.bitwise_and)
    v.tensor_single_scalar(DmS[:, C2:2 * C2], srcI[:, :], 4.0,
                           op=AL.logical_shift_right)
    v.tensor_copy(lnt[:, 0, 0:384], zeros2[:, 0:384])
    v.tensor_single_scalar(DmS[:, 2 * C2:3 * C2], DmS[:, 0:C2], 2.0,
                           op=AL.logical_shift_right)
    v.tensor_single_scalar(DmS[:, 0:C2], DmS[:, 0:C2], 3.0, op=AL.bitwise_and)
    v.tensor_copy(lnt[:, 0, 0:384], zeros2[:, 0:384])
    v.tensor_single_scalar(DmS[:, 2 * C2:3 * C2], DmS[:, 2 * C2:3 * C2], 12.0,
                           op=AL.logical_shift_left)
    v.tensor_single_scalar(DmS[:, 0:C2], DmS[:, 0:C2], 7.0,
                           op=AL.logical_shift_left)
    v.tensor_copy(lnt[:, 0, 0:384], zeros2[:, 0:384])
    v.tensor_tensor(srcI[:, :], DmS[:, C2:2 * C2], DmS[:, 0:C2],
                    op=AL.bitwise_or)
    v.tensor_copy(lnt[:, 0, 0:384], zeros2[:, 0:384])
    v.tensor_tensor(srcI[:, :], srcI[:, :], DmS[:, 2 * C2:3 * C2],
                    op=AL.bitwise_or)
    v.tensor_copy(lnt[:, 0, 0:384], zeros2[:, 0:384])
    v.tensor_scalar(srcI[:, :], srcI[:, :], eab[:, 0:1], None, op0=AL.bitwise_or)
    v.tensor_copy(lnt[:, 0, 0:384], zeros2[:, 0:384])
    BAR()
    if dbg == "bwd":
        v.tensor_copy(lnt[:, 0, 67:99], msI[:, 0:C2])
        v.tensor_copy(lnt[:, 0, 99:131], srcI[:, :])
        v.tensor_copy(lnt[:, 0, 131:163], dstI[:, :])
        BAR()
    v.select(DmS[:, 0:C2], msI[:, 0:C2], srcI[:, :], sentA[:, :])
    v.select(DmS[:, C2:2 * C2], msI[:, 0:C2], dstI[:, :], sentB[:, :])
    BAR()
    v.tensor_copy(srcI[:, :], DmS[:, 0:C2])
    v.tensor_copy(dstI[:, :], DmS[:, C2:2 * C2])
    g.wait_ge(qc, cc[0])
    BAR()
    if dbg == "bwd":
        v.tensor_copy(lnt[:, 0, 0:32], srcI[:, :])
        v.tensor_copy(lnt[:, 0, 32:64], dstI[:, :])
        v.tensor_copy(lnt[:, 0, 64:65], cnt2[:, :])
        v.tensor_copy(lnt[:, 0, 65:66], obase[:, :])
        v.tensor_copy(lnt[:, 0, 66:67], cnts[:, :])
        v.memset(lnt[:, 0, 163:1024], 0.0)
        BAR()
        for m in range(16):
            sdma(out.rearrange("(m p) c -> p m c", p=128)[:, m:m + 1, :],
                 lnt[:, 0:1, :])
        swait()
        BAR()
        return nc
    sdma(di_d.ap()[:, :], dstI[:, :])
    swait()
    BAR()
    sdma(dstE[:, :, :], di_d.ap().rearrange("(e r) gi -> r e gi", e=8))
    swait()
    BAR()
    for gi in range(C2):
        g.indirect_dma_start(
            out=Gbig[:, gi, :], out_offset=None,
            in_=y_all.ap(),
            in_offset=IndirectOffsetOnAxis(srcI[:, gi:gi + 1], axis=0),
        ).then_inc(qg, 16)
        cg[0] += 16
    gwait()
    BAR()
    for e in range(8):
        for gi in range(C2):
            g.indirect_dma_start(
                out=acc_d.ap(),
                out_offset=IndirectOffsetOnAxis(dstE[0:16, e, gi:gi + 1], axis=0),
                in_=Gbig[16 * e:16 * e + 16, gi, :], in_offset=None,
                compute_op=AL.add,
            ).then_inc(qh, 16)
            ch[0] += 16
        g.wait_ge(qh, ch[0])
    BAR()

    ln2_base = (eVX.n, None, eSV.n, eVD.n)
    ld_cnt = []

    def ln2_load(mm):
        sdma(acct4[:, mm % 4:mm % 4 + 1, :],
             acc_d.ap()[0:T, :].rearrange("(m p) c -> p m c", p=128)[:, mm:mm + 1, :])
        sdma(x1t4[:, mm % 4:mm % 4 + 1, :],
             ag_x_in.ap().rearrange("(m p) c -> p m c", p=128)[:, mm:mm + 1, :])
        ld_cnt.append(cs[0])

    for mm in range(3):
        ln2_load(mm)
    for m in range(16):
        b4 = m % 4
        b = m % 2
        if m + 3 <= 15:
            if m >= 1:
                sy.wait_ge(eVX.sem, ln2_base[0] + m)
            ln2_load(m + 3)
        v.wait_ge(qs, ld_cnt[m])
        r = v.scalar_tensor_tensor(out=lnt[:, b, :], in0=x1t4[:, b4, :], scalar=1.0,
                                   in1=acct4[:, b4, :], op0=AL.mult, op1=AL.add,
                                   accum_out=st1[:, m:m + 1])
        eVX.inc(r)
        v.scalar_tensor_tensor(out=lnsq[:, :], in0=lnt[:, b, :], scalar=1.0,
                               in1=lnt[:, b, :], op0=AL.mult, op1=AL.mult,
                               accum_out=st2[:, m:m + 1])
        if m >= 2:
            eSV.wait(v, ln2_base[2] + 2 * (m - 1))
        r = v.tensor_scalar_mul(st1[:, m:m + 1], st1[:, m:m + 1], 1.0 / C)
        eVS.inc(r)
        eVS.wait(sc)
        r = sc.activation(out=stt[:, 4 + b:5 + b], in_=st1[:, m:m + 1],
                          func=AF.Square)
        eSV.inc(r)
        eSV.wait(v)
        r = v.tensor_scalar(stt[:, b:b + 1], st2[:, m:m + 1], 1.0 / C,
                            stt[:, 4 + b:5 + b], op0=AL.mult, op1=AL.subtract)
        eVS.inc(r)
        eVS.wait(sc)
        r = sc.activation(out=stt[:, 2 + b:3 + b], in_=stt[:, b:b + 1],
                          func=AF.Sqrt, bias=epsv[:, 0:1])
        eSV.inc(r)
        eSV.wait(v)
        v.reciprocal(stt[:, 6 + b:7 + b], stt[:, 2 + b:3 + b])
        v.tensor_copy(lnsq[:, 0:384], zeros2[:, 0:384])
        if m >= 2:
            v.wait_ge(qo, 16 * (m - 1))
        r = v.tensor_scalar(outt[:, b, :], lnt[:, b, :], st1[:, m:m + 1],
                            stt[:, 6 + b:7 + b], op0=AL.subtract, op1=AL.mult)
        eVD.inc(r)
        eVD.wait(sy)
        sy.dma_start(out=out.rearrange("(m p) c -> p m c", p=128)[:, m:m + 1, :],
                     in_=outt[:, b:b + 1, :]).then_inc(qo, 16)
        co[0] += 16
    sy.wait_ge(qo, co[0])
    BAR()
    return nc


def _host_prep(inputs):
    x = np.asarray(inputs["L0_attn_shot_embeddings"], np.float32).reshape(16384, C)
    Wd = np.ascontiguousarray(np.asarray(inputs["L0_attn_Wd"], np.float32))
    Wi = np.ascontiguousarray(np.asarray(inputs["L0_attn_Wi"], np.float32))
    Wo = np.ascontiguousarray(np.asarray(inputs["L0_attn_Wo"], np.float32))
    Ws = np.ascontiguousarray(np.asarray(inputs["L0_attn_Ws"], np.float32))
    bd = np.ascontiguousarray(np.asarray(inputs["L0_attn_bd"], np.float32))
    bi = np.ascontiguousarray(np.asarray(inputs["L0_attn_bi"], np.float32))
    alog = np.ascontiguousarray(np.asarray(inputs["L0_attn_A_log"], np.float32))
    bo = np.ascontiguousarray(np.asarray(inputs["L0_attn_bo"], np.float32))
    bs = np.ascontiguousarray(np.asarray(inputs["L0_attn_bs"], np.float32))
    Wg = np.asarray(inputs["L0_ffn_Wg"], np.float32)
    W1 = np.asarray(inputs["L0_ffn_W1"], np.float32)
    W2 = np.asarray(inputs["L0_ffn_W2"], np.float32)
    W3 = np.asarray(inputs["L0_ffn_W3"], np.float32)

    ext = np.zeros((8, 128, 144), np.float32)
    for e in range(8):
        for j in range(8):
            ext[j, :, 16 + 16 * e] = Wg[e, 128 * j:128 * (j + 1)]

    maps = []
    for k in range(NCORE):
        xk = np.ascontiguousarray(x[2048 * k:2048 * (k + 1)])
        kv = np.zeros((128, 2), np.float32)
        kv[:, 0] = 2048.0 * k
        kv[:, 1] = 2048.0 * (k + 1)
        eabv = np.zeros((128, 1), np.int32)
        for p in range(128):
            eabv[p, 0] = 512 * (p // 16)
        ohk16v = np.zeros((1, 128), np.float32)
        ohk16v[0, 16 * k:16 * (k + 1)] = 1.0
        row16v = np.arange(16 * k, 16 * k + 16, dtype=np.int32).reshape(1, 16)
        rot = np.array([[(k + e) % 8 for e in range(8)]], np.int32)
        ohpv = np.zeros((128, 8), np.float32)
        pos = k % 4
        ohpv[:, 2 * pos] = 1.0
        ohpv[:, 2 * pos + 1] = 1.0
        ohkv = np.zeros((128, 8), np.float32)
        ohkv[:, k] = 1.0
        maps.append({
            "x_tok": xk,
            "x_feat": np.ascontiguousarray(xk.T),
            "wd": Wd, "wi": Wi, "wo": Wo, "ws": Ws,
            "bdd": np.ascontiguousarray(bd.reshape(2, 2, 128).transpose(2, 0, 1)),
            "bii": np.ascontiguousarray(bi.reshape(2, 2, 128).transpose(2, 0, 1)),
            "alog": np.ascontiguousarray(alog.reshape(2, 2, 128).transpose(2, 0, 1)),
            "bo": bo, "bss": bs,
            "ext": ext,
            "wgt": np.ascontiguousarray(Wg.T),
            "w1t": np.ascontiguousarray(W1[k].T),
            "w3t": np.ascontiguousarray(W3[k].T),
            "w2t": np.ascontiguousarray(W2[k].T),
            "kvi": kv, "eabi": eabv, "rot": rot, "ohpi": ohpv, "ohki": ohkv,
            "ohk16": ohk16v, "row16": row16v,
        })
    return maps


def kernel(**inputs):
    maps = _host_prep(inputs)
    nc = build(dbg=DBG_STAGE)
    res = run_bass_kernel_spmd(nc, maps, core_ids=list(range(NCORE)))
    outs = np.concatenate([np.asarray(r["out"]) for r in res.results], axis=0)
    return outs.reshape(2, 8192, 1024).astype(np.float32)


# version 09
# speedup vs baseline: 1.3240x; 1.0063x over previous
import numpy as np
import ml_dtypes
import concourse.bass as bass
import concourse.mybir as mybir
from concourse.bass import IndirectOffsetOnAxis
from concourse.bass_utils import run_bass_kernel_spmd

F32 = mybir.dt.float32
BF16 = mybir.dt.bfloat16
I32 = mybir.dt.int32
AX = mybir.AxisListType.X
AF = mybir.ActivationFunctionType
AL = mybir.AluOpType

NCORE = 8
T = 2048
C = 1024
NSD = 256
HF = 2048
CAP = 2048
C2 = 32
EPS = 1e-5
NBIS = 30

DBG_STAGE = None


def build(dbg=None):
    nc = bass.Bass("TRN2", target_bir_lowering=False)

    def din(name, shape, dt=F32):
        return nc.dram_tensor(name, shape, dt, kind="ExternalInput").ap()

    x_tok = din("x_tok", [T, C])
    x_feat = din("x_feat", [C, T])
    wd_in = din("wd", [2, C, NSD])
    wi_in = din("wi", [2, C, NSD])
    wo_in = din("wo", [2, NSD, C])
    ws_in = din("ws", [2, NSD, C])
    bd_in = din("bdd", [128, 2, 2])
    bi_in = din("bii", [128, 2, 2])
    alog_in = din("alog", [128, 2, 2])
    bo_in = din("bo", [2, C])
    bs_in = din("bss", [2, C])
    ext_in = din("ext", [8, 128, 144])
    wgt_in = din("wgt", [C, 8])
    w1t_in = din("w1t", [C, HF])
    w3t_in = din("w3t", [C, HF])
    w2t_in = din("w2t", [HF, C])
    kv_in = din("kvi", [128, 2])
    eab_in = din("eabi", [128, 1], I32)
    rot_in = din("rot", [1, 8], I32)
    ohp_in = din("ohpi", [128, 8])
    ohk_in = din("ohki", [128, 8])
    ohk16_in = din("ohk16", [1, 128])
    row16_in = din("row16", [1, 16], I32)

    out = nc.dram_tensor("out", [T, C], F32, kind="ExternalOutput").ap()

    def dint(name, shape, dt):
        return nc.dram_tensor(name, shape, dt, kind="Internal")

    ag_s_in = [dint(f"ag_s_in{l}", [4, 128], F32) for l in range(2)]
    ag_s_out = [dint(f"ag_s_out{l}", [16, 128], F32) for l in range(2)]
    ag_t_in = dint("ag_t_in", [128, 128], F32)
    ag_t_out = dint("ag_t_out", [1024, 128], F32)
    ag_x_in = dint("ag_x_in", [T, C], BF16)
    ag_x_out = dint("ag_x_out", [NCORE * T, C], BF16)
    ag_y_in = dint("ag_y_in", [CAP, C], BF16)
    y_all = dint("y_all", [NCORE * CAP + 1, C], BF16)
    l_list = dint("l_list", [1, CAP + 1152], I32)
    acc_d = dint("acc_d", [T + 1, C], BF16)
    orow_d = dint("orow_d", [1, 128], F32)
    vc_d = dint("vc_d", [128, 1024], I32)
    sc_d = dint("sc_d", [128, 1024], F32)
    s_list = dint("s_list", [1, CAP + 1152], F32)
    di_d = dint("di_d", [128, C2], I32)

    sbt = nc.alloc_sbuf_tensor
    sat = nc.alloc_sbuf_tensor_at

    def addr(h):
        return nc.lookup_mloc(h).addr

    bT = sbt("bT", [128, 8, T], BF16)
    wdv = sbt("wdv", [128, 2, 8, NSD], BF16)
    wiv = sbt("wiv", [128, 2, 8, NSD], BF16)
    wov = sbt("wov", [128, 2, 2, C], BF16)
    wsv = sbt("wsv", [128, 2, 2, C], BF16)
    sA = sbt("sA", [128, 2, T], F32)
    sB = sbt("sB", [128, 2, T], F32)
    sC = sbt("sC", [128, 2, T], F32)
    sD = sbt("sD", [128, 2, T], F32)
    A_bt = addr(bT)
    A_w = addr(wdv)
    A_s = addr(sA)

    w2v = sbt("w2v", [128, 16, C], BF16)
    zeros2 = sbt("zeros2", [128, 2048], F32)
    lnt = sbt("lnt", [128, 2, C], F32)
    bbvb = sbt("bbvb", [1, 2, C], BF16)
    ct2 = sbt("ct2", [128, C], F32)
    x1s = sbt("x1s", [128, 2, C], BF16)
    rowT = sbt("rowT", [128, 16], I32)
    colT = sbt("colT", [128, 16], F32)
    idT = sbt("idT", [128, 128], F32)
    extv = sbt("extv", [128, 8, 144], BF16)
    wgv = sbt("wgv", [128, 8, 8], BF16)
    bd16 = sbt("bd16", [128, 128], F32)
    iocf = sbt("iocf", [128, 1024], F32)
    mskF = sbt("mskF", [128, 1024], F32)
    msI = sbt("msI", [128, 1024], I32)
    bdv = sbt("bdv", [128, 2, 2], F32)
    biv = sbt("biv", [128, 2, 2], F32)
    spav = sbt("spav", [128, 2, 2], F32)
    kv0 = sbt("kv0", [128, 2], F32)
    eab = sbt("eab", [128, 1], I32)
    ohp = sbt("ohp", [128, 8], F32)
    ohk = sbt("ohk", [128, 8], F32)
    exsb = sbt("exsb", [16, 128], F32)
    exch = sbt("exch", [128, 16], F32)
    sm4 = sbt("sm4", [128, 4], F32)
    hins = sbt("hins", [128, 12], F32)
    stb = sbt("stb", [128, 2, 128], BF16)
    bis = sbt("bis", [128, 8], F32)
    bisI = sbt("bisI", [128, 1], I32)
    onesf = sbt("onesf", [1, 128], F32)
    ones1b = sbt("ones1b", [1, 128], BF16)
    cnts = sbt("cnts", [128, 1], F32)
    orow = sbt("orow", [1, 128], F32)
    orow8 = sbt("orow8", [8, 16], F32)
    obase = sbt("obase", [128, 1], F32)
    oint = sbt("oint", [1, 16], I32)
    FL = sbt("FL", [128, 16], I32)
    wraw = sbt("wraw", [128, 16], F32)
    wnrm = sbt("wnrm", [128, 16], F32)
    srcI = sbt("srcI", [128, C2], I32)
    dstI = sbt("dstI", [128, C2], I32)
    sentA = sbt("sentA", [128, C2], I32)
    sentB = sbt("sentB", [128, C2], I32)
    cnt2 = sbt("cnt2", [128, 1], F32)
    st1 = sbt("st1", [128, 16], F32)
    st2 = sbt("st2", [128, 16], F32)
    stt = sbt("stt", [128, 8], F32)
    scr8 = sbt("scr8", [128, 8], F32)
    epsv = sbt("epsv", [128, 1], F32)
    onesc = sbt("onesc", [128, 1], F32)
    iop = sbt("iop", [128, 1], I32)
    iopf = sbt("iopf", [128, 1], F32)
    ctTb = sbt("ctTb", [128, 8], BF16)

    KB = 1024
    lnsq = sat("lnsq", [128, C], F32, offset=A_s + 32 * KB)
    Ssc = sat("Ssc", [128, 1024], F32, offset=A_bt + 0)
    vid = sat("vid", [128, 1024], F32, offset=A_bt + 4 * KB)
    junk = sat("junk", [128, 1024], F32, offset=A_bt + 8 * KB)
    Vb = sat("Vb", [128, 1024], F32, offset=A_bt + 12 * KB)
    Vb2 = sat("Vb2", [128, 1024], F32, offset=A_bt + 16 * KB)
    Db = sat("Db", [128, 1024], I32, offset=A_bt + 20 * KB)
    Db2 = sat("Db2", [128, 1024], I32, offset=A_bt + 24 * KB)
    DmS = sat("DmS", [128, 1024], I32, offset=A_bt + 28 * KB)
    yst = sat("yst", [128, 2, C], BF16, offset=A_bt + 0)
    scr512 = sat("scr512", [128, 512], F32, offset=A_bt + 4 * KB)
    scrG = sat("scrG", [128, 512], F32, offset=A_w + 0)
    cc16 = sat("cc16", [16, 1024], I32, offset=A_w + 4 * KB)
    Sb = sat("Sb", [128, 1024], F32, offset=A_w + 8 * KB)
    Sb2 = sat("Sb2", [128, 1024], F32, offset=A_w + 12 * KB)
    cs16 = sat("cs16", [16, 1024], F32, offset=A_w + 16 * KB)
    dstE = sat("dstE", [16, 8, C2], I32, offset=A_w + 4 * KB)
    Gbig = sat("Gbig", [128, C2, C], BF16, offset=A_s + 0)
    Vc = Db
    xgt = sat("xgt", [128, 4, 8, 128], BF16, offset=A_w + 0)
    xg4 = sat("xg4", [128, 4, C], BF16, offset=A_w + 8 * KB)
    xg4b = sat("xg4b", [128, 4, C], BF16, offset=A_bt + 24 * KB)
    hpr = sat("hpr", [128, 16, 512], BF16, offset=A_w + 16 * KB)
    zrowb = sat("zrowb", [128, C], BF16, offset=addr(lnt) + 4 * KB)
    ohk16 = sat("ohk16s", [1, 128], F32, offset=addr(stb))
    row16 = sat("row16s", [1, 16], I32, offset=addr(exch))
    hsb = sat("hsb", [128, 2, T], BF16, offset=A_s + 0)
    xpt = sat("xpt", [128, 2, C], F32, offset=A_s + 8 * KB)
    w1v = sat("w1v", [128, 8, HF], BF16, offset=A_s + 0)
    w3v = sat("w3v", [128, 8, HF], BF16, offset=A_s + 32 * KB)
    Gb = sat("Gb", [128, 2, C], BF16, offset=A_s + 0)
    acct4 = sat("acct4", [128, 4, C], BF16, offset=A_s + 0)
    x1t4 = sat("x1t4", [128, 4, C], BF16, offset=A_s + 24 * KB)
    acct = sat("acct", [128, 2, C], BF16, offset=A_s + 8 * KB)
    outt = sat("outt", [128, 2, C], F32, offset=A_s + 16 * KB)

    pjA = nc.alloc_psum_tensor("pjA", [128, 512], F32)
    pjB = nc.alloc_psum_tensor("pjB", [128, 512], F32)
    pyA = nc.alloc_psum_tensor("pyA", [128, 1024], F32)
    pyB = nc.alloc_psum_tensor("pyB", [128, 1024], F32)
    pX = nc.alloc_psum_tensor("pX", [128, 128], F32)
    pSm = nc.alloc_psum_tensor("pSm", [128, 2], F32)

    g = nc.gpsimd
    sy = nc.sync
    v = nc.vector
    sc = nc.scalar
    te = nc.tensor
    BAR = nc.all_engine_barrier

    qg = nc.alloc_semaphore("qg")
    qs = nc.alloc_semaphore("qs")
    qc = nc.alloc_semaphore("qc")
    qh = nc.alloc_semaphore("qh")
    qh2 = nc.alloc_semaphore("qh2")
    qo = nc.alloc_semaphore("qo")
    cg = [0]; cs = [0]; cc = [0]; ch = [0]; co = [0]; ch2 = [0]

    def gdma(dst, src, **kw):
        r = g.dma_start(out=dst, in_=src, **kw).then_inc(qg, 16)
        cg[0] += 16
        return r

    def gwait():
        g.wait_ge(qg, cg[0])

    def sdma(dst, src, **kw):
        r = sy.dma_start(out=dst, in_=src, **kw).then_inc(qs, 16)
        cs[0] += 16
        return r

    def swait():
        sy.wait_ge(qs, cs[0])

    def coll(groups, ins, outs):
        g.collective_compute("AllGather", AL.bypass, replica_groups=groups,
                             ins=ins, outs=outs).then_inc(qc, 1)
        cc[0] += 1

    def mm(out_ps, lhsT, rhs, start, stop, width):
        nh = (width + 511) // 512
        r = None
        for h in range(nh):
            w0 = h * 512
            w1 = min(width, w0 + 512)
            r = te.matmul(out_ps[:, w0:w1], lhsT, rhs[:, w0:w1], start=start, stop=stop)
        return r

    class Edge:
        __slots__ = ("sem", "n")
        def __init__(self, name):
            self.sem = nc.alloc_semaphore(name)
            self.n = 0
        def inc(self, r):
            r.then_inc(self.sem, 1)
            self.n += 1
            return self.n
        def wait(self, eng, n=None):
            eng.wait_ge(self.sem, self.n if n is None else n)

    eTV = Edge("eTV")
    eVT = Edge("eVT")
    eVS = Edge("eVS")
    eSV = Edge("eSV")
    eVX = Edge("eVX")
    eVD = Edge("eVD")
    eVG = Edge("eVG")
    eTS = Edge("eTS")

    GRP8 = [list(range(8))]
    GRP4 = [[0, 1, 2, 3], [4, 5, 6, 7]]

    gdma(wdv[:, 0:1, :, :], wd_in.rearrange("l (j p) n -> p l j n", p=128)[:, 0:1, :, :])
    gdma(wiv[:, 0:1, :, :], wi_in.rearrange("l (j p) n -> p l j n", p=128)[:, 0:1, :, :])
    c_wdwi = cg[0]
    bt_cnt = []
    for j in range(8):
        gdma(bT[:, j, :],
             x_feat.rearrange("(j p) t -> p j t", p=128)[:, j, :])
        bt_cnt.append(cg[0])
    gdma(wdv[:, 1:2, :, :], wd_in.rearrange("l (j p) n -> p l j n", p=128)[:, 1:2, :, :])
    gdma(wiv[:, 1:2, :, :], wi_in.rearrange("l (j p) n -> p l j n", p=128)[:, 1:2, :, :])
    c_wdwi2 = cg[0]
    gdma(wov[:, :, :, :], wo_in.rearrange("l (a p) c -> p l a c", p=128))
    c_wov = cg[0]
    gdma(wsv[:, :, :, :], ws_in.rearrange("l (a p) c -> p l a c", p=128))
    c_wsv = cg[0]
    gdma(extv[:, :, :], ext_in.rearrange("j p f -> p j f"))
    c_ext = cg[0]
    gdma(wgv[:, :, :], wgt_in.rearrange("(j p) e -> p j e", p=128))
    gdma(w2v[:, :, :], w2t_in.rearrange("(j p) c -> p j c", p=128))
    c_w2v = cg[0]
    sdma(bdv[:, :, :], bd_in[:, :, :])
    sdma(biv[:, :, :], bi_in[:, :, :])
    sdma(spav[:, :, :], alog_in[:, :, :])
    sdma(kv0[:, :], kv_in[:, :])
    sdma(eab[:, :], eab_in[:, :])
    sdma(ohp[:, :], ohp_in[:, :])
    sdma(ohk[:, :], ohk_in[:, :])
    sdma(lnt[0:1, 0, :], bo_in[0:1, :])
    sdma(lnt[0:1, 1, :], bs_in[0:1, :])
    sdma(ct2[0:1, 0:C], bo_in[1:2, :])
    sdma(sA[0:1, 0, 0:C], bs_in[1:2, :])
    g.iota(iop[:, :], pattern=[[0, 1]], base=0, channel_multiplier=1)
    g.iota(iocf[:, :], pattern=[[1, 1024]], base=0, channel_multiplier=0,
           allow_small_or_imprecise_dtypes=True)
    swait()
    BAR()
    v.tensor_add(bbvb[0:1, 0, :], lnt[0:1, 0, :], lnt[0:1, 1, :])
    v.tensor_add(bbvb[0:1, 1, :], ct2[0:1, 0:C], sA[0:1, 0, 0:C])
    v.memset(zeros2[:, :], 0.0)
    v.memset(zrowb[:, :], 0.0)
    v.memset(onesf[:, :], 1.0)
    v.memset(ones1b[:, :], 1.0)
    v.memset(sentA[:, :], NCORE * CAP)
    v.memset(sentB[:, :], T)
    v.memset(epsv[:, :], float(EPS))
    v.memset(onesc[:, :], 1.0)
    v.tensor_copy(iopf[:, :], iop[:, :])
    sc.activation(out=stt[:, 4:8], in_=spav[:, :, :].rearrange("p l a -> p (l a)"),
                  func=AF.Abs)
    BAR()
    sc.activation(out=stt[:, 4:8], in_=stt[:, 4:8], func=AF.Exp, scale=-1.0)
    BAR()
    sc.activation(out=stt[:, 4:8], in_=stt[:, 4:8], func=AF.Ln, bias=onesc[:, 0:1])
    BAR()
    v.tensor_scalar_max(st2[:, 4:8], spav[:, :, :].rearrange("p l a -> p (l a)"), 0.0)
    BAR()
    v.tensor_add(st2[:, 4:8], st2[:, 4:8], stt[:, 4:8])
    BAR()
    v.tensor_scalar_mul(spav[:, :, :].rearrange("p l a -> p (l a)"), st2[:, 4:8], -1.0)
    v.tensor_scalar(idT[:, :], iocf[:, 0:128], iopf[:, 0:1], None, op0=AL.is_equal)
    v.tensor_single_scalar(bisI[:, :], iop[:, :], 15.0, op=AL.bitwise_and)
    v.tensor_single_scalar(msI[:, 0:1], iop[:, :], 4.0, op=AL.logical_shift_right)
    g.iota(bd16[:, :], pattern=[[1, 8], [0, 16]], base=0, channel_multiplier=0,
           allow_small_or_imprecise_dtypes=True)
    BAR()
    v.tensor_copy(stt[:, 1:2], msI[:, 0:1])
    BAR()
    v.tensor_scalar(bd16[:, :], bd16[:, :], stt[:, 1:2], None, op0=AL.is_equal)
    BAR()
    for m in range(16):
        sdma(acc_d.ap()[0:T, :].rearrange("(m p) c -> p m c", p=128)[:, m:m + 1, :],
             zrowb[:, :].rearrange("p (o c) -> p o c", o=1))
    sdma(acc_d.ap()[T:T + 1, :], zrowb[0:1, :])
    sdma(y_all.ap()[NCORE * CAP:NCORE * CAP + 1, :], zrowb[0:1, :])

    def ssm_layer(l, rhsbuf):
        evac_cnt = []
        te.wait_ge(qg, c_wdwi if l == 0 else c_wdwi2)
        for it in range(8):
            tsl, a = it % 4, it // 4
            t0 = tsl * 512
            pA = pjA[:, :] if it % 2 == 0 else pyA[:, 0:512]
            pB = pjB[:, :] if it % 2 == 0 else pyA[:, 512:1024]
            if it >= 2:
                te.wait_ge(eVT.sem, evac_cnt[it - 2])
            r = None
            for j in range(8):
                if l == 0 and it == 0:
                    te.wait_ge(qg, bt_cnt[j])
                r = te.matmul(pA, wdv[:, l, j, a * 128:(a + 1) * 128],
                              rhsbuf[:, j, t0:t0 + 512], start=(j == 0), stop=(j == 7))
            for j in range(8):
                r = te.matmul(pB, wiv[:, l, j, a * 128:(a + 1) * 128],
                              rhsbuf[:, j, t0:t0 + 512], start=(j == 0), stop=(j == 7))
            eTV.inc(r)
            v.wait_ge(eTV.sem, eTV.n)
            v.tensor_scalar(sA[:, a, t0:t0 + 512], pA,
                            bdv[:, l, a:a + 1], None, op0=AL.add)
            r = v.tensor_scalar(sB[:, a, t0:t0 + 512], pB,
                                biv[:, l, a:a + 1], None, op0=AL.add)
            eVT.inc(r)
            evac_cnt.append(eVT.n)
        if build.dbg_layer == (l, "raw"):
            for a in range(2):
                for h in range(2):
                    sdma(build.dbg_out.rearrange("(z p) c -> p z c", p=128)[:, 2 * a + h:2 * a + h + 1, :],
                         sA[:, a, h * 1024:(h + 1) * 1024].rearrange("p (o c) -> p o c", o=1))
                    sdma(build.dbg_out.rearrange("(z p) c -> p z c", p=128)[:, 8 + 2 * a + h:8 + 2 * a + h + 1, :],
                         sB[:, a, h * 1024:(h + 1) * 1024].rearrange("p (o c) -> p o c", o=1))
            swait()
            BAR()
            build.dbg_early[0] = True
        if build.dbg_early[0]:
            return
        for a in range(2):
            sc.wait_ge(eVT.sem, evac_cnt[3 + 4 * a])
            r = sc.activation(out=sC[:, a, :], in_=sA[:, a, :], func=AF.Abs)
            eSV.inc(r)
            sc.activation(out=sC[:, a, :], in_=sC[:, a, :], func=AF.Exp, scale=-1.0)
            r = sc.activation(out=sC[:, a, :], in_=sC[:, a, :], func=AF.Ln,
                              bias=onesc[:, 0:1])
            eSV.inc(r)
        sv_sp = eSV.n
        for a in range(2):
            eSV.wait(v, sv_sp - 3 + 2 * a)
            v.tensor_scalar_max(sA[:, a, :], sA[:, a, :], 0.0)
            eSV.wait(v, sv_sp - 2 + 2 * a)
            v.tensor_add(sA[:, a, :], sA[:, a, :], sC[:, a, :])
            v.tensor_tensor_scan(out=sC[:, a, :], data0=sA[:, a, :],
                                 data1=zeros2[:, :], initial=0.0,
                                 op0=AL.add, op1=AL.add)
            r = v.tensor_mul(sB[:, a, :], sA[:, a, :], sB[:, a, :])
            eVS.inc(r)
            eVS.wait(sc)
            sc.activation(out=sA[:, a, :], in_=sA[:, a, :], func=AF.Exp,
                          scale=spav[:, l, a:a + 1])
            sc.activation(out=sm4[:, a:a + 1], in_=sC[:, a, T - 1:T], func=AF.Exp,
                          scale=spav[:, l, a:a + 1])
            r = sc.activation(out=sC[:, a, :], in_=sC[:, a, :], func=AF.Exp,
                              scale=spav[:, l, a:a + 1])
            eSV.inc(r)
            eSV.wait(v)
            r = v.tensor_tensor_scan(out=sD[:, a, :], data0=sA[:, a, :],
                                     data1=sB[:, a, :], initial=0.0,
                                     op0=AL.mult, op1=AL.add)
        v.tensor_copy(sm4[:, 2:3], sD[:, 0, T - 1:T])
        r = v.tensor_copy(sm4[:, 3:4], sD[:, 1, T - 1:T])
        eVT.inc(r)
        te.wait_ge(eVT.sem, eVT.n)
        te.wait_ge(eSV.sem, eSV.n)
        r = te.transpose(pX[0:4, :], sm4[:, 0:4], idT[:, :])
        eTV.inc(r)
        eTV.wait(v)
        r = v.tensor_copy(exsb[0:4, :], pX[0:4, :])
        eVD.inc(r)
        eVD.wait(sy)
        sdma(ag_s_in[l].ap()[:, :], exsb[0:4, :])
        g.wait_ge(qs, cs[0])
        coll(GRP4, [ag_s_in[l].ap().opt()], [ag_s_out[l].ap().opt()])
        g.wait_ge(qc, cc[0])
        gdma(exsb[:, :], ag_s_out[l].ap()[:, :])
        te.wait_ge(qg, cg[0])
        r = te.transpose(pX[:, 0:16], exsb[0:16, :], idT[0:16, 0:16])
        eTV.inc(r)
        eTV.wait(v)
        v.tensor_copy(exch[:, 0:16], pX[:, 0:16])
        v.memset(hins[:, 0:2], 0.0)
        v.tensor_copy(lnt[:, 0, 0:384], zeros2[:, 0:384])
        for m in range(1, 5):
            for cch in range(2):
                pcol = 4 * (m - 1) + cch
                hcol = 4 * (m - 1) + 2 + cch
                v.tensor_mul(stt[:, cch:cch + 1], exch[:, pcol:pcol + 1],
                             hins[:, 2 * (m - 1) + cch:2 * (m - 1) + cch + 1])
            v.tensor_copy(lnt[:, 0, 0:384], zeros2[:, 0:384])
            for cch in range(2):
                hcol = 4 * (m - 1) + 2 + cch
                v.tensor_add(hins[:, 2 * m + cch:2 * m + cch + 1],
                             stt[:, cch:cch + 1], exch[:, hcol:hcol + 1])
            v.tensor_copy(lnt[:, 0, 0:384], zeros2[:, 0:384])
        v.tensor_mul(stt[:, 0:8], hins[:, 0:8], ohp[:, 0:8])
        v.tensor_copy(lnt[:, 0, 0:384], zeros2[:, 0:384])
        v.reduce_sum(hins[:, 10:12], stt[:, 0:8].rearrange("p (m c) -> p c m", c=2),
                     axis=AX)
        v.tensor_copy(lnt[:, 0, 0:384], zeros2[:, 0:384])
        for a in range(2):
            v.scalar_tensor_tensor(out=sD[:, a, :], in0=sC[:, a, :],
                                   scalar=hins[:, 10 + a:11 + a], in1=sD[:, a, :],
                                   op0=AL.mult, op1=AL.add)
        if build.dbg_layer == (l, "exch"):
            v.tensor_copy(lnt[:, 0, 0:16], exch[:, 0:16])
            v.tensor_copy(lnt[:, 0, 16:28], hins[:, 0:12])
            v.tensor_copy(lnt[:, 0, 28:32], sm4[:, 0:4])
            v.memset(lnt[:, 0, 32:1024], 0.0)
            BAR()
            for mm2 in range(16):
                sdma(out.rearrange("(m p) c -> p m c", p=128)[:, mm2:mm2 + 1, :],
                     lnt[:, 0:1, :])
            swait()
            BAR()
            build.dbg_early[0] = True
            return
        v.tensor_copy(hsb[:, 0, :], sD[:, 0, :])
        v.tensor_copy(hsb[:, 1, :], sD[:, 1, :])
        for a in range(2):
            r = v.tensor_scalar(stb[:, a, :], zeros2[:, 0:128], hins[:, 8 + a:9 + a],
                                None, op0=AL.add)
        eVT.inc(r)
        te.wait_ge(eVT.sem, eVT.n)

    build.dbg_layer = (0, "raw") if dbg == "raw1" else (
        (0, "exch") if dbg == "exch" else (
            (1, "exch") if dbg == "exch2" else None))
    build.dbg_out = out
    build.dbg_early = [False]
    ssm_layer(0, bT)

    if build.dbg_early[0]:
        return nc

    if dbg in ("hs1", "dec1", "drv1", "cum1"):
        buf = {"hs1": sD, "dec1": sA, "drv1": sB, "cum1": sC}[dbg]
        for a in range(2):
            for h in range(2):
                sdma(out.rearrange("(z p) c -> p z c", p=128)[:, 2 * a + h:2 * a + h + 1, :],
                     buf[:, a, h * 1024:(h + 1) * 1024].rearrange("p (o c) -> p o c", o=1))
        for z in range(4, 16):
            sdma(out.rearrange("(z p) c -> p z c", p=128)[:, z:z + 1, :],
                 sD[:, 0, 0:1024].rearrange("p (o c) -> p o c", o=1))
        swait()
        BAR()
        return nc

    te.wait_ge(qg, c_wsv)
    for j in range(8):
        for a in range(2):
            te.matmul(pX[:, 0:1], wsv[:, 0, a, j * 128:(j + 1) * 128],
                      stb[:, a, 0:1], start=(a == 0), stop=False)
        te.matmul(pX[:, 0:1], bbvb[0:1, 0, j * 128:(j + 1) * 128], ones1b[0:1, 0:1],
                  start=False, stop=True)
        BAR()
        v.tensor_copy(ctTb[:, j:j + 1], pX[:, 0:1])
        BAR()
    for (wv, bv) in ((wdv, bdv), (wiv, biv)):
        for a in range(2):
            for j in range(8):
                te.matmul(pX[:, 1:2], wv[:, 1, j, a * 128:(a + 1) * 128],
                          ctTb[:, j:j + 1], start=(j == 0), stop=(j == 7))
            BAR()
            v.tensor_add(bv[:, 1, a:a + 1], bv[:, 1, a:a + 1], pX[:, 1:2])
            BAR()
    te.wait_ge(qg, c_wov)
    y1_cnt = []
    for j in range(8):
        for th in range(2):
            it = 2 * j + th
            py = pyA if it % 2 == 0 else pyB
            if it >= 2:
                te.wait_ge(eVT.sem, y1_cnt[it - 2])
            r = None
            for a in range(2):
                r = mm(py, wov[:, 0, a, j * 128:(j + 1) * 128],
                       hsb[:, a, th * 1024:(th + 1) * 1024], (a == 0), (a == 1), 1024)
            eTV.inc(r)
            v.wait_ge(eTV.sem, eTV.n)
            r = v.tensor_copy(bT[:, j, th * 1024:(th + 1) * 1024], py[:, :])
            eVT.inc(r)
            y1_cnt.append(eVT.n)
    BAR()

    ssm_layer(1, bT)
    if build.dbg_early[0]:
        return nc

    for a in range(2):
        mm(pyA, stb[:, a, :], wsv[:, 1, a, :], (a == 0), False, C)
    mm(pyA, ones1b[0:1, 0:128], bbvb[0:1, 1, :], False, True, C)
    BAR()
    v.tensor_copy(ct2[:, :], pyA[:, :])
    BAR()

    bTf = bT.rearrange("p a t -> p (a t)")
    ln1_base = (eTV.n, eVT.n, eVS.n, eSV.n, eVX.n, eVD.n)
    xpt_cnt = []
    tr_cnt = []
    for m in range(16):
        py = pyA if m % 2 == 0 else pyB
        b = m % 2
        if m >= 2:
            eVT.wait(te, ln1_base[1] + m - 1)
        r = None
        for a in range(2):
            r = mm(py, hsb[:, a, m * 128:(m + 1) * 128], wov[:, 1, a, :],
                   (a == 0), (a == 1), C)
        eTV.inc(r)
        if m >= 2:
            sy.wait_ge(eVX.sem, ln1_base[4] + m - 1)
        sdma(xpt[:, b:b + 1, :],
             x_tok.rearrange("(m p) c -> p m c", p=128)[:, m:m + 1, :])
        xpt_cnt.append(cs[0])
        eTV.wait(v)
        r = v.tensor_add(lnt[:, b, :], py[:, :], ct2[:, :])
        eVT.inc(r)
        v.wait_ge(qs, xpt_cnt[m])
        r = v.scalar_tensor_tensor(out=lnt[:, b, :], in0=lnt[:, b, :], scalar=1.0,
                                   in1=xpt[:, b, :], op0=AL.mult, op1=AL.add,
                                   accum_out=st1[:, m:m + 1])
        eVX.inc(r)
        v.scalar_tensor_tensor(out=lnsq[:, :], in0=lnt[:, b, :], scalar=1.0,
                               in1=lnt[:, b, :], op0=AL.mult, op1=AL.mult,
                               accum_out=st2[:, m:m + 1])
        if m >= 2:
            eSV.wait(v, ln1_base[3] + 2 * (m - 1))
        r = v.tensor_scalar_mul(st1[:, m:m + 1], st1[:, m:m + 1], 1.0 / C)
        eVS.inc(r)
        eVS.wait(sc)
        r = sc.activation(out=stt[:, 4 + b:5 + b], in_=st1[:, m:m + 1],
                          func=AF.Square)
        eSV.inc(r)
        eSV.wait(v)
        r = v.tensor_scalar(stt[:, b:b + 1], st2[:, m:m + 1], 1.0 / C,
                            stt[:, 4 + b:5 + b], op0=AL.mult, op1=AL.subtract)
        eVS.inc(r)
        eVS.wait(sc)
        r = sc.activation(out=stt[:, 2 + b:3 + b], in_=stt[:, b:b + 1],
                          func=AF.Sqrt, bias=epsv[:, 0:1])
        eSV.inc(r)
        eSV.wait(v)
        v.reciprocal(stt[:, 6 + b:7 + b], stt[:, 2 + b:3 + b])
        v.tensor_copy(lnsq[:, 0:384], zeros2[:, 0:384])
        if m >= 2:
            v.wait_ge(qs, tr_cnt[m - 2])
        r = v.tensor_scalar(x1s[:, b, :], lnt[:, b, :], st1[:, m:m + 1],
                            stt[:, 6 + b:7 + b], op0=AL.subtract, op1=AL.mult)
        eVD.inc(r)
        eVD.wait(sy)
        sdma(ag_x_in.ap().rearrange("(m p) c -> p m c", p=128)[:, m:m + 1, :],
             x1s[:, b:b + 1, :])
        sy.dma_start_transpose(
            bTf[:, m * 1024:(m + 1) * 1024].rearrange("p (a q) -> p a q", a=8),
            x1s[:, b, :].rearrange("p (a q) -> p a q", a=8)
        ).then_inc(qs, 16)
        cs[0] += 16
        tr_cnt.append(cs[0])
    swait()
    BAR()
    gdma(w1v[:, :, :], w1t_in.rearrange("(j p) h -> p j h", p=128))
    gdma(w3v[:, :, :], w3t_in.rearrange("(j p) h -> p j h", p=128))

    if dbg == "x1":
        for m in range(16):
            b = m % 2
            gdma(x1s[:, b:b + 1, :],
                 ag_x_in.ap().rearrange("(m p) c -> p m c", p=128)[:, m:m + 1, :])
            gwait()
            BAR()
            v.tensor_copy(outt[:, 0, :], x1s[:, b, :])
            BAR()
            sdma(out.rearrange("(m p) c -> p m c", p=128)[:, m:m + 1, :],
                 outt[:, 0:1, :])
            swait()
        BAR()
        return nc

    x1Tv = bTf.rearrange("p (m a q) -> p m a q", m=16, a=8)
    te.wait_ge(qg, c_ext)
    for rt in range(16):
        for j in range(8):
            te.matmul(pX[:, :], extv[:, j, 16 - rt:144 - rt],
                      x1Tv[:, :, j, :].rearrange("p m (q r) -> p r m q", r=16)[:, rt, :, :],
                      start=(rt == 0 and j == 0), stop=(rt == 15 and j == 7))
    BAR()
    v.tensor_copy(junk[:, 0:128], pX[:, :])
    BAR()
    sdma(ag_t_in.ap()[:, :], junk[:, 0:128])
    swait()
    BAR()
    coll(GRP8, [ag_t_in.ap().opt()], [ag_t_out.ap().opt()])
    cc_agt = cc[0]
    g.wait_ge(qc, cc[0])
    coll(GRP8, [ag_x_in.ap().opt()], [ag_x_out.ap().opt()])
    sy.wait_ge(qc, cc_agt)
    sdma(Ssc[:, :].rearrange("er (kk c) -> er kk c", c=128),
         ag_t_out.ap().rearrange("(kk er) c -> er kk c", er=128))
    sdma(ohk16[:, :], ohk16_in[:, :])
    sdma(row16[:, :], row16_in[:, :])
    swait()
    BAR()
    if dbg == "bis":
        v.tensor_copy(lnt[:, 0, 233:361], Ssc[:, 0:128])
        v.reduce_max(lnt[:, 0, 361:362], Ssc[:, :], axis=AX)
        v.reduce_max(lnt[:, 0, 362:363], Ssc[:, 0:128], axis=AX)
        BAR()

    v.tensor_single_scalar(bisI[:, :], iop[:, :], 15.0, op=AL.bitwise_and)
    BAR()
    v.tensor_copy(stt[:, 0:1], bisI[:, :])
    BAR()
    v.tensor_scalar(vid[:, :], iocf[:, :], 16.0, stt[:, 0:1], op0=AL.mult, op1=AL.add)
    v.reduce_max(bis[:, 0:1], Ssc[:, :], axis=AX)
    v.tensor_reduce(bis[:, 1:2], Ssc[:, :], axis=AX, op=AL.min, negate=True)
    BAR()
    if dbg == "bis":
        v.tensor_copy(lnt[:, 0, 100:102], bis[:, 0:2])
        BAR()
    te.transpose(pX[0:2, :], bis[:, 0:2], idT[:, :])
    BAR()
    v.tensor_copy(scrG[0:2, 0:128], pX[0:2, 0:128])
    BAR()
    if dbg == "bis":
        v.tensor_copy(lnt[0:2, 0, 102:230], scrG[0:2, 0:128])
        BAR()
    v.reduce_max(scr8[0:2, 0:1], scrG[0:2, 0:128], axis=AX)
    BAR()
    te.transpose(pX[0:1, 0:2], scr8[0:2, 0:1], idT[0:2, 0:2])
    BAR()
    v.tensor_copy(stt[0:1, 0:2], pX[0:1, 0:2])
    BAR()
    if dbg == "bis":
        v.tensor_copy(lnt[0:2, 0, 230:231], scr8[0:2, 0:1])
        v.tensor_copy(lnt[0:1, 0, 231:233], stt[0:1, 0:2])
        BAR()
    te.matmul(pSm[:, 0:2], onesf[0:1, 0:128], stt[0:1, 0:2], start=True, stop=True)
    BAR()
    v.tensor_scalar_add(bis[:, 0:1], pSm[:, 0:1], 1.0)
    v.tensor_scalar(bis[:, 1:2], pSm[:, 1:2], -1.0, -1.0, op0=AL.mult, op1=AL.add)
    v.tensor_scalar_mul(bis[:, 4:5], pSm[:, 0:1], -1.0)
    v.tensor_copy(lnt[:, 0, 0:384], zeros2[:, 0:384])
    v.tensor_sub(bis[:, 3:4], bis[:, 0:1], bis[:, 1:2])
    v.tensor_copy(lnt[:, 0, 0:384], zeros2[:, 0:384])
    v.tensor_scalar_mul(bis[:, 3:4], bis[:, 3:4], 0.5)
    v.tensor_copy(lnt[:, 0, 0:384], zeros2[:, 0:384])
    if dbg == "bis":
        v.tensor_copy(lnt[:, 0, 0:8], bis[:, 0:8])
        BAR()
    for it in range(NBIS if dbg != "bis" else 1):
        v.tensor_scalar(junk[:, :], Ssc[:, :], bis[:, 1:2], bis[:, 3:4],
                        op0=AL.subtract, op1=AL.is_ge)
        r = v.reduce_sum(cnts[:, :], junk[:, :], axis=AX)
        eVT.inc(r)
        te.wait_ge(eVT.sem, eVT.n)
        r = te.matmul(pSm[:, 0:1], bd16[:, :], cnts[:, :], start=True, stop=True)
        eTV.inc(r)
        v.wait_ge(eTV.sem, eTV.n)
        r = v.tensor_scalar(bis[:, 6:7], pSm[:, 0:1], 2048.0, bis[:, 3:4],
                            op0=AL.is_ge, op1=AL.mult)
        eVT.inc(r)
        v.tensor_copy(lnt[:, 0, 0:384], zeros2[:, 0:384])
        v.tensor_add(bis[:, 1:2], bis[:, 1:2], bis[:, 6:7])
        v.tensor_scalar_mul(bis[:, 3:4], bis[:, 3:4], 0.5)
        v.tensor_copy(lnt[:, 0, 0:384], zeros2[:, 0:384])
    if dbg == "bis":
        v.tensor_copy(lnt[:, 0, 8:9], cnts[:, :])
        v.tensor_copy(lnt[:, 0, 9:10], pSm[:, 0:1])
        v.tensor_copy(lnt[:, 0, 10:11], bisI[:, :])
        v.tensor_copy(lnt[:, 0, 11:14], bis[:, 0:3])
        v.tensor_copy(lnt[:, 0, 14:142], Ssc[:, 0:128])
        BAR()
        for m in range(16):
            sdma(out.rearrange("(m p) c -> p m c", p=128)[:, m:m + 1, :],
                 lnt[:, 0:1, :])
        swait()
        BAR()
        return nc
    v.tensor_scalar(msI[:, :], Ssc[:, :], bis[:, 1:2], None, op0=AL.is_ge)
    v.tensor_scalar(mskF[:, :], Ssc[:, :], bis[:, 1:2], None, op0=AL.is_ge)
    BAR()
    v.reduce_sum(cnts[:, :], mskF[:, :], axis=AX)
    sc.activation(out=junk[:, :], in_=Ssc[:, :], func=AF.Exp, bias=bis[:, 4:5])
    BAR()
    v.tensor_mul(junk[:, :], junk[:, :], mskF[:, :])
    BAR()
    v.reduce_sum(st1[:, 0:1], junk[:, :], axis=AX)
    BAR()
    te.transpose(pX[0:1, :], st1[:, 0:1], idT[:, :])
    BAR()
    v.tensor_copy(orow[0:1, :], pX[0:1, 0:128])
    BAR()
    v.tensor_mul(orow[0:1, :], orow[0:1, :], ohk16[0:1, :])
    BAR()
    v.reduce_sum(stt[0:1, 4:5], orow[0:1, :], axis=AX)
    BAR()
    te.matmul(pSm[:, 1:2], onesf[0:1, 0:128], stt[0:1, 4:5], start=True, stop=True)
    BAR()
    v.reciprocal(bis[:, 5:6], pSm[:, 1:2])
    v.scalar_tensor_tensor(out=Vb[:, :], in0=vid[:, :], scalar=1.0, in1=mskF[:, :],
                           op0=AL.add, op1=AL.mult)
    v.tensor_tensor_scan(out=junk[:, :], data0=mskF[:, :], data1=zeros2[:, 0:1024],
                         initial=0.0, op0=AL.add, op1=AL.add)
    BAR()
    v.tensor_scalar_add(Vb[:, :], Vb[:, :], -1.0)
    v.tensor_sub(junk[:, :], iocf[:, :], junk[:, :])
    BAR()
    v.tensor_scalar_add(junk[:, :], junk[:, :], 1.0)
    BAR()
    v.tensor_mul(junk[:, :], junk[:, :], mskF[:, :])
    BAR()
    v.tensor_copy(Db[:, :], junk[:, :])
    BAR()

    def butterfly(Vb_, Vb2_, Db_, Db2_, Sb_=None, Sb2_=None):
        Vcur, Vnx, Dcur, Dnx = Vb_, Vb2_, Db_, Db2_
        Scur, Snx = Sb_, Sb2_
        for b in range(10):
            s = 1 << b
            v.tensor_single_scalar(msI[:, :], Dcur[:, :], float(s), op=AL.bitwise_and)
            v.tensor_single_scalar(DmS[:, :], Dcur[:, :], float(s), op=AL.subtract)
            v.select(Vnx[:, 0:1024 - s], msI[:, s:], Vcur[:, s:], Vcur[:, 0:1024 - s])
            v.tensor_copy(Vnx[:, 1024 - s:], Vcur[:, 1024 - s:])
            if Scur is not None:
                v.select(Snx[:, 0:1024 - s], msI[:, s:], Scur[:, s:],
                         Scur[:, 0:1024 - s])
                v.tensor_copy(Snx[:, 1024 - s:], Scur[:, 1024 - s:])
                Scur, Snx = Snx, Scur
            v.select(Dnx[:, 0:1024 - s], msI[:, s:], DmS[:, s:], Dcur[:, 0:1024 - s])
            v.tensor_copy(Dnx[:, 1024 - s:], Dcur[:, 1024 - s:])
            Vcur, Vnx, Dcur, Dnx = Vnx, Vcur, Dnx, Dcur
        return Vcur, Scur

    v.tensor_copy(Sb[:, :], Ssc[:, :])
    BAR()
    CT, SCb = butterfly(Vb, Vb2, Db, Db2, Sb, Sb2)
    te.transpose(pX[0:1, :], cnts[:, :], idT[:, :])
    BAR()
    v.tensor_copy(orow[0:1, :], pX[0:1, 0:128])
    BAR()
    sdma(orow_d.ap()[:, :], orow[0:1, :])
    swait()
    BAR()
    gdma(orow8[:, :], orow_d.ap().rearrange("o (e r) -> (o e) r", e=8))
    gwait()
    BAR()
    v.tensor_tensor_scan(out=orow8[:, :], data0=orow8[:, :], data1=zeros2[0:8, 0:16],
                         initial=0.0, op0=AL.add, op1=AL.add)
    BAR()
    gdma(exsb[0:8, 0:16], orow_d.ap().rearrange("o (e r) -> (o e) r", e=8))
    gwait()
    BAR()
    v.tensor_sub(orow8[:, :], orow8[:, :], exsb[0:8, 0:16])
    BAR()
    gdma(orow_d.ap().rearrange("o (e r) -> (o e) r", e=8), orow8[:, :])
    gwait()
    BAR()
    sdma(orow[0:1, :], orow_d.ap()[:, :])
    swait()
    BAR()
    v.tensor_mul(exsb[0:1, 0:128], orow[0:1, :], ohk16[0:1, :])
    BAR()
    v.reduce_sum(wraw[0:1, 0:16], exsb[0:1, 0:128].rearrange("o (e r) -> o r e", e=8),
                 axis=AX)
    BAR()
    v.tensor_copy(oint[0:1, :], wraw[0:1, 0:16])
    te.transpose(pX[:, 0:1], orow[0:1, :], idT[0:1, 0:1])
    BAR()
    v.tensor_copy(obase[:, :], pX[:, 0:1])
    v.tensor_copy(Vc[:, :], CT[:, :])
    BAR()
    sdma(vc_d.ap()[:, :], Vc[:, :])
    sdma(sc_d.ap()[:, :], SCb[:, :])
    swait()
    BAR()
    rv = nc.values_load(row16[0:1, 0:1], engines=[mybir.EngineType.SP])
    sy.dma_start(out=cc16[:, :], in_=vc_d.ap()[bass.ds(rv, 16), :]).then_inc(qh, 16)
    ch[0] += 16
    sy.dma_start(out=cs16[:, :], in_=sc_d.ap()[bass.ds(rv, 16), :]).then_inc(qh2, 16)
    ch2[0] += 16
    sy.wait_ge(qh, ch[0])
    sy.wait_ge(qh2, ch2[0])
    sc.wait_ge(qh, ch[0])
    sc.wait_ge(qh2, ch2[0])
    for r in range(16):
        eng = mybir.EngineType.SP if r < 8 else mybir.EngineType.Activation
        eng2 = mybir.EngineType.Activation if r < 8 else mybir.EngineType.SP
        ov = nc.values_load(oint[0:1, r:r + 1], engines=[eng])
        ov2 = nc.values_load(oint[0:1, r:r + 1], engines=[eng2])
        issuer = sy if r < 8 else sc
        issuer2 = sc if r < 8 else sy
        issuer.wait_ge(qh, ch[0])
        issuer.dma_start(out=l_list.ap()[0:1, bass.ds(ov, 1024)],
                         in_=cc16[r:r + 1, :]).then_inc(qh, 16)
        ch[0] += 16
        issuer2.wait_ge(qh2, ch2[0])
        issuer2.dma_start(out=s_list.ap()[0:1, bass.ds(ov2, 1024)],
                          in_=cs16[r:r + 1, :]).then_inc(qh2, 16)
        ch2[0] += 16
    sy.wait_ge(qh, ch[0])
    g.wait_ge(qh, ch[0])
    g.wait_ge(qh2, ch2[0])
    gdma(FL[:, :], l_list.ap()[0:1, 0:2048].rearrange("o (p q) -> (o p) q", q=16))
    gdma(wraw[:, :], s_list.ap()[0:1, 0:2048].rearrange("o (p q) -> (o p) q", q=16))
    gwait()
    g.wait_ge(qc, cc[0])
    BAR()

    if dbg in ("sel", "sel2"):
        v.tensor_copy(lnt[:, 0, 0:16], FL[:, :])
        v.tensor_copy(lnt[:, 0, 16:17], cnts[:, :])
        v.tensor_copy(lnt[:, 0, 17:18], bis[:, 1:2])
        v.tensor_copy(lnt[:, 0, 18:19], obase[:, :])
        v.tensor_copy(lnt[:, 0, 19:20], bis[:, 5:6])
        v.memset(lnt[:, 0, 20:1024], 0.0)
        BAR()
        for m in range(16):
            sdma(out.rearrange("(m p) c -> p m c", p=128)[:, m:m + 1, :],
                 lnt[:, 0:1, :])
        swait()
        BAR()
        return nc

    sc.activation(out=wnrm[:, :], in_=wraw[:, :], func=AF.Exp, bias=bis[:, 4:5])
    BAR()
    v.tensor_scalar(wnrm[:, :], wnrm[:, :], bis[:, 5:6], None, op0=AL.mult)
    BAR()

    xgB = [xg4, xg4b]
    ffn_gq = []
    ffn_tq = []
    ydma_cnt = []
    for m4 in range(4):
        g.indirect_dma_start(
            out=xgB[0][:, m4, :], out_offset=None, in_=ag_x_out.ap(),
            in_offset=IndirectOffsetOnAxis(FL[:, m4:m4 + 1], axis=0),
        ).then_inc(qg, 16)
        cg[0] += 16
    ffn_gq.append(cg[0])
    tv0 = eTV.n
    vt0 = eVT.n
    ts0 = eTS.n
    xgt_free = []
    for G in range(4):
        b = G % 2
        sy.wait_ge(qg, ffn_gq[G])
        if G >= 1:
            sy.wait_ge(eTV.sem, xgt_free[G - 1])
        for m4 in range(4):
            sy.dma_start_transpose(
                xgt[:, m4, :, :],
                xgB[b][:, m4, :].rearrange("p (a q) -> p a q", a=8)
            ).then_inc(qs, 16)
            cs[0] += 16
        ffn_tq.append(cs[0])
        if G < 3:
            if G >= 1:
                g.wait_ge(qs, ffn_tq[G - 1])
            for m4 in range(4):
                gi = 4 * (G + 1) + m4
                g.indirect_dma_start(
                    out=xgB[(G + 1) % 2][:, m4, :], out_offset=None,
                    in_=ag_x_out.ap(),
                    in_offset=IndirectOffsetOnAxis(FL[:, gi:gi + 1], axis=0),
                ).then_inc(qg, 16)
                cg[0] += 16
            ffn_gq.append(cg[0])
        te.wait_ge(qs, ffn_tq[G])
        if G >= 1:
            te.wait_ge(eVD.sem, eVD.n)
        last_w3 = None
        for hc in range(16):
            hcg = 16 * G + hc
            pA = pjA[:, :] if hc % 2 == 0 else pyA[:, 0:512]
            pB = pjB[:, :] if hc % 2 == 0 else pyA[:, 512:1024]
            if hcg >= 2:
                te.wait_ge(eVT.sem, vt0 + hcg - 1)
            r = None
            for a in range(8):
                r = te.matmul(pA, w1v[:, a, hc * 128:(hc + 1) * 128],
                              xgt[:, :, a, :], start=(a == 0), stop=(a == 7))
            eTS.inc(r)
            ts_cnt = eTS.n
            for a in range(8):
                r = te.matmul(pB, w3v[:, a, hc * 128:(hc + 1) * 128],
                              xgt[:, :, a, :], start=(a == 0), stop=(a == 7))
            eTV.inc(r)
            last_w3 = eTV.n
            sc.wait_ge(eTS.sem, ts_cnt)
            if hcg >= 1:
                sc.wait_ge(eVT.sem, vt0 + hcg)
            r = sc.activation(out=scr512[:, :], in_=pA, func=AF.Silu)
            eSV.inc(r)
            eSV.wait(v)
            v.wait_ge(eTV.sem, last_w3)
            r = v.tensor_mul(hpr[:, hc, :], scr512[:, :], pB)
            eVT.inc(r)
        xgt_free.append(last_w3)
        for d in range(4):
            py = pyB if d % 2 == 0 else pyA
            gi = 4 * G + d
            if G == 0 and d == 0:
                te.wait_ge(qg, c_w2v)
            if d <= 1:
                te.wait_ge(eVT.sem, vt0 + 16 * G + 16)
            if 4 * G + d >= 2:
                te.wait_ge(eVD.sem, eVD.n - 2 + 1)
            r = None
            for hc in range(16):
                r = mm(py, hpr[:, hc, d * 128:(d + 1) * 128], w2v[:, hc, :],
                       (hc == 0), (hc == 15), C)
            eTV.inc(r)
            v.wait_ge(eTV.sem, eTV.n)
            if len(ydma_cnt) >= 2:
                v.wait_ge(qs, ydma_cnt[-2])
            r = v.tensor_scalar(yst[:, d % 2, :], py[:, :], wnrm[:, gi:gi + 1],
                                None, op0=AL.mult)
            eVD.inc(r)
            eVD.wait(sy)
            sdma(ag_y_in.ap()[128 * gi:128 * gi + 128, :], yst[:, d % 2, :])
            ydma_cnt.append(cs[0])
        g.wait_ge(qs, cs[0])
        coll(GRP8,
             [ag_y_in.ap()[512 * G:512 * (G + 1), :].opt()],
             [y_all.ap()[4096 * G:4096 * (G + 1), :].opt()])
    swait()
    BAR()

    v.tensor_scalar(mskF[:, :], iocf[:, :], cnts[:, 0:1], None, op0=AL.is_lt)
    BAR()
    v.scalar_tensor_tensor(out=CT[:, :], in0=CT[:, :], scalar=1.0, in1=mskF[:, :],
                           op0=AL.add, op1=AL.mult)
    BAR()
    v.tensor_scalar_add(CT[:, :], CT[:, :], -1.0)
    BAR()
    v.tensor_scalar(junk[:, :], iocf[:, :], obase[:, 0:1], None, op0=AL.add)
    BAR()
    v.scalar_tensor_tensor(out=junk[:, :], in0=junk[:, :], scalar=2048.0,
                           in1=CT[:, :], op0=AL.mult, op1=AL.add)
    BAR()
    v.tensor_scalar(junk[:, :], junk[:, :], kv0[:, 0:1], None, op0=AL.subtract)
    BAR()
    v.tensor_scalar(msI[:, :], CT[:, :], kv0[:, 0:1], None, op0=AL.is_ge)
    v.tensor_scalar(DmS[:, :], CT[:, :], kv0[:, 1:2], None, op0=AL.is_lt)
    BAR()
    v.tensor_tensor(msI[:, :], msI[:, :], DmS[:, :], op=AL.bitwise_and)
    BAR()
    v.tensor_copy(mskF[:, :], msI[:, :])
    BAR()
    v.scalar_tensor_tensor(out=Vb2[:, :], in0=junk[:, :], scalar=1.0, in1=mskF[:, :],
                           op0=AL.add, op1=AL.mult)
    v.reduce_sum(cnt2[:, :], mskF[:, :], axis=AX)
    v.tensor_tensor_scan(out=junk[:, :], data0=mskF[:, :], data1=zeros2[:, 0:1024],
                         initial=0.0, op0=AL.add, op1=AL.add)
    BAR()
    v.tensor_scalar_add(Vb2[:, :], Vb2[:, :], -1.0)
    v.tensor_sub(junk[:, :], iocf[:, :], junk[:, :])
    BAR()
    v.tensor_scalar_add(junk[:, :], junk[:, :], 1.0)
    BAR()
    v.tensor_mul(junk[:, :], junk[:, :], mskF[:, :])
    BAR()
    v.tensor_copy(Vb[:, :], Vb2[:, :])
    v.tensor_copy(Db2[:, :], junk[:, :])
    BAR()
    v.tensor_copy(Db[:, :], Db2[:, :])
    BAR()
    CB, _ = butterfly(Vb, Vb2, Db, Db2)
    v.tensor_copy(Db2[:, 0:C2], CB[:, 0:C2])
    BAR()
    v.tensor_single_scalar(srcI[:, :], Db2[:, 0:C2], 11.0, op=AL.logical_shift_right)
    v.tensor_single_scalar(dstI[:, :], Db2[:, 0:C2], 2047.0, op=AL.bitwise_and)
    v.tensor_scalar(msI[:, 0:C2], iocf[:, 0:C2], cnt2[:, 0:1], None, op0=AL.is_lt)
    v.tensor_copy(lnt[:, 0, 0:384], zeros2[:, 0:384])
    v.tensor_single_scalar(DmS[:, 0:C2], srcI[:, :], 15.0, op=AL.bitwise_and)
    v.tensor_single_scalar(DmS[:, C2:2 * C2], srcI[:, :], 4.0,
                           op=AL.logical_shift_right)
    v.tensor_copy(lnt[:, 0, 0:384], zeros2[:, 0:384])
    v.tensor_single_scalar(DmS[:, 2 * C2:3 * C2], DmS[:, 0:C2], 2.0,
                           op=AL.logical_shift_right)
    v.tensor_single_scalar(DmS[:, 0:C2], DmS[:, 0:C2], 3.0, op=AL.bitwise_and)
    v.tensor_copy(lnt[:, 0, 0:384], zeros2[:, 0:384])
    v.tensor_single_scalar(DmS[:, 2 * C2:3 * C2], DmS[:, 2 * C2:3 * C2], 12.0,
                           op=AL.logical_shift_left)
    v.tensor_single_scalar(DmS[:, 0:C2], DmS[:, 0:C2], 7.0,
                           op=AL.logical_shift_left)
    v.tensor_copy(lnt[:, 0, 0:384], zeros2[:, 0:384])
    v.tensor_tensor(srcI[:, :], DmS[:, C2:2 * C2], DmS[:, 0:C2],
                    op=AL.bitwise_or)
    v.tensor_copy(lnt[:, 0, 0:384], zeros2[:, 0:384])
    v.tensor_tensor(srcI[:, :], srcI[:, :], DmS[:, 2 * C2:3 * C2],
                    op=AL.bitwise_or)
    v.tensor_copy(lnt[:, 0, 0:384], zeros2[:, 0:384])
    v.tensor_scalar(srcI[:, :], srcI[:, :], eab[:, 0:1], None, op0=AL.bitwise_or)
    v.tensor_copy(lnt[:, 0, 0:384], zeros2[:, 0:384])
    BAR()
    if dbg == "bwd":
        v.tensor_copy(lnt[:, 0, 67:99], msI[:, 0:C2])
        v.tensor_copy(lnt[:, 0, 99:131], srcI[:, :])
        v.tensor_copy(lnt[:, 0, 131:163], dstI[:, :])
        BAR()
    v.select(DmS[:, 0:C2], msI[:, 0:C2], srcI[:, :], sentA[:, :])
    v.select(DmS[:, C2:2 * C2], msI[:, 0:C2], dstI[:, :], sentB[:, :])
    BAR()
    v.tensor_copy(srcI[:, :], DmS[:, 0:C2])
    v.tensor_copy(dstI[:, :], DmS[:, C2:2 * C2])
    g.wait_ge(qc, cc[0])
    BAR()
    if dbg == "bwd":
        v.tensor_copy(lnt[:, 0, 0:32], srcI[:, :])
        v.tensor_copy(lnt[:, 0, 32:64], dstI[:, :])
        v.tensor_copy(lnt[:, 0, 64:65], cnt2[:, :])
        v.tensor_copy(lnt[:, 0, 65:66], obase[:, :])
        v.tensor_copy(lnt[:, 0, 66:67], cnts[:, :])
        v.memset(lnt[:, 0, 163:1024], 0.0)
        BAR()
        for m in range(16):
            sdma(out.rearrange("(m p) c -> p m c", p=128)[:, m:m + 1, :],
                 lnt[:, 0:1, :])
        swait()
        BAR()
        return nc
    sdma(di_d.ap()[:, :], dstI[:, :])
    swait()
    BAR()
    sdma(dstE[:, :, :], di_d.ap().rearrange("(e r) gi -> r e gi", e=8))
    swait()
    BAR()
    for gi in range(C2):
        g.indirect_dma_start(
            out=Gbig[:, gi, :], out_offset=None,
            in_=y_all.ap(),
            in_offset=IndirectOffsetOnAxis(srcI[:, gi:gi + 1], axis=0),
        ).then_inc(qg, 16)
        cg[0] += 16
    gwait()
    BAR()
    for e in range(8):
        for gi in range(C2):
            g.indirect_dma_start(
                out=acc_d.ap(),
                out_offset=IndirectOffsetOnAxis(dstE[0:16, e, gi:gi + 1], axis=0),
                in_=Gbig[16 * e:16 * e + 16, gi, :], in_offset=None,
                compute_op=AL.add,
            ).then_inc(qh, 16)
            ch[0] += 16
        g.wait_ge(qh, ch[0])
    BAR()

    ln2_base = (eVX.n, None, eSV.n, eVD.n)
    ld_cnt = []

    def ln2_load(mm):
        sdma(acct4[:, mm % 4:mm % 4 + 1, :],
             acc_d.ap()[0:T, :].rearrange("(m p) c -> p m c", p=128)[:, mm:mm + 1, :])
        sdma(x1t4[:, mm % 4:mm % 4 + 1, :],
             ag_x_in.ap().rearrange("(m p) c -> p m c", p=128)[:, mm:mm + 1, :])
        ld_cnt.append(cs[0])

    for mm in range(3):
        ln2_load(mm)
    for m in range(16):
        b4 = m % 4
        b = m % 2
        if m + 3 <= 15:
            if m >= 1:
                sy.wait_ge(eVX.sem, ln2_base[0] + m)
            ln2_load(m + 3)
        v.wait_ge(qs, ld_cnt[m])
        r = v.scalar_tensor_tensor(out=lnt[:, b, :], in0=x1t4[:, b4, :], scalar=1.0,
                                   in1=acct4[:, b4, :], op0=AL.mult, op1=AL.add,
                                   accum_out=st1[:, m:m + 1])
        eVX.inc(r)
        v.scalar_tensor_tensor(out=lnsq[:, :], in0=lnt[:, b, :], scalar=1.0,
                               in1=lnt[:, b, :], op0=AL.mult, op1=AL.mult,
                               accum_out=st2[:, m:m + 1])
        if m >= 2:
            eSV.wait(v, ln2_base[2] + 2 * (m - 1))
        r = v.tensor_scalar_mul(st1[:, m:m + 1], st1[:, m:m + 1], 1.0 / C)
        eVS.inc(r)
        eVS.wait(sc)
        r = sc.activation(out=stt[:, 4 + b:5 + b], in_=st1[:, m:m + 1],
                          func=AF.Square)
        eSV.inc(r)
        eSV.wait(v)
        r = v.tensor_scalar(stt[:, b:b + 1], st2[:, m:m + 1], 1.0 / C,
                            stt[:, 4 + b:5 + b], op0=AL.mult, op1=AL.subtract)
        eVS.inc(r)
        eVS.wait(sc)
        r = sc.activation(out=stt[:, 2 + b:3 + b], in_=stt[:, b:b + 1],
                          func=AF.Sqrt, bias=epsv[:, 0:1])
        eSV.inc(r)
        eSV.wait(v)
        v.reciprocal(stt[:, 6 + b:7 + b], stt[:, 2 + b:3 + b])
        v.tensor_copy(lnsq[:, 0:384], zeros2[:, 0:384])
        if m >= 2:
            v.wait_ge(qo, 16 * (m - 1))
        r = v.tensor_scalar(outt[:, b, :], lnt[:, b, :], st1[:, m:m + 1],
                            stt[:, 6 + b:7 + b], op0=AL.subtract, op1=AL.mult)
        eVD.inc(r)
        eVD.wait(sy)
        sy.dma_start(out=out.rearrange("(m p) c -> p m c", p=128)[:, m:m + 1, :],
                     in_=outt[:, b:b + 1, :]).then_inc(qo, 16)
        co[0] += 16
    sy.wait_ge(qo, co[0])
    BAR()
    return nc


def _host_prep(inputs):
    x = np.asarray(inputs["L0_attn_shot_embeddings"], np.float32).reshape(16384, C)
    Wd = np.ascontiguousarray(np.asarray(inputs["L0_attn_Wd"], np.float32))
    Wi = np.ascontiguousarray(np.asarray(inputs["L0_attn_Wi"], np.float32))
    Wo = np.ascontiguousarray(np.asarray(inputs["L0_attn_Wo"], np.float32))
    Ws = np.ascontiguousarray(np.asarray(inputs["L0_attn_Ws"], np.float32))
    bd = np.ascontiguousarray(np.asarray(inputs["L0_attn_bd"], np.float32))
    bi = np.ascontiguousarray(np.asarray(inputs["L0_attn_bi"], np.float32))
    alog = np.ascontiguousarray(np.asarray(inputs["L0_attn_A_log"], np.float32))
    bo = np.ascontiguousarray(np.asarray(inputs["L0_attn_bo"], np.float32))
    bs = np.ascontiguousarray(np.asarray(inputs["L0_attn_bs"], np.float32))
    Wg = np.asarray(inputs["L0_ffn_Wg"], np.float32)
    W1 = np.asarray(inputs["L0_ffn_W1"], np.float32)
    W2 = np.asarray(inputs["L0_ffn_W2"], np.float32)
    W3 = np.asarray(inputs["L0_ffn_W3"], np.float32)

    ext = np.zeros((8, 128, 144), np.float32)
    for e in range(8):
        for j in range(8):
            ext[j, :, 16 + 16 * e] = Wg[e, 128 * j:128 * (j + 1)]

    maps = []
    for k in range(NCORE):
        xk = np.ascontiguousarray(x[2048 * k:2048 * (k + 1)])
        kv = np.zeros((128, 2), np.float32)
        kv[:, 0] = 2048.0 * k
        kv[:, 1] = 2048.0 * (k + 1)
        eabv = np.zeros((128, 1), np.int32)
        for p in range(128):
            eabv[p, 0] = 512 * (p // 16)
        ohk16v = np.zeros((1, 128), np.float32)
        ohk16v[0, 16 * k:16 * (k + 1)] = 1.0
        row16v = np.arange(16 * k, 16 * k + 16, dtype=np.int32).reshape(1, 16)
        rot = np.array([[(k + e) % 8 for e in range(8)]], np.int32)
        ohpv = np.zeros((128, 8), np.float32)
        pos = k % 4
        ohpv[:, 2 * pos] = 1.0
        ohpv[:, 2 * pos + 1] = 1.0
        ohkv = np.zeros((128, 8), np.float32)
        ohkv[:, k] = 1.0
        maps.append({
            "x_tok": xk,
            "x_feat": np.ascontiguousarray(xk.T),
            "wd": Wd, "wi": Wi, "wo": Wo, "ws": Ws,
            "bdd": np.ascontiguousarray(bd.reshape(2, 2, 128).transpose(2, 0, 1)),
            "bii": np.ascontiguousarray(bi.reshape(2, 2, 128).transpose(2, 0, 1)),
            "alog": np.ascontiguousarray(alog.reshape(2, 2, 128).transpose(2, 0, 1)),
            "bo": bo, "bss": bs,
            "ext": ext,
            "wgt": np.ascontiguousarray(Wg.T),
            "w1t": np.ascontiguousarray(W1[k].T),
            "w3t": np.ascontiguousarray(W3[k].T),
            "w2t": np.ascontiguousarray(W2[k].T),
            "kvi": kv, "eabi": eabv, "rot": rot, "ohpi": ohpv, "ohki": ohkv,
            "ohk16": ohk16v, "row16": row16v,
        })
    return maps


def kernel(**inputs):
    maps = _host_prep(inputs)
    nc = build(dbg=DBG_STAGE)
    res = run_bass_kernel_spmd(nc, maps, core_ids=list(range(NCORE)))
    outs = np.concatenate([np.asarray(r["out"]) for r in res.results], axis=0)
    return outs.reshape(2, 8192, 1024).astype(np.float32)
